# version 04
# baseline (speedup 1.0000x reference)
import numpy as np
from contextlib import ExitStack

import concourse.bass as bass
import concourse.mybir as mybir
from concourse.bass_utils import run_bass_kernel_spmd

F32 = mybir.dt.float32
BF16 = mybir.dt.bfloat16
U16 = mybir.dt.uint16
AF = mybir.ActivationFunctionType
ALU = mybir.AluOpType
AX = mybir.AxisListType.X

NCORES = 8
D = 1024
DC = 8
T = 2048
J = 16
NG = 32
KC = 4
NB = 4
SCALE = 1.0 / 32.0
EPS = 1e-5

SEM_NAMES = [
    "init", "s_w", "s_x", "s_xfree", "lnA", "lnB", "lnC", "lnD", "lnE", "ln_h", "tr_p", "tr_ev",
    "a_mm", "a_nmm", "a_qev", "a_kev", "a_vev", "a_nev", "a_sq", "a_kst", "a_vst", "a_nst",
    "cc", "b_nld", "nm", "kv_ld", "pe_s", "pe_p", "pe_o", "pe_e", "pe_g", "rcp",
    "c_x", "c_xf", "c_sp", "ag3w", "ag3r", "cA", "cB", "cC", "cD", "cE", "c_h",
    "c_trp", "c_trev", "bnd", "bnd5",
    "d_mix", "d_rv", "d_kmm", "d_krel", "d_kact", "d_rmm", "d_rev", "d_vmm", "d_vev",
    "d_oev", "d_xp", "y_st", "dbgs", "s_x1", "a_kst1", "a_vst1", "kv_ld1", "d_xp1", "y_st1",
    "start", "fin",
]
BIG = 1 << 30


def build(dbg=False, stop_after=None):
    nc = bass.Bass("TRN2", target_bir_lowering=False)

    x_sh = nc.dram_tensor("x_sh", [T, D], F32, kind="ExternalInput")
    wqT = nc.dram_tensor("wqT", [D, D], F32, kind="ExternalInput")
    wkT = nc.dram_tensor("wkT", [D, D], F32, kind="ExternalInput")
    wvT = nc.dram_tensor("wvT", [D, D], F32, kind="ExternalInput")
    wrT = nc.dram_tensor("wrT", [D, D], F32, kind="ExternalInput")
    wfkT = nc.dram_tensor("wfkT", [D, D], F32, kind="ExternalInput")
    wfvT = nc.dram_tensor("wfvT", [D, D], F32, kind="ExternalInput")
    bq_d = nc.dram_tensor("bq", [128, DC], F32, kind="ExternalInput")
    bk_d = nc.dram_tensor("bk", [128, DC], F32, kind="ExternalInput")
    bv_d = nc.dram_tensor("bv", [1, D], F32, kind="ExternalInput")
    mur_d = nc.dram_tensor("mur", [128, DC], F32, kind="ExternalInput")
    muk_d = nc.dram_tensor("muk", [128, DC], F32, kind="ExternalInput")
    ident_d = nc.dram_tensor("ident", [128, 128], F32, kind="ExternalInput")
    bsel_d = nc.dram_tensor("bsel", [NCORES, 1], F32, kind="ExternalInput")
    y = nc.dram_tensor("y", [T, D], F32, kind="ExternalOutput")

    k_ag_in = nc.dram_tensor("k_ag_in", [4, DC, 128, 512], BF16, kind="Internal")
    k_ag_out = nc.dram_tensor("k_ag_out", [NG, DC, 128, 512], BF16, kind="Internal")
    v_ag_in = nc.dram_tensor("v_ag_in", [T, D], BF16, kind="Internal")
    v_ag_out = nc.dram_tensor("v_ag_out", [NCORES * T, D], BF16, kind="Internal")
    n_ag_in = nc.dram_tensor("n_ag_in", [1, T], BF16, kind="Internal")
    n_ag_out = nc.dram_tensor("n_ag_out", [128, 128], BF16, kind="Internal")
    xb_ag_in = nc.dram_tensor("xb_ag_in", [1, D], F32, kind="Internal")
    xb_ag_out = nc.dram_tensor("xb_ag_out", [NCORES, D], F32, kind="Internal")
    xpost_d = nc.dram_tensor("xpost_d", [T, D], F32, kind="Internal")
    if dbg:
        dbg_oacc = nc.dram_tensor("dbg_oacc", [128, J * D], F32, kind="ExternalOutput")
        dbg_rl = nc.dram_tensor("dbg_rl", [128, 16], F32, kind="ExternalOutput")
        dbg_negm = nc.dram_tensor("dbg_negm", [128, 1], F32, kind="ExternalOutput")
        dbg_h2 = nc.dram_tensor("dbg_h2", [128, DC * (T + 1)], U16, kind="ExternalOutput")
        dbg_xp = nc.dram_tensor("dbg_xp", [128, J * D], F32, kind="ExternalOutput")
        dbg_misc = nc.dram_tensor("dbg_misc", [1, 8], F32, kind="ExternalOutput")
        dbg_qn = nc.dram_tensor("dbg_qn", [1, T], U16, kind="ExternalOutput")
        dbg_kn = nc.dram_tensor("dbg_kn", [1, T], U16, kind="ExternalOutput")
        dbg_nrm = nc.dram_tensor("dbg_nrm", [128, 128], U16, kind="ExternalOutput")
        dbg_kag = nc.dram_tensor("dbg_kag", [128, 512], U16, kind="ExternalOutput")
        dbg_vag = nc.dram_tensor("dbg_vag", [128, 1024], U16, kind="ExternalOutput")
        dbg_h1 = nc.dram_tensor("dbg_h1", [128, DC * T], U16, kind="ExternalOutput")
        dbg_q = nc.dram_tensor("dbg_q", [128, DC * T], U16, kind="ExternalOutput")
        dbg_hbf = nc.dram_tensor("dbg_hbf", [128, 2 * D], U16, kind="ExternalOutput")

    RG = [list(range(NCORES))]

    root = ExitStack()
    E = root.enter_context
    ident_b = E(nc.sbuf_tensor("ident_b", [128, 128], BF16))
    ones_col = E(nc.sbuf_tensor("ones_col", [128, 1], BF16))
    ones_row = E(nc.sbuf_tensor("ones_row", [1, 128], BF16))
    bq_t = E(nc.sbuf_tensor("bq_t", [128, DC], F32))
    bk_t = E(nc.sbuf_tensor("bk_t", [128, DC], F32))
    mur_t = E(nc.sbuf_tensor("mur_t", [128, DC], F32))
    muk_t = E(nc.sbuf_tensor("muk_t", [128, DC], F32))
    bsel_t = E(nc.sbuf_tensor("bsel_t", [NCORES, 1], F32))
    neg_m = E(nc.sbuf_tensor("neg_m", [128, 1], F32))
    qmax2 = E(nc.sbuf_tensor("qmax2", [1, 1], F32))
    kmax2 = E(nc.sbuf_tensor("kmax2", [1, 1], F32))
    m2 = E(nc.sbuf_tensor("m2", [1, 1], F32))
    negm_s = E(nc.sbuf_tensor("negm_s", [1, 1], F32))
    negm_b = E(nc.sbuf_tensor("negm_b", [1, 1], BF16))
    nrm_all = E(nc.sbuf_tensor("nrm_all", [128, 128], BF16))
    knmax_c = E(nc.sbuf_tensor("knmax_c", [128, 1], BF16))
    recip_l = E(nc.sbuf_tensor("recip_l", [128, 16], F32))
    mneg = [E(nc.sbuf_tensor(f"mneg{i}", [128, 1], F32)) for i in range(2)]
    v_raw = [E(nc.sbuf_tensor(f"v_raw{i}", [128, 1], F32)) for i in range(2)]
    sq_in = [E(nc.sbuf_tensor(f"sq_in{i}", [128, 1], F32)) for i in range(2)]
    std_t = [E(nc.sbuf_tensor(f"std_t{i}", [128, 1], F32)) for i in range(2)]
    inv_t = [E(nc.sbuf_tensor(f"inv_t{i}", [128, 1], F32)) for i in range(2)]
    recip_g = E(nc.sbuf_tensor("recip_g", [128, 16], F32))
    negm_s2 = E(nc.sbuf_tensor("negm_s2", [1, 1], F32))
    bm2 = E(nc.sbuf_tensor("bm2", [1, 1], F32))
    btmpv = E(nc.sbuf_tensor("btmpv", [1, 1], F32))
    bnmi2 = E(nc.sbuf_tensor("bnmi2", [1, 1], F32))

    qctx = ExitStack()
    q_fm = qctx.enter_context(nc.sbuf_tensor("q_fm", [128, DC, T], BF16, side="right"))

    sems = {name: E(nc.semaphore(name)) for name in SEM_NAMES}
    S = type("S", (), sems)
    _nums = sorted(h.num for h in sems.values())
    assert _nums[-1] - _nums[0] == len(_nums) - 1, _nums
    sem_lo, sem_hi = _nums[0], _nums[-1] + 1

    ctxA = ExitStack()
    EA = ctxA.enter_context
    wq_sb = EA(nc.sbuf_tensor("wq_sb", [128, DC, D], BF16))
    wk_sb = EA(nc.sbuf_tensor("wk_sb", [128, DC, D], BF16))
    wv_sb = EA(nc.sbuf_tensor("wv_sb", [128, DC, D], BF16))
    h1_fm = EA(nc.sbuf_tensor("h1_fm", [128, DC, T], BF16))
    xstag = EA(nc.sbuf_tensor("xstag", [128, 2, D], F32))
    t_stag = EA(nc.sbuf_tensor("t_stag", [128, D], F32))
    tsq_st = EA(nc.sbuf_tensor("tsq_st", [128, D], BF16))
    h_bf = EA(nc.sbuf_tensor("h_bf", [128, 2, D], BF16))
    kstag = EA(nc.sbuf_tensor("kstag", [128, 2, T], BF16))
    vstag = EA(nc.sbuf_tensor("vstag", [128, 2, D], BF16))
    sq_st = EA(nc.sbuf_tensor("sq_st", [128, 2, T], BF16))
    qn_sb = EA(nc.sbuf_tensor("qn_sb", [1, T], BF16))
    kn_sb = EA(nc.sbuf_tensor("kn_sb", [1, T], BF16))
    bv_row = EA(nc.sbuf_tensor("bv_row", [1, D], BF16))

    with ExitStack() as pctx, nc.Block() as blk:
        trp = [pctx.enter_context(nc.psum_tensor(f"trpA{i}", [128, 128], BF16)) for i in range(2)]

        @blk.sync
        def _(sync):
            sync.wait_ge(S.start, BIG)
            sync.dma_start(out=bq_t[:, :], in_=bq_d[:, :]).then_inc(S.s_w, 16)
            sync.dma_start(out=bk_t[:, :], in_=bk_d[:, :]).then_inc(S.s_w, 16)
            sync.dma_start(out=mur_t[:, :], in_=mur_d[:, :]).then_inc(S.s_w, 16)
            sync.dma_start(out=muk_t[:, :], in_=muk_d[:, :]).then_inc(S.s_w, 16)
            sync.dma_start(out=bsel_t[:, :], in_=bsel_d[:, :]).then_inc(S.s_w, 16)
            for j in range(J):
                if j >= 2:
                    sync.wait_ge(S.s_xfree, j - 1)
                sync.dma_start(out=xstag[:, j % 2, :], in_=x_sh[j * 128:(j + 1) * 128, :]).then_inc(
                    S.s_x if j % 2 == 0 else S.s_x1, 16)

        @blk.gpsimd
        def _(gpsimd):
            gpsimd.sem_clear(range(sem_lo, sem_hi))
            gpsimd.sem_inc(S.start, BIG)
            gpsimd.dma_start(out=ident_b[:, :], in_=ident_d[:, :]).then_inc(S.init, 16)
            gpsimd.dma_start(out=bv_row[:, :], in_=bv_d[:, :]).then_inc(S.init, 16)
            gpsimd.memset(ones_col[:, :], 1.0)
            gpsimd.memset(ones_row[:, :], 1.0).then_inc(S.init, 1)
            gpsimd.dma_start(out=wq_sb.ap(),
                             in_=wqT.ap().rearrange("(c p) n -> p c n", p=128)).then_inc(S.init, 16)
            gpsimd.dma_start(out=wk_sb.ap(),
                             in_=wkT.ap().rearrange("(c p) n -> p c n", p=128)).then_inc(S.init, 16)
            gpsimd.dma_start(out=wv_sb.ap(),
                             in_=wvT.ap().rearrange("(c p) n -> p c n", p=128)).then_inc(S.init, 16)

        @blk.vector
        def _(vector):
            vector.wait_ge(S.start, BIG)
            for j in range(J):
                r = j % 2
                vector.wait_ge(S.s_x if j % 2 == 0 else S.s_x1, 16 * (j // 2 + 1))
                if j >= 2:
                    vector.wait_ge(S.s_xfree, j - 1)
                vector.reduce_sum(mneg[r][:, :], xstag[:, r, :], axis=AX, negate=True).then_inc(S.lnA, 1)
                vector.wait_ge(S.lnB, j + 1)
                if j >= 2:
                    vector.wait_ge(S.lnD, j - 1)
                vector.tensor_scalar(sq_in[r][:, :], v_raw[r][:, :], 1.0 / D, D * D * EPS, ALU.mult, ALU.add).then_inc(S.lnC, 1)
                vector.wait_ge(S.lnD, j + 1)
                if j >= 2:
                    vector.wait_ge(S.ln_h, j - 1)
                vector.reciprocal(inv_t[r][:, :], std_t[r][:, :]).then_inc(S.lnE, 1)
                for dc in range(DC):
                    cnt = j * DC + dc
                    vector.wait_ge(S.tr_p, cnt + 1)
                    vector.tensor_copy(h1_fm[:, dc, j * 128:(j + 1) * 128], trp[cnt % 2][:, :]).then_inc(S.tr_ev, 1)

        @blk.scalar
        def _(scalar):
            scalar.wait_ge(S.start, BIG)
            for j in range(J):
                r = j % 2
                scalar.wait_ge(S.lnA, j + 1)
                scalar.activation(t_stag[:, :], xstag[:, r, :], AF.Identity,
                                  bias=mneg[r][:, :], scale=float(D)).then_inc(S.s_xfree, 1)
                if j >= 2:
                    scalar.wait_ge(S.lnC, j - 1)
                scalar.activation(tsq_st[:, :], t_stag[:, :], AF.Square,
                                  accum_out=v_raw[r][:, :]).then_inc(S.lnB, 1)
                scalar.wait_ge(S.lnC, j + 1)
                if j >= 2:
                    scalar.wait_ge(S.lnE, j - 1)
                scalar.activation(std_t[r][:, :], sq_in[r][:, :], AF.Sqrt).then_inc(S.lnD, 1)
                scalar.wait_ge(S.lnE, j + 1)
                if j >= 2:
                    scalar.wait_ge(S.tr_p, 8 * (j - 1))
                scalar.activation(h_bf[:, r, :], t_stag[:, :], AF.Copy, scale=inv_t[r][:, :]).then_inc(S.ln_h, 1)

        @blk.tensor
        def _(tensor):
            tensor.wait_ge(S.start, BIG)
            tensor.wait_ge(S.init, 81)
            for j in range(J):
                tensor.wait_ge(S.ln_h, j + 1)
                for dc in range(DC):
                    cnt = j * DC + dc
                    if cnt >= 2:
                        tensor.wait_ge(S.tr_ev, cnt - 1)
                    tensor.transpose(trp[cnt % 2][:, :], h_bf[:, j % 2, dc * 128:(dc + 1) * 128],
                                     ident_b[:, :]).then_inc(S.tr_p, 1)

    with ExitStack() as pctx, nc.Block() as blk:
        qkv_ps = [pctx.enter_context(nc.psum_tensor(f"qkvA{i}", [128, D], F32)) for i in range(2)]
        nrm_ps = pctx.enter_context(nc.psum_tensor("nrmA", [1, T], F32))

        @blk.tensor
        def _(tensor):
            tensor.wait_ge(S.tr_ev, J * DC)
            tensor.wait_ge(S.init, 81)
            for c in range(DC):
                for b in range(NB):
                    n = c * NB + b
                    ps = qkv_ps[n % 2]
                    if n >= 2:
                        tensor.wait_ge(S.a_qev, n - 1)
                    for dc in range(DC):
                        mm = tensor.matmul(ps[:, 0:512], wq_sb[:, dc, c * 128:(c + 1) * 128],
                                           h1_fm[:, dc, b * 512:(b + 1) * 512],
                                           start=(dc == 0), stop=(dc == DC - 1))
                        if dc == DC - 1:
                            mm.then_inc(S.a_mm, 1)
                tensor.wait_ge(S.a_sq, c + 1)
                for b in range(NB):
                    mm = tensor.matmul(nrm_ps[0:1, b * 512:(b + 1) * 512], ones_col[:, :],
                                       sq_st[:, c % 2, b * 512:(b + 1) * 512],
                                       start=(c == 0), stop=(c == DC - 1))
                    if b == NB - 1:
                        mm.then_inc(S.a_nmm, 1)
            for c in range(DC):
                for b in range(NB):
                    n = c * NB + b
                    ps = qkv_ps[n % 2]
                    if n == 0:
                        tensor.wait_ge(S.a_qev, 31)
                    elif n == 1:
                        tensor.wait_ge(S.a_qev, 32)
                    else:
                        tensor.wait_ge(S.a_kev, n - 1)
                    for dc in range(DC):
                        mm = tensor.matmul(ps[:, 0:512], wk_sb[:, dc, c * 128:(c + 1) * 128],
                                           h1_fm[:, dc, b * 512:(b + 1) * 512],
                                           start=(dc == 0), stop=(dc == DC - 1))
                        if dc == DC - 1:
                            mm.then_inc(S.a_mm, 1)
                tensor.wait_ge(S.a_sq, DC + c + 1)
                if c == 0:
                    tensor.wait_ge(S.a_nev, 1)
                for b in range(NB):
                    mm = tensor.matmul(nrm_ps[0:1, b * 512:(b + 1) * 512], ones_col[:, :],
                                       sq_st[:, c % 2, b * 512:(b + 1) * 512],
                                       start=(c == 0), stop=(c == DC - 1))
                    if b == NB - 1:
                        mm.then_inc(S.a_nmm, 1)
            for t in range(J):
                ps = qkv_ps[t % 2]
                if t == 0:
                    tensor.wait_ge(S.a_kev, 31)
                elif t == 1:
                    tensor.wait_ge(S.a_kev, 32)
                else:
                    tensor.wait_ge(S.a_vev, t - 1)
                for dc in range(DC):
                    for half in range(2):
                        tensor.matmul(ps[:, half * 512:(half + 1) * 512],
                                      h1_fm[:, dc, t * 128:(t + 1) * 128],
                                      wv_sb[:, dc, half * 512:(half + 1) * 512],
                                      start=(dc == 0), stop=False)
                for half in range(2):
                    mm = tensor.matmul(ps[:, half * 512:(half + 1) * 512], ones_row[:, :],
                                       bv_row[:, half * 512:(half + 1) * 512], start=False, stop=True)
                    if half == 1:
                        mm.then_inc(S.a_mm, 1)

        @blk.scalar
        def _(scalar):
            scalar.wait_ge(S.s_w, 80)
            for c in range(DC):
                for b in range(NB):
                    n = c * NB + b
                    scalar.wait_ge(S.a_mm, n + 1)
                    scalar.activation(q_fm[:, c, b * 512:(b + 1) * 512], qkv_ps[n % 2][:, 0:512],
                                      AF.Identity, bias=bq_t[:, c:c + 1]).then_inc(S.a_qev, 1)
                if c >= 2:
                    scalar.wait_ge(S.a_nmm, c - 1)
                scalar.activation(sq_st[:, c % 2, :], q_fm[:, c, :], AF.Square).then_inc(S.a_sq, 1)
            scalar.wait_ge(S.a_nmm, DC)
            scalar.activation(qn_sb[:, :], nrm_ps[0:1, :], AF.Copy).then_inc(S.a_nev, 1)
            for c in range(DC):
                if c >= 2:
                    scalar.wait_ge(S.a_kst if c % 2 == 0 else S.a_kst1, 64 * (c // 2))
                for b in range(NB):
                    n = 32 + c * NB + b
                    scalar.wait_ge(S.a_mm, n + 1)
                    scalar.activation(kstag[:, c % 2, b * 512:(b + 1) * 512], qkv_ps[(n - 32) % 2][:, 0:512],
                                      AF.Identity, bias=bk_t[:, c:c + 1]).then_inc(S.a_kev, 1)
                scalar.wait_ge(S.a_nmm, 7 + c)
                scalar.activation(sq_st[:, c % 2, :], kstag[:, c % 2, :], AF.Square).then_inc(S.a_sq, 1)
            scalar.wait_ge(S.a_nmm, 2 * DC)
            scalar.activation(kn_sb[:, :], nrm_ps[0:1, :], AF.Copy).then_inc(S.a_nst, 1)
            if dbg:
                scalar.dma_start(out=dbg_qn[:, :], in_=qn_sb.ap().bitcast(U16)).then_inc(S.dbgs, 16)
                scalar.dma_start(out=dbg_kn[:, :], in_=kn_sb.ap().bitcast(U16)).then_inc(S.dbgs, 16)

        @blk.vector
        def _(vector):
            for t in range(J):
                n = 64 + t
                if t >= 2:
                    vector.wait_ge(S.a_vst if t % 2 == 0 else S.a_vst1, 16 * (t // 2))
                vector.wait_ge(S.a_mm, n + 1)
                vector.tensor_copy(vstag[:, t % 2, :], qkv_ps[t % 2][:, :]).then_inc(S.a_vev, 1)
            vector.wait_ge(S.a_nev, 1)
            vector.reduce_max(qmax2[:, :], qn_sb[:, :], axis=AX)

        @blk.sync
        def _(sync):
            for c in range(DC):
                sync.wait_ge(S.a_kev, NB * (c + 1))
                for gl in range(4):
                    sync.dma_start(out=k_ag_in[gl, c, :, :],
                                   in_=kstag[:, c % 2, gl * 512:(gl + 1) * 512]).then_inc(
                                       S.a_kst if c % 2 == 0 else S.a_kst1, 16)
            for t in range(J):
                sync.wait_ge(S.a_vev, t + 1)
                sync.dma_start(out=v_ag_in[t * 128:(t + 1) * 128, :], in_=vstag[:, t % 2, :]).then_inc(
                    S.a_vst if t % 2 == 0 else S.a_vst1, 16)
            sync.wait_ge(S.a_nst, 1)
            sync.dma_start(out=n_ag_in[:, :], in_=kn_sb[:, :]).then_inc(S.a_nst, 16)

        @blk.gpsimd
        def _(gpsimd):
            if dbg:
                gpsimd.wait_ge(S.tr_ev, J * DC)
                gpsimd.dma_start(out=dbg_hbf[:, :],
                                 in_=h_bf.ap().rearrange("p a b -> p (a b)").bitcast(U16)).then_inc(S.dbgs, 16)
                gpsimd.dma_start(out=dbg_h1[:, :],
                                 in_=h1_fm.ap().rearrange("p c n -> p (c n)").bitcast(U16)).then_inc(S.dbgs, 16)
                gpsimd.wait_ge(S.a_qev, 32)
                gpsimd.dma_start(out=dbg_q[:, :],
                                 in_=q_fm.ap().rearrange("p c n -> p (c n)").bitcast(U16)).then_inc(S.dbgs, 16)
            gpsimd.wait_ge(S.a_nst, 17)
            gpsimd.collective_compute("AllGather", ALU.bypass, replica_groups=RG,
                                      ins=[n_ag_in.ap().opt()], outs=[n_ag_out.ap().opt()]).then_inc(S.cc, 1)
            gpsimd.wait_ge(S.a_kst, 16 * 16)
            gpsimd.wait_ge(S.a_kst1, 16 * 16)
            gpsimd.collective_compute("AllGather", ALU.bypass, replica_groups=RG,
                                      ins=[k_ag_in.ap().opt()], outs=[k_ag_out.ap().opt()]).then_inc(S.cc, 1)
            gpsimd.wait_ge(S.a_vst, 16 * 8)
            gpsimd.wait_ge(S.a_vst1, 16 * 8)
            gpsimd.collective_compute("AllGather", ALU.bypass, replica_groups=RG,
                                      ins=[v_ag_in.ap().opt()], outs=[v_ag_out.ap().opt()]).then_inc(S.cc, 1)
            gpsimd.wait_ge(S.cc, 1)
            gpsimd.dma_start(out=nrm_all[:, :], in_=n_ag_out[:, :]).then_inc(S.b_nld, 16)

    ctxA.close()
    if stop_after == "A":
        root.close()
        return nc
    o_acc = E(nc.sbuf_tensor("o_acc", [128, J, D], F32))

    ctxB = ExitStack()
    EB = ctxB.enter_context
    kbuf = EB(nc.sbuf_tensor("kbuf", [128, 2, DC, 512], BF16))
    vbuf = EB(nc.sbuf_tensor("vbuf", [128, 2, KC, 1025], BF16))
    pt = [EB(nc.sbuf_tensor(f"pt{i}", [128, KC, T], BF16)) for i in range(2)]

    with ExitStack() as pctx, nc.Block() as blk:
        s_ps = [pctx.enter_context(nc.psum_tensor(f"sB{i}", [128, 512], F32)) for i in range(2)]
        o_ps = [pctx.enter_context(nc.psum_tensor(f"oB{i}", [128, D], F32)) for i in range(2)]
        l_ps = pctx.enter_context(nc.psum_tensor("l_ps", [128, 512], F32))
        nm_ps = pctx.enter_context(nc.psum_tensor("nmB", [128, 512], F32))

        @blk.gpsimd
        def _(gpsimd):
            gpsimd.memset(vbuf[:, :, :, 1024:1025], 1.0).then_inc(S.init, 1)
            gpsimd.wait_ge(S.nm, 3)
            gpsimd.tensor_mul(m2[:, :], kmax2[:, :], qmax2[:, :]).then_inc(S.nm, 1)
            gpsimd.wait_ge(S.nm, 5)
            gpsimd.tensor_scalar_mul(negm_s2[:, :], negm_s[:, :], -SCALE).then_inc(S.nm, 1)
            gpsimd.wait_ge(S.rcp, 1)
            gpsimd.tensor_copy(recip_g[:, :], recip_l[:, :]).then_inc(S.rcp, 1)

        @blk.vector
        def _(vector):
            vector.wait_ge(S.b_nld, 16)
            vector.reduce_max(knmax_c[:, :], nrm_all[:, :], axis=AX).then_inc(S.nm, 1)
            vector.wait_ge(S.nm, 2)
            vector.reduce_max(kmax2[:, :], nm_ps[0:1, 0:64].bitcast(BF16), axis=AX).then_inc(S.nm, 1)
            vector.wait_ge(S.nm, 8)
            vector.tensor_copy(neg_m[:, :], nm_ps[:, 128:129]).then_inc(S.nm, 1)
            n = 0
            for g in range(NG):
                for j in range(J):
                    vector.wait_ge(S.pe_o, n + 1)
                    if g == 0:
                        vector.tensor_copy(o_acc[:, j, :], o_ps[n % 2][:, :]).then_inc(S.pe_e, 1)
                    else:
                        vector.tensor_tensor(o_acc[:, j, :], o_acc[:, j, :], o_ps[n % 2][:, :],
                                             ALU.add).then_inc(S.pe_e, 1)
                    n += 1
            vector.wait_ge(S.pe_g, NG)
            vector.reciprocal(recip_l[:, :], l_ps[:, 0:16]).then_inc(S.rcp, 1)

        @blk.scalar
        def _(scalar):
            scalar.wait_ge(S.nm, 4)
            scalar.activation(negm_s[:, :], m2[:, :], AF.Sqrt).then_inc(S.nm, 1)
            scalar.wait_ge(S.nm, 6)
            scalar.activation(negm_b[:, :], negm_s2[:, :], AF.Copy).then_inc(S.nm, 1)
            n = 0
            for g in range(NG):
                if g == 0:
                    scalar.wait_ge(S.nm, 9)
                for kc in range(KC):
                    for b in range(NB):
                        scalar.wait_ge(S.pe_s, n + 1)
                        scalar.activation(pt[g % 2][:, kc, b * 512:(b + 1) * 512], s_ps[n % 2][:, :],
                                          AF.Exp, bias=neg_m[:, :], scale=SCALE).then_inc(S.pe_p, 1)
                        n += 1

        @blk.tensor
        def _(tensor):
            tensor.wait_ge(S.nm, 1)
            tensor.transpose(nm_ps[0:1, 0:64].bitcast(BF16), knmax_c[:, :], ident_b[:, :]).then_inc(S.nm, 1)
            tensor.wait_ge(S.nm, 7)
            tensor.matmul(nm_ps[:, 128:129], ones_row[:, :], negm_b[:, :], start=True, stop=True).then_inc(S.nm, 1)
            n_s = 0
            n_o = 0
            for g in range(NG + 1):
                if g < NG:
                    tensor.wait_ge(S.kv_ld if g % 2 == 0 else S.kv_ld1, 32 * (g // 2 + 1))
                    if g == 0:
                        tensor.wait_ge(S.init, 82)
                    for kc in range(KC):
                        for b in range(NB):
                            ps = s_ps[n_s % 2]
                            if n_s >= 2:
                                tensor.wait_ge(S.pe_p, n_s - 1)
                            for dc in range(DC):
                                mm = tensor.matmul(ps[:, :], kbuf[:, g % 2, dc, kc * 128:(kc + 1) * 128],
                                                   q_fm[:, dc, b * 512:(b + 1) * 512],
                                                   start=(dc == 0), stop=(dc == DC - 1))
                                if dc == DC - 1:
                                    mm.then_inc(S.pe_s, 1)
                            n_s += 1
                if g >= 1:
                    gp = g - 1
                    tensor.wait_ge(S.pe_p, gp * 16 + 16)
                    ptg = pt[gp % 2]
                    for j in range(J):
                        op = o_ps[n_o % 2]
                        if n_o >= 2:
                            tensor.wait_ge(S.pe_e, n_o - 1)
                        first_l = (gp == 0 and j == 0)
                        last_l = (gp == NG - 1 and j == J - 1)
                        for kc in range(KC):
                            stat = ptg[:, kc, j * 128:(j + 1) * 128]
                            tensor.matmul(op[:, 0:512], stat, vbuf[:, gp % 2, kc, 0:512],
                                          start=(kc == 0), stop=(kc == KC - 1))
                            tensor.matmul(op[:, 512:1024], stat, vbuf[:, gp % 2, kc, 512:1024],
                                          start=(kc == 0), stop=(kc == KC - 1))
                            lmm = tensor.matmul(l_ps[:, j:j + 1], stat, vbuf[:, gp % 2, kc, 1024:1025],
                                                start=(first_l and kc == 0),
                                                stop=(last_l and kc == KC - 1))
                            if kc == KC - 1:
                                lmm.then_inc(S.pe_o, 1)
                        n_o += 1
                    tensor.sem_inc(S.pe_g, 1)

        @blk.sync
        def _(sync):
            sync.wait_ge(S.cc, 3)
            for g in range(NG):
                if g >= 2:
                    sync.wait_ge(S.pe_g, g - 1)
                sync.dma_start(out=kbuf.ap()[:, g % 2, :, :],
                               in_=k_ag_out.ap()[g, :, :, :].rearrange("c p t -> p c t")).then_inc(S.kv_ld if g % 2 == 0 else S.kv_ld1, 16)
                sync.dma_start(out=vbuf.ap()[:, g % 2, :, 0:1024],
                               in_=v_ag_out.ap()[g * 512:(g + 1) * 512, :].rearrange("(kc p) d -> p kc d", p=128),
                               ).then_inc(S.kv_ld if g % 2 == 0 else S.kv_ld1, 16)

        if dbg:
            @blk.scalar
            def _(scalar):
                scalar.wait_ge(S.rcp, 1)
                scalar.dma_start(out=dbg_rl[:, :], in_=recip_l[:, :]).then_inc(S.dbgs, 16)
                scalar.dma_start(out=dbg_negm[:, :], in_=neg_m[:, :]).then_inc(S.dbgs, 16)
                scalar.dma_start(out=dbg_oacc[:, :], in_=o_acc.ap().rearrange("p j d -> p (j d)")).then_inc(S.dbgs, 16)
                scalar.dma_start(out=dbg_misc[:, 0:1], in_=qmax2[:, :]).then_inc(S.dbgs, 16)
                scalar.dma_start(out=dbg_misc[:, 1:2], in_=kmax2[:, :]).then_inc(S.dbgs, 16)
                scalar.dma_start(out=dbg_misc[:, 2:3], in_=m2[:, :]).then_inc(S.dbgs, 16)
                scalar.dma_start(out=dbg_nrm[:, :], in_=nrm_all.ap().bitcast(U16)).then_inc(S.dbgs, 16)
                scalar.dma_start(out=dbg_kag[:, :],
                                 in_=k_ag_out.ap()[0, 0, :, :].bitcast(U16)).then_inc(S.dbgs, 16)
                scalar.dma_start(out=dbg_vag[:, :],
                                 in_=v_ag_out.ap()[0:128, :].bitcast(U16)).then_inc(S.dbgs, 16)

    ctxB.close()
    qctx.close()
    if stop_after == "B":
        root.close()
        return nc
    wr_sb = E(nc.sbuf_tensor("wr_sb", [128, DC, D], BF16))
    wfk_sb = E(nc.sbuf_tensor("wfk_sb", [128, DC, D], BF16))
    wfv_sb = E(nc.sbuf_tensor("wfv_sb", [128, DC, D], BF16))
    xstag2 = E(nc.sbuf_tensor("xstag2", [128, D], F32))
    h2tok = E(nc.sbuf_tensor("h2tok", [128, D], BF16))
    tsq2 = E(nc.sbuf_tensor("tsq2", [128, D], F32))
    cdust = E(nc.sbuf_tensor("cdust", [128, D], BF16))
    ag3_sb = E(nc.sbuf_tensor("ag3_sb", [NCORES, D], F32))
    xbnd_f = E(nc.sbuf_tensor("xbnd_f", [128, DC], F32))
    xbnd_b = E(nc.sbuf_tensor("xbnd_b", [128, DC], BF16))
    sqb = E(nc.sbuf_tensor("sqb", [128, DC], BF16))
    bm = E(nc.sbuf_tensor("bm", [1, 1], F32))
    bs2 = E(nc.sbuf_tensor("bs2", [1, 1], F32))
    bvar = E(nc.sbuf_tensor("bvar", [1, 1], F32))
    bstd = E(nc.sbuf_tensor("bstd", [1, 1], F32))
    binv = E(nc.sbuf_tensor("binv", [1, 1], F32))
    bnegmi = E(nc.sbuf_tensor("bnegmi", [1, 1], F32))
    binv_b = E(nc.sbuf_tensor("binv_b", [1, 1], BF16))
    bnmi_b = E(nc.sbuf_tensor("bnmi_b", [1, 1], BF16))
    binv_t = E(nc.sbuf_tensor("binv_t", [128, 1], F32))
    bnmi_t = E(nc.sbuf_tensor("bnmi_t", [128, 1], F32))
    h2_fm = E(nc.sbuf_tensor("h2_fm", [128, DC, T + 1], BF16))

    JORD = [15] + list(range(15))

    with ExitStack() as pctx, nc.Block() as blk:
        trp2 = [pctx.enter_context(nc.psum_tensor(f"trpC{i}", [128, 128], BF16)) for i in range(2)]
        bnd_ps = pctx.enter_context(nc.psum_tensor("bndC", [128, 512], F32))
        st_ps = pctx.enter_context(nc.psum_tensor("stC", [1, 512], F32))

        @blk.sync
        def _(sync):
            for i, j in enumerate(JORD):
                if i >= 1:
                    sync.wait_ge(S.c_xf, i)
                sync.dma_start(out=xstag2[:, :], in_=x_sh[j * 128:(j + 1) * 128, :]).then_inc(S.c_x, 16)
            for i, j in enumerate(JORD):
                sync.wait_ge(S.c_xf, i + 1)
                sync.dma_start(out=xpost_d[j * 128:(j + 1) * 128, :], in_=o_acc[:, j, :]).then_inc(S.c_sp, 16)
            sync.wait_ge(S.cc, 4)
            sync.dma_start(out=ag3_sb[:, :], in_=xb_ag_out[:, :]).then_inc(S.ag3r, 16)

        @blk.gpsimd
        def _(gpsimd):
            gpsimd.wait_ge(S.c_xf, 1)
            gpsimd.dma_start(out=xb_ag_in[:, :], in_=o_acc.ap()[127:128, 15, :]).then_inc(S.ag3w, 16)
            gpsimd.wait_ge(S.ag3w, 16)
            gpsimd.collective_compute("AllGather", ALU.bypass, replica_groups=RG,
                                      ins=[xb_ag_in.ap().opt()], outs=[xb_ag_out.ap().opt()]).then_inc(S.cc, 1)
            gpsimd.wait_ge(S.pe_g, NG)
            gpsimd.dma_start(out=wfk_sb.ap(),
                             in_=wfkT.ap().rearrange("(c p) n -> p c n", p=128)).then_inc(S.init, 16)
            gpsimd.dma_start(out=wr_sb.ap(),
                             in_=wrT.ap().rearrange("(c p) n -> p c n", p=128)).then_inc(S.init, 16)
            gpsimd.dma_start(out=wfv_sb.ap(),
                             in_=wfvT.ap().rearrange("(c p) n -> p c n", p=128)).then_inc(S.init, 16)
            gpsimd.wait_ge(S.bnd, 4)
            gpsimd.tensor_scalar_mul(bm[:, :], bm[:, :], 1.0 / D).then_inc(S.bnd, 1)
            gpsimd.wait_ge(S.bnd, 6)
            gpsimd.tensor_scalar(btmpv[:, :], bs2[:, :], 1.0 / D, EPS, ALU.mult, ALU.add).then_inc(S.bnd, 1)
            gpsimd.wait_ge(S.bnd, 10)
            gpsimd.tensor_mul(bnegmi[:, :], bm[:, :], binv[:, :]).then_inc(S.bnd, 1)

        @blk.vector
        def _(vector):
            for i, j in enumerate(JORD):
                r = i % 2
                vector.wait_ge(S.c_x, 16 * (i + 1))
                vector.wait_ge(S.rcp, 2)
                if dbg and i == 0:
                    vector.wait_ge(S.dbgs, 16 * 14)
                vector.scalar_tensor_tensor(o_acc[:, j, :], o_acc[:, j, :], recip_g[:, j:j + 1],
                                            xstag2[:, :], ALU.mult, ALU.add).then_inc(S.c_xf, 1)
                if i >= 2:
                    vector.wait_ge(S.cB, i - 1)
                vector.reduce_sum(mneg[r][:, :], o_acc[:, j, :], axis=AX, negate=True).then_inc(S.cA, 1)
                vector.wait_ge(S.cB, i + 1)
                if i >= 2:
                    vector.wait_ge(S.cD, i - 1)
                vector.tensor_scalar(sq_in[r][:, :], v_raw[r][:, :], 1.0 / D, D * D * EPS, ALU.mult, ALU.add).then_inc(S.cC, 1)
                vector.wait_ge(S.cD, i + 1)
                if i >= 2:
                    vector.wait_ge(S.c_h, i - 1)
                vector.reciprocal(inv_t[r][:, :], std_t[r][:, :]).then_inc(S.cE, 1)
                for dc in range(DC):
                    cnt = i * DC + dc
                    vector.wait_ge(S.c_trp, cnt + 1)
                    vector.tensor_copy(h2_fm[:, dc, 1 + j * 128:1 + (j + 1) * 128],
                                       trp2[cnt % 2][:, :]).then_inc(S.c_trev, 1)
            vector.wait_ge(S.bnd, 3)
            vector.reduce_sum(bm[:, :], st_ps[0:1, 0:DC], axis=AX)
            vector.reduce_sum(bs2[:, :], st_ps[0:1, 256:256 + DC], axis=AX).then_inc(S.bnd, 1)
            vector.wait_ge(S.bnd, 5)
            vector.tensor_mul(bm2[:, :], bm[:, :], bm[:, :]).then_inc(S.bnd, 1)
            vector.wait_ge(S.bnd, 7)
            vector.tensor_sub(bvar[:, :], btmpv[:, :], bm2[:, :]).then_inc(S.bnd, 1)
            vector.wait_ge(S.bnd, 9)
            vector.reciprocal(binv[:, :], bstd[:, :]).then_inc(S.bnd, 1)
            vector.wait_ge(S.bnd, 11)
            vector.tensor_scalar_mul(bnmi2[:, :], bnegmi[:, :], -1.0).then_inc(S.bnd, 1)
            vector.wait_ge(S.bnd, 14)
            vector.tensor_copy(binv_t[:, :], bnd_ps[:, 64:65])
            vector.tensor_copy(bnmi_t[:, :], bnd_ps[:, 65:66]).then_inc(S.bnd, 1)

        @blk.scalar
        def _(scalar):
            scalar.wait_ge(S.start, BIG)
            for i, j in enumerate(JORD):
                r = i % 2
                scalar.wait_ge(S.cA, i + 1)
                scalar.activation(tsq2[:, :], o_acc[:, j, :], AF.Identity,
                                  bias=mneg[r][:, :], scale=float(D))
                if i >= 2:
                    scalar.wait_ge(S.cC, i - 1)
                scalar.activation(cdust[:, :], tsq2[:, :], AF.Square,
                                  accum_out=v_raw[r][:, :]).then_inc(S.cB, 1)
                scalar.wait_ge(S.cC, i + 1)
                if i >= 2:
                    scalar.wait_ge(S.cE, i - 1)
                scalar.activation(std_t[r][:, :], sq_in[r][:, :], AF.Sqrt).then_inc(S.cD, 1)
                scalar.wait_ge(S.cE, i + 1)
                scalar.wait_ge(S.c_trp, 8 * i)
                scalar.activation(h2tok[:, :], tsq2[:, :], AF.Copy, scale=inv_t[r][:, :]).then_inc(S.c_h, 1)
            scalar.wait_ge(S.bnd, 1)
            scalar.activation(xbnd_f[:, :], bnd_ps[:, 0:DC], AF.Copy)
            scalar.activation(xbnd_b[:, :], bnd_ps[:, 0:DC], AF.Copy)
            scalar.activation(sqb[:, :], bnd_ps[:, 0:DC], AF.Square).then_inc(S.bnd, 1)
            scalar.wait_ge(S.bnd, 8)
            scalar.activation(bstd[:, :], bvar[:, :], AF.Sqrt).then_inc(S.bnd, 1)
            scalar.wait_ge(S.bnd, 12)
            scalar.activation(binv_b[:, :], binv[:, :], AF.Copy)
            scalar.activation(bnmi_b[:, :], bnmi2[:, :], AF.Copy).then_inc(S.bnd, 1)
            scalar.wait_ge(S.bnd, 15)
            for dc in range(DC):
                scalar.activation(h2_fm[:, dc, 0:1], xbnd_f[:, dc:dc + 1], AF.Identity,
                                  scale=binv_t[:, :], bias=bnmi_t[:, :]).then_inc(S.bnd5, 1)

        @blk.tensor
        def _(tensor):
            for i, j in enumerate(JORD):
                tensor.wait_ge(S.c_h, i + 1)
                for dc in range(DC):
                    cnt = i * DC + dc
                    if cnt >= 2:
                        tensor.wait_ge(S.c_trev, cnt - 1)
                    tensor.transpose(trp2[cnt % 2][:, :], h2tok[:, dc * 128:(dc + 1) * 128],
                                     ident_b[:, :]).then_inc(S.c_trp, 1)
            tensor.wait_ge(S.ag3r, 16)
            tensor.wait_ge(S.s_w, 80)
            for dc in range(DC):
                mm = tensor.matmul(bnd_ps[:, dc:dc + 1], ag3_sb[:, dc * 128:(dc + 1) * 128], bsel_t[:, :],
                                   start=(dc == 0), stop=(dc == DC - 1))
                if dc == DC - 1:
                    mm.then_inc(S.bnd, 1)
            tensor.wait_ge(S.bnd, 2)
            tensor.matmul(st_ps[0:1, 0:DC], ones_col[:, :], xbnd_b[:, :], start=True, stop=False)
            tensor.matmul(st_ps[0:1, 256:256 + DC], ones_col[:, :], sqb[:, :],
                          start=False, stop=True).then_inc(S.bnd, 1)
            tensor.wait_ge(S.bnd, 13)
            tensor.matmul(bnd_ps[:, 64:65], ones_row[:, :], binv_b[:, :], start=True, stop=False)
            tensor.matmul(bnd_ps[:, 65:66], ones_row[:, :], bnmi_b[:, :],
                          start=False, stop=True).then_inc(S.bnd, 1)

        if dbg:
            @blk.scalar
            def _(scalar):
                scalar.wait_ge(S.bnd5, DC)
                scalar.wait_ge(S.c_trev, 128)
                scalar.dma_start(out=dbg_h2[:, :],
                                 in_=h2_fm.ap().rearrange("p c n -> p (c n)").bitcast(U16)).then_inc(S.dbgs, 16)
                scalar.wait_ge(S.c_xf, 16)
                scalar.dma_start(out=dbg_xp[:, :], in_=o_acc.ap().rearrange("p j d -> p (j d)")).then_inc(S.dbgs, 16)

    if stop_after == "C":
        root.close()
        return nc
    xr_b = E(nc.sbuf_tensor("xr_b", [128, DC, 512], BF16))
    xk_b = E(nc.sbuf_tensor("xk_b", [128, DC, 512], BF16))
    kact_b = E(nc.sbuf_tensor("kact_b", [128, DC, 512], BF16))
    d_stag = E(nc.sbuf_tensor("d_stag", [128, 512], BF16))
    krelu = E(nc.sbuf_tensor("krelu", [128, 512], F32))
    r_tok = E(nc.sbuf_tensor("r_tok", [128, 2, D], F32))
    xpstag = E(nc.sbuf_tensor("xpstag", [128, 2, D], F32))

    with ExitStack() as pctx, nc.Block() as blk:
        kfm_ps = [pctx.enter_context(nc.psum_tensor(f"kfmD{i}", [128, 512], F32)) for i in range(2)]
        r_ps = pctx.enter_context(nc.psum_tensor("rD", [128, D], F32))
        v_ps = pctx.enter_context(nc.psum_tensor("vD", [128, D], F32))

        @blk.vector
        def _(vector):
            vector.wait_ge(S.c_trev, 128)
            vector.wait_ge(S.bnd5, DC)
            vector.wait_ge(S.s_w, 80)
            for b in range(NB):
                if b >= 1:
                    vector.wait_ge(S.d_rv, b)
                for dc in range(DC):
                    lo = b * 512
                    vector.tensor_sub(d_stag[:, :], h2_fm[:, dc, lo + 1:lo + 513], h2_fm[:, dc, lo:lo + 512])
                    vector.scalar_tensor_tensor(xr_b[:, dc, :], d_stag[:, :], mur_t[:, dc:dc + 1],
                                                h2_fm[:, dc, lo:lo + 512], ALU.mult, ALU.add)
                    mx = vector.scalar_tensor_tensor(xk_b[:, dc, :], d_stag[:, :], muk_t[:, dc:dc + 1],
                                                     h2_fm[:, dc, lo:lo + 512], ALU.mult, ALU.add)
                    if dc == DC - 1:
                        mx.then_inc(S.d_mix, 1)
                for jj in range(NB):
                    j = b * NB + jj
                    vector.wait_ge(S.d_vmm, j + 1)
                    vector.wait_ge(S.d_rev, j + 1)
                    vector.tensor_mul(r_tok[:, j % 2, :], r_tok[:, j % 2, :], v_ps[:, :]).then_inc(S.d_vev, 1)
                    vector.wait_ge(S.d_xp if j % 2 == 0 else S.d_xp1, 16 * (j // 2 + 1))
                    vector.tensor_add(r_tok[:, j % 2, :], r_tok[:, j % 2, :], xpstag[:, j % 2, :]).then_inc(S.d_oev, 1)
            vector.sem_inc(S.fin, 1)

        @blk.tensor
        def _(tensor):
            tensor.wait_ge(S.init, 82 + 48)
            for b in range(NB):
                tensor.wait_ge(S.d_mix, b + 1)
                for c in range(DC):
                    rot = b * DC + c
                    ps = kfm_ps[rot % 2]
                    if rot >= 2:
                        tensor.wait_ge(S.d_krel, rot - 1)
                    for dc in range(DC):
                        mm = tensor.matmul(ps[:, :], wfk_sb[:, dc, c * 128:(c + 1) * 128],
                                           xk_b[:, dc, :],
                                           start=(dc == 0), stop=(dc == DC - 1))
                        if dc == DC - 1:
                            mm.then_inc(S.d_kmm, 1)
                for jj in range(NB):
                    j = b * NB + jj
                    if j >= 1:
                        tensor.wait_ge(S.d_rev, j)
                    for dc in range(DC):
                        for half in range(2):
                            mm = tensor.matmul(r_ps[:, half * 512:(half + 1) * 512],
                                               xr_b[:, dc, jj * 128:(jj + 1) * 128],
                                               wr_sb[:, dc, half * 512:(half + 1) * 512],
                                               start=(dc == 0), stop=(dc == DC - 1))
                            if dc == DC - 1 and half == 1:
                                mm.then_inc(S.d_rmm, 1)
                    if j >= 1:
                        tensor.wait_ge(S.d_vev, j)
                    tensor.wait_ge(S.d_kact, DC * (b + 1))
                    for dc in range(DC):
                        for half in range(2):
                            mm = tensor.matmul(v_ps[:, half * 512:(half + 1) * 512],
                                               kact_b[:, dc, jj * 128:(jj + 1) * 128],
                                               wfv_sb[:, dc, half * 512:(half + 1) * 512],
                                               start=(dc == 0), stop=(dc == DC - 1))
                            if dc == DC - 1 and half == 1:
                                mm.then_inc(S.d_vmm, 1)
                                if jj == NB - 1:
                                    tensor.sem_inc(S.d_rv, 1)
            tensor.sem_inc(S.fin, 1)

        @blk.scalar
        def _(scalar):
            for b in range(NB):
                for c in range(DC):
                    rot = b * DC + c
                    scalar.wait_ge(S.d_kmm, rot + 1)
                    scalar.activation(krelu[:, :], kfm_ps[rot % 2][:, :], AF.Relu).then_inc(S.d_krel, 1)
                    scalar.activation(kact_b[:, c, :], krelu[:, :], AF.Square).then_inc(S.d_kact, 1)
                for jj in range(NB):
                    j = b * NB + jj
                    scalar.wait_ge(S.d_rmm, j + 1)
                    if j >= 2:
                        scalar.wait_ge(S.y_st if j % 2 == 0 else S.y_st1, 16 * (j // 2))
                    scalar.activation(r_tok[:, j % 2, :], r_ps[:, :], AF.Sigmoid).then_inc(S.d_rev, 1)
            scalar.sem_inc(S.fin, 1)

        @blk.sync
        def _(sync):
            sync.wait_ge(S.c_sp, 256)
            for j in range(J + 2):
                if j < J:
                    if j >= 2:
                        sync.wait_ge(S.d_oev, j - 1)
                    sync.dma_start(out=xpstag[:, j % 2, :],
                                   in_=xpost_d[j * 128:(j + 1) * 128, :]).then_inc(
                                       S.d_xp if j % 2 == 0 else S.d_xp1, 16)
                if j >= 2:
                    sync.wait_ge(S.d_oev, j - 1)
                    sync.dma_start(out=y[(j - 2) * 128:(j - 1) * 128, :],
                                   in_=r_tok[:, (j - 2) % 2, :]).then_inc(
                                       S.y_st if (j - 2) % 2 == 0 else S.y_st1, 16)
            sync.wait_ge(S.y_st, 16 * (J // 2))
            sync.wait_ge(S.y_st1, 16 * (J // 2))
            if dbg:
                sync.wait_ge(S.dbgs, 16 * 16)

    root.close()
    return nc


_NC_CACHE = {}


def _get_nc(dbg=False):
    if dbg not in _NC_CACHE:
        _NC_CACHE[dbg] = build(dbg)
    return _NC_CACHE[dbg]


def make_in_maps(inputs):
    asn = lambda a: np.ascontiguousarray(np.asarray(a, dtype=np.float32))
    x = asn(inputs["L0_attn_x"]).reshape(NCORES * T, D)
    common = {
        "wqT": asn(np.asarray(inputs["L0_attn_wq"]).T),
        "wkT": asn(np.asarray(inputs["L0_attn_wk"]).T),
        "wvT": asn(np.asarray(inputs["L0_attn_wv"]).T),
        "wrT": asn(np.asarray(inputs["L0_ffn_W_r"]).T),
        "wfkT": asn(np.asarray(inputs["L0_ffn_W_k"]).T),
        "wfvT": asn(np.asarray(inputs["L0_ffn_W_v"]).T),
        "bq": np.ascontiguousarray(asn(inputs["L0_attn_bq"]).reshape(DC, 128).T),
        "bk": np.ascontiguousarray(asn(inputs["L0_attn_bk"]).reshape(DC, 128).T),
        "bv": asn(inputs["L0_attn_bv"]).reshape(1, D),
        "mur": np.ascontiguousarray(asn(inputs["L0_ffn_mu_r"]).reshape(DC, 128).T),
        "muk": np.ascontiguousarray(asn(inputs["L0_ffn_mu_k"]).reshape(DC, 128).T),
        "ident": np.eye(128, dtype=np.float32),
    }
    in_maps = []
    for i in range(NCORES):
        sel = np.zeros((NCORES, 1), np.float32)
        if i > 0:
            sel[i - 1, 0] = 1.0
        im = dict(common)
        im["x_sh"] = np.ascontiguousarray(x[i * T:(i + 1) * T, :])
        im["bsel"] = sel
        in_maps.append(im)
    return in_maps


def kernel(**inputs):
    nc = _get_nc(False)
    in_maps = make_in_maps(inputs)
    res = run_bass_kernel_spmd(nc, in_maps, core_ids=list(range(NCORES)))
    out = np.concatenate([r["y"] for r in res.results], axis=0)
    return out.reshape(1, NCORES * T, D)


# version 05
# speedup vs baseline: 1.2315x; 1.2315x over previous
import numpy as np
from contextlib import ExitStack

import concourse.bass as bass
import concourse.mybir as mybir
from concourse.bass_utils import run_bass_kernel_spmd

F32 = mybir.dt.float32
BF16 = mybir.dt.bfloat16
U16 = mybir.dt.uint16
AF = mybir.ActivationFunctionType
ALU = mybir.AluOpType
AX = mybir.AxisListType.X

NCORES = 8
D = 1024
DC = 8
T = 2048
J = 16
NG = 32
KC = 4
NB = 4
SCALE = 1.0 / 32.0
EPS = 1e-5

SEM_NAMES = [
    "init", "s_w", "s_x", "s_xfree", "lnA", "lnB", "lnC", "lnD", "lnE", "ln_h", "tr_p", "tr_ev",
    "a_mm", "a_nmm", "a_qev", "a_kev", "a_vev", "a_nev", "a_sq", "a_kst", "a_vst", "a_nst",
    "cc", "b_nld", "nm", "pe_s", "pe_p", "pe_o", "pe_e", "pe_g", "pe_sk", "pe_pts", "rcp",
    "c_x", "c_xf", "ag3w", "ag3r", "cA", "cB", "cC", "cD", "cE", "c_h",
    "c_trp", "c_trev", "bnd", "bnd5",
    "d_mix", "d_rv", "d_kmm", "d_krel", "d_kact", "d_rmm", "d_rev", "d_vmm", "d_vev",
    "d_oev", "y_st", "dbgs", "s_x1", "a_kst1", "a_vst1", "kA", "kA1", "kB", "kB1", "y_st1", "a_qm",
    "start", "fin",
]
BIG = 1 << 30


def build(dbg=False, stop_after=None):
    nc = bass.Bass("TRN2", target_bir_lowering=False)

    x_sh = nc.dram_tensor("x_sh", [T, D], F32, kind="ExternalInput")
    wqT = nc.dram_tensor("wqT", [D, D], F32, kind="ExternalInput")
    wkT = nc.dram_tensor("wkT", [D, D], F32, kind="ExternalInput")
    wvT = nc.dram_tensor("wvT", [D, D], F32, kind="ExternalInput")
    wrT = nc.dram_tensor("wrT", [D, D], F32, kind="ExternalInput")
    wfkT = nc.dram_tensor("wfkT", [D, D], F32, kind="ExternalInput")
    wfvT = nc.dram_tensor("wfvT", [D, D], F32, kind="ExternalInput")
    bq_d = nc.dram_tensor("bq", [128, DC], F32, kind="ExternalInput")
    bk_d = nc.dram_tensor("bk", [128, DC], F32, kind="ExternalInput")
    bv_d = nc.dram_tensor("bv", [1, D], F32, kind="ExternalInput")
    mur_d = nc.dram_tensor("mur", [128, DC], F32, kind="ExternalInput")
    muk_d = nc.dram_tensor("muk", [128, DC], F32, kind="ExternalInput")
    ident_d = nc.dram_tensor("ident", [128, 128], F32, kind="ExternalInput")
    bsel_d = nc.dram_tensor("bsel", [NCORES, 1], F32, kind="ExternalInput")
    y = nc.dram_tensor("y", [T, D], F32, kind="ExternalOutput")

    k_ag_in = nc.dram_tensor("k_ag_in", [4, DC, 128, 512], BF16, kind="Internal")
    k_ag_o = [nc.dram_tensor(f"k_ag_o{j}", [NCORES, DC, 128, 512], BF16, kind="Internal",
                             addr_space="Shared") for j in range(4)]
    v_ag_in = nc.dram_tensor("v_ag_in", [T, D], BF16, kind="Internal")
    v_ag_o = [nc.dram_tensor(f"v_ag_o{j}", [NCORES * 512, D], BF16, kind="Internal",
                             addr_space="Shared") for j in range(4)]
    n_ag_in = nc.dram_tensor("n_ag_in", [1, T], BF16, kind="Internal")
    n_ag_out = nc.dram_tensor("n_ag_out", [128, 128], BF16, kind="Internal")
    xb_ag_in = nc.dram_tensor("xb_ag_in", [1, D], F32, kind="Internal")
    xb_ag_out = nc.dram_tensor("xb_ag_out", [NCORES, D], F32, kind="Internal")
    if dbg:
        dbg_oacc = nc.dram_tensor("dbg_oacc", [128, J * D], F32, kind="ExternalOutput")
        dbg_rl = nc.dram_tensor("dbg_rl", [128, 16], F32, kind="ExternalOutput")
        dbg_negm = nc.dram_tensor("dbg_negm", [128, 1], F32, kind="ExternalOutput")
        dbg_h2 = nc.dram_tensor("dbg_h2", [128, DC * (T + 1)], U16, kind="ExternalOutput")
        dbg_xp = nc.dram_tensor("dbg_xp", [128, J * D], F32, kind="ExternalOutput")
        dbg_misc = nc.dram_tensor("dbg_misc", [1, 8], F32, kind="ExternalOutput")
        dbg_qn = nc.dram_tensor("dbg_qn", [1, T], U16, kind="ExternalOutput")
        dbg_kn = nc.dram_tensor("dbg_kn", [1, T], U16, kind="ExternalOutput")
        dbg_nrm = nc.dram_tensor("dbg_nrm", [128, 128], U16, kind="ExternalOutput")
        dbg_kag = nc.dram_tensor("dbg_kag", [128, 512], U16, kind="ExternalOutput")
        dbg_vag = nc.dram_tensor("dbg_vag", [128, 1024], U16, kind="ExternalOutput")
        dbg_h1 = nc.dram_tensor("dbg_h1", [128, DC * T], U16, kind="ExternalOutput")
        dbg_q = nc.dram_tensor("dbg_q", [128, DC * T], U16, kind="ExternalOutput")
        dbg_hbf = nc.dram_tensor("dbg_hbf", [128, 2 * D], U16, kind="ExternalOutput")

    RG = [list(range(NCORES))]

    root = ExitStack()
    E = root.enter_context
    ident_b = E(nc.sbuf_tensor("ident_b", [128, 128], BF16))
    ones_col = E(nc.sbuf_tensor("ones_col", [128, 1], BF16))
    ones_row = E(nc.sbuf_tensor("ones_row", [1, 128], BF16))
    ones32 = E(nc.sbuf_tensor("ones32", [128, 1], F32))
    bq_t = E(nc.sbuf_tensor("bq_t", [128, DC], F32))
    bk_t = E(nc.sbuf_tensor("bk_t", [128, DC], F32))
    mur_t = E(nc.sbuf_tensor("mur_t", [128, DC], F32))
    muk_t = E(nc.sbuf_tensor("muk_t", [128, DC], F32))
    bsel_t = E(nc.sbuf_tensor("bsel_t", [NCORES, 1], F32))
    neg_m = E(nc.sbuf_tensor("neg_m", [128, 1], F32))
    qmax2 = E(nc.sbuf_tensor("qmax2", [1, 1], F32))
    kmax2 = E(nc.sbuf_tensor("kmax2", [1, 1], F32))
    m2 = E(nc.sbuf_tensor("m2", [1, 1], F32))
    negm_s = E(nc.sbuf_tensor("negm_s", [1, 1], F32))
    negm_b = E(nc.sbuf_tensor("negm_b", [1, 1], BF16))
    nrm_all = E(nc.sbuf_tensor("nrm_all", [128, 128], BF16))
    knmax_c = E(nc.sbuf_tensor("knmax_c", [128, 1], BF16))
    recip_l = E(nc.sbuf_tensor("recip_l", [128, 16], F32))
    mneg = [E(nc.sbuf_tensor(f"mneg{i}", [128, 1], F32)) for i in range(2)]
    v_raw = [E(nc.sbuf_tensor(f"v_raw{i}", [128, 1], F32)) for i in range(2)]
    sq_in = [E(nc.sbuf_tensor(f"sq_in{i}", [128, 1], F32)) for i in range(2)]
    std_t = [E(nc.sbuf_tensor(f"std_t{i}", [128, 1], F32)) for i in range(2)]
    inv_t = [E(nc.sbuf_tensor(f"inv_t{i}", [128, 1], F32)) for i in range(2)]
    recip_g = E(nc.sbuf_tensor("recip_g", [128, 16], F32))
    negm_s2 = E(nc.sbuf_tensor("negm_s2", [1, 1], F32))
    bm2 = E(nc.sbuf_tensor("bm2", [1, 1], F32))
    btmpv = E(nc.sbuf_tensor("btmpv", [1, 1], F32))
    bnmi2 = E(nc.sbuf_tensor("bnmi2", [1, 1], F32))

    qctx = ExitStack()
    q_fm = qctx.enter_context(nc.sbuf_tensor("q_fm", [128, DC, T], BF16, side="right"))

    sems = {name: E(nc.semaphore(name)) for name in SEM_NAMES}
    S = type("S", (), sems)
    _nums = sorted(h.num for h in sems.values())
    assert _nums[-1] - _nums[0] == len(_nums) - 1, _nums
    sem_lo, sem_hi = _nums[0], _nums[-1] + 1

    ctxA = ExitStack()
    EA = ctxA.enter_context
    wq_sb = EA(nc.sbuf_tensor("wq_sb", [128, DC, D], BF16))
    wk_sb = EA(nc.sbuf_tensor("wk_sb", [128, DC, D], BF16))
    wv_sb = EA(nc.sbuf_tensor("wv_sb", [128, DC, D], BF16))
    h1_fm = EA(nc.sbuf_tensor("h1_fm", [128, DC, T], BF16))
    xstag = EA(nc.sbuf_tensor("xstag", [128, 2, D], F32))
    t_stag = EA(nc.sbuf_tensor("t_stag", [128, D], F32))
    tsq_st = EA(nc.sbuf_tensor("tsq_st", [128, D], BF16))
    h_bf = EA(nc.sbuf_tensor("h_bf", [128, 2, D], BF16))
    kstag = EA(nc.sbuf_tensor("kstag", [128, 2, T], BF16))
    vstag = EA(nc.sbuf_tensor("vstag", [128, 2, D], BF16))
    sq_st = EA(nc.sbuf_tensor("sq_st", [128, 2, T], BF16))
    qn_sb = EA(nc.sbuf_tensor("qn_sb", [1, T], BF16))
    kn_sb = EA(nc.sbuf_tensor("kn_sb", [1, T], BF16))
    bv_row = EA(nc.sbuf_tensor("bv_row", [1, D], BF16))

    with ExitStack() as pctx, nc.Block() as blk:
        trp = [pctx.enter_context(nc.psum_tensor(f"trpA{i}", [128, 128], BF16)) for i in range(2)]

        @blk.sync
        def _(sync):
            sync.wait_ge(S.start, BIG)
            sync.dma_start(out=bq_t[:, :], in_=bq_d[:, :]).then_inc(S.s_w, 16)
            sync.dma_start(out=bk_t[:, :], in_=bk_d[:, :]).then_inc(S.s_w, 16)
            sync.dma_start(out=mur_t[:, :], in_=mur_d[:, :]).then_inc(S.s_w, 16)
            sync.dma_start(out=muk_t[:, :], in_=muk_d[:, :]).then_inc(S.s_w, 16)
            sync.dma_start(out=bsel_t[:, :], in_=bsel_d[:, :]).then_inc(S.s_w, 16)
            for j in range(J):
                if j >= 2:
                    sync.wait_ge(S.s_xfree, j - 1)
                sync.dma_start(out=xstag[:, j % 2, :], in_=x_sh[j * 128:(j + 1) * 128, :]).then_inc(
                    S.s_x if j % 2 == 0 else S.s_x1, 16)

        @blk.gpsimd
        def _(gpsimd):
            gpsimd.sem_clear(range(sem_lo, sem_hi))
            gpsimd.sem_inc(S.start, BIG)
            gpsimd.dma_start(out=ident_b[:, :], in_=ident_d[:, :]).then_inc(S.init, 16)
            gpsimd.dma_start(out=bv_row[:, :], in_=bv_d[:, :]).then_inc(S.init, 16)
            gpsimd.memset(ones_col[:, :], 1.0)
            gpsimd.memset(ones_row[:, :], 1.0).then_inc(S.init, 1)
            gpsimd.dma_start(out=wq_sb.ap(),
                             in_=wqT.ap().rearrange("(c p) n -> p c n", p=128)).then_inc(S.init, 16)
            gpsimd.dma_start(out=wk_sb.ap(),
                             in_=wkT.ap().rearrange("(c p) n -> p c n", p=128)).then_inc(S.init, 16)
            gpsimd.dma_start(out=wv_sb.ap(),
                             in_=wvT.ap().rearrange("(c p) n -> p c n", p=128)).then_inc(S.init, 16)

        @blk.vector
        def _(vector):
            vector.wait_ge(S.start, BIG)
            for j in range(J):
                r = j % 2
                vector.wait_ge(S.s_x if j % 2 == 0 else S.s_x1, 16 * (j // 2 + 1))
                if j >= 2:
                    vector.wait_ge(S.s_xfree, j - 1)
                vector.reduce_sum(mneg[r][:, :], xstag[:, r, :], axis=AX, negate=True).then_inc(S.lnA, 1)
                vector.wait_ge(S.lnB, j + 1)
                if j >= 2:
                    vector.wait_ge(S.lnD, j - 1)
                vector.tensor_scalar(sq_in[r][:, :], v_raw[r][:, :], 1.0 / D, D * D * EPS, ALU.mult, ALU.add).then_inc(S.lnC, 1)
                vector.wait_ge(S.lnD, j + 1)
                if j >= 2:
                    vector.wait_ge(S.ln_h, j - 1)
                vector.reciprocal(inv_t[r][:, :], std_t[r][:, :]).then_inc(S.lnE, 1)
                for dc in range(DC):
                    cnt = j * DC + dc
                    vector.wait_ge(S.tr_p, cnt + 1)
                    vector.tensor_copy(h1_fm[:, dc, j * 128:(j + 1) * 128], trp[cnt % 2][:, :]).then_inc(S.tr_ev, 1)

        @blk.scalar
        def _(scalar):
            scalar.wait_ge(S.start, BIG)
            for j in range(J):
                r = j % 2
                scalar.wait_ge(S.lnA, j + 1)
                scalar.activation(t_stag[:, :], xstag[:, r, :], AF.Identity,
                                  bias=mneg[r][:, :], scale=float(D)).then_inc(S.s_xfree, 1)
                if j >= 2:
                    scalar.wait_ge(S.lnC, j - 1)
                scalar.activation(tsq_st[:, :], t_stag[:, :], AF.Square,
                                  accum_out=v_raw[r][:, :]).then_inc(S.lnB, 1)
                scalar.wait_ge(S.lnC, j + 1)
                if j >= 2:
                    scalar.wait_ge(S.lnE, j - 1)
                scalar.activation(std_t[r][:, :], sq_in[r][:, :], AF.Sqrt).then_inc(S.lnD, 1)
                scalar.wait_ge(S.lnE, j + 1)
                if j >= 2:
                    scalar.wait_ge(S.tr_p, 8 * (j - 1))
                scalar.activation(h_bf[:, r, :], t_stag[:, :], AF.Copy, scale=inv_t[r][:, :]).then_inc(S.ln_h, 1)

        @blk.tensor
        def _(tensor):
            tensor.wait_ge(S.start, BIG)
            tensor.wait_ge(S.init, 81)
            for j in range(J):
                tensor.wait_ge(S.ln_h, j + 1)
                for dc in range(DC):
                    cnt = j * DC + dc
                    if cnt >= 2:
                        tensor.wait_ge(S.tr_ev, cnt - 1)
                    tensor.transpose(trp[cnt % 2][:, :], h_bf[:, j % 2, dc * 128:(dc + 1) * 128],
                                     ident_b[:, :]).then_inc(S.tr_p, 1)

    with ExitStack() as pctx, nc.Block() as blk:
        qkv_ps = [pctx.enter_context(nc.psum_tensor(f"qkvA{i}", [128, D], F32)) for i in range(2)]
        nrm_ps = pctx.enter_context(nc.psum_tensor("nrmA", [1, T], F32))

        def _nrm_mms(tensor, c, off):
            tensor.wait_ge(S.a_sq, off + c + 1)
            if off == DC and c == 0:
                tensor.wait_ge(S.a_nst, 1)
            for b in range(NB):
                mm = tensor.matmul(nrm_ps[0:1, b * 512:(b + 1) * 512], ones_col[:, :],
                                   sq_st[:, c % 2, b * 512:(b + 1) * 512],
                                   start=(c == 0), stop=(c == DC - 1))
                if b == NB - 1:
                    mm.then_inc(S.a_nmm, 1)

        @blk.tensor
        def _(tensor):
            tensor.wait_ge(S.tr_ev, J * DC)
            tensor.wait_ge(S.init, 81)
            for c in range(DC):
                for b in range(NB):
                    n = c * NB + b
                    ps = qkv_ps[n % 2]
                    if n >= 2:
                        tensor.wait_ge(S.a_kev, n - 1)
                    for dc in range(DC):
                        mm = tensor.matmul(ps[:, 0:512], wk_sb[:, dc, c * 128:(c + 1) * 128],
                                           h1_fm[:, dc, b * 512:(b + 1) * 512],
                                           start=(dc == 0), stop=(dc == DC - 1))
                        if dc == DC - 1:
                            mm.then_inc(S.a_mm, 1)
                if c >= 1:
                    _nrm_mms(tensor, c - 1, 0)
            _nrm_mms(tensor, DC - 1, 0)
            for t in range(J):
                ps = qkv_ps[t % 2]
                if t == 0:
                    tensor.wait_ge(S.a_kev, 31)
                elif t == 1:
                    tensor.wait_ge(S.a_kev, 32)
                else:
                    tensor.wait_ge(S.a_vev, t - 1)
                for dc in range(DC):
                    for half in range(2):
                        tensor.matmul(ps[:, half * 512:(half + 1) * 512],
                                      h1_fm[:, dc, t * 128:(t + 1) * 128],
                                      wv_sb[:, dc, half * 512:(half + 1) * 512],
                                      start=(dc == 0), stop=False)
                for half in range(2):
                    mm = tensor.matmul(ps[:, half * 512:(half + 1) * 512], ones_row[:, :],
                                       bv_row[:, half * 512:(half + 1) * 512], start=False, stop=True)
                    if half == 1:
                        mm.then_inc(S.a_mm, 1)
            for c in range(DC):
                for b in range(NB):
                    n = c * NB + b
                    ps = qkv_ps[n % 2]
                    if n == 0:
                        tensor.wait_ge(S.a_vev, 15)
                    elif n == 1:
                        tensor.wait_ge(S.a_vev, 16)
                    else:
                        tensor.wait_ge(S.a_qev, n - 1)
                    for dc in range(DC):
                        mm = tensor.matmul(ps[:, 0:512], wq_sb[:, dc, c * 128:(c + 1) * 128],
                                           h1_fm[:, dc, b * 512:(b + 1) * 512],
                                           start=(dc == 0), stop=(dc == DC - 1))
                        if dc == DC - 1:
                            mm.then_inc(S.a_mm, 1)
                if c >= 1:
                    _nrm_mms(tensor, c - 1, DC)
            _nrm_mms(tensor, DC - 1, DC)

        @blk.scalar
        def _(scalar):
            scalar.wait_ge(S.s_w, 80)
            for c in range(DC):
                if c >= 2:
                    scalar.wait_ge(S.a_kst if c % 2 == 0 else S.a_kst1, 64 * (c // 2))
                for b in range(NB):
                    n = c * NB + b
                    scalar.wait_ge(S.a_mm, n + 1)
                    scalar.activation(kstag[:, c % 2, b * 512:(b + 1) * 512], qkv_ps[n % 2][:, 0:512],
                                      AF.Identity, bias=bk_t[:, c:c + 1]).then_inc(S.a_kev, 1)
                if c >= 2:
                    scalar.wait_ge(S.a_nmm, c - 1)
                scalar.activation(sq_st[:, c % 2, :], kstag[:, c % 2, :], AF.Square).then_inc(S.a_sq, 1)
            scalar.wait_ge(S.a_nmm, DC)
            scalar.activation(kn_sb[:, :], nrm_ps[0:1, :], AF.Copy).then_inc(S.a_nst, 1)
            for c in range(DC):
                for b in range(NB):
                    n = 48 + c * NB + b
                    scalar.wait_ge(S.a_mm, n + 1)
                    scalar.activation(q_fm[:, c, b * 512:(b + 1) * 512], qkv_ps[(c * NB + b) % 2][:, 0:512],
                                      AF.Identity, bias=bq_t[:, c:c + 1]).then_inc(S.a_qev, 1)
                scalar.wait_ge(S.a_nmm, 7 + c)
                scalar.activation(sq_st[:, c % 2, :], q_fm[:, c, :], AF.Square).then_inc(S.a_sq, 1)
            scalar.wait_ge(S.a_nmm, 2 * DC)
            scalar.activation(qn_sb[:, :], nrm_ps[0:1, :], AF.Copy).then_inc(S.a_nev, 1)

        @blk.vector
        def _(vector):
            for t in range(J):
                n = 32 + t
                if t >= 2:
                    vector.wait_ge(S.a_vst if t % 2 == 0 else S.a_vst1, 16 * (t // 2))
                vector.wait_ge(S.a_mm, n + 1)
                vector.tensor_copy(vstag[:, t % 2, :], qkv_ps[t % 2][:, :]).then_inc(S.a_vev, 1)
            vector.wait_ge(S.a_nev, 1)
            vector.reduce_max(qmax2[:, :], qn_sb[:, :], axis=AX).then_inc(S.a_qm, 1)

        @blk.sync
        def _(sync):
            for c in range(DC):
                sync.wait_ge(S.a_kev, NB * (c + 1))
                for gl in range(4):
                    sync.dma_start(out=k_ag_in[gl, c, :, :],
                                   in_=kstag[:, c % 2, gl * 512:(gl + 1) * 512]).then_inc(
                                       S.a_kst if c % 2 == 0 else S.a_kst1, 16)
            for t in range(J):
                sync.wait_ge(S.a_vev, t + 1)
                sync.dma_start(out=v_ag_in[t * 128:(t + 1) * 128, :], in_=vstag[:, t % 2, :]).then_inc(
                    S.a_vst if t % 2 == 0 else S.a_vst1, 16)
            sync.wait_ge(S.a_nst, 1)
            sync.dma_start(out=n_ag_in[:, :], in_=kn_sb[:, :]).then_inc(S.a_nst, 16)

        @blk.gpsimd
        def _(gpsimd):
            gpsimd.wait_ge(S.a_kst, 16 * 16)
            gpsimd.wait_ge(S.a_kst1, 16 * 16)
            gpsimd.collective_compute("AllGather", ALU.bypass, replica_groups=RG,
                                      ins=[k_ag_in.ap()[0:1, :, :, :].opt()],
                                      outs=[k_ag_o[0].ap().opt()]).then_inc(S.cc, 1)
            gpsimd.wait_ge(S.a_nst, 17)
            gpsimd.collective_compute("AllGather", ALU.bypass, replica_groups=RG,
                                      ins=[n_ag_in.ap().opt()], outs=[n_ag_out.ap().opt()]).then_inc(S.cc, 1)
            for j in range(4):
                gpsimd.wait_ge(S.a_vst, 16 * (2 * j + 2))
                gpsimd.wait_ge(S.a_vst1, 16 * (2 * j + 2))
                gpsimd.collective_compute("AllGather", ALU.bypass, replica_groups=RG,
                                          ins=[v_ag_in.ap()[j * 512:(j + 1) * 512, :].opt()],
                                          outs=[v_ag_o[j].ap().opt()]).then_inc(S.cc, 1)
                if j < 3:
                    gpsimd.collective_compute("AllGather", ALU.bypass, replica_groups=RG,
                                              ins=[k_ag_in.ap()[j + 1:j + 2, :, :, :].opt()],
                                              outs=[k_ag_o[j + 1].ap().opt()]).then_inc(S.cc, 1)
            gpsimd.wait_ge(S.cc, 2)
            gpsimd.dma_start(out=nrm_all[:, :], in_=n_ag_out[:, :]).then_inc(S.b_nld, 16)

    ctxA.close()
    if stop_after == "A":
        root.close()
        return nc
    o_acc = E(nc.sbuf_tensor("o_acc", [128, J, D], F32))

    ctxB = ExitStack()
    EB = ctxB.enter_context
    vbuf = EB(nc.sbuf_tensor("vbuf", [128, 2, KC, 1024], BF16))
    kbuf = EB(nc.sbuf_tensor("kbuf", [128, 2, DC, 512], BF16))
    pt = [EB(nc.sbuf_tensor(f"pt{i}", [128, KC, T], BF16)) for i in range(2)]
    ptsum = [EB(nc.sbuf_tensor(f"ptsum{i}", [128, T], F32)) for i in range(2)]

    with ExitStack() as pctx, nc.Block() as blk:
        s_ps = [pctx.enter_context(nc.psum_tensor(f"sB{i}", [128, 512], F32)) for i in range(2)]
        o_ps = [pctx.enter_context(nc.psum_tensor(f"oB{i}", [128, D], F32)) for i in range(2)]
        l_ps = pctx.enter_context(nc.psum_tensor("l_ps", [128, 512], F32))
        nm_ps = pctx.enter_context(nc.psum_tensor("nmB", [128, 512], F32))

        @blk.gpsimd
        def _(gpsimd):
            gpsimd.memset(ones32[:, :], 1.0).then_inc(S.init, 1)
            gpsimd.wait_ge(S.nm, 3)
            gpsimd.tensor_mul(m2[:, :], kmax2[:, :], qmax2[:, :]).then_inc(S.nm, 1)
            gpsimd.wait_ge(S.nm, 5)
            gpsimd.tensor_scalar_mul(negm_s2[:, :], negm_s[:, :], -SCALE).then_inc(S.nm, 1)
            gpsimd.wait_ge(S.rcp, 1)
            gpsimd.tensor_copy(recip_g[:, :], recip_l[:, :]).then_inc(S.rcp, 1)

        @blk.vector
        def _(vector):
            vector.wait_ge(S.b_nld, 16)
            vector.reduce_max(knmax_c[:, :], nrm_all[:, :], axis=AX).then_inc(S.nm, 1)
            vector.wait_ge(S.nm, 2)
            vector.reduce_max(kmax2[:, :], nm_ps[0:1, 0:64].bitcast(BF16), axis=AX).then_inc(S.nm, 1)
            vector.wait_ge(S.nm, 8)
            vector.tensor_copy(neg_m[:, :], nm_ps[:, 128:129]).then_inc(S.nm, 1)
            n = 0
            for g in range(NG):
                vector.wait_ge(S.pe_p, 16 * (g + 1))
                if g >= 2:
                    vector.wait_ge(S.pe_g, g - 1)
                pg = ptsum[g % 2]
                ptg2 = pt[g % 2]
                vector.tensor_tensor(pg[:, :], ptg2[:, 0, :], ptg2[:, 1, :], ALU.add)
                vector.tensor_tensor(pg[:, :], pg[:, :], ptg2[:, 2, :], ALU.add)
                vector.tensor_tensor(pg[:, :], pg[:, :], ptg2[:, 3, :], ALU.add).then_inc(S.pe_pts, 1)
                for j in range(J):
                    vector.wait_ge(S.pe_o, n + 1)
                    if g == 0:
                        vector.tensor_copy(o_acc[:, j, :], o_ps[n % 2][:, :]).then_inc(S.pe_e, 1)
                    else:
                        vector.tensor_tensor(o_acc[:, j, :], o_acc[:, j, :], o_ps[n % 2][:, :],
                                             ALU.add).then_inc(S.pe_e, 1)
                    n += 1
            vector.wait_ge(S.pe_g, NG)
            vector.reciprocal(recip_l[:, :], l_ps[:, 0:16]).then_inc(S.rcp, 1)

        @blk.scalar
        def _(scalar):
            scalar.wait_ge(S.nm, 4)
            scalar.activation(negm_s[:, :], m2[:, :], AF.Sqrt).then_inc(S.nm, 1)
            scalar.wait_ge(S.nm, 6)
            scalar.activation(negm_b[:, :], negm_s2[:, :], AF.Copy).then_inc(S.nm, 1)
            n = 0
            scalar.wait_ge(S.a_nst, 17)
            scalar.wait_ge(S.a_qm, 1)
            for g in range(NG):
                if g == 0:
                    scalar.wait_ge(S.nm, 9)
                for kc in range(KC):
                    for b in range(NB):
                        scalar.wait_ge(S.pe_s, n + 1)
                        scalar.activation(pt[g % 2][:, kc, b * 512:(b + 1) * 512], s_ps[n % 2][:, :],
                                          AF.Exp, bias=neg_m[:, :], scale=SCALE).then_inc(S.pe_p, 1)
                        n += 1

        @blk.tensor
        def _(tensor):
            tensor.wait_ge(S.nm, 1)
            tensor.transpose(nm_ps[0:1, 0:64].bitcast(BF16), knmax_c[:, :], ident_b[:, :]).then_inc(S.nm, 1)
            tensor.wait_ge(S.nm, 7)
            tensor.matmul(nm_ps[:, 128:129], ones_row[:, :], negm_b[:, :], start=True, stop=True).then_inc(S.nm, 1)
            tensor.wait_ge(S.a_qev, 32)
            tensor.wait_ge(S.init, 82)
            n_s = 0
            n_o = 0
            for g in range(NG + 1):
                if g < NG:
                    tensor.wait_ge(S.kA if g % 2 == 0 else S.kA1, 16 * (g // 2 + 1))
                    for kc in range(KC):
                        for b in range(NB):
                            ps = s_ps[n_s % 2]
                            if n_s >= 2:
                                tensor.wait_ge(S.pe_p, n_s - 1)
                            for dc in range(DC):
                                mm = tensor.matmul(ps[:, :], kbuf[:, g % 2, dc, kc * 128:(kc + 1) * 128],
                                                   q_fm[:, dc, b * 512:(b + 1) * 512],
                                                   start=(dc == 0), stop=(dc == DC - 1))
                                if dc == DC - 1:
                                    mm.then_inc(S.pe_s, 1)
                            n_s += 1
                    tensor.nop().then_inc(S.pe_sk, 1)
                if g >= 1:
                    gp = g - 1
                    tensor.wait_ge(S.pe_p, gp * 16 + 16)
                    tensor.wait_ge(S.kB if gp % 2 == 0 else S.kB1, 16 * (gp // 2 + 1))
                    tensor.wait_ge(S.pe_pts, gp + 1)
                    ptg = pt[gp % 2]
                    for j in range(J):
                        op = o_ps[n_o % 2]
                        if n_o >= 2:
                            tensor.wait_ge(S.pe_e, n_o - 1)
                        for kc in range(KC):
                            stat = ptg[:, kc, j * 128:(j + 1) * 128]
                            tensor.matmul(op[:, 0:512], stat, vbuf[:, gp % 2, kc, 0:512],
                                          start=(kc == 0), stop=(kc == KC - 1))
                            mm = tensor.matmul(op[:, 512:1024], stat, vbuf[:, gp % 2, kc, 512:1024],
                                               start=(kc == 0), stop=(kc == KC - 1))
                            if kc == KC - 1:
                                mm.then_inc(S.pe_o, 1)
                        tensor.matmul(l_ps[:, j:j + 1], ptsum[gp % 2][:, j * 128:(j + 1) * 128],
                                      ones32[:, :],
                                      start=(gp == 0 and j == 0),
                                      stop=(gp == NG - 1 and j == J - 1))
                        n_o += 1
                    tensor.nop().then_inc(S.pe_g, 1)

        @blk.sync
        def _(sync):
            for gi in range(NG):
                j, r = gi // 8, gi % 8
                ka = S.kA if gi % 2 == 0 else S.kA1
                kb = S.kB if gi % 2 == 0 else S.kB1
                if gi >= 2:
                    sync.wait_ge(S.pe_sk, gi - 1)
                if gi == 0:
                    sync.wait_ge(S.a_vst, 16 * 8)
                    sync.wait_ge(S.a_vst1, 16 * 8)
                sync.wait_ge(S.cc, 1 if j == 0 else 2 * j + 2)
                sync.dma_start(out=kbuf.ap()[:, gi % 2:gi % 2 + 1, :, :],
                               in_=k_ag_o[j].ap()[r:r + 1, :, :, :].rearrange(
                                   "o c p t -> p o c t")).then_inc(ka, 16)
                if gi >= 2:
                    sync.wait_ge(S.pe_g, gi - 1)
                if gi == 0:
                    sync.wait_ge(S.a_mm, 80)
                sync.wait_ge(S.cc, 2 * j + 3)
                sync.dma_start(out=vbuf.ap()[:, gi % 2, :, :],
                               in_=v_ag_o[j].ap()[r * 512:(r + 1) * 512, :].rearrange(
                                   "(kc p) d -> p kc d", p=128)).then_inc(kb, 16)

        if dbg:
            @blk.scalar
            def _(scalar):
                scalar.wait_ge(S.rcp, 1)
                scalar.dma_start(out=dbg_rl[:, :], in_=recip_l[:, :]).then_inc(S.dbgs, 16)
                scalar.dma_start(out=dbg_negm[:, :], in_=neg_m[:, :]).then_inc(S.dbgs, 16)
                scalar.dma_start(out=dbg_oacc[:, :], in_=o_acc.ap().rearrange("p j d -> p (j d)")).then_inc(S.dbgs, 16)
                scalar.dma_start(out=dbg_misc[:, 0:1], in_=qmax2[:, :]).then_inc(S.dbgs, 16)
                scalar.dma_start(out=dbg_misc[:, 1:2], in_=kmax2[:, :]).then_inc(S.dbgs, 16)
                scalar.dma_start(out=dbg_misc[:, 2:3], in_=m2[:, :]).then_inc(S.dbgs, 16)
                scalar.dma_start(out=dbg_nrm[:, :], in_=nrm_all.ap().bitcast(U16)).then_inc(S.dbgs, 16)
                scalar.dma_start(out=dbg_kag[:, :],
                                 in_=k_ag_o[0].ap()[0, 0, :, :].bitcast(U16)).then_inc(S.dbgs, 16)
                scalar.dma_start(out=dbg_vag[:, :],
                                 in_=v_ag_o[0].ap()[0:128, :].bitcast(U16)).then_inc(S.dbgs, 16)

    ctxB.close()
    qctx.close()
    if stop_after == "B":
        root.close()
        return nc
    wr_sb = E(nc.sbuf_tensor("wr_sb", [128, DC, D], BF16))
    wfk_sb = E(nc.sbuf_tensor("wfk_sb", [128, DC, D], BF16))
    wfv_sb = E(nc.sbuf_tensor("wfv_sb", [128, DC, D], BF16))
    h2_fm = E(nc.sbuf_tensor("h2_fm", [128, DC, T + 1], BF16))
    cctx = ExitStack()
    EC = cctx.enter_context
    xstag2 = EC(nc.sbuf_tensor("xstag2", [128, D], F32))
    h2tok = EC(nc.sbuf_tensor("h2tok", [128, D], BF16))
    tsq2 = EC(nc.sbuf_tensor("tsq2", [128, D], F32))
    cdust = EC(nc.sbuf_tensor("cdust", [128, D], BF16))
    ag3_sb = EC(nc.sbuf_tensor("ag3_sb", [NCORES, D], F32))
    xbnd_f = EC(nc.sbuf_tensor("xbnd_f", [128, DC], F32))
    xbnd_b = EC(nc.sbuf_tensor("xbnd_b", [128, DC], BF16))
    sqb = EC(nc.sbuf_tensor("sqb", [128, DC], BF16))
    bm = EC(nc.sbuf_tensor("bm", [1, 1], F32))
    bs2 = EC(nc.sbuf_tensor("bs2", [1, 1], F32))
    bvar = EC(nc.sbuf_tensor("bvar", [1, 1], F32))
    bstd = EC(nc.sbuf_tensor("bstd", [1, 1], F32))
    binv = EC(nc.sbuf_tensor("binv", [1, 1], F32))
    bnegmi = EC(nc.sbuf_tensor("bnegmi", [1, 1], F32))
    binv_b = EC(nc.sbuf_tensor("binv_b", [1, 1], BF16))
    bnmi_b = EC(nc.sbuf_tensor("bnmi_b", [1, 1], BF16))
    binv_t = EC(nc.sbuf_tensor("binv_t", [128, 1], F32))
    bnmi_t = EC(nc.sbuf_tensor("bnmi_t", [128, 1], F32))

    JORD = [15] + list(range(15))

    with ExitStack() as pctx, nc.Block() as blk:
        trp2 = [pctx.enter_context(nc.psum_tensor(f"trpC{i}", [128, 128], BF16)) for i in range(2)]
        bnd_ps = pctx.enter_context(nc.psum_tensor("bndC", [128, 512], F32))
        st_ps = pctx.enter_context(nc.psum_tensor("stC", [1, 512], F32))

        @blk.sync
        def _(sync):
            for i, j in enumerate(JORD):
                if i >= 1:
                    sync.wait_ge(S.c_xf, i)
                sync.dma_start(out=xstag2[:, :], in_=x_sh[j * 128:(j + 1) * 128, :]).then_inc(S.c_x, 16)
            sync.wait_ge(S.cc, 10)
            sync.dma_start(out=ag3_sb[:, :], in_=xb_ag_out[:, :]).then_inc(S.ag3r, 16)

        @blk.gpsimd
        def _(gpsimd):
            gpsimd.wait_ge(S.c_xf, 1)
            gpsimd.dma_start(out=xb_ag_in[:, :], in_=o_acc.ap()[127:128, 15, :]).then_inc(S.ag3w, 16)
            gpsimd.wait_ge(S.ag3w, 16)
            gpsimd.collective_compute("AllGather", ALU.bypass, replica_groups=RG,
                                      ins=[xb_ag_in.ap().opt()], outs=[xb_ag_out.ap().opt()]).then_inc(S.cc, 1)
            gpsimd.wait_ge(S.pe_g, NG)
            gpsimd.dma_start(out=wfk_sb.ap(),
                             in_=wfkT.ap().rearrange("(c p) n -> p c n", p=128)).then_inc(S.init, 16)
            gpsimd.dma_start(out=wr_sb.ap(),
                             in_=wrT.ap().rearrange("(c p) n -> p c n", p=128)).then_inc(S.init, 16)
            gpsimd.dma_start(out=wfv_sb.ap(),
                             in_=wfvT.ap().rearrange("(c p) n -> p c n", p=128)).then_inc(S.init, 16)
            gpsimd.wait_ge(S.bnd, 4)
            gpsimd.tensor_scalar_mul(bm[:, :], bm[:, :], 1.0 / D).then_inc(S.bnd, 1)
            gpsimd.wait_ge(S.bnd, 6)
            gpsimd.tensor_scalar(btmpv[:, :], bs2[:, :], 1.0 / D, EPS, ALU.mult, ALU.add).then_inc(S.bnd, 1)
            gpsimd.wait_ge(S.bnd, 10)
            gpsimd.tensor_mul(bnegmi[:, :], bm[:, :], binv[:, :]).then_inc(S.bnd, 1)

        @blk.vector
        def _(vector):
            for i, j in enumerate(JORD):
                r = i % 2
                vector.wait_ge(S.c_x, 16 * (i + 1))
                vector.wait_ge(S.rcp, 2)
                if dbg and i == 0:
                    vector.wait_ge(S.dbgs, 16 * 14)
                vector.scalar_tensor_tensor(o_acc[:, j, :], o_acc[:, j, :], recip_g[:, j:j + 1],
                                            xstag2[:, :], ALU.mult, ALU.add).then_inc(S.c_xf, 1)
                if i >= 2:
                    vector.wait_ge(S.cB, i - 1)
                vector.reduce_sum(mneg[r][:, :], o_acc[:, j, :], axis=AX, negate=True).then_inc(S.cA, 1)
                vector.wait_ge(S.cB, i + 1)
                if i >= 2:
                    vector.wait_ge(S.cD, i - 1)
                vector.tensor_scalar(sq_in[r][:, :], v_raw[r][:, :], 1.0 / D, D * D * EPS, ALU.mult, ALU.add).then_inc(S.cC, 1)
                vector.wait_ge(S.cD, i + 1)
                if i >= 2:
                    vector.wait_ge(S.c_h, i - 1)
                vector.reciprocal(inv_t[r][:, :], std_t[r][:, :]).then_inc(S.cE, 1)
                for dc in range(DC):
                    cnt = i * DC + dc
                    vector.wait_ge(S.c_trp, cnt + 1)
                    vector.tensor_copy(h2_fm[:, dc, 1 + j * 128:1 + (j + 1) * 128],
                                       trp2[cnt % 2][:, :]).then_inc(S.c_trev, 1)
            vector.wait_ge(S.bnd, 3)
            vector.reduce_sum(bm[:, :], st_ps[0:1, 0:DC], axis=AX)
            vector.reduce_sum(bs2[:, :], st_ps[0:1, 256:256 + DC], axis=AX).then_inc(S.bnd, 1)
            vector.wait_ge(S.bnd, 5)
            vector.tensor_mul(bm2[:, :], bm[:, :], bm[:, :]).then_inc(S.bnd, 1)
            vector.wait_ge(S.bnd, 7)
            vector.tensor_sub(bvar[:, :], btmpv[:, :], bm2[:, :]).then_inc(S.bnd, 1)
            vector.wait_ge(S.bnd, 9)
            vector.reciprocal(binv[:, :], bstd[:, :]).then_inc(S.bnd, 1)
            vector.wait_ge(S.bnd, 11)
            vector.tensor_scalar_mul(bnmi2[:, :], bnegmi[:, :], -1.0).then_inc(S.bnd, 1)
            vector.wait_ge(S.bnd, 14)
            vector.tensor_copy(binv_t[:, :], bnd_ps[:, 64:65])
            vector.tensor_copy(bnmi_t[:, :], bnd_ps[:, 65:66]).then_inc(S.bnd, 1)

        @blk.scalar
        def _(scalar):
            scalar.wait_ge(S.start, BIG)
            for i, j in enumerate(JORD):
                r = i % 2
                scalar.wait_ge(S.cA, i + 1)
                scalar.activation(tsq2[:, :], o_acc[:, j, :], AF.Identity,
                                  bias=mneg[r][:, :], scale=float(D))
                if i >= 2:
                    scalar.wait_ge(S.cC, i - 1)
                scalar.activation(cdust[:, :], tsq2[:, :], AF.Square,
                                  accum_out=v_raw[r][:, :]).then_inc(S.cB, 1)
                scalar.wait_ge(S.cC, i + 1)
                if i >= 2:
                    scalar.wait_ge(S.cE, i - 1)
                scalar.activation(std_t[r][:, :], sq_in[r][:, :], AF.Sqrt).then_inc(S.cD, 1)
                scalar.wait_ge(S.cE, i + 1)
                scalar.wait_ge(S.c_trp, 8 * i)
                scalar.activation(h2tok[:, :], tsq2[:, :], AF.Copy, scale=inv_t[r][:, :]).then_inc(S.c_h, 1)
            scalar.wait_ge(S.bnd, 1)
            scalar.activation(xbnd_f[:, :], bnd_ps[:, 0:DC], AF.Copy)
            scalar.activation(xbnd_b[:, :], bnd_ps[:, 0:DC], AF.Copy)
            scalar.activation(sqb[:, :], bnd_ps[:, 0:DC], AF.Square).then_inc(S.bnd, 1)
            scalar.wait_ge(S.bnd, 8)
            scalar.activation(bstd[:, :], bvar[:, :], AF.Sqrt).then_inc(S.bnd, 1)
            scalar.wait_ge(S.bnd, 12)
            scalar.activation(binv_b[:, :], binv[:, :], AF.Copy)
            scalar.activation(bnmi_b[:, :], bnmi2[:, :], AF.Copy).then_inc(S.bnd, 1)
            scalar.wait_ge(S.bnd, 15)
            for dc in range(DC):
                scalar.activation(h2_fm[:, dc, 0:1], xbnd_f[:, dc:dc + 1], AF.Identity,
                                  scale=binv_t[:, :], bias=bnmi_t[:, :]).then_inc(S.bnd5, 1)

        @blk.tensor
        def _(tensor):
            for i, j in enumerate(JORD):
                tensor.wait_ge(S.c_h, i + 1)
                for dc in range(DC):
                    cnt = i * DC + dc
                    if cnt >= 2:
                        tensor.wait_ge(S.c_trev, cnt - 1)
                    tensor.transpose(trp2[cnt % 2][:, :], h2tok[:, dc * 128:(dc + 1) * 128],
                                     ident_b[:, :]).then_inc(S.c_trp, 1)
            tensor.wait_ge(S.ag3r, 16)
            tensor.wait_ge(S.s_w, 80)
            for dc in range(DC):
                mm = tensor.matmul(bnd_ps[:, dc:dc + 1], ag3_sb[:, dc * 128:(dc + 1) * 128], bsel_t[:, :],
                                   start=(dc == 0), stop=(dc == DC - 1))
                if dc == DC - 1:
                    mm.then_inc(S.bnd, 1)
            tensor.wait_ge(S.bnd, 2)
            tensor.matmul(st_ps[0:1, 0:DC], ones_col[:, :], xbnd_b[:, :], start=True, stop=False)
            tensor.matmul(st_ps[0:1, 256:256 + DC], ones_col[:, :], sqb[:, :],
                          start=False, stop=True).then_inc(S.bnd, 1)
            tensor.wait_ge(S.bnd, 13)
            tensor.matmul(bnd_ps[:, 64:65], ones_row[:, :], binv_b[:, :], start=True, stop=False)
            tensor.matmul(bnd_ps[:, 65:66], ones_row[:, :], bnmi_b[:, :],
                          start=False, stop=True).then_inc(S.bnd, 1)

        if dbg:
            @blk.scalar
            def _(scalar):
                scalar.wait_ge(S.bnd5, DC)
                scalar.wait_ge(S.c_trev, 128)
                scalar.dma_start(out=dbg_h2[:, :],
                                 in_=h2_fm.ap().rearrange("p c n -> p (c n)").bitcast(U16)).then_inc(S.dbgs, 16)
                scalar.wait_ge(S.c_xf, 16)
                scalar.dma_start(out=dbg_xp[:, :], in_=o_acc.ap().rearrange("p j d -> p (j d)")).then_inc(S.dbgs, 16)

    cctx.close()
    if stop_after == "C":
        root.close()
        return nc
    xr_b = E(nc.sbuf_tensor("xr_b", [128, 2, DC, 512], BF16))
    xk_b = E(nc.sbuf_tensor("xk_b", [128, 2, DC, 512], BF16))
    kact_b = E(nc.sbuf_tensor("kact_b", [128, DC, 512], BF16))
    d_stag = E(nc.sbuf_tensor("d_stag", [128, 512], BF16))
    krelu = E(nc.sbuf_tensor("krelu", [128, 512], F32))
    r_tok = E(nc.sbuf_tensor("r_tok", [128, 2, D], F32))

    with ExitStack() as pctx, nc.Block() as blk:
        kfm_ps = [pctx.enter_context(nc.psum_tensor(f"kfmD{i}", [128, 512], F32)) for i in range(2)]
        r_ps = pctx.enter_context(nc.psum_tensor("rD", [128, D], F32))
        v_ps = pctx.enter_context(nc.psum_tensor("vD", [128, D], F32))

        @blk.vector
        def _(vector):
            vector.wait_ge(S.c_trev, 128)
            vector.wait_ge(S.bnd5, DC)
            vector.wait_ge(S.s_w, 80)
            for b in range(NB):
                if b >= 2:
                    vector.wait_ge(S.d_rv, b - 1)
                for dc in range(DC):
                    lo = b * 512
                    vector.tensor_sub(d_stag[:, :], h2_fm[:, dc, lo + 1:lo + 513], h2_fm[:, dc, lo:lo + 512])
                    vector.scalar_tensor_tensor(xr_b[:, b % 2, dc, :], d_stag[:, :], mur_t[:, dc:dc + 1],
                                                h2_fm[:, dc, lo:lo + 512], ALU.mult, ALU.add)
                    mx = vector.scalar_tensor_tensor(xk_b[:, b % 2, dc, :], d_stag[:, :], muk_t[:, dc:dc + 1],
                                                     h2_fm[:, dc, lo:lo + 512], ALU.mult, ALU.add)
                    if dc == DC - 1:
                        mx.then_inc(S.d_mix, 1)
                for jj in range(NB):
                    j = b * NB + jj
                    vector.wait_ge(S.d_vmm, j + 1)
                    vector.wait_ge(S.d_rev, j + 1)
                    vector.tensor_mul(r_tok[:, j % 2, :], r_tok[:, j % 2, :], v_ps[:, :]).then_inc(S.d_vev, 1)
                    vector.tensor_add(r_tok[:, j % 2, :], r_tok[:, j % 2, :], o_acc[:, j, :]).then_inc(S.d_oev, 1)
            vector.sem_inc(S.fin, 1)

        @blk.tensor
        def _(tensor):
            tensor.wait_ge(S.init, 82 + 48)
            for b in range(NB):
                tensor.wait_ge(S.d_mix, b + 1)
                for c in range(DC):
                    rot = b * DC + c
                    ps = kfm_ps[rot % 2]
                    if rot >= 2:
                        tensor.wait_ge(S.d_krel, rot - 1)
                    for dc in range(DC):
                        mm = tensor.matmul(ps[:, :], wfk_sb[:, dc, c * 128:(c + 1) * 128],
                                           xk_b[:, b % 2, dc, :],
                                           start=(dc == 0), stop=(dc == DC - 1))
                        if dc == DC - 1:
                            mm.then_inc(S.d_kmm, 1)
                for jj in range(NB):
                    j = b * NB + jj
                    if j >= 1:
                        tensor.wait_ge(S.d_rev, j)
                    for dc in range(DC):
                        for half in range(2):
                            mm = tensor.matmul(r_ps[:, half * 512:(half + 1) * 512],
                                               xr_b[:, b % 2, dc, jj * 128:(jj + 1) * 128],
                                               wr_sb[:, dc, half * 512:(half + 1) * 512],
                                               start=(dc == 0), stop=(dc == DC - 1))
                            if dc == DC - 1 and half == 1:
                                mm.then_inc(S.d_rmm, 1)
                    if j >= 1:
                        tensor.wait_ge(S.d_vev, j)
                    tensor.wait_ge(S.d_kact, DC * (b + 1))
                    for dc in range(DC):
                        for half in range(2):
                            mm = tensor.matmul(v_ps[:, half * 512:(half + 1) * 512],
                                               kact_b[:, dc, jj * 128:(jj + 1) * 128],
                                               wfv_sb[:, dc, half * 512:(half + 1) * 512],
                                               start=(dc == 0), stop=(dc == DC - 1))
                            if dc == DC - 1 and half == 1:
                                mm.then_inc(S.d_vmm, 1)
                                if jj == NB - 1:
                                    tensor.sem_inc(S.d_rv, 1)
            tensor.sem_inc(S.fin, 1)

        @blk.scalar
        def _(scalar):
            for b in range(NB):
                for c in range(DC):
                    rot = b * DC + c
                    scalar.wait_ge(S.d_kmm, rot + 1)
                    scalar.activation(krelu[:, :], kfm_ps[rot % 2][:, :], AF.Relu).then_inc(S.d_krel, 1)
                    scalar.activation(kact_b[:, c, :], krelu[:, :], AF.Square).then_inc(S.d_kact, 1)
                for jj in range(NB):
                    j = b * NB + jj
                    scalar.wait_ge(S.d_rmm, j + 1)
                    if j >= 2:
                        scalar.wait_ge(S.y_st if j % 2 == 0 else S.y_st1, 16 * (j // 2))
                    scalar.activation(r_tok[:, j % 2, :], r_ps[:, :], AF.Sigmoid).then_inc(S.d_rev, 1)
            scalar.sem_inc(S.fin, 1)

        @blk.sync
        def _(sync):
            for j in range(J):
                sync.wait_ge(S.d_oev, j + 1)
                sync.dma_start(out=y[j * 128:(j + 1) * 128, :],
                               in_=r_tok[:, j % 2, :]).then_inc(
                                   S.y_st if j % 2 == 0 else S.y_st1, 16)
            sync.wait_ge(S.y_st, 16 * (J // 2))
            sync.wait_ge(S.y_st1, 16 * (J // 2))
            if dbg:
                sync.wait_ge(S.dbgs, 16 * 16)

    root.close()
    return nc


_NC_CACHE = {}


def _get_nc(dbg=False):
    if dbg not in _NC_CACHE:
        _NC_CACHE[dbg] = build(dbg)
    return _NC_CACHE[dbg]


def make_in_maps(inputs):
    asn = lambda a: np.ascontiguousarray(np.asarray(a, dtype=np.float32))
    x = asn(inputs["L0_attn_x"]).reshape(NCORES * T, D)
    common = {
        "wqT": asn(np.asarray(inputs["L0_attn_wq"]).T),
        "wkT": asn(np.asarray(inputs["L0_attn_wk"]).T),
        "wvT": asn(np.asarray(inputs["L0_attn_wv"]).T),
        "wrT": asn(np.asarray(inputs["L0_ffn_W_r"]).T),
        "wfkT": asn(np.asarray(inputs["L0_ffn_W_k"]).T),
        "wfvT": asn(np.asarray(inputs["L0_ffn_W_v"]).T),
        "bq": np.ascontiguousarray(asn(inputs["L0_attn_bq"]).reshape(DC, 128).T),
        "bk": np.ascontiguousarray(asn(inputs["L0_attn_bk"]).reshape(DC, 128).T),
        "bv": asn(inputs["L0_attn_bv"]).reshape(1, D),
        "mur": np.ascontiguousarray(asn(inputs["L0_ffn_mu_r"]).reshape(DC, 128).T),
        "muk": np.ascontiguousarray(asn(inputs["L0_ffn_mu_k"]).reshape(DC, 128).T),
        "ident": np.eye(128, dtype=np.float32),
    }
    in_maps = []
    for i in range(NCORES):
        sel = np.zeros((NCORES, 1), np.float32)
        if i > 0:
            sel[i - 1, 0] = 1.0
        im = dict(common)
        im["x_sh"] = np.ascontiguousarray(x[i * T:(i + 1) * T, :])
        im["bsel"] = sel
        in_maps.append(im)
    return in_maps


def kernel(**inputs):
    nc = _get_nc(False)
    in_maps = make_in_maps(inputs)
    res = run_bass_kernel_spmd(nc, in_maps, core_ids=list(range(NCORES)))
    out = np.concatenate([r["y"] for r in res.results], axis=0)
    return out.reshape(1, NCORES * T, D)


# version 06
# speedup vs baseline: 1.2977x; 1.0537x over previous
import numpy as np
from contextlib import ExitStack

import concourse.bass as bass
import concourse.mybir as mybir
from concourse.bass_utils import run_bass_kernel_spmd

F32 = mybir.dt.float32
BF16 = mybir.dt.bfloat16
U16 = mybir.dt.uint16
AF = mybir.ActivationFunctionType
ALU = mybir.AluOpType
AX = mybir.AxisListType.X

NCORES = 8
D = 1024
DC = 8
T = 2048
J = 16
NG = 32
KC = 4
NB = 4
SCALE = 1.0 / 32.0
EPS = 1e-5

SEM_NAMES = [
    "init", "s_w", "s_x", "s_xfree", "lnA", "lnB", "lnC", "lnD", "lnE", "ln_h", "tr_p", "tr_ev",
    "a_mm", "a_nmm", "a_qev", "a_kev", "a_vev", "a_nev", "a_sq", "a_kst", "a_vst", "a_nst",
    "cc", "b_nld", "nm", "pe_s", "pe_p", "pe_o", "pe_e", "pe_g", "pe_sk", "pe_pts", "rcp",
    "c_x", "c_xf", "ag3w", "ag3r", "cA", "cB", "cC", "cD", "cE", "c_h",
    "c_trp", "c_trev", "bnd", "bnd5",
    "d_mix", "d_rv", "d_kmm", "d_krel", "d_kact", "d_rmm", "d_rev", "d_vmm", "d_vev",
    "d_oev", "y_st", "dbgs", "s_x1", "a_kst1", "a_vst1", "kA", "kA1", "kB", "kB1", "y_st1", "a_qm",
    "start", "fin",
]
BIG = 1 << 30


def build(dbg=False, stop_after=None):
    nc = bass.Bass("TRN2", target_bir_lowering=False)

    x_sh = nc.dram_tensor("x_sh", [T, D], F32, kind="ExternalInput")
    wqT = nc.dram_tensor("wqT", [D, D], F32, kind="ExternalInput")
    wkT = nc.dram_tensor("wkT", [D, D], F32, kind="ExternalInput")
    wvT = nc.dram_tensor("wvT", [D, D], F32, kind="ExternalInput")
    wrT = nc.dram_tensor("wrT", [D, D], F32, kind="ExternalInput")
    wfkT = nc.dram_tensor("wfkT", [D, D], F32, kind="ExternalInput")
    wfvT = nc.dram_tensor("wfvT", [D, D], F32, kind="ExternalInput")
    bq_d = nc.dram_tensor("bq", [128, DC], F32, kind="ExternalInput")
    bk_d = nc.dram_tensor("bk", [128, DC], F32, kind="ExternalInput")
    bv_d = nc.dram_tensor("bv", [1, D], F32, kind="ExternalInput")
    mur_d = nc.dram_tensor("mur", [128, DC], F32, kind="ExternalInput")
    muk_d = nc.dram_tensor("muk", [128, DC], F32, kind="ExternalInput")
    ident_d = nc.dram_tensor("ident", [128, 128], F32, kind="ExternalInput")
    bsel_d = nc.dram_tensor("bsel", [NCORES, 1], F32, kind="ExternalInput")
    y = nc.dram_tensor("y", [T, D], F32, kind="ExternalOutput")

    k_ag_in = nc.dram_tensor("k_ag_in", [4, DC, 128, 512], BF16, kind="Internal")
    k_ag_o = [nc.dram_tensor(f"k_ag_o{j}", [NCORES, DC, 128, 512], BF16, kind="Internal",
                             addr_space="Shared") for j in range(4)]
    v_ag_in = nc.dram_tensor("v_ag_in", [T, D], BF16, kind="Internal")
    v_ag_o = [nc.dram_tensor(f"v_ag_o{j}", [NCORES * 512, D], BF16, kind="Internal",
                             addr_space="Shared") for j in range(4)]
    n_ag_in = nc.dram_tensor("n_ag_in", [1, T], BF16, kind="Internal")
    n_ag_out = nc.dram_tensor("n_ag_out", [128, 128], BF16, kind="Internal")
    xb_ag_in = nc.dram_tensor("xb_ag_in", [1, D], F32, kind="Internal")
    xb_ag_out = nc.dram_tensor("xb_ag_out", [NCORES, D], F32, kind="Internal")
    if dbg:
        dbg_oacc = nc.dram_tensor("dbg_oacc", [128, J * D], F32, kind="ExternalOutput")
        dbg_rl = nc.dram_tensor("dbg_rl", [128, 16], F32, kind="ExternalOutput")
        dbg_negm = nc.dram_tensor("dbg_negm", [128, 1], F32, kind="ExternalOutput")
        dbg_h2 = nc.dram_tensor("dbg_h2", [128, DC * (T + 1)], U16, kind="ExternalOutput")
        dbg_xp = nc.dram_tensor("dbg_xp", [128, J * D], F32, kind="ExternalOutput")
        dbg_misc = nc.dram_tensor("dbg_misc", [1, 8], F32, kind="ExternalOutput")
        dbg_qn = nc.dram_tensor("dbg_qn", [1, T], U16, kind="ExternalOutput")
        dbg_kn = nc.dram_tensor("dbg_kn", [1, T], U16, kind="ExternalOutput")
        dbg_nrm = nc.dram_tensor("dbg_nrm", [128, 128], U16, kind="ExternalOutput")
        dbg_kag = nc.dram_tensor("dbg_kag", [128, 512], U16, kind="ExternalOutput")
        dbg_vag = nc.dram_tensor("dbg_vag", [128, 1024], U16, kind="ExternalOutput")
        dbg_h1 = nc.dram_tensor("dbg_h1", [128, DC * T], U16, kind="ExternalOutput")
        dbg_q = nc.dram_tensor("dbg_q", [128, DC * T], U16, kind="ExternalOutput")
        dbg_hbf = nc.dram_tensor("dbg_hbf", [128, 2 * D], U16, kind="ExternalOutput")

    RG = [list(range(NCORES))]

    root = ExitStack()
    E = root.enter_context
    ident_b = E(nc.sbuf_tensor("ident_b", [128, 128], BF16))
    ones_col = E(nc.sbuf_tensor("ones_col", [128, 1], BF16))
    ones_row = E(nc.sbuf_tensor("ones_row", [1, 128], BF16))
    ones32 = E(nc.sbuf_tensor("ones32", [128, 1], F32))
    bq_t = E(nc.sbuf_tensor("bq_t", [128, DC], F32))
    bk_t = E(nc.sbuf_tensor("bk_t", [128, DC], F32))
    mur_t = E(nc.sbuf_tensor("mur_t", [128, DC], F32))
    muk_t = E(nc.sbuf_tensor("muk_t", [128, DC], F32))
    bsel_t = E(nc.sbuf_tensor("bsel_t", [NCORES, 1], F32))
    neg_m = E(nc.sbuf_tensor("neg_m", [128, 1], F32))
    qmax2 = E(nc.sbuf_tensor("qmax2", [1, 1], F32))
    kmax2 = E(nc.sbuf_tensor("kmax2", [1, 1], F32))
    m2 = E(nc.sbuf_tensor("m2", [1, 1], F32))
    negm_s = E(nc.sbuf_tensor("negm_s", [1, 1], F32))
    negm_b = E(nc.sbuf_tensor("negm_b", [1, 1], BF16))
    nrm_all = E(nc.sbuf_tensor("nrm_all", [128, 128], BF16))
    knmax_c = E(nc.sbuf_tensor("knmax_c", [128, 1], BF16))
    recip_l = E(nc.sbuf_tensor("recip_l", [128, 16], F32))
    mneg = [E(nc.sbuf_tensor(f"mneg{i}", [128, 1], F32)) for i in range(2)]
    v_raw = [E(nc.sbuf_tensor(f"v_raw{i}", [128, 1], F32)) for i in range(2)]
    sq_in = [E(nc.sbuf_tensor(f"sq_in{i}", [128, 1], F32)) for i in range(2)]
    std_t = [E(nc.sbuf_tensor(f"std_t{i}", [128, 1], F32)) for i in range(2)]
    inv_t = [E(nc.sbuf_tensor(f"inv_t{i}", [128, 1], F32)) for i in range(2)]
    recip_g = E(nc.sbuf_tensor("recip_g", [128, 16], F32))
    negm_s2 = E(nc.sbuf_tensor("negm_s2", [1, 1], F32))
    bm2 = E(nc.sbuf_tensor("bm2", [1, 1], F32))
    btmpv = E(nc.sbuf_tensor("btmpv", [1, 1], F32))
    bnmi2 = E(nc.sbuf_tensor("bnmi2", [1, 1], F32))

    qctx = ExitStack()
    q_fm = qctx.enter_context(nc.sbuf_tensor("q_fm", [128, DC, T], BF16, side="right"))

    sems = {name: E(nc.semaphore(name)) for name in SEM_NAMES}
    S = type("S", (), sems)
    _nums = sorted(h.num for h in sems.values())
    assert _nums[-1] - _nums[0] == len(_nums) - 1, _nums
    sem_lo, sem_hi = _nums[0], _nums[-1] + 1

    ctxA = ExitStack()
    EA = ctxA.enter_context
    wq_sb = EA(nc.sbuf_tensor("wq_sb", [128, DC, D], BF16))
    wk_sb = EA(nc.sbuf_tensor("wk_sb", [128, DC, D], BF16))
    wv_sb = EA(nc.sbuf_tensor("wv_sb", [128, DC, D], BF16))
    h1_fm = EA(nc.sbuf_tensor("h1_fm", [128, DC, T], BF16))
    xstag = EA(nc.sbuf_tensor("xstag", [128, 2, D], F32))
    t_stag = EA(nc.sbuf_tensor("t_stag", [128, D], F32))
    tsq_st = EA(nc.sbuf_tensor("tsq_st", [128, D], BF16))
    h_bf = EA(nc.sbuf_tensor("h_bf", [128, 2, D], BF16))
    kstag = EA(nc.sbuf_tensor("kstag", [128, 2, T], BF16))
    vstag = EA(nc.sbuf_tensor("vstag", [128, 2, D], BF16))
    sq_st = EA(nc.sbuf_tensor("sq_st", [128, 2, T], BF16))
    qn_sb = EA(nc.sbuf_tensor("qn_sb", [1, T], BF16))
    kn_sb = EA(nc.sbuf_tensor("kn_sb", [1, T], BF16))
    bv_row = EA(nc.sbuf_tensor("bv_row", [1, D], BF16))

    with ExitStack() as pctx, nc.Block() as blk:
        trp = [pctx.enter_context(nc.psum_tensor(f"trpA{i}", [128, 128], BF16)) for i in range(2)]

        @blk.sync
        def _(sync):
            sync.wait_ge(S.start, BIG)
            sync.dma_start(out=bq_t[:, :], in_=bq_d[:, :]).then_inc(S.s_w, 16)
            sync.dma_start(out=bk_t[:, :], in_=bk_d[:, :]).then_inc(S.s_w, 16)
            sync.dma_start(out=mur_t[:, :], in_=mur_d[:, :]).then_inc(S.s_w, 16)
            sync.dma_start(out=muk_t[:, :], in_=muk_d[:, :]).then_inc(S.s_w, 16)
            sync.dma_start(out=bsel_t[:, :], in_=bsel_d[:, :]).then_inc(S.s_w, 16)
            for j in range(J):
                if j >= 2:
                    sync.wait_ge(S.s_xfree, j - 1)
                sync.dma_start(out=xstag[:, j % 2, :], in_=x_sh[j * 128:(j + 1) * 128, :]).then_inc(
                    S.s_x if j % 2 == 0 else S.s_x1, 16)

        @blk.gpsimd
        def _(gpsimd):
            gpsimd.sem_clear(range(sem_lo, sem_hi))
            gpsimd.sem_inc(S.start, BIG)
            gpsimd.dma_start(out=ident_b[:, :], in_=ident_d[:, :]).then_inc(S.init, 16)
            gpsimd.dma_start(out=bv_row[:, :], in_=bv_d[:, :]).then_inc(S.init, 16)
            gpsimd.memset(ones_col[:, :], 1.0)
            gpsimd.memset(ones_row[:, :], 1.0).then_inc(S.init, 1)
            gpsimd.dma_start(out=wq_sb.ap(),
                             in_=wqT.ap().rearrange("(c p) n -> p c n", p=128)).then_inc(S.init, 16)
            gpsimd.dma_start(out=wk_sb.ap(),
                             in_=wkT.ap().rearrange("(c p) n -> p c n", p=128)).then_inc(S.init, 16)
            gpsimd.dma_start(out=wv_sb.ap(),
                             in_=wvT.ap().rearrange("(c p) n -> p c n", p=128)).then_inc(S.init, 16)

        @blk.vector
        def _(vector):
            vector.wait_ge(S.start, BIG)
            for j in range(J):
                r = j % 2
                vector.wait_ge(S.s_x if j % 2 == 0 else S.s_x1, 16 * (j // 2 + 1))
                if j >= 2:
                    vector.wait_ge(S.s_xfree, j - 1)
                vector.reduce_sum(mneg[r][:, :], xstag[:, r, :], axis=AX, negate=True).then_inc(S.lnA, 1)
                vector.wait_ge(S.lnB, j + 1)
                if j >= 2:
                    vector.wait_ge(S.lnD, j - 1)
                vector.tensor_scalar(sq_in[r][:, :], v_raw[r][:, :], 1.0 / D, D * D * EPS, ALU.mult, ALU.add).then_inc(S.lnC, 1)
                vector.wait_ge(S.lnD, j + 1)
                if j >= 2:
                    vector.wait_ge(S.ln_h, j - 1)
                vector.reciprocal(inv_t[r][:, :], std_t[r][:, :]).then_inc(S.lnE, 1)
                for dc in range(DC):
                    cnt = j * DC + dc
                    vector.wait_ge(S.tr_p, cnt + 1)
                    vector.tensor_copy(h1_fm[:, dc, j * 128:(j + 1) * 128], trp[cnt % 2][:, :]).then_inc(S.tr_ev, 1)

        @blk.scalar
        def _(scalar):
            scalar.wait_ge(S.start, BIG)
            for j in range(J):
                r = j % 2
                scalar.wait_ge(S.lnA, j + 1)
                scalar.activation(t_stag[:, :], xstag[:, r, :], AF.Identity,
                                  bias=mneg[r][:, :], scale=float(D)).then_inc(S.s_xfree, 1)
                if j >= 2:
                    scalar.wait_ge(S.lnC, j - 1)
                scalar.activation(tsq_st[:, :], t_stag[:, :], AF.Square,
                                  accum_out=v_raw[r][:, :]).then_inc(S.lnB, 1)
                scalar.wait_ge(S.lnC, j + 1)
                if j >= 2:
                    scalar.wait_ge(S.lnE, j - 1)
                scalar.activation(std_t[r][:, :], sq_in[r][:, :], AF.Sqrt).then_inc(S.lnD, 1)
                scalar.wait_ge(S.lnE, j + 1)
                if j >= 2:
                    scalar.wait_ge(S.tr_p, 8 * (j - 1))
                scalar.activation(h_bf[:, r, :], t_stag[:, :], AF.Copy, scale=inv_t[r][:, :]).then_inc(S.ln_h, 1)

        @blk.tensor
        def _(tensor):
            tensor.wait_ge(S.start, BIG)
            tensor.wait_ge(S.init, 33)
            for j in range(J):
                tensor.wait_ge(S.ln_h, j + 1)
                for dc in range(DC):
                    cnt = j * DC + dc
                    if cnt >= 2:
                        tensor.wait_ge(S.tr_ev, cnt - 1)
                    tensor.transpose(trp[cnt % 2][:, :], h_bf[:, j % 2, dc * 128:(dc + 1) * 128],
                                     ident_b[:, :]).then_inc(S.tr_p, 1)

    with ExitStack() as pctx, nc.Block() as blk:
        qkv_ps = [pctx.enter_context(nc.psum_tensor(f"qkvA{i}", [128, D], F32)) for i in range(2)]
        nrm_ps = pctx.enter_context(nc.psum_tensor("nrmA", [1, T], F32))

        def _nrm_mms(tensor, c, off):
            tensor.wait_ge(S.a_sq, off + c + 1)
            if off == DC and c == 0:
                tensor.wait_ge(S.a_nst, 1)
            for b in range(NB):
                mm = tensor.matmul(nrm_ps[0:1, b * 512:(b + 1) * 512], ones_col[:, :],
                                   sq_st[:, c % 2, b * 512:(b + 1) * 512],
                                   start=(c == 0), stop=(c == DC - 1))
                if b == NB - 1:
                    mm.then_inc(S.a_nmm, 1)

        @blk.tensor
        def _(tensor):
            tensor.wait_ge(S.tr_ev, J * DC)
            tensor.wait_ge(S.init, 81)
            for c in range(DC):
                for b in range(NB):
                    n = c * NB + b
                    ps = qkv_ps[n % 2]
                    if n >= 2:
                        tensor.wait_ge(S.a_kev, n - 1)
                    for dc in range(DC):
                        mm = tensor.matmul(ps[:, 0:512], wk_sb[:, dc, c * 128:(c + 1) * 128],
                                           h1_fm[:, dc, b * 512:(b + 1) * 512],
                                           start=(dc == 0), stop=(dc == DC - 1))
                        if dc == DC - 1:
                            mm.then_inc(S.a_mm, 1)
                if c >= 1:
                    _nrm_mms(tensor, c - 1, 0)
            _nrm_mms(tensor, DC - 1, 0)
            for t in range(J):
                ps = qkv_ps[t % 2]
                if t == 0:
                    tensor.wait_ge(S.a_kev, 31)
                elif t == 1:
                    tensor.wait_ge(S.a_kev, 32)
                else:
                    tensor.wait_ge(S.a_vev, t - 1)
                for dc in range(DC):
                    for half in range(2):
                        tensor.matmul(ps[:, half * 512:(half + 1) * 512],
                                      h1_fm[:, dc, t * 128:(t + 1) * 128],
                                      wv_sb[:, dc, half * 512:(half + 1) * 512],
                                      start=(dc == 0), stop=False)
                for half in range(2):
                    mm = tensor.matmul(ps[:, half * 512:(half + 1) * 512], ones_row[:, :],
                                       bv_row[:, half * 512:(half + 1) * 512], start=False, stop=True)
                    if half == 1:
                        mm.then_inc(S.a_mm, 1)
            for c in range(DC):
                for b in range(NB):
                    n = c * NB + b
                    ps = qkv_ps[n % 2]
                    if n == 0:
                        tensor.wait_ge(S.a_vev, 15)
                    elif n == 1:
                        tensor.wait_ge(S.a_vev, 16)
                    else:
                        tensor.wait_ge(S.a_qev, n - 1)
                    for dc in range(DC):
                        mm = tensor.matmul(ps[:, 0:512], wq_sb[:, dc, c * 128:(c + 1) * 128],
                                           h1_fm[:, dc, b * 512:(b + 1) * 512],
                                           start=(dc == 0), stop=(dc == DC - 1))
                        if dc == DC - 1:
                            mm.then_inc(S.a_mm, 1)
                if c >= 1:
                    _nrm_mms(tensor, c - 1, DC)
            _nrm_mms(tensor, DC - 1, DC)

        @blk.scalar
        def _(scalar):
            scalar.wait_ge(S.s_w, 80)
            for c in range(DC):
                if c >= 2:
                    scalar.wait_ge(S.a_kst if c % 2 == 0 else S.a_kst1, 64 * (c // 2))
                for b in range(NB):
                    n = c * NB + b
                    scalar.wait_ge(S.a_mm, n + 1)
                    scalar.activation(kstag[:, c % 2, b * 512:(b + 1) * 512], qkv_ps[n % 2][:, 0:512],
                                      AF.Identity, bias=bk_t[:, c:c + 1]).then_inc(S.a_kev, 1)
                if c >= 2:
                    scalar.wait_ge(S.a_nmm, c - 1)
                scalar.activation(sq_st[:, c % 2, :], kstag[:, c % 2, :], AF.Square).then_inc(S.a_sq, 1)
            scalar.wait_ge(S.a_nmm, DC)
            scalar.activation(kn_sb[:, :], nrm_ps[0:1, :], AF.Copy).then_inc(S.a_nst, 1)
            for c in range(DC):
                for b in range(NB):
                    n = 48 + c * NB + b
                    scalar.wait_ge(S.a_mm, n + 1)
                    scalar.activation(q_fm[:, c, b * 512:(b + 1) * 512], qkv_ps[(c * NB + b) % 2][:, 0:512],
                                      AF.Identity, bias=bq_t[:, c:c + 1]).then_inc(S.a_qev, 1)
                scalar.wait_ge(S.a_nmm, 7 + c)
                scalar.activation(sq_st[:, c % 2, :], q_fm[:, c, :], AF.Square).then_inc(S.a_sq, 1)
            scalar.wait_ge(S.a_nmm, 2 * DC)
            scalar.activation(qn_sb[:, :], nrm_ps[0:1, :], AF.Copy).then_inc(S.a_nev, 1)

        @blk.vector
        def _(vector):
            for t in range(J):
                n = 32 + t
                if t >= 2:
                    vector.wait_ge(S.a_vst if t % 2 == 0 else S.a_vst1, 16 * (t // 2))
                vector.wait_ge(S.a_mm, n + 1)
                vector.tensor_copy(vstag[:, t % 2, :], qkv_ps[t % 2][:, :]).then_inc(S.a_vev, 1)
            vector.wait_ge(S.a_nev, 1)
            vector.reduce_max(qmax2[:, :], qn_sb[:, :], axis=AX).then_inc(S.a_qm, 1)

        @blk.sync
        def _(sync):
            for c in range(DC):
                sync.wait_ge(S.a_kev, NB * (c + 1))
                for gl in range(4):
                    sync.dma_start(out=k_ag_in[gl, c, :, :],
                                   in_=kstag[:, c % 2, gl * 512:(gl + 1) * 512]).then_inc(
                                       S.a_kst if c % 2 == 0 else S.a_kst1, 16)
            for t in range(J):
                sync.wait_ge(S.a_vev, t + 1)
                sync.dma_start(out=v_ag_in[t * 128:(t + 1) * 128, :], in_=vstag[:, t % 2, :]).then_inc(
                    S.a_vst if t % 2 == 0 else S.a_vst1, 16)
            sync.wait_ge(S.a_nst, 1)
            sync.dma_start(out=n_ag_in[:, :], in_=kn_sb[:, :]).then_inc(S.a_nst, 16)

        @blk.gpsimd
        def _(gpsimd):
            gpsimd.wait_ge(S.a_kst, 16 * 16)
            gpsimd.wait_ge(S.a_kst1, 16 * 16)
            gpsimd.collective_compute("AllGather", ALU.bypass, replica_groups=RG,
                                      ins=[k_ag_in.ap()[0:1, :, :, :].opt()],
                                      outs=[k_ag_o[0].ap().opt()]).then_inc(S.cc, 1)
            gpsimd.wait_ge(S.a_nst, 17)
            gpsimd.collective_compute("AllGather", ALU.bypass, replica_groups=RG,
                                      ins=[n_ag_in.ap().opt()], outs=[n_ag_out.ap().opt()]).then_inc(S.cc, 1)
            for j in range(4):
                gpsimd.wait_ge(S.a_vst, 16 * (2 * j + 2))
                gpsimd.wait_ge(S.a_vst1, 16 * (2 * j + 2))
                gpsimd.collective_compute("AllGather", ALU.bypass, replica_groups=RG,
                                          ins=[v_ag_in.ap()[j * 512:(j + 1) * 512, :].opt()],
                                          outs=[v_ag_o[j].ap().opt()]).then_inc(S.cc, 1)
                if j < 3:
                    gpsimd.collective_compute("AllGather", ALU.bypass, replica_groups=RG,
                                              ins=[k_ag_in.ap()[j + 1:j + 2, :, :, :].opt()],
                                              outs=[k_ag_o[j + 1].ap().opt()]).then_inc(S.cc, 1)
            gpsimd.wait_ge(S.cc, 2)
            gpsimd.dma_start(out=nrm_all[:, :], in_=n_ag_out[:, :]).then_inc(S.b_nld, 16)

    ctxA.close()
    if stop_after == "A":
        root.close()
        return nc
    o_acc = E(nc.sbuf_tensor("o_acc", [128, J, D], F32))

    ctxB = ExitStack()
    EB = ctxB.enter_context
    vbuf = EB(nc.sbuf_tensor("vbuf", [128, 2, KC, 1024], BF16))
    kbuf = EB(nc.sbuf_tensor("kbuf", [128, 2, DC, 512], BF16))
    pt = [EB(nc.sbuf_tensor(f"pt{i}", [128, KC, T], BF16)) for i in range(2)]
    gsum = EB(nc.sbuf_tensor("gsum", [128, T], F32))

    with ExitStack() as pctx, nc.Block() as blk:
        s_ps = [pctx.enter_context(nc.psum_tensor(f"sB{i}", [128, 512], F32)) for i in range(2)]
        o_ps = [pctx.enter_context(nc.psum_tensor(f"oB{i}", [128, D], F32)) for i in range(2)]
        l_ps = pctx.enter_context(nc.psum_tensor("l_ps", [128, 512], F32))
        nm_ps = pctx.enter_context(nc.psum_tensor("nmB", [128, 512], F32))

        @blk.gpsimd
        def _(gpsimd):
            gpsimd.memset(ones32[:, :], 1.0).then_inc(S.init, 1)
            gpsimd.wait_ge(S.nm, 3)
            gpsimd.tensor_mul(m2[:, :], kmax2[:, :], qmax2[:, :]).then_inc(S.nm, 1)
            gpsimd.wait_ge(S.nm, 5)
            gpsimd.tensor_scalar_mul(negm_s2[:, :], negm_s[:, :], -SCALE).then_inc(S.nm, 1)
            gpsimd.wait_ge(S.rcp, 1)
            gpsimd.tensor_copy(recip_g[:, :], recip_l[:, :]).then_inc(S.rcp, 1)

        @blk.vector
        def _(vector):
            vector.wait_ge(S.b_nld, 16)
            vector.reduce_max(knmax_c[:, :], nrm_all[:, :], axis=AX).then_inc(S.nm, 1)
            vector.wait_ge(S.nm, 2)
            vector.reduce_max(kmax2[:, :], nm_ps[0:1, 0:64].bitcast(BF16), axis=AX).then_inc(S.nm, 1)
            vector.wait_ge(S.nm, 8)
            vector.tensor_copy(neg_m[:, :], nm_ps[:, 128:129]).then_inc(S.nm, 1)
            n = 0
            for g in range(NG):
                vector.wait_ge(S.pe_p, 16 * (g + 1))
                ptg2 = pt[g % 2]
                if g == 0:
                    vector.tensor_tensor(gsum[:, :], ptg2[:, 0, :], ptg2[:, 1, :], ALU.add)
                else:
                    vector.tensor_tensor(gsum[:, :], gsum[:, :], ptg2[:, 0, :], ALU.add)
                    vector.tensor_tensor(gsum[:, :], gsum[:, :], ptg2[:, 1, :], ALU.add)
                vector.tensor_tensor(gsum[:, :], gsum[:, :], ptg2[:, 2, :], ALU.add)
                vector.tensor_tensor(gsum[:, :], gsum[:, :], ptg2[:, 3, :], ALU.add).then_inc(S.pe_pts, 1)
                for j in range(J):
                    vector.wait_ge(S.pe_o, n + 1)
                    if g == 0:
                        vector.tensor_copy(o_acc[:, j, :], o_ps[n % 2][:, :]).then_inc(S.pe_e, 1)
                    else:
                        vector.tensor_tensor(o_acc[:, j, :], o_acc[:, j, :], o_ps[n % 2][:, :],
                                             ALU.add).then_inc(S.pe_e, 1)
                    n += 1
            vector.wait_ge(S.pe_g, NG + 1)
            vector.reciprocal(recip_l[:, :], l_ps[:, 0:16]).then_inc(S.rcp, 1)

        @blk.scalar
        def _(scalar):
            scalar.wait_ge(S.nm, 4)
            scalar.activation(negm_s[:, :], m2[:, :], AF.Sqrt).then_inc(S.nm, 1)
            scalar.wait_ge(S.nm, 6)
            scalar.activation(negm_b[:, :], negm_s2[:, :], AF.Copy).then_inc(S.nm, 1)
            n = 0
            scalar.wait_ge(S.a_nst, 17)
            scalar.wait_ge(S.a_qm, 1)
            for g in range(NG):
                if g == 0:
                    scalar.wait_ge(S.nm, 9)
                for kc in range(KC):
                    for b in range(NB):
                        scalar.wait_ge(S.pe_s, n + 1)
                        scalar.activation(pt[g % 2][:, kc, b * 512:(b + 1) * 512], s_ps[n % 2][:, :],
                                          AF.Exp, bias=neg_m[:, :], scale=SCALE).then_inc(S.pe_p, 1)
                        n += 1

        @blk.tensor
        def _(tensor):
            tensor.wait_ge(S.nm, 1)
            tensor.transpose(nm_ps[0:1, 0:64].bitcast(BF16), knmax_c[:, :], ident_b[:, :]).then_inc(S.nm, 1)
            tensor.wait_ge(S.nm, 7)
            tensor.matmul(nm_ps[:, 128:129], ones_row[:, :], negm_b[:, :], start=True, stop=True).then_inc(S.nm, 1)
            tensor.wait_ge(S.a_qev, 32)
            tensor.wait_ge(S.init, 82)
            n_s = 0
            n_o = 0
            for g in range(NG + 1):
                if g < NG:
                    tensor.wait_ge(S.kA if g % 2 == 0 else S.kA1, 16 * (g // 2 + 1))
                    for kc in range(KC):
                        for b in range(NB):
                            ps = s_ps[n_s % 2]
                            if n_s >= 2:
                                tensor.wait_ge(S.pe_p, n_s - 1)
                            for dc in range(DC):
                                mm = tensor.matmul(ps[:, :], kbuf[:, g % 2, dc, kc * 128:(kc + 1) * 128],
                                                   q_fm[:, dc, b * 512:(b + 1) * 512],
                                                   start=(dc == 0), stop=(dc == DC - 1))
                                if dc == DC - 1:
                                    mm.then_inc(S.pe_s, 1)
                            n_s += 1
                    tensor.nop().then_inc(S.pe_sk, 1)
                if g >= 1:
                    gp = g - 1
                    tensor.wait_ge(S.pe_p, gp * 16 + 16)
                    tensor.wait_ge(S.kB if gp % 2 == 0 else S.kB1, 16 * (gp // 2 + 1))
                    ptg = pt[gp % 2]
                    for j in range(J):
                        op = o_ps[n_o % 2]
                        if n_o >= 2:
                            tensor.wait_ge(S.pe_e, n_o - 1)
                        for kc in range(KC):
                            stat = ptg[:, kc, j * 128:(j + 1) * 128]
                            tensor.matmul(op[:, 0:512], stat, vbuf[:, gp % 2, kc, 0:512],
                                          start=(kc == 0), stop=(kc == KC - 1))
                            mm = tensor.matmul(op[:, 512:1024], stat, vbuf[:, gp % 2, kc, 512:1024],
                                               start=(kc == 0), stop=(kc == KC - 1))
                            if kc == KC - 1:
                                mm.then_inc(S.pe_o, 1)
                        n_o += 1
                    tensor.nop().then_inc(S.pe_g, 1)
            tensor.wait_ge(S.pe_pts, NG)
            for j in range(J):
                mm = tensor.matmul(l_ps[:, j:j + 1], gsum[:, j * 128:(j + 1) * 128],
                                   ones32[:, :], start=(j == 0), stop=(j == J - 1))
            mm.then_inc(S.pe_g, 1)

        @blk.sync
        def _(sync):
            for gi in range(NG):
                j, r = gi // 8, gi % 8
                ka = S.kA if gi % 2 == 0 else S.kA1
                kb = S.kB if gi % 2 == 0 else S.kB1
                if gi >= 2:
                    sync.wait_ge(S.pe_sk, gi - 1)
                if gi == 0:
                    sync.wait_ge(S.a_vst, 16 * 8)
                    sync.wait_ge(S.a_vst1, 16 * 8)
                sync.wait_ge(S.cc, 1 if j == 0 else 2 * j + 2)
                sync.dma_start(out=kbuf.ap()[:, gi % 2:gi % 2 + 1, :, :],
                               in_=k_ag_o[j].ap()[r:r + 1, :, :, :].rearrange(
                                   "o c p t -> p o c t")).then_inc(ka, 16)
                if gi >= 2:
                    sync.wait_ge(S.pe_g, gi - 1)
                if gi == 0:
                    sync.wait_ge(S.a_mm, 80)
                sync.wait_ge(S.cc, 2 * j + 3)
                sync.dma_start(out=vbuf.ap()[:, gi % 2, :, :],
                               in_=v_ag_o[j].ap()[r * 512:(r + 1) * 512, :].rearrange(
                                   "(kc p) d -> p kc d", p=128)).then_inc(kb, 16)

        if dbg:
            @blk.scalar
            def _(scalar):
                scalar.wait_ge(S.rcp, 1)
                scalar.dma_start(out=dbg_rl[:, :], in_=recip_l[:, :]).then_inc(S.dbgs, 16)
                scalar.dma_start(out=dbg_negm[:, :], in_=neg_m[:, :]).then_inc(S.dbgs, 16)
                scalar.dma_start(out=dbg_oacc[:, :], in_=o_acc.ap().rearrange("p j d -> p (j d)")).then_inc(S.dbgs, 16)
                scalar.dma_start(out=dbg_misc[:, 0:1], in_=qmax2[:, :]).then_inc(S.dbgs, 16)
                scalar.dma_start(out=dbg_misc[:, 1:2], in_=kmax2[:, :]).then_inc(S.dbgs, 16)
                scalar.dma_start(out=dbg_misc[:, 2:3], in_=m2[:, :]).then_inc(S.dbgs, 16)
                scalar.dma_start(out=dbg_nrm[:, :], in_=nrm_all.ap().bitcast(U16)).then_inc(S.dbgs, 16)
                scalar.dma_start(out=dbg_kag[:, :],
                                 in_=k_ag_o[0].ap()[0, 0, :, :].bitcast(U16)).then_inc(S.dbgs, 16)
                scalar.dma_start(out=dbg_vag[:, :],
                                 in_=v_ag_o[0].ap()[0:128, :].bitcast(U16)).then_inc(S.dbgs, 16)

    ctxB.close()
    qctx.close()
    if stop_after == "B":
        root.close()
        return nc
    wr_sb = E(nc.sbuf_tensor("wr_sb", [128, DC, D], BF16))
    wfk_sb = E(nc.sbuf_tensor("wfk_sb", [128, DC, D], BF16))
    wfv_sb = E(nc.sbuf_tensor("wfv_sb", [128, DC, D], BF16))
    h2_fm = E(nc.sbuf_tensor("h2_fm", [128, DC, T + 1], BF16))
    cctx = ExitStack()
    EC = cctx.enter_context
    xstag2 = EC(nc.sbuf_tensor("xstag2", [128, D], F32))
    h2tok = EC(nc.sbuf_tensor("h2tok", [128, D], BF16))
    tsq2 = EC(nc.sbuf_tensor("tsq2", [128, D], F32))
    cdust = EC(nc.sbuf_tensor("cdust", [128, D], BF16))
    ag3_sb = EC(nc.sbuf_tensor("ag3_sb", [NCORES, D], F32))
    xbnd_f = EC(nc.sbuf_tensor("xbnd_f", [128, DC], F32))
    xbnd_b = EC(nc.sbuf_tensor("xbnd_b", [128, DC], BF16))
    sqb = EC(nc.sbuf_tensor("sqb", [128, DC], BF16))
    bm = EC(nc.sbuf_tensor("bm", [1, 1], F32))
    bs2 = EC(nc.sbuf_tensor("bs2", [1, 1], F32))
    bvar = EC(nc.sbuf_tensor("bvar", [1, 1], F32))
    bstd = EC(nc.sbuf_tensor("bstd", [1, 1], F32))
    binv = EC(nc.sbuf_tensor("binv", [1, 1], F32))
    bnegmi = EC(nc.sbuf_tensor("bnegmi", [1, 1], F32))
    binv_b = EC(nc.sbuf_tensor("binv_b", [1, 1], BF16))
    bnmi_b = EC(nc.sbuf_tensor("bnmi_b", [1, 1], BF16))
    binv_t = EC(nc.sbuf_tensor("binv_t", [128, 1], F32))
    bnmi_t = EC(nc.sbuf_tensor("bnmi_t", [128, 1], F32))

    JORD = [15] + list(range(15))

    with ExitStack() as pctx, nc.Block() as blk:
        trp2 = [pctx.enter_context(nc.psum_tensor(f"trpC{i}", [128, 128], BF16)) for i in range(2)]
        bnd_ps = pctx.enter_context(nc.psum_tensor("bndC", [128, 512], F32))
        st_ps = pctx.enter_context(nc.psum_tensor("stC", [1, 512], F32))

        @blk.sync
        def _(sync):
            for i, j in enumerate(JORD):
                if i >= 1:
                    sync.wait_ge(S.c_xf, i)
                sync.dma_start(out=xstag2[:, :], in_=x_sh[j * 128:(j + 1) * 128, :]).then_inc(S.c_x, 16)
            sync.wait_ge(S.cc, 10)
            sync.dma_start(out=ag3_sb[:, :], in_=xb_ag_out[:, :]).then_inc(S.ag3r, 16)

        @blk.gpsimd
        def _(gpsimd):
            gpsimd.wait_ge(S.c_xf, 1)
            gpsimd.dma_start(out=xb_ag_in[:, :], in_=o_acc.ap()[127:128, 15, :]).then_inc(S.ag3w, 16)
            gpsimd.wait_ge(S.ag3w, 16)
            gpsimd.collective_compute("AllGather", ALU.bypass, replica_groups=RG,
                                      ins=[xb_ag_in.ap().opt()], outs=[xb_ag_out.ap().opt()]).then_inc(S.cc, 1)
            gpsimd.wait_ge(S.pe_g, NG)
            gpsimd.dma_start(out=wfk_sb.ap(),
                             in_=wfkT.ap().rearrange("(c p) n -> p c n", p=128)).then_inc(S.init, 16)
            gpsimd.dma_start(out=wr_sb.ap(),
                             in_=wrT.ap().rearrange("(c p) n -> p c n", p=128)).then_inc(S.init, 16)
            gpsimd.dma_start(out=wfv_sb.ap(),
                             in_=wfvT.ap().rearrange("(c p) n -> p c n", p=128)).then_inc(S.init, 16)
            gpsimd.wait_ge(S.bnd, 4)
            gpsimd.tensor_scalar_mul(bm[:, :], bm[:, :], 1.0 / D).then_inc(S.bnd, 1)
            gpsimd.wait_ge(S.bnd, 6)
            gpsimd.tensor_scalar(btmpv[:, :], bs2[:, :], 1.0 / D, EPS, ALU.mult, ALU.add).then_inc(S.bnd, 1)
            gpsimd.wait_ge(S.bnd, 10)
            gpsimd.tensor_mul(bnegmi[:, :], bm[:, :], binv[:, :]).then_inc(S.bnd, 1)

        @blk.vector
        def _(vector):
            for i, j in enumerate(JORD):
                r = i % 2
                vector.wait_ge(S.c_x, 16 * (i + 1))
                vector.wait_ge(S.rcp, 2)
                if dbg and i == 0:
                    vector.wait_ge(S.dbgs, 16 * 14)
                vector.scalar_tensor_tensor(o_acc[:, j, :], o_acc[:, j, :], recip_g[:, j:j + 1],
                                            xstag2[:, :], ALU.mult, ALU.add).then_inc(S.c_xf, 1)
                if i >= 2:
                    vector.wait_ge(S.cB, i - 1)
                vector.reduce_sum(mneg[r][:, :], o_acc[:, j, :], axis=AX, negate=True).then_inc(S.cA, 1)
                vector.wait_ge(S.cB, i + 1)
                if i >= 2:
                    vector.wait_ge(S.cD, i - 1)
                vector.tensor_scalar(sq_in[r][:, :], v_raw[r][:, :], 1.0 / D, D * D * EPS, ALU.mult, ALU.add).then_inc(S.cC, 1)
                vector.wait_ge(S.cD, i + 1)
                if i >= 2:
                    vector.wait_ge(S.c_h, i - 1)
                vector.reciprocal(inv_t[r][:, :], std_t[r][:, :]).then_inc(S.cE, 1)
                for dc in range(DC):
                    cnt = i * DC + dc
                    vector.wait_ge(S.c_trp, cnt + 1)
                    vector.tensor_copy(h2_fm[:, dc, 1 + j * 128:1 + (j + 1) * 128],
                                       trp2[cnt % 2][:, :]).then_inc(S.c_trev, 1)
            vector.wait_ge(S.bnd, 3)
            vector.reduce_sum(bm[:, :], st_ps[0:1, 0:DC], axis=AX)
            vector.reduce_sum(bs2[:, :], st_ps[0:1, 256:256 + DC], axis=AX).then_inc(S.bnd, 1)
            vector.wait_ge(S.bnd, 5)
            vector.tensor_mul(bm2[:, :], bm[:, :], bm[:, :]).then_inc(S.bnd, 1)
            vector.wait_ge(S.bnd, 7)
            vector.tensor_sub(bvar[:, :], btmpv[:, :], bm2[:, :]).then_inc(S.bnd, 1)
            vector.wait_ge(S.bnd, 9)
            vector.reciprocal(binv[:, :], bstd[:, :]).then_inc(S.bnd, 1)
            vector.wait_ge(S.bnd, 11)
            vector.tensor_scalar_mul(bnmi2[:, :], bnegmi[:, :], -1.0).then_inc(S.bnd, 1)
            vector.wait_ge(S.bnd, 14)
            vector.tensor_copy(binv_t[:, :], bnd_ps[:, 64:65])
            vector.tensor_copy(bnmi_t[:, :], bnd_ps[:, 65:66]).then_inc(S.bnd, 1)

        @blk.scalar
        def _(scalar):
            scalar.wait_ge(S.start, BIG)
            for i, j in enumerate(JORD):
                r = i % 2
                scalar.wait_ge(S.cA, i + 1)
                scalar.activation(tsq2[:, :], o_acc[:, j, :], AF.Identity,
                                  bias=mneg[r][:, :], scale=float(D))
                if i >= 2:
                    scalar.wait_ge(S.cC, i - 1)
                scalar.activation(cdust[:, :], tsq2[:, :], AF.Square,
                                  accum_out=v_raw[r][:, :]).then_inc(S.cB, 1)
                scalar.wait_ge(S.cC, i + 1)
                if i >= 2:
                    scalar.wait_ge(S.cE, i - 1)
                scalar.activation(std_t[r][:, :], sq_in[r][:, :], AF.Sqrt).then_inc(S.cD, 1)
                scalar.wait_ge(S.cE, i + 1)
                scalar.wait_ge(S.c_trp, 8 * i)
                scalar.activation(h2tok[:, :], tsq2[:, :], AF.Copy, scale=inv_t[r][:, :]).then_inc(S.c_h, 1)
            scalar.wait_ge(S.bnd, 1)
            scalar.activation(xbnd_f[:, :], bnd_ps[:, 0:DC], AF.Copy)
            scalar.activation(xbnd_b[:, :], bnd_ps[:, 0:DC], AF.Copy)
            scalar.activation(sqb[:, :], bnd_ps[:, 0:DC], AF.Square).then_inc(S.bnd, 1)
            scalar.wait_ge(S.bnd, 8)
            scalar.activation(bstd[:, :], bvar[:, :], AF.Sqrt).then_inc(S.bnd, 1)
            scalar.wait_ge(S.bnd, 12)
            scalar.activation(binv_b[:, :], binv[:, :], AF.Copy)
            scalar.activation(bnmi_b[:, :], bnmi2[:, :], AF.Copy).then_inc(S.bnd, 1)
            scalar.wait_ge(S.bnd, 15)
            for dc in range(DC):
                scalar.activation(h2_fm[:, dc, 0:1], xbnd_f[:, dc:dc + 1], AF.Identity,
                                  scale=binv_t[:, :], bias=bnmi_t[:, :]).then_inc(S.bnd5, 1)

        @blk.tensor
        def _(tensor):
            for i, j in enumerate(JORD):
                tensor.wait_ge(S.c_h, i + 1)
                for dc in range(DC):
                    cnt = i * DC + dc
                    if cnt >= 2:
                        tensor.wait_ge(S.c_trev, cnt - 1)
                    tensor.transpose(trp2[cnt % 2][:, :], h2tok[:, dc * 128:(dc + 1) * 128],
                                     ident_b[:, :]).then_inc(S.c_trp, 1)
            tensor.wait_ge(S.ag3r, 16)
            tensor.wait_ge(S.s_w, 80)
            for dc in range(DC):
                mm = tensor.matmul(bnd_ps[:, dc:dc + 1], ag3_sb[:, dc * 128:(dc + 1) * 128], bsel_t[:, :],
                                   start=(dc == 0), stop=(dc == DC - 1))
                if dc == DC - 1:
                    mm.then_inc(S.bnd, 1)
            tensor.wait_ge(S.bnd, 2)
            tensor.matmul(st_ps[0:1, 0:DC], ones_col[:, :], xbnd_b[:, :], start=True, stop=False)
            tensor.matmul(st_ps[0:1, 256:256 + DC], ones_col[:, :], sqb[:, :],
                          start=False, stop=True).then_inc(S.bnd, 1)
            tensor.wait_ge(S.bnd, 13)
            tensor.matmul(bnd_ps[:, 64:65], ones_row[:, :], binv_b[:, :], start=True, stop=False)
            tensor.matmul(bnd_ps[:, 65:66], ones_row[:, :], bnmi_b[:, :],
                          start=False, stop=True).then_inc(S.bnd, 1)

        if dbg:
            @blk.scalar
            def _(scalar):
                scalar.wait_ge(S.bnd5, DC)
                scalar.wait_ge(S.c_trev, 128)
                scalar.dma_start(out=dbg_h2[:, :],
                                 in_=h2_fm.ap().rearrange("p c n -> p (c n)").bitcast(U16)).then_inc(S.dbgs, 16)
                scalar.wait_ge(S.c_xf, 16)
                scalar.dma_start(out=dbg_xp[:, :], in_=o_acc.ap().rearrange("p j d -> p (j d)")).then_inc(S.dbgs, 16)

    cctx.close()
    if stop_after == "C":
        root.close()
        return nc
    xr_b = E(nc.sbuf_tensor("xr_b", [128, 2, DC, 512], BF16))
    xk_b = E(nc.sbuf_tensor("xk_b", [128, 2, DC, 512], BF16))
    kact_b = E(nc.sbuf_tensor("kact_b", [128, DC, 512], BF16))
    d_stag = E(nc.sbuf_tensor("d_stag", [128, 512], BF16))
    krelu = E(nc.sbuf_tensor("krelu", [128, 512], F32))
    r_tok = E(nc.sbuf_tensor("r_tok", [128, 2, D], F32))

    with ExitStack() as pctx, nc.Block() as blk:
        kfm_ps = [pctx.enter_context(nc.psum_tensor(f"kfmD{i}", [128, 512], F32)) for i in range(2)]
        r_ps = pctx.enter_context(nc.psum_tensor("rD", [128, D], F32))
        v_ps = pctx.enter_context(nc.psum_tensor("vD", [128, D], F32))

        @blk.vector
        def _(vector):
            vector.wait_ge(S.c_trev, 128)
            vector.wait_ge(S.bnd5, DC)
            vector.wait_ge(S.s_w, 80)
            def mix(b):
                if b >= 2:
                    vector.wait_ge(S.d_rv, b - 1)
                for dc in range(DC):
                    lo = b * 512
                    vector.tensor_sub(d_stag[:, :], h2_fm[:, dc, lo + 1:lo + 513], h2_fm[:, dc, lo:lo + 512])
                    vector.scalar_tensor_tensor(xr_b[:, b % 2, dc, :], d_stag[:, :], mur_t[:, dc:dc + 1],
                                                h2_fm[:, dc, lo:lo + 512], ALU.mult, ALU.add)
                    mx = vector.scalar_tensor_tensor(xk_b[:, b % 2, dc, :], d_stag[:, :], muk_t[:, dc:dc + 1],
                                                     h2_fm[:, dc, lo:lo + 512], ALU.mult, ALU.add)
                    if dc == DC - 1:
                        mx.then_inc(S.d_mix, 1)
            mix(0)
            mix(1)
            for b in range(NB):
                for jj in range(NB):
                    j = b * NB + jj
                    vector.wait_ge(S.d_vmm, j + 1)
                    vector.wait_ge(S.d_rev, j + 1)
                    vector.tensor_mul(r_tok[:, j % 2, :], r_tok[:, j % 2, :], v_ps[:, :]).then_inc(S.d_vev, 1)
                    vector.tensor_add(r_tok[:, j % 2, :], r_tok[:, j % 2, :], o_acc[:, j, :]).then_inc(S.d_oev, 1)
                if b + 2 < NB:
                    mix(b + 2)
            vector.sem_inc(S.fin, 1)

        @blk.tensor
        def _(tensor):
            tensor.wait_ge(S.init, 82 + 48)
            for b in range(NB):
                tensor.wait_ge(S.d_mix, b + 1)
                for c in range(DC):
                    rot = b * DC + c
                    ps = kfm_ps[rot % 2]
                    if rot >= 2:
                        tensor.wait_ge(S.d_krel, rot - 1)
                    for dc in range(DC):
                        mm = tensor.matmul(ps[:, :], wfk_sb[:, dc, c * 128:(c + 1) * 128],
                                           xk_b[:, b % 2, dc, :],
                                           start=(dc == 0), stop=(dc == DC - 1))
                        if dc == DC - 1:
                            mm.then_inc(S.d_kmm, 1)
                for jj in range(NB):
                    j = b * NB + jj
                    if j >= 1:
                        tensor.wait_ge(S.d_rev, j)
                    for dc in range(DC):
                        for half in range(2):
                            mm = tensor.matmul(r_ps[:, half * 512:(half + 1) * 512],
                                               xr_b[:, b % 2, dc, jj * 128:(jj + 1) * 128],
                                               wr_sb[:, dc, half * 512:(half + 1) * 512],
                                               start=(dc == 0), stop=(dc == DC - 1))
                            if dc == DC - 1 and half == 1:
                                mm.then_inc(S.d_rmm, 1)
                    if j >= 1:
                        tensor.wait_ge(S.d_vev, j)
                    tensor.wait_ge(S.d_kact, DC * (b + 1))
                    for dc in range(DC):
                        for half in range(2):
                            mm = tensor.matmul(v_ps[:, half * 512:(half + 1) * 512],
                                               kact_b[:, dc, jj * 128:(jj + 1) * 128],
                                               wfv_sb[:, dc, half * 512:(half + 1) * 512],
                                               start=(dc == 0), stop=(dc == DC - 1))
                            if dc == DC - 1 and half == 1:
                                mm.then_inc(S.d_vmm, 1)
                                if jj == NB - 1:
                                    tensor.sem_inc(S.d_rv, 1)
            tensor.sem_inc(S.fin, 1)

        @blk.scalar
        def _(scalar):
            for b in range(NB):
                for c in range(DC):
                    rot = b * DC + c
                    scalar.wait_ge(S.d_kmm, rot + 1)
                    scalar.activation(krelu[:, :], kfm_ps[rot % 2][:, :], AF.Relu).then_inc(S.d_krel, 1)
                    scalar.activation(kact_b[:, c, :], krelu[:, :], AF.Square).then_inc(S.d_kact, 1)
                for jj in range(NB):
                    j = b * NB + jj
                    scalar.wait_ge(S.d_rmm, j + 1)
                    if j >= 2:
                        scalar.wait_ge(S.y_st if j % 2 == 0 else S.y_st1, 16 * (j // 2))
                    scalar.activation(r_tok[:, j % 2, :], r_ps[:, :], AF.Sigmoid).then_inc(S.d_rev, 1)
            scalar.sem_inc(S.fin, 1)

        @blk.sync
        def _(sync):
            for j in range(J):
                sync.wait_ge(S.d_oev, j + 1)
                sync.dma_start(out=y[j * 128:(j + 1) * 128, :],
                               in_=r_tok[:, j % 2, :]).then_inc(
                                   S.y_st if j % 2 == 0 else S.y_st1, 16)
            sync.wait_ge(S.y_st, 16 * (J // 2))
            sync.wait_ge(S.y_st1, 16 * (J // 2))
            if dbg:
                sync.wait_ge(S.dbgs, 16 * 16)

    root.close()
    return nc


_NC_CACHE = {}


def _get_nc(dbg=False):
    if dbg not in _NC_CACHE:
        _NC_CACHE[dbg] = build(dbg)
    return _NC_CACHE[dbg]


def make_in_maps(inputs):
    asn = lambda a: np.ascontiguousarray(np.asarray(a, dtype=np.float32))
    x = asn(inputs["L0_attn_x"]).reshape(NCORES * T, D)
    common = {
        "wqT": asn(np.asarray(inputs["L0_attn_wq"]).T),
        "wkT": asn(np.asarray(inputs["L0_attn_wk"]).T),
        "wvT": asn(np.asarray(inputs["L0_attn_wv"]).T),
        "wrT": asn(np.asarray(inputs["L0_ffn_W_r"]).T),
        "wfkT": asn(np.asarray(inputs["L0_ffn_W_k"]).T),
        "wfvT": asn(np.asarray(inputs["L0_ffn_W_v"]).T),
        "bq": np.ascontiguousarray(asn(inputs["L0_attn_bq"]).reshape(DC, 128).T),
        "bk": np.ascontiguousarray(asn(inputs["L0_attn_bk"]).reshape(DC, 128).T),
        "bv": asn(inputs["L0_attn_bv"]).reshape(1, D),
        "mur": np.ascontiguousarray(asn(inputs["L0_ffn_mu_r"]).reshape(DC, 128).T),
        "muk": np.ascontiguousarray(asn(inputs["L0_ffn_mu_k"]).reshape(DC, 128).T),
        "ident": np.eye(128, dtype=np.float32),
    }
    in_maps = []
    for i in range(NCORES):
        sel = np.zeros((NCORES, 1), np.float32)
        if i > 0:
            sel[i - 1, 0] = 1.0
        im = dict(common)
        im["x_sh"] = np.ascontiguousarray(x[i * T:(i + 1) * T, :])
        im["bsel"] = sel
        in_maps.append(im)
    return in_maps


def kernel(**inputs):
    nc = _get_nc(False)
    in_maps = make_in_maps(inputs)
    res = run_bass_kernel_spmd(nc, in_maps, core_ids=list(range(NCORES)))
    out = np.concatenate([r["y"] for r in res.results], axis=0)
    return out.reshape(1, NCORES * T, D)


# version 07
# speedup vs baseline: 1.3487x; 1.0393x over previous
import numpy as np
from contextlib import ExitStack

import concourse.bass as bass
import concourse.mybir as mybir
from concourse.bass_utils import run_bass_kernel_spmd

F32 = mybir.dt.float32
BF16 = mybir.dt.bfloat16
U16 = mybir.dt.uint16
AF = mybir.ActivationFunctionType
ALU = mybir.AluOpType
AX = mybir.AxisListType.X

NCORES = 8
D = 1024
DC = 8
T = 2048
J = 16
NG = 32
KC = 4
NB = 4
SCALE = 1.0 / 32.0
EPS = 1e-5

SEM_NAMES = [
    "init", "s_w", "s_x", "s_xfree", "lnA", "lnB", "lnC", "lnD", "lnE", "ln_h", "tr_p", "tr_ev",
    "a_mm", "a_nmm", "a_qev", "a_kev", "a_vev", "a_nev", "a_sq", "a_kst", "a_vst", "a_nst",
    "cc", "b_nld", "nm", "pe_s", "pe_p", "pe_o", "pe_e", "pe_g", "pe_sk", "pe_pts", "rcp",
    "c_x", "c_xf", "ag3w", "ag3r", "cA", "cB", "cC", "cD", "cE", "c_h",
    "c_trp", "c_trev", "bnd", "bnd5",
    "d_mix", "d_rv", "d_kmm", "d_krel", "d_kact", "d_rmm", "d_rev", "d_vmm", "d_vev",
    "d_oev", "y_st", "dbgs", "s_x1", "a_kst1", "a_vst1", "kA", "kA1", "kB", "kB1", "y_st1", "a_qm", "c_x1",
    "start", "fin",
]
BIG = 1 << 30


def build(dbg=False, stop_after=None):
    nc = bass.Bass("TRN2", target_bir_lowering=False)

    x_sh = nc.dram_tensor("x_sh", [T, D], F32, kind="ExternalInput")
    wqT = nc.dram_tensor("wqT", [D, D], F32, kind="ExternalInput")
    wkT = nc.dram_tensor("wkT", [D, D], F32, kind="ExternalInput")
    wvT = nc.dram_tensor("wvT", [D, D], F32, kind="ExternalInput")
    wrT = nc.dram_tensor("wrT", [D, D], F32, kind="ExternalInput")
    wfkT = nc.dram_tensor("wfkT", [D, D], F32, kind="ExternalInput")
    wfvT = nc.dram_tensor("wfvT", [D, D], F32, kind="ExternalInput")
    bq_d = nc.dram_tensor("bq", [128, DC], F32, kind="ExternalInput")
    bk_d = nc.dram_tensor("bk", [128, DC], F32, kind="ExternalInput")
    bv_d = nc.dram_tensor("bv", [1, D], F32, kind="ExternalInput")
    mur_d = nc.dram_tensor("mur", [128, DC], F32, kind="ExternalInput")
    muk_d = nc.dram_tensor("muk", [128, DC], F32, kind="ExternalInput")
    ident_d = nc.dram_tensor("ident", [128, 128], F32, kind="ExternalInput")
    bsel_d = nc.dram_tensor("bsel", [NCORES, 1], F32, kind="ExternalInput")
    y = nc.dram_tensor("y", [T, D], F32, kind="ExternalOutput")

    k_ag_in = nc.dram_tensor("k_ag_in", [4, DC, 128, 512], BF16, kind="Internal")
    k_ag_o = [nc.dram_tensor(f"k_ag_o{j}", [NCORES, DC, 128, 512], BF16, kind="Internal",
                             addr_space="Shared") for j in range(4)]
    v_ag_in = nc.dram_tensor("v_ag_in", [T, D], BF16, kind="Internal")
    v_ag_o = [nc.dram_tensor(f"v_ag_o{j}", [NCORES * 512, D], BF16, kind="Internal",
                             addr_space="Shared") for j in range(4)]
    n_ag_in = nc.dram_tensor("n_ag_in", [1, T], BF16, kind="Internal")
    n_ag_out = nc.dram_tensor("n_ag_out", [128, 128], BF16, kind="Internal")
    xb_ag_in = nc.dram_tensor("xb_ag_in", [1, D], F32, kind="Internal")
    xb_ag_out = nc.dram_tensor("xb_ag_out", [NCORES, D], F32, kind="Internal")
    if dbg:
        dbg_oacc = nc.dram_tensor("dbg_oacc", [128, J * D], F32, kind="ExternalOutput")
        dbg_rl = nc.dram_tensor("dbg_rl", [128, 16], F32, kind="ExternalOutput")
        dbg_negm = nc.dram_tensor("dbg_negm", [128, 1], F32, kind="ExternalOutput")
        dbg_h2 = nc.dram_tensor("dbg_h2", [128, DC * (T + 1)], U16, kind="ExternalOutput")
        dbg_xp = nc.dram_tensor("dbg_xp", [128, J * D], F32, kind="ExternalOutput")
        dbg_misc = nc.dram_tensor("dbg_misc", [1, 8], F32, kind="ExternalOutput")
        dbg_qn = nc.dram_tensor("dbg_qn", [1, T], U16, kind="ExternalOutput")
        dbg_kn = nc.dram_tensor("dbg_kn", [1, T], U16, kind="ExternalOutput")
        dbg_nrm = nc.dram_tensor("dbg_nrm", [128, 128], U16, kind="ExternalOutput")
        dbg_kag = nc.dram_tensor("dbg_kag", [128, 512], U16, kind="ExternalOutput")
        dbg_vag = nc.dram_tensor("dbg_vag", [128, 1024], U16, kind="ExternalOutput")
        dbg_h1 = nc.dram_tensor("dbg_h1", [128, DC * T], U16, kind="ExternalOutput")
        dbg_q = nc.dram_tensor("dbg_q", [128, DC * T], U16, kind="ExternalOutput")
        dbg_hbf = nc.dram_tensor("dbg_hbf", [128, 2 * D], U16, kind="ExternalOutput")

    RG = [list(range(NCORES))]

    root = ExitStack()
    E = root.enter_context
    ident_b = E(nc.sbuf_tensor("ident_b", [128, 128], BF16))
    ones_col = E(nc.sbuf_tensor("ones_col", [128, 1], BF16))
    ones_row = E(nc.sbuf_tensor("ones_row", [1, 128], BF16))
    ones32 = E(nc.sbuf_tensor("ones32", [128, 1], F32))
    bq_t = E(nc.sbuf_tensor("bq_t", [128, DC], F32))
    bk_t = E(nc.sbuf_tensor("bk_t", [128, DC], F32))
    mur_t = E(nc.sbuf_tensor("mur_t", [128, DC], F32))
    muk_t = E(nc.sbuf_tensor("muk_t", [128, DC], F32))
    bsel_t = E(nc.sbuf_tensor("bsel_t", [NCORES, 1], F32))
    neg_m = E(nc.sbuf_tensor("neg_m", [128, 1], F32))
    qmax2 = E(nc.sbuf_tensor("qmax2", [1, 1], F32))
    kmax2 = E(nc.sbuf_tensor("kmax2", [1, 1], F32))
    m2 = E(nc.sbuf_tensor("m2", [1, 1], F32))
    negm_s = E(nc.sbuf_tensor("negm_s", [1, 1], F32))
    negm_b = E(nc.sbuf_tensor("negm_b", [1, 1], BF16))
    nrm_all = E(nc.sbuf_tensor("nrm_all", [128, 128], BF16))
    knmax_c = E(nc.sbuf_tensor("knmax_c", [128, 1], BF16))
    recip_l = E(nc.sbuf_tensor("recip_l", [128, 16], F32))
    mneg = [E(nc.sbuf_tensor(f"mneg{i}", [128, 1], F32)) for i in range(4)]
    v_raw = [E(nc.sbuf_tensor(f"v_raw{i}", [128, 1], F32)) for i in range(4)]
    sq_in = [E(nc.sbuf_tensor(f"sq_in{i}", [128, 1], F32)) for i in range(4)]
    std_t = [E(nc.sbuf_tensor(f"std_t{i}", [128, 1], F32)) for i in range(4)]
    inv_t = [E(nc.sbuf_tensor(f"inv_t{i}", [128, 1], F32)) for i in range(4)]
    recip_g = E(nc.sbuf_tensor("recip_g", [128, 16], F32))
    negm_s2 = E(nc.sbuf_tensor("negm_s2", [1, 1], F32))
    bm2 = E(nc.sbuf_tensor("bm2", [1, 1], F32))
    btmpv = E(nc.sbuf_tensor("btmpv", [1, 1], F32))
    bnmi2 = E(nc.sbuf_tensor("bnmi2", [1, 1], F32))

    qctx = ExitStack()
    q_fm = qctx.enter_context(nc.sbuf_tensor("q_fm", [128, DC, T], BF16, side="right"))

    sems = {name: E(nc.semaphore(name)) for name in SEM_NAMES}
    S = type("S", (), sems)
    _nums = sorted(h.num for h in sems.values())
    assert _nums[-1] - _nums[0] == len(_nums) - 1, _nums
    sem_lo, sem_hi = _nums[0], _nums[-1] + 1

    ctxA = ExitStack()
    EA = ctxA.enter_context
    wq_sb = EA(nc.sbuf_tensor("wq_sb", [128, DC, D], BF16))
    wk_sb = EA(nc.sbuf_tensor("wk_sb", [128, DC, D], BF16))
    wv_sb = EA(nc.sbuf_tensor("wv_sb", [128, DC, D], BF16))
    h1_fm = EA(nc.sbuf_tensor("h1_fm", [128, DC, T], BF16))
    xstag = EA(nc.sbuf_tensor("xstag", [128, 2, D], F32))
    t_stag = EA(nc.sbuf_tensor("t_stag", [128, 2, D], F32))
    tsq_st = EA(nc.sbuf_tensor("tsq_st", [128, D], BF16))
    h_bf = EA(nc.sbuf_tensor("h_bf", [128, 2, D], BF16))
    kstag = EA(nc.sbuf_tensor("kstag", [128, 2, T], BF16))
    vstag = EA(nc.sbuf_tensor("vstag", [128, 2, D], BF16))
    sq_st = EA(nc.sbuf_tensor("sq_st", [128, 2, T], BF16))
    qn_sb = EA(nc.sbuf_tensor("qn_sb", [1, T], BF16))
    kn_sb = EA(nc.sbuf_tensor("kn_sb", [1, T], BF16))
    bv_row = EA(nc.sbuf_tensor("bv_row", [1, D], BF16))

    with ExitStack() as pctx, nc.Block() as blk:
        trp = [pctx.enter_context(nc.psum_tensor(f"trpA{i}", [128, 128], BF16)) for i in range(2)]

        @blk.sync
        def _(sync):
            sync.wait_ge(S.start, BIG)
            sync.dma_start(out=bq_t[:, :], in_=bq_d[:, :]).then_inc(S.s_w, 16)
            sync.dma_start(out=bk_t[:, :], in_=bk_d[:, :]).then_inc(S.s_w, 16)
            sync.dma_start(out=mur_t[:, :], in_=mur_d[:, :]).then_inc(S.s_w, 16)
            sync.dma_start(out=muk_t[:, :], in_=muk_d[:, :]).then_inc(S.s_w, 16)
            sync.dma_start(out=bsel_t[:, :], in_=bsel_d[:, :]).then_inc(S.s_w, 16)
            for j in range(J):
                if j >= 2:
                    sync.wait_ge(S.s_xfree, j - 1)
                sync.dma_start(out=xstag[:, j % 2, :], in_=x_sh[j * 128:(j + 1) * 128, :]).then_inc(
                    S.s_x if j % 2 == 0 else S.s_x1, 16)

        @blk.gpsimd
        def _(gpsimd):
            gpsimd.sem_clear(range(sem_lo, sem_hi))
            gpsimd.sem_inc(S.start, BIG)
            gpsimd.dma_start(out=ident_b[:, :], in_=ident_d[:, :]).then_inc(S.init, 16)
            gpsimd.dma_start(out=bv_row[:, :], in_=bv_d[:, :]).then_inc(S.init, 16)
            gpsimd.memset(ones_col[:, :], 1.0)
            gpsimd.memset(ones_row[:, :], 1.0).then_inc(S.init, 1)
            gpsimd.dma_start(out=wq_sb.ap(),
                             in_=wqT.ap().rearrange("(c p) n -> p c n", p=128)).then_inc(S.init, 16)
            gpsimd.dma_start(out=wk_sb.ap(),
                             in_=wkT.ap().rearrange("(c p) n -> p c n", p=128)).then_inc(S.init, 16)
            gpsimd.dma_start(out=wv_sb.ap(),
                             in_=wvT.ap().rearrange("(c p) n -> p c n", p=128)).then_inc(S.init, 16)

        @blk.vector
        def _(vector):
            vector.wait_ge(S.start, BIG)

            def lnA(j):
                vector.wait_ge(S.s_x if j % 2 == 0 else S.s_x1, 16 * (j // 2 + 1))
                if j >= 4:
                    vector.wait_ge(S.s_xfree, j - 3)
                vector.reduce_sum(mneg[j % 4][:, :], xstag[:, j % 2, :], axis=AX,
                                  negate=True).then_inc(S.lnA, 1)

            def evacs(j):
                for dc in range(DC):
                    cnt = j * DC + dc
                    vector.wait_ge(S.tr_p, cnt + 1)
                    vector.tensor_copy(h1_fm[:, dc, j * 128:(j + 1) * 128],
                                       trp[cnt % 2][:, :]).then_inc(S.tr_ev, 1)

            lnA(0)
            lnA(1)
            for j in range(J):
                if j + 2 < J:
                    lnA(j + 2)
                vector.wait_ge(S.lnB, j + 1)
                if j >= 4:
                    vector.wait_ge(S.lnD, j - 3)
                vector.tensor_scalar(sq_in[j % 4][:, :], v_raw[j % 4][:, :], 1.0 / D,
                                     D * D * EPS, ALU.mult, ALU.add).then_inc(S.lnC, 1)
                vector.wait_ge(S.lnD, j + 1)
                if j >= 4:
                    vector.wait_ge(S.ln_h, j - 3)
                vector.reciprocal(inv_t[j % 4][:, :], std_t[j % 4][:, :]).then_inc(S.lnE, 1)
                if j >= 1:
                    evacs(j - 1)
            evacs(J - 1)

        @blk.scalar
        def _(scalar):
            scalar.wait_ge(S.start, BIG)

            def lnBC(j):
                scalar.wait_ge(S.lnA, j + 1)
                scalar.activation(t_stag[:, j % 2, :], xstag[:, j % 2, :], AF.Identity,
                                  bias=mneg[j % 4][:, :], scale=float(D)).then_inc(S.s_xfree, 1)
                if j >= 4:
                    scalar.wait_ge(S.lnC, j - 3)
                scalar.activation(tsq_st[:, :], t_stag[:, j % 2, :], AF.Square,
                                  accum_out=v_raw[j % 4][:, :]).then_inc(S.lnB, 1)

            lnBC(0)
            for j in range(J):
                if j + 1 < J:
                    lnBC(j + 1)
                scalar.wait_ge(S.lnC, j + 1)
                if j >= 4:
                    scalar.wait_ge(S.lnE, j - 3)
                scalar.activation(std_t[j % 4][:, :], sq_in[j % 4][:, :], AF.Sqrt).then_inc(S.lnD, 1)
                scalar.wait_ge(S.lnE, j + 1)
                if j >= 2:
                    scalar.wait_ge(S.tr_p, 8 * (j - 1))
                scalar.activation(h_bf[:, j % 2, :], t_stag[:, j % 2, :], AF.Copy,
                                  scale=inv_t[j % 4][:, :]).then_inc(S.ln_h, 1)

        @blk.tensor
        def _(tensor):
            tensor.wait_ge(S.start, BIG)
            tensor.wait_ge(S.init, 33)
            for j in range(J):
                tensor.wait_ge(S.ln_h, j + 1)
                for dc in range(DC):
                    cnt = j * DC + dc
                    if cnt >= 2:
                        tensor.wait_ge(S.tr_ev, cnt - 1)
                    tensor.transpose(trp[cnt % 2][:, :], h_bf[:, j % 2, dc * 128:(dc + 1) * 128],
                                     ident_b[:, :]).then_inc(S.tr_p, 1)

    with ExitStack() as pctx, nc.Block() as blk:
        qkv_ps = [pctx.enter_context(nc.psum_tensor(f"qkvA{i}", [128, D], F32)) for i in range(2)]
        nrm_ps = pctx.enter_context(nc.psum_tensor("nrmA", [1, T], F32))

        def _nrm_mms(tensor, c, off):
            tensor.wait_ge(S.a_sq, off + c + 1)
            if off == DC and c == 0:
                tensor.wait_ge(S.a_nst, 1)
            for b in range(NB):
                mm = tensor.matmul(nrm_ps[0:1, b * 512:(b + 1) * 512], ones_col[:, :],
                                   sq_st[:, c % 2, b * 512:(b + 1) * 512],
                                   start=(c == 0), stop=(c == DC - 1))
                if b == NB - 1:
                    mm.then_inc(S.a_nmm, 1)

        @blk.tensor
        def _(tensor):
            tensor.wait_ge(S.tr_ev, J * DC)
            tensor.wait_ge(S.init, 81)
            for c in range(DC):
                for b in range(NB):
                    n = c * NB + b
                    ps = qkv_ps[n % 2]
                    if n >= 2:
                        tensor.wait_ge(S.a_kev, n - 1)
                    for dc in range(DC):
                        mm = tensor.matmul(ps[:, 0:512], wk_sb[:, dc, c * 128:(c + 1) * 128],
                                           h1_fm[:, dc, b * 512:(b + 1) * 512],
                                           start=(dc == 0), stop=(dc == DC - 1))
                        if dc == DC - 1:
                            mm.then_inc(S.a_mm, 1)
                if c >= 1:
                    _nrm_mms(tensor, c - 1, 0)
            _nrm_mms(tensor, DC - 1, 0)
            for t in range(J):
                ps = qkv_ps[t % 2]
                if t == 0:
                    tensor.wait_ge(S.a_kev, 31)
                elif t == 1:
                    tensor.wait_ge(S.a_kev, 32)
                else:
                    tensor.wait_ge(S.a_vev, t - 1)
                for dc in range(DC):
                    for half in range(2):
                        tensor.matmul(ps[:, half * 512:(half + 1) * 512],
                                      h1_fm[:, dc, t * 128:(t + 1) * 128],
                                      wv_sb[:, dc, half * 512:(half + 1) * 512],
                                      start=(dc == 0), stop=False)
                for half in range(2):
                    mm = tensor.matmul(ps[:, half * 512:(half + 1) * 512], ones_row[:, :],
                                       bv_row[:, half * 512:(half + 1) * 512], start=False, stop=True)
                    if half == 1:
                        mm.then_inc(S.a_mm, 1)
            for c in range(DC):
                for b in range(NB):
                    n = c * NB + b
                    ps = qkv_ps[n % 2]
                    if n == 0:
                        tensor.wait_ge(S.a_vev, 15)
                    elif n == 1:
                        tensor.wait_ge(S.a_vev, 16)
                    else:
                        tensor.wait_ge(S.a_qev, n - 1)
                    for dc in range(DC):
                        mm = tensor.matmul(ps[:, 0:512], wq_sb[:, dc, c * 128:(c + 1) * 128],
                                           h1_fm[:, dc, b * 512:(b + 1) * 512],
                                           start=(dc == 0), stop=(dc == DC - 1))
                        if dc == DC - 1:
                            mm.then_inc(S.a_mm, 1)
                if c >= 1:
                    _nrm_mms(tensor, c - 1, DC)
            _nrm_mms(tensor, DC - 1, DC)

        @blk.scalar
        def _(scalar):
            scalar.wait_ge(S.s_w, 80)
            for c in range(DC):
                if c >= 2:
                    scalar.wait_ge(S.a_kst if c % 2 == 0 else S.a_kst1, 64 * (c // 2))
                for b in range(NB):
                    n = c * NB + b
                    scalar.wait_ge(S.a_mm, n + 1)
                    scalar.activation(kstag[:, c % 2, b * 512:(b + 1) * 512], qkv_ps[n % 2][:, 0:512],
                                      AF.Identity, bias=bk_t[:, c:c + 1]).then_inc(S.a_kev, 1)
                if c >= 2:
                    scalar.wait_ge(S.a_nmm, c - 1)
                scalar.activation(sq_st[:, c % 2, :], kstag[:, c % 2, :], AF.Square).then_inc(S.a_sq, 1)
            scalar.wait_ge(S.a_nmm, DC)
            scalar.activation(kn_sb[:, :], nrm_ps[0:1, :], AF.Copy).then_inc(S.a_nst, 1)
            for c in range(DC):
                for b in range(NB):
                    n = 48 + c * NB + b
                    scalar.wait_ge(S.a_mm, n + 1)
                    scalar.activation(q_fm[:, c, b * 512:(b + 1) * 512], qkv_ps[(c * NB + b) % 2][:, 0:512],
                                      AF.Identity, bias=bq_t[:, c:c + 1]).then_inc(S.a_qev, 1)
                scalar.wait_ge(S.a_nmm, 7 + c)
                scalar.activation(sq_st[:, c % 2, :], q_fm[:, c, :], AF.Square).then_inc(S.a_sq, 1)
            scalar.wait_ge(S.a_nmm, 2 * DC)
            scalar.activation(qn_sb[:, :], nrm_ps[0:1, :], AF.Copy).then_inc(S.a_nev, 1)

        @blk.vector
        def _(vector):
            for t in range(J):
                n = 32 + t
                if t >= 2:
                    vector.wait_ge(S.a_vst if t % 2 == 0 else S.a_vst1, 16 * (t // 2))
                vector.wait_ge(S.a_mm, n + 1)
                vector.tensor_copy(vstag[:, t % 2, :], qkv_ps[t % 2][:, :]).then_inc(S.a_vev, 1)
            vector.wait_ge(S.a_nev, 1)
            vector.reduce_max(qmax2[:, :], qn_sb[:, :], axis=AX).then_inc(S.a_qm, 1)

        @blk.sync
        def _(sync):
            for c in range(DC):
                sync.wait_ge(S.a_kev, NB * (c + 1))
                for gl in range(4):
                    sync.dma_start(out=k_ag_in[gl, c, :, :],
                                   in_=kstag[:, c % 2, gl * 512:(gl + 1) * 512]).then_inc(
                                       S.a_kst if c % 2 == 0 else S.a_kst1, 16)
            for t in range(J):
                sync.wait_ge(S.a_vev, t + 1)
                sync.dma_start(out=v_ag_in[t * 128:(t + 1) * 128, :], in_=vstag[:, t % 2, :]).then_inc(
                    S.a_vst if t % 2 == 0 else S.a_vst1, 16)
            sync.wait_ge(S.a_nst, 1)
            sync.dma_start(out=n_ag_in[:, :], in_=kn_sb[:, :]).then_inc(S.a_nst, 16)

        @blk.gpsimd
        def _(gpsimd):
            gpsimd.wait_ge(S.a_kst, 16 * 16)
            gpsimd.wait_ge(S.a_kst1, 16 * 16)
            gpsimd.collective_compute("AllGather", ALU.bypass, replica_groups=RG,
                                      ins=[k_ag_in.ap()[0:1, :, :, :].opt()],
                                      outs=[k_ag_o[0].ap().opt()]).then_inc(S.cc, 1)
            gpsimd.wait_ge(S.a_nst, 17)
            gpsimd.collective_compute("AllGather", ALU.bypass, replica_groups=RG,
                                      ins=[n_ag_in.ap().opt()], outs=[n_ag_out.ap().opt()]).then_inc(S.cc, 1)
            for j in range(4):
                gpsimd.wait_ge(S.a_vst, 16 * (2 * j + 2))
                gpsimd.wait_ge(S.a_vst1, 16 * (2 * j + 2))
                gpsimd.collective_compute("AllGather", ALU.bypass, replica_groups=RG,
                                          ins=[v_ag_in.ap()[j * 512:(j + 1) * 512, :].opt()],
                                          outs=[v_ag_o[j].ap().opt()]).then_inc(S.cc, 1)
                if j < 3:
                    gpsimd.collective_compute("AllGather", ALU.bypass, replica_groups=RG,
                                              ins=[k_ag_in.ap()[j + 1:j + 2, :, :, :].opt()],
                                              outs=[k_ag_o[j + 1].ap().opt()]).then_inc(S.cc, 1)
            gpsimd.wait_ge(S.cc, 2)
            gpsimd.dma_start(out=nrm_all[:, :], in_=n_ag_out[:, :]).then_inc(S.b_nld, 16)

    ctxA.close()
    if stop_after == "A":
        root.close()
        return nc
    o_acc = E(nc.sbuf_tensor("o_acc", [128, J, D], F32))

    ctxB = ExitStack()
    EB = ctxB.enter_context
    vbuf = EB(nc.sbuf_tensor("vbuf", [128, 2, KC, 1024], BF16))
    kbuf = EB(nc.sbuf_tensor("kbuf", [128, 2, DC, 512], BF16))
    pt = [EB(nc.sbuf_tensor(f"pt{i}", [128, KC, T], BF16)) for i in range(2)]
    gsum = EB(nc.sbuf_tensor("gsum", [128, T], F32))

    with ExitStack() as pctx, nc.Block() as blk:
        s_ps = [pctx.enter_context(nc.psum_tensor(f"sB{i}", [128, 512], F32)) for i in range(2)]
        o_ps = [pctx.enter_context(nc.psum_tensor(f"oB{i}", [128, D], F32)) for i in range(2)]
        l_ps = pctx.enter_context(nc.psum_tensor("l_ps", [128, 512], F32))
        nm_ps = pctx.enter_context(nc.psum_tensor("nmB", [128, 512], F32))

        @blk.gpsimd
        def _(gpsimd):
            gpsimd.memset(ones32[:, :], 1.0).then_inc(S.init, 1)
            gpsimd.wait_ge(S.nm, 3)
            gpsimd.tensor_mul(m2[:, :], kmax2[:, :], qmax2[:, :]).then_inc(S.nm, 1)
            gpsimd.wait_ge(S.nm, 5)
            gpsimd.tensor_scalar_mul(negm_s2[:, :], negm_s[:, :], -SCALE).then_inc(S.nm, 1)
            gpsimd.wait_ge(S.rcp, 1)
            gpsimd.tensor_copy(recip_g[:, :], recip_l[:, :]).then_inc(S.rcp, 1)

        @blk.vector
        def _(vector):
            vector.wait_ge(S.b_nld, 16)
            vector.reduce_max(knmax_c[:, :], nrm_all[:, :], axis=AX).then_inc(S.nm, 1)
            vector.wait_ge(S.nm, 2)
            vector.reduce_max(kmax2[:, :], nm_ps[0:1, 0:64].bitcast(BF16), axis=AX).then_inc(S.nm, 1)
            vector.wait_ge(S.nm, 8)
            vector.tensor_copy(neg_m[:, :], nm_ps[:, 128:129]).then_inc(S.nm, 1)
            n = 0
            for g in range(NG):
                vector.wait_ge(S.pe_p, 16 * (g + 1))
                ptg2 = pt[g % 2]
                if g == 0:
                    vector.tensor_tensor(gsum[:, :], ptg2[:, 0, :], ptg2[:, 1, :], ALU.add)
                else:
                    vector.tensor_tensor(gsum[:, :], gsum[:, :], ptg2[:, 0, :], ALU.add)
                    vector.tensor_tensor(gsum[:, :], gsum[:, :], ptg2[:, 1, :], ALU.add)
                vector.tensor_tensor(gsum[:, :], gsum[:, :], ptg2[:, 2, :], ALU.add)
                vector.tensor_tensor(gsum[:, :], gsum[:, :], ptg2[:, 3, :], ALU.add).then_inc(S.pe_pts, 1)
                for j in range(J):
                    vector.wait_ge(S.pe_o, n + 1)
                    if g == 0:
                        vector.tensor_copy(o_acc[:, j, :], o_ps[n % 2][:, :]).then_inc(S.pe_e, 1)
                    else:
                        vector.tensor_tensor(o_acc[:, j, :], o_acc[:, j, :], o_ps[n % 2][:, :],
                                             ALU.add).then_inc(S.pe_e, 1)
                    n += 1
            vector.wait_ge(S.pe_g, NG + 1)
            vector.reciprocal(recip_l[:, :], l_ps[:, 0:16]).then_inc(S.rcp, 1)

        @blk.scalar
        def _(scalar):
            scalar.wait_ge(S.nm, 4)
            scalar.activation(negm_s[:, :], m2[:, :], AF.Sqrt).then_inc(S.nm, 1)
            scalar.wait_ge(S.nm, 6)
            scalar.activation(negm_b[:, :], negm_s2[:, :], AF.Copy).then_inc(S.nm, 1)
            n = 0
            scalar.wait_ge(S.a_nst, 17)
            scalar.wait_ge(S.a_qm, 1)
            for g in range(NG):
                if g == 0:
                    scalar.wait_ge(S.nm, 9)
                for kc in range(KC):
                    for b in range(NB):
                        scalar.wait_ge(S.pe_s, n + 1)
                        scalar.activation(pt[g % 2][:, kc, b * 512:(b + 1) * 512], s_ps[n % 2][:, :],
                                          AF.Exp, bias=neg_m[:, :], scale=SCALE).then_inc(S.pe_p, 1)
                        n += 1

        @blk.tensor
        def _(tensor):
            tensor.wait_ge(S.nm, 1)
            tensor.transpose(nm_ps[0:1, 0:64].bitcast(BF16), knmax_c[:, :], ident_b[:, :]).then_inc(S.nm, 1)
            tensor.wait_ge(S.nm, 7)
            tensor.matmul(nm_ps[:, 128:129], ones_row[:, :], negm_b[:, :], start=True, stop=True).then_inc(S.nm, 1)
            tensor.wait_ge(S.a_qev, 32)
            tensor.wait_ge(S.init, 82)
            n_s = 0
            n_o = 0
            for g in range(NG + 1):
                if g < NG:
                    tensor.wait_ge(S.kA if g % 2 == 0 else S.kA1, 16 * (g // 2 + 1))
                    for kc in range(KC):
                        for b in range(NB):
                            ps = s_ps[n_s % 2]
                            if n_s >= 2:
                                tensor.wait_ge(S.pe_p, n_s - 1)
                            for dc in range(DC):
                                mm = tensor.matmul(ps[:, :], kbuf[:, g % 2, dc, kc * 128:(kc + 1) * 128],
                                                   q_fm[:, dc, b * 512:(b + 1) * 512],
                                                   start=(dc == 0), stop=(dc == DC - 1))
                                if dc == DC - 1:
                                    mm.then_inc(S.pe_s, 1)
                            n_s += 1
                    tensor.nop().then_inc(S.pe_sk, 1)
                if g >= 1:
                    gp = g - 1
                    tensor.wait_ge(S.pe_p, gp * 16 + 16)
                    tensor.wait_ge(S.kB if gp % 2 == 0 else S.kB1, 16 * (gp // 2 + 1))
                    ptg = pt[gp % 2]
                    for j in range(J):
                        op = o_ps[n_o % 2]
                        if n_o >= 2:
                            tensor.wait_ge(S.pe_e, n_o - 1)
                        for kc in range(KC):
                            stat = ptg[:, kc, j * 128:(j + 1) * 128]
                            tensor.matmul(op[:, 0:512], stat, vbuf[:, gp % 2, kc, 0:512],
                                          start=(kc == 0), stop=(kc == KC - 1))
                            mm = tensor.matmul(op[:, 512:1024], stat, vbuf[:, gp % 2, kc, 512:1024],
                                               start=(kc == 0), stop=(kc == KC - 1))
                            if kc == KC - 1:
                                mm.then_inc(S.pe_o, 1)
                        n_o += 1
                    tensor.nop().then_inc(S.pe_g, 1)
            tensor.wait_ge(S.pe_pts, NG)
            for j in range(J):
                mm = tensor.matmul(l_ps[:, j:j + 1], gsum[:, j * 128:(j + 1) * 128],
                                   ones32[:, :], start=(j == 0), stop=(j == J - 1))
            mm.then_inc(S.pe_g, 1)

        @blk.sync
        def _(sync):
            for gi in range(NG):
                j, r = gi // 8, gi % 8
                ka = S.kA if gi % 2 == 0 else S.kA1
                kb = S.kB if gi % 2 == 0 else S.kB1
                if gi >= 2:
                    sync.wait_ge(S.pe_sk, gi - 1)
                if gi == 0:
                    sync.wait_ge(S.a_vst, 16 * 8)
                    sync.wait_ge(S.a_vst1, 16 * 8)
                sync.wait_ge(S.cc, 1 if j == 0 else 2 * j + 2)
                sync.dma_start(out=kbuf.ap()[:, gi % 2:gi % 2 + 1, :, :],
                               in_=k_ag_o[j].ap()[r:r + 1, :, :, :].rearrange(
                                   "o c p t -> p o c t")).then_inc(ka, 16)
                if gi >= 2:
                    sync.wait_ge(S.pe_g, gi - 1)
                if gi == 0:
                    sync.wait_ge(S.a_mm, 80)
                sync.wait_ge(S.cc, 2 * j + 3)
                sync.dma_start(out=vbuf.ap()[:, gi % 2, :, :],
                               in_=v_ag_o[j].ap()[r * 512:(r + 1) * 512, :].rearrange(
                                   "(kc p) d -> p kc d", p=128)).then_inc(kb, 16)

        if dbg:
            @blk.scalar
            def _(scalar):
                scalar.wait_ge(S.rcp, 1)
                scalar.dma_start(out=dbg_rl[:, :], in_=recip_l[:, :]).then_inc(S.dbgs, 16)
                scalar.dma_start(out=dbg_negm[:, :], in_=neg_m[:, :]).then_inc(S.dbgs, 16)
                scalar.dma_start(out=dbg_oacc[:, :], in_=o_acc.ap().rearrange("p j d -> p (j d)")).then_inc(S.dbgs, 16)
                scalar.dma_start(out=dbg_misc[:, 0:1], in_=qmax2[:, :]).then_inc(S.dbgs, 16)
                scalar.dma_start(out=dbg_misc[:, 1:2], in_=kmax2[:, :]).then_inc(S.dbgs, 16)
                scalar.dma_start(out=dbg_misc[:, 2:3], in_=m2[:, :]).then_inc(S.dbgs, 16)
                scalar.dma_start(out=dbg_nrm[:, :], in_=nrm_all.ap().bitcast(U16)).then_inc(S.dbgs, 16)
                scalar.dma_start(out=dbg_kag[:, :],
                                 in_=k_ag_o[0].ap()[0, 0, :, :].bitcast(U16)).then_inc(S.dbgs, 16)
                scalar.dma_start(out=dbg_vag[:, :],
                                 in_=v_ag_o[0].ap()[0:128, :].bitcast(U16)).then_inc(S.dbgs, 16)

    ctxB.close()
    qctx.close()
    if stop_after == "B":
        root.close()
        return nc
    wr_sb = E(nc.sbuf_tensor("wr_sb", [128, DC, D], BF16))
    wfk_sb = E(nc.sbuf_tensor("wfk_sb", [128, DC, D], BF16))
    wfv_sb = E(nc.sbuf_tensor("wfv_sb", [128, DC, D], BF16))
    h2_fm = E(nc.sbuf_tensor("h2_fm", [128, DC, T + 1], BF16))
    cctx = ExitStack()
    EC = cctx.enter_context
    xstag2 = EC(nc.sbuf_tensor("xstag2", [128, 2, D], F32))
    h2tok = EC(nc.sbuf_tensor("h2tok", [128, 2, D], BF16))
    tsq2 = EC(nc.sbuf_tensor("tsq2", [128, 2, D], F32))
    cdust = EC(nc.sbuf_tensor("cdust", [128, D], BF16))
    ag3_sb = EC(nc.sbuf_tensor("ag3_sb", [NCORES, D], F32))
    xbnd_f = EC(nc.sbuf_tensor("xbnd_f", [128, DC], F32))
    xbnd_b = EC(nc.sbuf_tensor("xbnd_b", [128, DC], BF16))
    sqb = EC(nc.sbuf_tensor("sqb", [128, DC], BF16))
    bm = EC(nc.sbuf_tensor("bm", [1, 1], F32))
    bs2 = EC(nc.sbuf_tensor("bs2", [1, 1], F32))
    bvar = EC(nc.sbuf_tensor("bvar", [1, 1], F32))
    bstd = EC(nc.sbuf_tensor("bstd", [1, 1], F32))
    binv = EC(nc.sbuf_tensor("binv", [1, 1], F32))
    bnegmi = EC(nc.sbuf_tensor("bnegmi", [1, 1], F32))
    binv_b = EC(nc.sbuf_tensor("binv_b", [1, 1], BF16))
    bnmi_b = EC(nc.sbuf_tensor("bnmi_b", [1, 1], BF16))
    binv_t = EC(nc.sbuf_tensor("binv_t", [128, 1], F32))
    bnmi_t = EC(nc.sbuf_tensor("bnmi_t", [128, 1], F32))

    JORD = [15] + list(range(15))

    with ExitStack() as pctx, nc.Block() as blk:
        trp2 = [pctx.enter_context(nc.psum_tensor(f"trpC{i}", [128, 128], BF16)) for i in range(2)]
        bnd_ps = pctx.enter_context(nc.psum_tensor("bndC", [128, 512], F32))
        st_ps = pctx.enter_context(nc.psum_tensor("stC", [1, 512], F32))

        @blk.sync
        def _(sync):
            for i, j in enumerate(JORD):
                if i >= 2:
                    sync.wait_ge(S.c_xf, i - 1)
                sync.dma_start(out=xstag2[:, i % 2, :],
                               in_=x_sh[j * 128:(j + 1) * 128, :]).then_inc(
                                   S.c_x if i % 2 == 0 else S.c_x1, 16)
            sync.wait_ge(S.cc, 10)
            sync.dma_start(out=ag3_sb[:, :], in_=xb_ag_out[:, :]).then_inc(S.ag3r, 16)

        @blk.gpsimd
        def _(gpsimd):
            gpsimd.wait_ge(S.c_xf, 1)
            gpsimd.dma_start(out=xb_ag_in[:, :], in_=o_acc.ap()[127:128, 15, :]).then_inc(S.ag3w, 16)
            gpsimd.wait_ge(S.ag3w, 16)
            gpsimd.collective_compute("AllGather", ALU.bypass, replica_groups=RG,
                                      ins=[xb_ag_in.ap().opt()], outs=[xb_ag_out.ap().opt()]).then_inc(S.cc, 1)
            gpsimd.wait_ge(S.pe_g, NG)
            gpsimd.dma_start(out=wfk_sb.ap(),
                             in_=wfkT.ap().rearrange("(c p) n -> p c n", p=128)).then_inc(S.init, 16)
            gpsimd.dma_start(out=wr_sb.ap(),
                             in_=wrT.ap().rearrange("(c p) n -> p c n", p=128)).then_inc(S.init, 16)
            gpsimd.dma_start(out=wfv_sb.ap(),
                             in_=wfvT.ap().rearrange("(c p) n -> p c n", p=128)).then_inc(S.init, 16)
            gpsimd.wait_ge(S.bnd, 4)
            gpsimd.tensor_scalar_mul(bm[:, :], bm[:, :], 1.0 / D).then_inc(S.bnd, 1)
            gpsimd.wait_ge(S.bnd, 6)
            gpsimd.tensor_scalar(btmpv[:, :], bs2[:, :], 1.0 / D, EPS, ALU.mult, ALU.add).then_inc(S.bnd, 1)
            gpsimd.wait_ge(S.bnd, 10)
            gpsimd.tensor_mul(bnegmi[:, :], bm[:, :], binv[:, :]).then_inc(S.bnd, 1)

        @blk.vector
        def _(vector):
            def cXA(i):
                j = JORD[i]
                vector.wait_ge(S.c_x if i % 2 == 0 else S.c_x1, 16 * (i // 2 + 1))
                if i == 0:
                    vector.wait_ge(S.rcp, 2)
                    if dbg:
                        vector.wait_ge(S.dbgs, 16 * 14)
                vector.scalar_tensor_tensor(o_acc[:, j, :], o_acc[:, j, :], recip_g[:, j:j + 1],
                                            xstag2[:, i % 2, :], ALU.mult, ALU.add).then_inc(S.c_xf, 1)
                if i >= 4:
                    vector.wait_ge(S.cB, i - 3)
                vector.reduce_sum(mneg[i % 4][:, :], o_acc[:, j, :], axis=AX,
                                  negate=True).then_inc(S.cA, 1)

            def cevacs(i):
                j = JORD[i]
                for dc in range(DC):
                    cnt = i * DC + dc
                    vector.wait_ge(S.c_trp, cnt + 1)
                    vector.tensor_copy(h2_fm[:, dc, 1 + j * 128:1 + (j + 1) * 128],
                                       trp2[cnt % 2][:, :]).then_inc(S.c_trev, 1)

            cXA(0)
            cXA(1)
            for i in range(J):
                if i + 2 < J:
                    cXA(i + 2)
                vector.wait_ge(S.cB, i + 1)
                if i >= 4:
                    vector.wait_ge(S.cD, i - 3)
                vector.tensor_scalar(sq_in[i % 4][:, :], v_raw[i % 4][:, :], 1.0 / D,
                                     D * D * EPS, ALU.mult, ALU.add).then_inc(S.cC, 1)
                vector.wait_ge(S.cD, i + 1)
                if i >= 4:
                    vector.wait_ge(S.c_h, i - 3)
                vector.reciprocal(inv_t[i % 4][:, :], std_t[i % 4][:, :]).then_inc(S.cE, 1)
                if i >= 1:
                    cevacs(i - 1)
            cevacs(J - 1)
            vector.wait_ge(S.bnd, 3)
            vector.reduce_sum(bm[:, :], st_ps[0:1, 0:DC], axis=AX)
            vector.reduce_sum(bs2[:, :], st_ps[0:1, 256:256 + DC], axis=AX).then_inc(S.bnd, 1)
            vector.wait_ge(S.bnd, 5)
            vector.tensor_mul(bm2[:, :], bm[:, :], bm[:, :]).then_inc(S.bnd, 1)
            vector.wait_ge(S.bnd, 7)
            vector.tensor_sub(bvar[:, :], btmpv[:, :], bm2[:, :]).then_inc(S.bnd, 1)
            vector.wait_ge(S.bnd, 9)
            vector.reciprocal(binv[:, :], bstd[:, :]).then_inc(S.bnd, 1)
            vector.wait_ge(S.bnd, 11)
            vector.tensor_scalar_mul(bnmi2[:, :], bnegmi[:, :], -1.0).then_inc(S.bnd, 1)
            vector.wait_ge(S.bnd, 14)
            vector.tensor_copy(binv_t[:, :], bnd_ps[:, 64:65])
            vector.tensor_copy(bnmi_t[:, :], bnd_ps[:, 65:66]).then_inc(S.bnd, 1)

        @blk.scalar
        def _(scalar):
            scalar.wait_ge(S.start, BIG)

            def cBC(i):
                j = JORD[i]
                scalar.wait_ge(S.cA, i + 1)
                scalar.activation(tsq2[:, i % 2, :], o_acc[:, j, :], AF.Identity,
                                  bias=mneg[i % 4][:, :], scale=float(D))
                if i >= 4:
                    scalar.wait_ge(S.cC, i - 3)
                scalar.activation(cdust[:, :], tsq2[:, i % 2, :], AF.Square,
                                  accum_out=v_raw[i % 4][:, :]).then_inc(S.cB, 1)

            cBC(0)
            for i in range(J):
                if i + 1 < J:
                    cBC(i + 1)
                scalar.wait_ge(S.cC, i + 1)
                if i >= 4:
                    scalar.wait_ge(S.cE, i - 3)
                scalar.activation(std_t[i % 4][:, :], sq_in[i % 4][:, :], AF.Sqrt).then_inc(S.cD, 1)
                scalar.wait_ge(S.cE, i + 1)
                if i >= 2:
                    scalar.wait_ge(S.c_trp, 8 * (i - 1))
                scalar.activation(h2tok[:, i % 2, :], tsq2[:, i % 2, :], AF.Copy,
                                  scale=inv_t[i % 4][:, :]).then_inc(S.c_h, 1)
            scalar.wait_ge(S.bnd, 1)
            scalar.activation(xbnd_f[:, :], bnd_ps[:, 0:DC], AF.Copy)
            scalar.activation(xbnd_b[:, :], bnd_ps[:, 0:DC], AF.Copy)
            scalar.activation(sqb[:, :], bnd_ps[:, 0:DC], AF.Square).then_inc(S.bnd, 1)
            scalar.wait_ge(S.bnd, 8)
            scalar.activation(bstd[:, :], bvar[:, :], AF.Sqrt).then_inc(S.bnd, 1)
            scalar.wait_ge(S.bnd, 12)
            scalar.activation(binv_b[:, :], binv[:, :], AF.Copy)
            scalar.activation(bnmi_b[:, :], bnmi2[:, :], AF.Copy).then_inc(S.bnd, 1)
            scalar.wait_ge(S.bnd, 15)
            for dc in range(DC):
                scalar.activation(h2_fm[:, dc, 0:1], xbnd_f[:, dc:dc + 1], AF.Identity,
                                  scale=binv_t[:, :], bias=bnmi_t[:, :]).then_inc(S.bnd5, 1)

        @blk.tensor
        def _(tensor):
            for i, j in enumerate(JORD):
                tensor.wait_ge(S.c_h, i + 1)
                for dc in range(DC):
                    cnt = i * DC + dc
                    if cnt >= 2:
                        tensor.wait_ge(S.c_trev, cnt - 1)
                    tensor.transpose(trp2[cnt % 2][:, :], h2tok[:, i % 2, dc * 128:(dc + 1) * 128],
                                     ident_b[:, :]).then_inc(S.c_trp, 1)
            tensor.wait_ge(S.ag3r, 16)
            tensor.wait_ge(S.s_w, 80)
            for dc in range(DC):
                mm = tensor.matmul(bnd_ps[:, dc:dc + 1], ag3_sb[:, dc * 128:(dc + 1) * 128], bsel_t[:, :],
                                   start=(dc == 0), stop=(dc == DC - 1))
                if dc == DC - 1:
                    mm.then_inc(S.bnd, 1)
            tensor.wait_ge(S.bnd, 2)
            tensor.matmul(st_ps[0:1, 0:DC], ones_col[:, :], xbnd_b[:, :], start=True, stop=False)
            tensor.matmul(st_ps[0:1, 256:256 + DC], ones_col[:, :], sqb[:, :],
                          start=False, stop=True).then_inc(S.bnd, 1)
            tensor.wait_ge(S.bnd, 13)
            tensor.matmul(bnd_ps[:, 64:65], ones_row[:, :], binv_b[:, :], start=True, stop=False)
            tensor.matmul(bnd_ps[:, 65:66], ones_row[:, :], bnmi_b[:, :],
                          start=False, stop=True).then_inc(S.bnd, 1)

        if dbg:
            @blk.scalar
            def _(scalar):
                scalar.wait_ge(S.bnd5, DC)
                scalar.wait_ge(S.c_trev, 128)
                scalar.dma_start(out=dbg_h2[:, :],
                                 in_=h2_fm.ap().rearrange("p c n -> p (c n)").bitcast(U16)).then_inc(S.dbgs, 16)
                scalar.wait_ge(S.c_xf, 16)
                scalar.dma_start(out=dbg_xp[:, :], in_=o_acc.ap().rearrange("p j d -> p (j d)")).then_inc(S.dbgs, 16)

    cctx.close()
    if stop_after == "C":
        root.close()
        return nc
    xr_b = E(nc.sbuf_tensor("xr_b", [128, 2, DC, 512], BF16))
    xk_b = E(nc.sbuf_tensor("xk_b", [128, 2, DC, 512], BF16))
    kact_b = E(nc.sbuf_tensor("kact_b", [128, DC, 512], BF16))
    d_stag = E(nc.sbuf_tensor("d_stag", [128, 512], BF16))
    krelu = E(nc.sbuf_tensor("krelu", [128, 512], F32))
    r_tok = E(nc.sbuf_tensor("r_tok", [128, 2, D], F32))

    with ExitStack() as pctx, nc.Block() as blk:
        kfm_ps = [pctx.enter_context(nc.psum_tensor(f"kfmD{i}", [128, 512], F32)) for i in range(2)]
        r_ps = pctx.enter_context(nc.psum_tensor("rD", [128, D], F32))
        v_ps = pctx.enter_context(nc.psum_tensor("vD", [128, D], F32))

        @blk.vector
        def _(vector):
            vector.wait_ge(S.c_trev, 128)
            vector.wait_ge(S.bnd5, DC)
            vector.wait_ge(S.s_w, 80)
            def mix(b):
                if b >= 2:
                    vector.wait_ge(S.d_rv, b - 1)
                for dc in range(DC):
                    lo = b * 512
                    vector.tensor_sub(d_stag[:, :], h2_fm[:, dc, lo + 1:lo + 513], h2_fm[:, dc, lo:lo + 512])
                    vector.scalar_tensor_tensor(xr_b[:, b % 2, dc, :], d_stag[:, :], mur_t[:, dc:dc + 1],
                                                h2_fm[:, dc, lo:lo + 512], ALU.mult, ALU.add)
                    mx = vector.scalar_tensor_tensor(xk_b[:, b % 2, dc, :], d_stag[:, :], muk_t[:, dc:dc + 1],
                                                     h2_fm[:, dc, lo:lo + 512], ALU.mult, ALU.add)
                    if dc == DC - 1:
                        mx.then_inc(S.d_mix, 1)
            mix(0)
            mix(1)
            for b in range(NB):
                for jj in range(NB):
                    j = b * NB + jj
                    vector.wait_ge(S.d_vmm, j + 1)
                    vector.wait_ge(S.d_rev, j + 1)
                    vector.tensor_mul(r_tok[:, j % 2, :], r_tok[:, j % 2, :], v_ps[:, :]).then_inc(S.d_vev, 1)
                    vector.tensor_add(r_tok[:, j % 2, :], r_tok[:, j % 2, :], o_acc[:, j, :]).then_inc(S.d_oev, 1)
                if b + 2 < NB:
                    mix(b + 2)
            vector.sem_inc(S.fin, 1)

        @blk.tensor
        def _(tensor):
            tensor.wait_ge(S.init, 82 + 48)
            for b in range(NB):
                tensor.wait_ge(S.d_mix, b + 1)
                for c in range(DC):
                    rot = b * DC + c
                    ps = kfm_ps[rot % 2]
                    if rot >= 2:
                        tensor.wait_ge(S.d_krel, rot - 1)
                    for dc in range(DC):
                        mm = tensor.matmul(ps[:, :], wfk_sb[:, dc, c * 128:(c + 1) * 128],
                                           xk_b[:, b % 2, dc, :],
                                           start=(dc == 0), stop=(dc == DC - 1))
                        if dc == DC - 1:
                            mm.then_inc(S.d_kmm, 1)
                for jj in range(NB):
                    j = b * NB + jj
                    if j >= 1:
                        tensor.wait_ge(S.d_rev, j)
                    for dc in range(DC):
                        for half in range(2):
                            mm = tensor.matmul(r_ps[:, half * 512:(half + 1) * 512],
                                               xr_b[:, b % 2, dc, jj * 128:(jj + 1) * 128],
                                               wr_sb[:, dc, half * 512:(half + 1) * 512],
                                               start=(dc == 0), stop=(dc == DC - 1))
                            if dc == DC - 1 and half == 1:
                                mm.then_inc(S.d_rmm, 1)
                    if j >= 1:
                        tensor.wait_ge(S.d_vev, j)
                    tensor.wait_ge(S.d_kact, DC * (b + 1))
                    for dc in range(DC):
                        for half in range(2):
                            mm = tensor.matmul(v_ps[:, half * 512:(half + 1) * 512],
                                               kact_b[:, dc, jj * 128:(jj + 1) * 128],
                                               wfv_sb[:, dc, half * 512:(half + 1) * 512],
                                               start=(dc == 0), stop=(dc == DC - 1))
                            if dc == DC - 1 and half == 1:
                                mm.then_inc(S.d_vmm, 1)
                                if jj == NB - 1:
                                    tensor.sem_inc(S.d_rv, 1)
            tensor.sem_inc(S.fin, 1)

        @blk.scalar
        def _(scalar):
            for b in range(NB):
                for c in range(DC):
                    rot = b * DC + c
                    scalar.wait_ge(S.d_kmm, rot + 1)
                    scalar.activation(krelu[:, :], kfm_ps[rot % 2][:, :], AF.Relu).then_inc(S.d_krel, 1)
                    scalar.activation(kact_b[:, c, :], krelu[:, :], AF.Square).then_inc(S.d_kact, 1)
                for jj in range(NB):
                    j = b * NB + jj
                    scalar.wait_ge(S.d_rmm, j + 1)
                    if j >= 2:
                        scalar.wait_ge(S.y_st if j % 2 == 0 else S.y_st1, 16 * (j // 2))
                    scalar.activation(r_tok[:, j % 2, :], r_ps[:, :], AF.Sigmoid).then_inc(S.d_rev, 1)
            scalar.sem_inc(S.fin, 1)

        @blk.sync
        def _(sync):
            for j in range(J):
                sync.wait_ge(S.d_oev, j + 1)
                sync.dma_start(out=y[j * 128:(j + 1) * 128, :],
                               in_=r_tok[:, j % 2, :]).then_inc(
                                   S.y_st if j % 2 == 0 else S.y_st1, 16)
            sync.wait_ge(S.y_st, 16 * (J // 2))
            sync.wait_ge(S.y_st1, 16 * (J // 2))
            if dbg:
                sync.wait_ge(S.dbgs, 16 * 16)

    root.close()
    return nc


_NC_CACHE = {}


def _get_nc(dbg=False):
    if dbg not in _NC_CACHE:
        _NC_CACHE[dbg] = build(dbg)
    return _NC_CACHE[dbg]


def make_in_maps(inputs):
    asn = lambda a: np.ascontiguousarray(np.asarray(a, dtype=np.float32))
    x = asn(inputs["L0_attn_x"]).reshape(NCORES * T, D)
    common = {
        "wqT": asn(np.asarray(inputs["L0_attn_wq"]).T),
        "wkT": asn(np.asarray(inputs["L0_attn_wk"]).T),
        "wvT": asn(np.asarray(inputs["L0_attn_wv"]).T),
        "wrT": asn(np.asarray(inputs["L0_ffn_W_r"]).T),
        "wfkT": asn(np.asarray(inputs["L0_ffn_W_k"]).T),
        "wfvT": asn(np.asarray(inputs["L0_ffn_W_v"]).T),
        "bq": np.ascontiguousarray(asn(inputs["L0_attn_bq"]).reshape(DC, 128).T),
        "bk": np.ascontiguousarray(asn(inputs["L0_attn_bk"]).reshape(DC, 128).T),
        "bv": asn(inputs["L0_attn_bv"]).reshape(1, D),
        "mur": np.ascontiguousarray(asn(inputs["L0_ffn_mu_r"]).reshape(DC, 128).T),
        "muk": np.ascontiguousarray(asn(inputs["L0_ffn_mu_k"]).reshape(DC, 128).T),
        "ident": np.eye(128, dtype=np.float32),
    }
    in_maps = []
    for i in range(NCORES):
        sel = np.zeros((NCORES, 1), np.float32)
        if i > 0:
            sel[i - 1, 0] = 1.0
        im = dict(common)
        im["x_sh"] = np.ascontiguousarray(x[i * T:(i + 1) * T, :])
        im["bsel"] = sel
        in_maps.append(im)
    return in_maps


def kernel(**inputs):
    nc = _get_nc(False)
    in_maps = make_in_maps(inputs)
    res = run_bass_kernel_spmd(nc, in_maps, core_ids=list(range(NCORES)))
    out = np.concatenate([r["y"] for r in res.results], axis=0)
    return out.reshape(1, NCORES * T, D)


# version 08
# speedup vs baseline: 1.3567x; 1.0059x over previous
import numpy as np
from contextlib import ExitStack

import concourse.bass as bass
import concourse.mybir as mybir
from concourse.bass_utils import run_bass_kernel_spmd

F32 = mybir.dt.float32
BF16 = mybir.dt.bfloat16
U16 = mybir.dt.uint16
AF = mybir.ActivationFunctionType
ALU = mybir.AluOpType
AX = mybir.AxisListType.X

NCORES = 8
D = 1024
DC = 8
T = 2048
J = 16
NG = 32
KC = 4
NB = 4
SCALE = 1.0 / 32.0
EPS = 1e-5

SEM_NAMES = [
    "init", "s_w", "s_x", "s_xfree", "lnA", "lnB", "lnC", "lnD", "lnE", "ln_h", "tr_p", "tr_ev",
    "a_mm", "a_nmm", "a_qev", "a_kev", "a_vev", "a_nev", "a_sq", "a_kst", "a_vst", "a_nst",
    "cc", "b_nld", "nm", "pe_s", "pe_p", "pe_o", "pe_e", "pe_g", "pe_sk", "pe_pts", "rcp",
    "c_x", "c_xf", "ag3w", "ag3r", "cA", "cB", "cC", "cD", "cE", "c_h",
    "c_trp", "c_trev", "bnd", "bnd5",
    "d_mix", "d_rv", "d_kmm", "d_krel", "d_kact", "d_rmm", "d_rev", "d_vmm", "d_vev",
    "d_oev", "y_st", "dbgs", "s_x1", "a_kst1", "a_vst1", "kA", "kA1", "kB", "kB1", "y_st1", "a_qm", "c_x1",
    "start", "fin",
]
BIG = 1 << 30


def build(dbg=False, stop_after=None):
    nc = bass.Bass("TRN2", target_bir_lowering=False)

    x_sh = nc.dram_tensor("x_sh", [T, D], F32, kind="ExternalInput")
    wqT = nc.dram_tensor("wqT", [D, D], F32, kind="ExternalInput")
    wkT = nc.dram_tensor("wkT", [D, D], F32, kind="ExternalInput")
    wvT = nc.dram_tensor("wvT", [D, D], F32, kind="ExternalInput")
    wrT = nc.dram_tensor("wrT", [D, D], F32, kind="ExternalInput")
    wfkT = nc.dram_tensor("wfkT", [D, D], F32, kind="ExternalInput")
    wfvT = nc.dram_tensor("wfvT", [D, D], F32, kind="ExternalInput")
    bq_d = nc.dram_tensor("bq", [128, DC], F32, kind="ExternalInput")
    bk_d = nc.dram_tensor("bk", [128, DC], F32, kind="ExternalInput")
    bv_d = nc.dram_tensor("bv", [1, D], F32, kind="ExternalInput")
    mur_d = nc.dram_tensor("mur", [128, DC], F32, kind="ExternalInput")
    muk_d = nc.dram_tensor("muk", [128, DC], F32, kind="ExternalInput")
    ident_d = nc.dram_tensor("ident", [128, 128], F32, kind="ExternalInput")
    bsel_d = nc.dram_tensor("bsel", [NCORES, 1], F32, kind="ExternalInput")
    y = nc.dram_tensor("y", [T, D], F32, kind="ExternalOutput")

    k_ag_in = nc.dram_tensor("k_ag_in", [4, DC, 128, 512], BF16, kind="Internal")
    k_ag_o = [nc.dram_tensor(f"k_ag_o{j}", [NCORES, DC, 128, 512], BF16, kind="Internal",
                             addr_space="Shared") for j in range(4)]
    v_ag_in = nc.dram_tensor("v_ag_in", [T, D], BF16, kind="Internal")
    v_ag_o = [nc.dram_tensor(f"v_ag_o{j}", [NCORES * 512, D], BF16, kind="Internal",
                             addr_space="Shared") for j in range(4)]
    n_ag_in = nc.dram_tensor("n_ag_in", [1, T], BF16, kind="Internal")
    n_ag_out = nc.dram_tensor("n_ag_out", [128, 128], BF16, kind="Internal")
    xb_ag_in = nc.dram_tensor("xb_ag_in", [1, D], F32, kind="Internal")
    xb_ag_out = nc.dram_tensor("xb_ag_out", [NCORES, D], F32, kind="Internal")
    if dbg:
        dbg_oacc = nc.dram_tensor("dbg_oacc", [128, J * D], F32, kind="ExternalOutput")
        dbg_rl = nc.dram_tensor("dbg_rl", [128, 16], F32, kind="ExternalOutput")
        dbg_negm = nc.dram_tensor("dbg_negm", [128, 1], F32, kind="ExternalOutput")
        dbg_h2 = nc.dram_tensor("dbg_h2", [128, DC * (T + 1)], U16, kind="ExternalOutput")
        dbg_xp = nc.dram_tensor("dbg_xp", [128, J * D], F32, kind="ExternalOutput")
        dbg_misc = nc.dram_tensor("dbg_misc", [1, 8], F32, kind="ExternalOutput")
        dbg_qn = nc.dram_tensor("dbg_qn", [1, T], U16, kind="ExternalOutput")
        dbg_kn = nc.dram_tensor("dbg_kn", [1, T], U16, kind="ExternalOutput")
        dbg_nrm = nc.dram_tensor("dbg_nrm", [128, 128], U16, kind="ExternalOutput")
        dbg_kag = nc.dram_tensor("dbg_kag", [128, 512], U16, kind="ExternalOutput")
        dbg_vag = nc.dram_tensor("dbg_vag", [128, 1024], U16, kind="ExternalOutput")
        dbg_h1 = nc.dram_tensor("dbg_h1", [128, DC * T], U16, kind="ExternalOutput")
        dbg_q = nc.dram_tensor("dbg_q", [128, DC * T], U16, kind="ExternalOutput")
        dbg_hbf = nc.dram_tensor("dbg_hbf", [128, 2 * D], U16, kind="ExternalOutput")

    RG = [list(range(NCORES))]

    root = ExitStack()
    E = root.enter_context
    ident_b = E(nc.sbuf_tensor("ident_b", [128, 128], BF16))
    ones_col = E(nc.sbuf_tensor("ones_col", [128, 1], BF16))
    ones_row = E(nc.sbuf_tensor("ones_row", [1, 128], BF16))
    ones32 = E(nc.sbuf_tensor("ones32", [128, 1], F32))
    bq_t = E(nc.sbuf_tensor("bq_t", [128, DC], F32))
    bk_t = E(nc.sbuf_tensor("bk_t", [128, DC], F32))
    mur_t = E(nc.sbuf_tensor("mur_t", [128, DC], F32))
    muk_t = E(nc.sbuf_tensor("muk_t", [128, DC], F32))
    bsel_t = E(nc.sbuf_tensor("bsel_t", [NCORES, 1], F32))
    neg_m = E(nc.sbuf_tensor("neg_m", [128, 1], F32))
    qmax2 = E(nc.sbuf_tensor("qmax2", [1, 1], F32))
    kmax2 = E(nc.sbuf_tensor("kmax2", [1, 1], F32))
    m2 = E(nc.sbuf_tensor("m2", [1, 1], F32))
    negm_s = E(nc.sbuf_tensor("negm_s", [1, 1], F32))
    negm_b = E(nc.sbuf_tensor("negm_b", [1, 1], BF16))
    nrm_all = E(nc.sbuf_tensor("nrm_all", [128, 128], BF16))
    knmax_c = E(nc.sbuf_tensor("knmax_c", [128, 1], BF16))
    recip_l = E(nc.sbuf_tensor("recip_l", [128, 16], F32))
    mneg = [E(nc.sbuf_tensor(f"mneg{i}", [128, 1], F32)) for i in range(4)]
    v_raw = [E(nc.sbuf_tensor(f"v_raw{i}", [128, 1], F32)) for i in range(4)]
    sq_in = [E(nc.sbuf_tensor(f"sq_in{i}", [128, 1], F32)) for i in range(4)]
    std_t = [E(nc.sbuf_tensor(f"std_t{i}", [128, 1], F32)) for i in range(4)]
    inv_t = [E(nc.sbuf_tensor(f"inv_t{i}", [128, 1], F32)) for i in range(4)]
    recip_g = E(nc.sbuf_tensor("recip_g", [128, 16], F32))
    negm_s2 = E(nc.sbuf_tensor("negm_s2", [1, 1], F32))
    bm2 = E(nc.sbuf_tensor("bm2", [1, 1], F32))
    btmpv = E(nc.sbuf_tensor("btmpv", [1, 1], F32))
    bnmi2 = E(nc.sbuf_tensor("bnmi2", [1, 1], F32))

    qctx = ExitStack()
    q_fm = qctx.enter_context(nc.sbuf_tensor("q_fm", [128, DC, T], BF16, side="right"))

    sems = {name: E(nc.semaphore(name)) for name in SEM_NAMES}
    S = type("S", (), sems)
    _nums = sorted(h.num for h in sems.values())
    assert _nums[-1] - _nums[0] == len(_nums) - 1, _nums
    sem_lo, sem_hi = _nums[0], _nums[-1] + 1

    ctxA = ExitStack()
    EA = ctxA.enter_context
    wq_sb = EA(nc.sbuf_tensor("wq_sb", [128, DC, D], BF16))
    wk_sb = EA(nc.sbuf_tensor("wk_sb", [128, DC, D], BF16))
    wv_sb = EA(nc.sbuf_tensor("wv_sb", [128, DC, D], BF16))
    h1_fm = EA(nc.sbuf_tensor("h1_fm", [128, DC, T], BF16))
    xstag = EA(nc.sbuf_tensor("xstag", [128, 2, D], F32))
    t_stag = EA(nc.sbuf_tensor("t_stag", [128, 2, D], F32))
    tsq_st = EA(nc.sbuf_tensor("tsq_st", [128, D], BF16))
    h_bf = EA(nc.sbuf_tensor("h_bf", [128, 2, D], BF16))
    kstag = EA(nc.sbuf_tensor("kstag", [128, 2, T], BF16))
    vstag = EA(nc.sbuf_tensor("vstag", [128, 2, D], BF16))
    sq_st = EA(nc.sbuf_tensor("sq_st", [128, 2, T], BF16))
    qn_sb = EA(nc.sbuf_tensor("qn_sb", [1, T], BF16))
    kn_sb = EA(nc.sbuf_tensor("kn_sb", [1, T], BF16))
    bv_row = EA(nc.sbuf_tensor("bv_row", [1, D], BF16))

    with ExitStack() as pctx, nc.Block() as blk:
        trp = [pctx.enter_context(nc.psum_tensor(f"trpA{i}", [128, 128], BF16)) for i in range(2)]

        @blk.sync
        def _(sync):
            sync.wait_ge(S.start, BIG)
            for j in range(J):
                if j >= 2:
                    sync.wait_ge(S.s_xfree, j - 1)
                sync.dma_start(out=xstag[:, j % 2, :], in_=x_sh[j * 128:(j + 1) * 128, :]).then_inc(
                    S.s_x if j % 2 == 0 else S.s_x1, 16)
                if j == 1:
                    sync.dma_start(out=bq_t[:, :], in_=bq_d[:, :]).then_inc(S.s_w, 16)
                    sync.dma_start(out=bk_t[:, :], in_=bk_d[:, :]).then_inc(S.s_w, 16)
                    sync.dma_start(out=mur_t[:, :], in_=mur_d[:, :]).then_inc(S.s_w, 16)
                    sync.dma_start(out=muk_t[:, :], in_=muk_d[:, :]).then_inc(S.s_w, 16)
                    sync.dma_start(out=bsel_t[:, :], in_=bsel_d[:, :]).then_inc(S.s_w, 16)

        @blk.gpsimd
        def _(gpsimd):
            gpsimd.sem_clear(range(sem_lo, sem_hi))
            gpsimd.sem_inc(S.start, BIG)
            gpsimd.dma_start(out=ident_b[:, :], in_=ident_d[:, :]).then_inc(S.init, 16)
            gpsimd.dma_start(out=bv_row[:, :], in_=bv_d[:, :]).then_inc(S.init, 16)
            gpsimd.memset(ones_col[:, :], 1.0)
            gpsimd.memset(ones_row[:, :], 1.0).then_inc(S.init, 1)
            gpsimd.dma_start(out=wq_sb.ap(),
                             in_=wqT.ap().rearrange("(c p) n -> p c n", p=128)).then_inc(S.init, 16)
            gpsimd.dma_start(out=wk_sb.ap(),
                             in_=wkT.ap().rearrange("(c p) n -> p c n", p=128)).then_inc(S.init, 16)
            gpsimd.dma_start(out=wv_sb.ap(),
                             in_=wvT.ap().rearrange("(c p) n -> p c n", p=128)).then_inc(S.init, 16)

        @blk.vector
        def _(vector):
            vector.wait_ge(S.start, BIG)

            def lnA(j):
                vector.wait_ge(S.s_x if j % 2 == 0 else S.s_x1, 16 * (j // 2 + 1))
                if j >= 4:
                    vector.wait_ge(S.s_xfree, j - 3)
                vector.reduce_sum(mneg[j % 4][:, :], xstag[:, j % 2, :], axis=AX,
                                  negate=True).then_inc(S.lnA, 1)

            def evacs(j):
                for dc in range(DC):
                    cnt = j * DC + dc
                    vector.wait_ge(S.tr_p, cnt + 1)
                    vector.tensor_copy(h1_fm[:, dc, j * 128:(j + 1) * 128],
                                       trp[cnt % 2][:, :]).then_inc(S.tr_ev, 1)

            lnA(0)
            lnA(1)
            for j in range(J):
                if j + 2 < J:
                    lnA(j + 2)
                vector.wait_ge(S.lnB, j + 1)
                if j >= 4:
                    vector.wait_ge(S.lnD, j - 3)
                vector.tensor_scalar(sq_in[j % 4][:, :], v_raw[j % 4][:, :], 1.0 / D,
                                     D * D * EPS, ALU.mult, ALU.add).then_inc(S.lnC, 1)
                vector.wait_ge(S.lnD, j + 1)
                if j >= 4:
                    vector.wait_ge(S.ln_h, j - 3)
                vector.reciprocal(inv_t[j % 4][:, :], std_t[j % 4][:, :]).then_inc(S.lnE, 1)
                if j >= 1:
                    evacs(j - 1)
            evacs(J - 1)

        @blk.scalar
        def _(scalar):
            scalar.wait_ge(S.start, BIG)

            def lnBC(j):
                scalar.wait_ge(S.lnA, j + 1)
                scalar.activation(t_stag[:, j % 2, :], xstag[:, j % 2, :], AF.Identity,
                                  bias=mneg[j % 4][:, :], scale=float(D)).then_inc(S.s_xfree, 1)
                if j >= 4:
                    scalar.wait_ge(S.lnC, j - 3)
                scalar.activation(tsq_st[:, :], t_stag[:, j % 2, :], AF.Square,
                                  accum_out=v_raw[j % 4][:, :]).then_inc(S.lnB, 1)

            lnBC(0)
            for j in range(J):
                if j + 1 < J:
                    lnBC(j + 1)
                scalar.wait_ge(S.lnC, j + 1)
                if j >= 4:
                    scalar.wait_ge(S.lnE, j - 3)
                scalar.activation(std_t[j % 4][:, :], sq_in[j % 4][:, :], AF.Sqrt).then_inc(S.lnD, 1)
                scalar.wait_ge(S.lnE, j + 1)
                if j >= 2:
                    scalar.wait_ge(S.tr_p, 8 * (j - 1))
                scalar.activation(h_bf[:, j % 2, :], t_stag[:, j % 2, :], AF.Copy,
                                  scale=inv_t[j % 4][:, :]).then_inc(S.ln_h, 1)

        @blk.tensor
        def _(tensor):
            tensor.wait_ge(S.start, BIG)
            tensor.wait_ge(S.init, 33)
            for j in range(J):
                tensor.wait_ge(S.ln_h, j + 1)
                for dc in range(DC):
                    cnt = j * DC + dc
                    if cnt >= 2:
                        tensor.wait_ge(S.tr_ev, cnt - 1)
                    tensor.transpose(trp[cnt % 2][:, :], h_bf[:, j % 2, dc * 128:(dc + 1) * 128],
                                     ident_b[:, :]).then_inc(S.tr_p, 1)

    with ExitStack() as pctx, nc.Block() as blk:
        kq_ps = [pctx.enter_context(nc.psum_tensor(f"kqA{i}", [128, 512], F32)) for i in range(3)]
        vv_ps = [pctx.enter_context(nc.psum_tensor(f"vvA{i}", [128, D], F32)) for i in range(2)]

        def _nrm_mms(tensor, c, off):
            tensor.wait_ge(S.a_sq, off + c + 1)
            if off == DC and c == 0:
                tensor.wait_ge(S.a_nst, 1)
                tensor.wait_ge(S.a_vev, 16)
            for b in range(NB):
                mm = tensor.matmul(vv_ps[b // 2][0:1, (b % 2) * 512:(b % 2 + 1) * 512],
                                   ones_col[:, :],
                                   sq_st[:, c % 2, b * 512:(b + 1) * 512],
                                   start=(c == 0), stop=(c == DC - 1))
                if b == NB - 1:
                    mm.then_inc(S.a_nmm, 1)

        @blk.tensor
        def _(tensor):
            tensor.wait_ge(S.tr_ev, J * DC)
            tensor.wait_ge(S.init, 81)
            for c in range(DC):
                for b in range(NB):
                    n = c * NB + b
                    ps = kq_ps[n % 3]
                    if n >= 3:
                        tensor.wait_ge(S.a_kev, n - 2)
                    for dc in range(DC):
                        mm = tensor.matmul(ps[:, :], wk_sb[:, dc, c * 128:(c + 1) * 128],
                                           h1_fm[:, dc, b * 512:(b + 1) * 512],
                                           start=(dc == 0), stop=(dc == DC - 1))
                        if dc == DC - 1:
                            mm.then_inc(S.a_mm, 1)
                if c >= 1:
                    _nrm_mms(tensor, c - 1, 0)
            _nrm_mms(tensor, DC - 1, 0)
            for t in range(J):
                ps = vv_ps[t % 2]
                if t == 0:
                    tensor.wait_ge(S.a_nst, 1)
                if t >= 2:
                    tensor.wait_ge(S.a_vev, t - 1)
                for dc in range(DC):
                    for half in range(2):
                        tensor.matmul(ps[:, half * 512:(half + 1) * 512],
                                      h1_fm[:, dc, t * 128:(t + 1) * 128],
                                      wv_sb[:, dc, half * 512:(half + 1) * 512],
                                      start=(dc == 0), stop=False)
                for half in range(2):
                    mm = tensor.matmul(ps[:, half * 512:(half + 1) * 512], ones_row[:, :],
                                       bv_row[:, half * 512:(half + 1) * 512], start=False, stop=True)
                    if half == 1:
                        mm.then_inc(S.a_mm, 1)
            for c in range(DC):
                for b in range(NB):
                    n = c * NB + b
                    ps = kq_ps[n % 3]
                    if n == 0:
                        tensor.wait_ge(S.a_kev, 31)
                    elif n == 1:
                        tensor.wait_ge(S.a_kev, 32)
                    elif n >= 3:
                        tensor.wait_ge(S.a_qev, n - 2)
                    for dc in range(DC):
                        mm = tensor.matmul(ps[:, :], wq_sb[:, dc, c * 128:(c + 1) * 128],
                                           h1_fm[:, dc, b * 512:(b + 1) * 512],
                                           start=(dc == 0), stop=(dc == DC - 1))
                        if dc == DC - 1:
                            mm.then_inc(S.a_mm, 1)
                if c >= 1:
                    _nrm_mms(tensor, c - 1, DC)
            _nrm_mms(tensor, DC - 1, DC)

        @blk.scalar
        def _(scalar):
            scalar.wait_ge(S.s_w, 80)
            for c in range(DC):
                if c >= 2:
                    scalar.wait_ge(S.a_kst if c % 2 == 0 else S.a_kst1, 64 * (c // 2))
                for b in range(NB):
                    n = c * NB + b
                    scalar.wait_ge(S.a_mm, n + 1)
                    scalar.activation(kstag[:, c % 2, b * 512:(b + 1) * 512], kq_ps[n % 3][:, :],
                                      AF.Identity, bias=bk_t[:, c:c + 1]).then_inc(S.a_kev, 1)
                if c >= 2:
                    scalar.wait_ge(S.a_nmm, c - 1)
                scalar.activation(sq_st[:, c % 2, :], kstag[:, c % 2, :], AF.Square).then_inc(S.a_sq, 1)
            scalar.wait_ge(S.a_nmm, DC)
            scalar.activation(kn_sb[0:1, 0:1024], vv_ps[0][0:1, 0:1024], AF.Copy)
            scalar.activation(kn_sb[0:1, 1024:2048], vv_ps[1][0:1, 0:1024], AF.Copy).then_inc(S.a_nst, 1)
            for c in range(DC):
                for b in range(NB):
                    n = 48 + c * NB + b
                    scalar.wait_ge(S.a_mm, n + 1)
                    scalar.activation(q_fm[:, c, b * 512:(b + 1) * 512], kq_ps[(c * NB + b) % 3][:, :],
                                      AF.Identity, bias=bq_t[:, c:c + 1]).then_inc(S.a_qev, 1)
                scalar.wait_ge(S.a_nmm, 7 + c)
                scalar.activation(sq_st[:, c % 2, :], q_fm[:, c, :], AF.Square).then_inc(S.a_sq, 1)
            scalar.wait_ge(S.a_nmm, 2 * DC)
            scalar.activation(qn_sb[0:1, 0:1024], vv_ps[0][0:1, 0:1024], AF.Copy)
            scalar.activation(qn_sb[0:1, 1024:2048], vv_ps[1][0:1, 0:1024], AF.Copy).then_inc(S.a_nev, 1)

        @blk.vector
        def _(vector):
            for t in range(J):
                n = 32 + t
                if t >= 2:
                    vector.wait_ge(S.a_vst if t % 2 == 0 else S.a_vst1, 16 * (t // 2))
                vector.wait_ge(S.a_mm, n + 1)
                vector.tensor_copy(vstag[:, t % 2, :], vv_ps[t % 2][:, :]).then_inc(S.a_vev, 1)
            vector.wait_ge(S.a_nev, 1)
            vector.reduce_max(qmax2[:, :], qn_sb[:, :], axis=AX).then_inc(S.a_qm, 1)

        @blk.sync
        def _(sync):
            for c in range(DC):
                sync.wait_ge(S.a_kev, NB * (c + 1))
                for gl in range(4):
                    sync.dma_start(out=k_ag_in[gl, c, :, :],
                                   in_=kstag[:, c % 2, gl * 512:(gl + 1) * 512]).then_inc(
                                       S.a_kst if c % 2 == 0 else S.a_kst1, 16)
            for t in range(J):
                sync.wait_ge(S.a_vev, t + 1)
                sync.dma_start(out=v_ag_in[t * 128:(t + 1) * 128, :], in_=vstag[:, t % 2, :]).then_inc(
                    S.a_vst if t % 2 == 0 else S.a_vst1, 16)
            sync.wait_ge(S.a_nst, 1)
            sync.dma_start(out=n_ag_in[:, :], in_=kn_sb[:, :]).then_inc(S.a_nst, 16)

        @blk.gpsimd
        def _(gpsimd):
            gpsimd.wait_ge(S.a_kst, 16 * 16)
            gpsimd.wait_ge(S.a_kst1, 16 * 16)
            gpsimd.collective_compute("AllGather", ALU.bypass, replica_groups=RG,
                                      ins=[k_ag_in.ap()[0:1, :, :, :].opt()],
                                      outs=[k_ag_o[0].ap().opt()]).then_inc(S.cc, 1)
            gpsimd.wait_ge(S.a_nst, 17)
            gpsimd.collective_compute("AllGather", ALU.bypass, replica_groups=RG,
                                      ins=[n_ag_in.ap().opt()], outs=[n_ag_out.ap().opt()]).then_inc(S.cc, 1)
            for j in range(4):
                gpsimd.wait_ge(S.a_vst, 16 * (2 * j + 2))
                gpsimd.wait_ge(S.a_vst1, 16 * (2 * j + 2))
                gpsimd.collective_compute("AllGather", ALU.bypass, replica_groups=RG,
                                          ins=[v_ag_in.ap()[j * 512:(j + 1) * 512, :].opt()],
                                          outs=[v_ag_o[j].ap().opt()]).then_inc(S.cc, 1)
                if j < 3:
                    gpsimd.collective_compute("AllGather", ALU.bypass, replica_groups=RG,
                                              ins=[k_ag_in.ap()[j + 1:j + 2, :, :, :].opt()],
                                              outs=[k_ag_o[j + 1].ap().opt()]).then_inc(S.cc, 1)
            gpsimd.wait_ge(S.cc, 2)
            gpsimd.dma_start(out=nrm_all[:, :], in_=n_ag_out[:, :]).then_inc(S.b_nld, 16)

    ctxA.close()
    if stop_after == "A":
        root.close()
        return nc
    o_acc = E(nc.sbuf_tensor("o_acc", [128, J, D], F32))

    ctxB = ExitStack()
    EB = ctxB.enter_context
    vbuf = EB(nc.sbuf_tensor("vbuf", [128, 2, KC, 1024], BF16))
    kbuf = EB(nc.sbuf_tensor("kbuf", [128, 2, DC, 512], BF16))
    pt = [EB(nc.sbuf_tensor(f"pt{i}", [128, KC, T], BF16)) for i in range(2)]
    gsum = EB(nc.sbuf_tensor("gsum", [128, T], F32))

    with ExitStack() as pctx, nc.Block() as blk:
        s_ps = [pctx.enter_context(nc.psum_tensor(f"sB{i}", [128, 512], F32)) for i in range(2)]
        o_ps = [pctx.enter_context(nc.psum_tensor(f"oB{i}", [128, D], F32)) for i in range(2)]
        l_ps = pctx.enter_context(nc.psum_tensor("l_ps", [128, 512], F32))
        nm_ps = pctx.enter_context(nc.psum_tensor("nmB", [128, 512], F32))

        @blk.gpsimd
        def _(gpsimd):
            gpsimd.memset(ones32[:, :], 1.0).then_inc(S.init, 1)
            gpsimd.wait_ge(S.nm, 3)
            gpsimd.tensor_mul(m2[:, :], kmax2[:, :], qmax2[:, :]).then_inc(S.nm, 1)
            gpsimd.wait_ge(S.nm, 5)
            gpsimd.tensor_scalar_mul(negm_s2[:, :], negm_s[:, :], -SCALE).then_inc(S.nm, 1)
            gpsimd.wait_ge(S.rcp, 1)
            gpsimd.tensor_copy(recip_g[:, :], recip_l[:, :]).then_inc(S.rcp, 1)

        @blk.vector
        def _(vector):
            vector.wait_ge(S.b_nld, 16)
            vector.reduce_max(knmax_c[:, :], nrm_all[:, :], axis=AX).then_inc(S.nm, 1)
            vector.wait_ge(S.nm, 2)
            vector.reduce_max(kmax2[:, :], nm_ps[0:1, 0:64].bitcast(BF16), axis=AX).then_inc(S.nm, 1)
            vector.wait_ge(S.nm, 8)
            vector.tensor_copy(neg_m[:, :], nm_ps[:, 128:129]).then_inc(S.nm, 1)
            n = 0
            for g in range(NG):
                vector.wait_ge(S.pe_p, 16 * (g + 1))
                ptg2 = pt[g % 2]
                if g == 0:
                    vector.tensor_tensor(gsum[:, :], ptg2[:, 0, :], ptg2[:, 1, :], ALU.add)
                else:
                    vector.tensor_tensor(gsum[:, :], gsum[:, :], ptg2[:, 0, :], ALU.add)
                    vector.tensor_tensor(gsum[:, :], gsum[:, :], ptg2[:, 1, :], ALU.add)
                vector.tensor_tensor(gsum[:, :], gsum[:, :], ptg2[:, 2, :], ALU.add)
                vector.tensor_tensor(gsum[:, :], gsum[:, :], ptg2[:, 3, :], ALU.add).then_inc(S.pe_pts, 1)
                for j in range(J):
                    vector.wait_ge(S.pe_o, n + 1)
                    if g == 0:
                        vector.tensor_copy(o_acc[:, j, :], o_ps[n % 2][:, :]).then_inc(S.pe_e, 1)
                    else:
                        vector.tensor_tensor(o_acc[:, j, :], o_acc[:, j, :], o_ps[n % 2][:, :],
                                             ALU.add).then_inc(S.pe_e, 1)
                    n += 1
            vector.wait_ge(S.pe_g, NG + 1)
            vector.reciprocal(recip_l[:, :], l_ps[:, 0:16]).then_inc(S.rcp, 1)

        @blk.scalar
        def _(scalar):
            scalar.wait_ge(S.nm, 4)
            scalar.activation(negm_s[:, :], m2[:, :], AF.Sqrt).then_inc(S.nm, 1)
            scalar.wait_ge(S.nm, 6)
            scalar.activation(negm_b[:, :], negm_s2[:, :], AF.Copy).then_inc(S.nm, 1)
            n = 0
            scalar.wait_ge(S.a_nst, 17)
            scalar.wait_ge(S.a_qm, 1)
            for g in range(NG):
                if g == 0:
                    scalar.wait_ge(S.nm, 9)
                for kc in range(KC):
                    for b in range(NB):
                        scalar.wait_ge(S.pe_s, n + 1)
                        scalar.activation(pt[g % 2][:, kc, b * 512:(b + 1) * 512], s_ps[n % 2][:, :],
                                          AF.Exp, bias=neg_m[:, :], scale=SCALE).then_inc(S.pe_p, 1)
                        n += 1

        @blk.tensor
        def _(tensor):
            tensor.wait_ge(S.nm, 1)
            tensor.transpose(nm_ps[0:1, 0:64].bitcast(BF16), knmax_c[:, :], ident_b[:, :]).then_inc(S.nm, 1)
            tensor.wait_ge(S.nm, 7)
            tensor.matmul(nm_ps[:, 128:129], ones_row[:, :], negm_b[:, :], start=True, stop=True).then_inc(S.nm, 1)
            tensor.wait_ge(S.a_qev, 32)
            tensor.wait_ge(S.init, 82)
            n_s = 0
            n_o = 0
            for g in range(NG + 1):
                if g < NG:
                    tensor.wait_ge(S.kA if g % 2 == 0 else S.kA1, 16 * (g // 2 + 1))
                    for kc in range(KC):
                        for b in range(NB):
                            ps = s_ps[n_s % 2]
                            if n_s >= 2:
                                tensor.wait_ge(S.pe_p, n_s - 1)
                            for dc in range(DC):
                                mm = tensor.matmul(ps[:, :], kbuf[:, g % 2, dc, kc * 128:(kc + 1) * 128],
                                                   q_fm[:, dc, b * 512:(b + 1) * 512],
                                                   start=(dc == 0), stop=(dc == DC - 1))
                                if dc == DC - 1:
                                    mm.then_inc(S.pe_s, 1)
                            n_s += 1
                    tensor.nop().then_inc(S.pe_sk, 1)
                if g >= 1:
                    gp = g - 1
                    tensor.wait_ge(S.pe_p, gp * 16 + 16)
                    tensor.wait_ge(S.kB if gp % 2 == 0 else S.kB1, 16 * (gp // 2 + 1))
                    ptg = pt[gp % 2]
                    for j in range(J):
                        op = o_ps[n_o % 2]
                        if n_o >= 2:
                            tensor.wait_ge(S.pe_e, n_o - 1)
                        for kc in range(KC):
                            stat = ptg[:, kc, j * 128:(j + 1) * 128]
                            tensor.matmul(op[:, 0:512], stat, vbuf[:, gp % 2, kc, 0:512],
                                          start=(kc == 0), stop=(kc == KC - 1))
                            mm = tensor.matmul(op[:, 512:1024], stat, vbuf[:, gp % 2, kc, 512:1024],
                                               start=(kc == 0), stop=(kc == KC - 1))
                            if kc == KC - 1:
                                mm.then_inc(S.pe_o, 1)
                        n_o += 1
                    tensor.nop().then_inc(S.pe_g, 1)
            tensor.wait_ge(S.pe_pts, NG)
            for j in range(J):
                mm = tensor.matmul(l_ps[:, j:j + 1], gsum[:, j * 128:(j + 1) * 128],
                                   ones32[:, :], start=(j == 0), stop=(j == J - 1))
            mm.then_inc(S.pe_g, 1)

        @blk.sync
        def _(sync):
            for gi in range(NG):
                j, r = gi // 8, gi % 8
                ka = S.kA if gi % 2 == 0 else S.kA1
                kb = S.kB if gi % 2 == 0 else S.kB1
                if gi >= 2:
                    sync.wait_ge(S.pe_sk, gi - 1)
                if gi == 0:
                    sync.wait_ge(S.a_vst, 16 * 8)
                    sync.wait_ge(S.a_vst1, 16 * 8)
                sync.wait_ge(S.cc, 1 if j == 0 else 2 * j + 2)
                sync.dma_start(out=kbuf.ap()[:, gi % 2:gi % 2 + 1, :, :],
                               in_=k_ag_o[j].ap()[r:r + 1, :, :, :].rearrange(
                                   "o c p t -> p o c t")).then_inc(ka, 16)
                if gi >= 2:
                    sync.wait_ge(S.pe_g, gi - 1)
                if gi == 0:
                    sync.wait_ge(S.a_mm, 80)
                sync.wait_ge(S.cc, 2 * j + 3)
                sync.dma_start(out=vbuf.ap()[:, gi % 2, :, :],
                               in_=v_ag_o[j].ap()[r * 512:(r + 1) * 512, :].rearrange(
                                   "(kc p) d -> p kc d", p=128)).then_inc(kb, 16)

        if dbg:
            @blk.scalar
            def _(scalar):
                scalar.wait_ge(S.rcp, 1)
                scalar.dma_start(out=dbg_rl[:, :], in_=recip_l[:, :]).then_inc(S.dbgs, 16)
                scalar.dma_start(out=dbg_negm[:, :], in_=neg_m[:, :]).then_inc(S.dbgs, 16)
                scalar.dma_start(out=dbg_oacc[:, :], in_=o_acc.ap().rearrange("p j d -> p (j d)")).then_inc(S.dbgs, 16)
                scalar.dma_start(out=dbg_misc[:, 0:1], in_=qmax2[:, :]).then_inc(S.dbgs, 16)
                scalar.dma_start(out=dbg_misc[:, 1:2], in_=kmax2[:, :]).then_inc(S.dbgs, 16)
                scalar.dma_start(out=dbg_misc[:, 2:3], in_=m2[:, :]).then_inc(S.dbgs, 16)
                scalar.dma_start(out=dbg_nrm[:, :], in_=nrm_all.ap().bitcast(U16)).then_inc(S.dbgs, 16)
                scalar.dma_start(out=dbg_kag[:, :],
                                 in_=k_ag_o[0].ap()[0, 0, :, :].bitcast(U16)).then_inc(S.dbgs, 16)
                scalar.dma_start(out=dbg_vag[:, :],
                                 in_=v_ag_o[0].ap()[0:128, :].bitcast(U16)).then_inc(S.dbgs, 16)

    ctxB.close()
    qctx.close()
    if stop_after == "B":
        root.close()
        return nc
    wr_sb = E(nc.sbuf_tensor("wr_sb", [128, DC, D], BF16))
    wfk_sb = E(nc.sbuf_tensor("wfk_sb", [128, DC, D], BF16))
    wfv_sb = E(nc.sbuf_tensor("wfv_sb", [128, DC, D], BF16))
    h2_fm = E(nc.sbuf_tensor("h2_fm", [128, DC, T + 1], BF16))
    cctx = ExitStack()
    EC = cctx.enter_context
    xstag2 = EC(nc.sbuf_tensor("xstag2", [128, 2, D], F32))
    h2tok = EC(nc.sbuf_tensor("h2tok", [128, 2, D], BF16))
    tsq2 = EC(nc.sbuf_tensor("tsq2", [128, 2, D], F32))
    cdust = EC(nc.sbuf_tensor("cdust", [128, D], BF16))
    ag3_sb = EC(nc.sbuf_tensor("ag3_sb", [NCORES, D], F32))
    xbnd_f = EC(nc.sbuf_tensor("xbnd_f", [128, DC], F32))
    xbnd_b = EC(nc.sbuf_tensor("xbnd_b", [128, DC], BF16))
    sqb = EC(nc.sbuf_tensor("sqb", [128, DC], BF16))
    bm = EC(nc.sbuf_tensor("bm", [1, 1], F32))
    bs2 = EC(nc.sbuf_tensor("bs2", [1, 1], F32))
    bvar = EC(nc.sbuf_tensor("bvar", [1, 1], F32))
    bstd = EC(nc.sbuf_tensor("bstd", [1, 1], F32))
    binv = EC(nc.sbuf_tensor("binv", [1, 1], F32))
    bnegmi = EC(nc.sbuf_tensor("bnegmi", [1, 1], F32))
    binv_b = EC(nc.sbuf_tensor("binv_b", [1, 1], BF16))
    bnmi_b = EC(nc.sbuf_tensor("bnmi_b", [1, 1], BF16))
    binv_t = EC(nc.sbuf_tensor("binv_t", [128, 1], F32))
    bnmi_t = EC(nc.sbuf_tensor("bnmi_t", [128, 1], F32))

    JORD = [15] + list(range(15))

    with ExitStack() as pctx, nc.Block() as blk:
        trp2 = [pctx.enter_context(nc.psum_tensor(f"trpC{i}", [128, 128], BF16)) for i in range(2)]
        bnd_ps = pctx.enter_context(nc.psum_tensor("bndC", [128, 512], F32))
        st_ps = pctx.enter_context(nc.psum_tensor("stC", [1, 512], F32))

        @blk.sync
        def _(sync):
            for i, j in enumerate(JORD):
                if i >= 2:
                    sync.wait_ge(S.c_xf, i - 1)
                sync.dma_start(out=xstag2[:, i % 2, :],
                               in_=x_sh[j * 128:(j + 1) * 128, :]).then_inc(
                                   S.c_x if i % 2 == 0 else S.c_x1, 16)
            sync.wait_ge(S.cc, 10)
            sync.dma_start(out=ag3_sb[:, :], in_=xb_ag_out[:, :]).then_inc(S.ag3r, 16)

        @blk.gpsimd
        def _(gpsimd):
            gpsimd.wait_ge(S.c_xf, 1)
            gpsimd.dma_start(out=xb_ag_in[:, :], in_=o_acc.ap()[127:128, 15, :]).then_inc(S.ag3w, 16)
            gpsimd.wait_ge(S.ag3w, 16)
            gpsimd.collective_compute("AllGather", ALU.bypass, replica_groups=RG,
                                      ins=[xb_ag_in.ap().opt()], outs=[xb_ag_out.ap().opt()]).then_inc(S.cc, 1)
            gpsimd.wait_ge(S.pe_g, NG)
            gpsimd.dma_start(out=wfk_sb.ap(),
                             in_=wfkT.ap().rearrange("(c p) n -> p c n", p=128)).then_inc(S.init, 16)
            gpsimd.dma_start(out=wr_sb.ap(),
                             in_=wrT.ap().rearrange("(c p) n -> p c n", p=128)).then_inc(S.init, 16)
            gpsimd.dma_start(out=wfv_sb.ap(),
                             in_=wfvT.ap().rearrange("(c p) n -> p c n", p=128)).then_inc(S.init, 16)
            gpsimd.wait_ge(S.bnd, 4)
            gpsimd.tensor_scalar_mul(bm[:, :], bm[:, :], 1.0 / D).then_inc(S.bnd, 1)
            gpsimd.wait_ge(S.bnd, 6)
            gpsimd.tensor_scalar(btmpv[:, :], bs2[:, :], 1.0 / D, EPS, ALU.mult, ALU.add).then_inc(S.bnd, 1)
            gpsimd.wait_ge(S.bnd, 10)
            gpsimd.tensor_mul(bnegmi[:, :], bm[:, :], binv[:, :]).then_inc(S.bnd, 1)

        @blk.vector
        def _(vector):
            def cXA(i):
                j = JORD[i]
                vector.wait_ge(S.c_x if i % 2 == 0 else S.c_x1, 16 * (i // 2 + 1))
                if i == 0:
                    vector.wait_ge(S.rcp, 2)
                    if dbg:
                        vector.wait_ge(S.dbgs, 16 * 14)
                vector.scalar_tensor_tensor(o_acc[:, j, :], o_acc[:, j, :], recip_g[:, j:j + 1],
                                            xstag2[:, i % 2, :], ALU.mult, ALU.add).then_inc(S.c_xf, 1)
                if i >= 4:
                    vector.wait_ge(S.cB, i - 3)
                vector.reduce_sum(mneg[i % 4][:, :], o_acc[:, j, :], axis=AX,
                                  negate=True).then_inc(S.cA, 1)

            def cevacs(i):
                j = JORD[i]
                for dc in range(DC):
                    cnt = i * DC + dc
                    vector.wait_ge(S.c_trp, cnt + 1)
                    vector.tensor_copy(h2_fm[:, dc, 1 + j * 128:1 + (j + 1) * 128],
                                       trp2[cnt % 2][:, :]).then_inc(S.c_trev, 1)

            cXA(0)
            cXA(1)
            for i in range(J):
                if i + 2 < J:
                    cXA(i + 2)
                vector.wait_ge(S.cB, i + 1)
                if i >= 4:
                    vector.wait_ge(S.cD, i - 3)
                vector.tensor_scalar(sq_in[i % 4][:, :], v_raw[i % 4][:, :], 1.0 / D,
                                     D * D * EPS, ALU.mult, ALU.add).then_inc(S.cC, 1)
                vector.wait_ge(S.cD, i + 1)
                if i >= 4:
                    vector.wait_ge(S.c_h, i - 3)
                vector.reciprocal(inv_t[i % 4][:, :], std_t[i % 4][:, :]).then_inc(S.cE, 1)
                if i >= 1:
                    cevacs(i - 1)
            cevacs(J - 1)
            vector.wait_ge(S.bnd, 3)
            vector.reduce_sum(bm[:, :], st_ps[0:1, 0:DC], axis=AX)
            vector.reduce_sum(bs2[:, :], st_ps[0:1, 256:256 + DC], axis=AX).then_inc(S.bnd, 1)
            vector.wait_ge(S.bnd, 5)
            vector.tensor_mul(bm2[:, :], bm[:, :], bm[:, :]).then_inc(S.bnd, 1)
            vector.wait_ge(S.bnd, 7)
            vector.tensor_sub(bvar[:, :], btmpv[:, :], bm2[:, :]).then_inc(S.bnd, 1)
            vector.wait_ge(S.bnd, 9)
            vector.reciprocal(binv[:, :], bstd[:, :]).then_inc(S.bnd, 1)
            vector.wait_ge(S.bnd, 11)
            vector.tensor_scalar_mul(bnmi2[:, :], bnegmi[:, :], -1.0).then_inc(S.bnd, 1)
            vector.wait_ge(S.bnd, 14)
            vector.tensor_copy(binv_t[:, :], bnd_ps[:, 64:65])
            vector.tensor_copy(bnmi_t[:, :], bnd_ps[:, 65:66]).then_inc(S.bnd, 1)

        @blk.scalar
        def _(scalar):
            scalar.wait_ge(S.start, BIG)

            def cBC(i):
                j = JORD[i]
                scalar.wait_ge(S.cA, i + 1)
                scalar.activation(tsq2[:, i % 2, :], o_acc[:, j, :], AF.Identity,
                                  bias=mneg[i % 4][:, :], scale=float(D))
                if i >= 4:
                    scalar.wait_ge(S.cC, i - 3)
                scalar.activation(cdust[:, :], tsq2[:, i % 2, :], AF.Square,
                                  accum_out=v_raw[i % 4][:, :]).then_inc(S.cB, 1)

            cBC(0)
            for i in range(J):
                if i + 1 < J:
                    cBC(i + 1)
                scalar.wait_ge(S.cC, i + 1)
                if i >= 4:
                    scalar.wait_ge(S.cE, i - 3)
                scalar.activation(std_t[i % 4][:, :], sq_in[i % 4][:, :], AF.Sqrt).then_inc(S.cD, 1)
                scalar.wait_ge(S.cE, i + 1)
                if i >= 2:
                    scalar.wait_ge(S.c_trp, 8 * (i - 1))
                scalar.activation(h2tok[:, i % 2, :], tsq2[:, i % 2, :], AF.Copy,
                                  scale=inv_t[i % 4][:, :]).then_inc(S.c_h, 1)
            scalar.wait_ge(S.bnd, 1)
            scalar.activation(xbnd_f[:, :], bnd_ps[:, 0:DC], AF.Copy)
            scalar.activation(xbnd_b[:, :], bnd_ps[:, 0:DC], AF.Copy)
            scalar.activation(sqb[:, :], bnd_ps[:, 0:DC], AF.Square).then_inc(S.bnd, 1)
            scalar.wait_ge(S.bnd, 8)
            scalar.activation(bstd[:, :], bvar[:, :], AF.Sqrt).then_inc(S.bnd, 1)
            scalar.wait_ge(S.bnd, 12)
            scalar.activation(binv_b[:, :], binv[:, :], AF.Copy)
            scalar.activation(bnmi_b[:, :], bnmi2[:, :], AF.Copy).then_inc(S.bnd, 1)
            scalar.wait_ge(S.bnd, 15)
            for dc in range(DC):
                scalar.activation(h2_fm[:, dc, 0:1], xbnd_f[:, dc:dc + 1], AF.Identity,
                                  scale=binv_t[:, :], bias=bnmi_t[:, :]).then_inc(S.bnd5, 1)

        @blk.tensor
        def _(tensor):
            for i, j in enumerate(JORD):
                tensor.wait_ge(S.c_h, i + 1)
                for dc in range(DC):
                    cnt = i * DC + dc
                    if cnt >= 2:
                        tensor.wait_ge(S.c_trev, cnt - 1)
                    tensor.transpose(trp2[cnt % 2][:, :], h2tok[:, i % 2, dc * 128:(dc + 1) * 128],
                                     ident_b[:, :]).then_inc(S.c_trp, 1)
            tensor.wait_ge(S.ag3r, 16)
            tensor.wait_ge(S.s_w, 80)
            for dc in range(DC):
                mm = tensor.matmul(bnd_ps[:, dc:dc + 1], ag3_sb[:, dc * 128:(dc + 1) * 128], bsel_t[:, :],
                                   start=(dc == 0), stop=(dc == DC - 1))
                if dc == DC - 1:
                    mm.then_inc(S.bnd, 1)
            tensor.wait_ge(S.bnd, 2)
            tensor.matmul(st_ps[0:1, 0:DC], ones_col[:, :], xbnd_b[:, :], start=True, stop=False)
            tensor.matmul(st_ps[0:1, 256:256 + DC], ones_col[:, :], sqb[:, :],
                          start=False, stop=True).then_inc(S.bnd, 1)
            tensor.wait_ge(S.bnd, 13)
            tensor.matmul(bnd_ps[:, 64:65], ones_row[:, :], binv_b[:, :], start=True, stop=False)
            tensor.matmul(bnd_ps[:, 65:66], ones_row[:, :], bnmi_b[:, :],
                          start=False, stop=True).then_inc(S.bnd, 1)

        if dbg:
            @blk.scalar
            def _(scalar):
                scalar.wait_ge(S.bnd5, DC)
                scalar.wait_ge(S.c_trev, 128)
                scalar.dma_start(out=dbg_h2[:, :],
                                 in_=h2_fm.ap().rearrange("p c n -> p (c n)").bitcast(U16)).then_inc(S.dbgs, 16)
                scalar.wait_ge(S.c_xf, 16)
                scalar.dma_start(out=dbg_xp[:, :], in_=o_acc.ap().rearrange("p j d -> p (j d)")).then_inc(S.dbgs, 16)

    cctx.close()
    if stop_after == "C":
        root.close()
        return nc
    xr_b = E(nc.sbuf_tensor("xr_b", [128, 2, DC, 512], BF16))
    xk_b = E(nc.sbuf_tensor("xk_b", [128, 2, DC, 512], BF16))
    kact_b = E(nc.sbuf_tensor("kact_b", [128, DC, 512], BF16))
    d_stag = E(nc.sbuf_tensor("d_stag", [128, 512], BF16))
    krelu = E(nc.sbuf_tensor("krelu", [128, 512], F32))
    r_tok = E(nc.sbuf_tensor("r_tok", [128, 2, D], F32))

    with ExitStack() as pctx, nc.Block() as blk:
        kfm_ps = [pctx.enter_context(nc.psum_tensor(f"kfmD{i}", [128, 512], F32)) for i in range(2)]
        r_ps = pctx.enter_context(nc.psum_tensor("rD", [128, D], F32))
        v_ps = pctx.enter_context(nc.psum_tensor("vD", [128, D], F32))

        @blk.vector
        def _(vector):
            vector.wait_ge(S.c_trev, 128)
            vector.wait_ge(S.bnd5, DC)
            vector.wait_ge(S.s_w, 80)
            def mix(b):
                if b >= 2:
                    vector.wait_ge(S.d_rv, b - 1)
                for dc in range(DC):
                    lo = b * 512
                    vector.tensor_sub(d_stag[:, :], h2_fm[:, dc, lo + 1:lo + 513], h2_fm[:, dc, lo:lo + 512])
                    vector.scalar_tensor_tensor(xr_b[:, b % 2, dc, :], d_stag[:, :], mur_t[:, dc:dc + 1],
                                                h2_fm[:, dc, lo:lo + 512], ALU.mult, ALU.add)
                    mx = vector.scalar_tensor_tensor(xk_b[:, b % 2, dc, :], d_stag[:, :], muk_t[:, dc:dc + 1],
                                                     h2_fm[:, dc, lo:lo + 512], ALU.mult, ALU.add)
                    if dc == DC - 1:
                        mx.then_inc(S.d_mix, 1)
            mix(0)
            mix(1)
            for b in range(NB):
                for jj in range(NB):
                    j = b * NB + jj
                    vector.wait_ge(S.d_vmm, j + 1)
                    vector.wait_ge(S.d_rev, j + 1)
                    vector.tensor_mul(r_tok[:, j % 2, :], r_tok[:, j % 2, :], v_ps[:, :]).then_inc(S.d_vev, 1)
                    vector.tensor_add(r_tok[:, j % 2, :], r_tok[:, j % 2, :], o_acc[:, j, :]).then_inc(S.d_oev, 1)
                if b + 2 < NB:
                    mix(b + 2)
            vector.sem_inc(S.fin, 1)

        @blk.tensor
        def _(tensor):
            tensor.wait_ge(S.init, 82 + 48)
            for b in range(NB):
                tensor.wait_ge(S.d_mix, b + 1)
                for c in range(DC):
                    rot = b * DC + c
                    ps = kfm_ps[rot % 2]
                    if rot >= 2:
                        tensor.wait_ge(S.d_krel, rot - 1)
                    for dc in range(DC):
                        mm = tensor.matmul(ps[:, :], wfk_sb[:, dc, c * 128:(c + 1) * 128],
                                           xk_b[:, b % 2, dc, :],
                                           start=(dc == 0), stop=(dc == DC - 1))
                        if dc == DC - 1:
                            mm.then_inc(S.d_kmm, 1)
                for jj in range(NB):
                    j = b * NB + jj
                    if j >= 1:
                        tensor.wait_ge(S.d_rev, j)
                    for dc in range(DC):
                        for half in range(2):
                            mm = tensor.matmul(r_ps[:, half * 512:(half + 1) * 512],
                                               xr_b[:, b % 2, dc, jj * 128:(jj + 1) * 128],
                                               wr_sb[:, dc, half * 512:(half + 1) * 512],
                                               start=(dc == 0), stop=(dc == DC - 1))
                            if dc == DC - 1 and half == 1:
                                mm.then_inc(S.d_rmm, 1)
                    if j >= 1:
                        tensor.wait_ge(S.d_vev, j)
                    tensor.wait_ge(S.d_kact, DC * (b + 1))
                    for dc in range(DC):
                        for half in range(2):
                            mm = tensor.matmul(v_ps[:, half * 512:(half + 1) * 512],
                                               kact_b[:, dc, jj * 128:(jj + 1) * 128],
                                               wfv_sb[:, dc, half * 512:(half + 1) * 512],
                                               start=(dc == 0), stop=(dc == DC - 1))
                            if dc == DC - 1 and half == 1:
                                mm.then_inc(S.d_vmm, 1)
                                if jj == NB - 1:
                                    tensor.sem_inc(S.d_rv, 1)
            tensor.sem_inc(S.fin, 1)

        @blk.scalar
        def _(scalar):
            for b in range(NB):
                for c in range(DC):
                    rot = b * DC + c
                    scalar.wait_ge(S.d_kmm, rot + 1)
                    scalar.activation(krelu[:, :], kfm_ps[rot % 2][:, :], AF.Relu).then_inc(S.d_krel, 1)
                    scalar.activation(kact_b[:, c, :], krelu[:, :], AF.Square).then_inc(S.d_kact, 1)
                for jj in range(NB):
                    j = b * NB + jj
                    scalar.wait_ge(S.d_rmm, j + 1)
                    if j >= 2:
                        scalar.wait_ge(S.y_st if j % 2 == 0 else S.y_st1, 16 * (j // 2))
                    scalar.activation(r_tok[:, j % 2, :], r_ps[:, :], AF.Sigmoid).then_inc(S.d_rev, 1)
            scalar.sem_inc(S.fin, 1)

        @blk.sync
        def _(sync):
            for j in range(J):
                sync.wait_ge(S.d_oev, j + 1)
                sync.dma_start(out=y[j * 128:(j + 1) * 128, :],
                               in_=r_tok[:, j % 2, :]).then_inc(
                                   S.y_st if j % 2 == 0 else S.y_st1, 16)
            sync.wait_ge(S.y_st, 16 * (J // 2))
            sync.wait_ge(S.y_st1, 16 * (J // 2))
            if dbg:
                sync.wait_ge(S.dbgs, 16 * 16)

    root.close()
    return nc


_NC_CACHE = {}


def _get_nc(dbg=False):
    if dbg not in _NC_CACHE:
        _NC_CACHE[dbg] = build(dbg)
    return _NC_CACHE[dbg]


def make_in_maps(inputs):
    asn = lambda a: np.ascontiguousarray(np.asarray(a, dtype=np.float32))
    x = asn(inputs["L0_attn_x"]).reshape(NCORES * T, D)
    common = {
        "wqT": asn(np.asarray(inputs["L0_attn_wq"]).T),
        "wkT": asn(np.asarray(inputs["L0_attn_wk"]).T),
        "wvT": asn(np.asarray(inputs["L0_attn_wv"]).T),
        "wrT": asn(np.asarray(inputs["L0_ffn_W_r"]).T),
        "wfkT": asn(np.asarray(inputs["L0_ffn_W_k"]).T),
        "wfvT": asn(np.asarray(inputs["L0_ffn_W_v"]).T),
        "bq": np.ascontiguousarray(asn(inputs["L0_attn_bq"]).reshape(DC, 128).T),
        "bk": np.ascontiguousarray(asn(inputs["L0_attn_bk"]).reshape(DC, 128).T),
        "bv": asn(inputs["L0_attn_bv"]).reshape(1, D),
        "mur": np.ascontiguousarray(asn(inputs["L0_ffn_mu_r"]).reshape(DC, 128).T),
        "muk": np.ascontiguousarray(asn(inputs["L0_ffn_mu_k"]).reshape(DC, 128).T),
        "ident": np.eye(128, dtype=np.float32),
    }
    in_maps = []
    for i in range(NCORES):
        sel = np.zeros((NCORES, 1), np.float32)
        if i > 0:
            sel[i - 1, 0] = 1.0
        im = dict(common)
        im["x_sh"] = np.ascontiguousarray(x[i * T:(i + 1) * T, :])
        im["bsel"] = sel
        in_maps.append(im)
    return in_maps


def kernel(**inputs):
    nc = _get_nc(False)
    in_maps = make_in_maps(inputs)
    res = run_bass_kernel_spmd(nc, in_maps, core_ids=list(range(NCORES)))
    out = np.concatenate([r["y"] for r in res.results], axis=0)
    return out.reshape(1, NCORES * T, D)


# version 09
# speedup vs baseline: 1.3575x; 1.0006x over previous
import dataclasses
import numpy as np
from contextlib import ExitStack

import concourse.bass as bass
import concourse.mybir as mybir
from concourse.bass_utils import run_bass_kernel_spmd

F32 = mybir.dt.float32
BF16 = mybir.dt.bfloat16
U16 = mybir.dt.uint16
AF = mybir.ActivationFunctionType
ALU = mybir.AluOpType
AX = mybir.AxisListType.X

NCORES = 8
D = 1024
DC = 8
T = 2048
J = 16
NG = 32
KC = 4
NB = 4
SCALE = 1.0 / 32.0
EPS = 1e-5

SEM_NAMES = [
    "init", "s_w", "s_x", "s_xfree", "lnA", "lnB", "lnC", "lnD", "lnE", "ln_h", "tr_p", "tr_ev",
    "a_mm", "a_nmm", "a_qev", "a_kev", "a_vev", "a_nev", "a_sq", "a_kst", "a_vst", "a_nst",
    "cc", "b_nld", "nm", "pe_s", "pe_p", "pe_o", "pe_e", "pe_g", "pe_sk", "pe_pts", "rcp",
    "c_x", "c_xf", "ag3w", "ag3r", "cA", "cB", "cC", "cD", "cE", "c_h",
    "c_trp", "c_trev", "bnd", "bnd5",
    "d_mix", "d_rv", "d_kmm", "d_krel", "d_kact", "d_rmm", "d_rev", "d_vmm", "d_vev",
    "d_oev", "y_st", "dbgs", "s_x1", "a_kst1", "a_vst1", "kA", "kA1", "kB", "kB1", "y_st1", "a_qm", "c_x1",
    "start", "fin",
]
BIG = 1 << 30


def build(dbg=False, stop_after=None):
    nc = bass.Bass("TRN2", target_bir_lowering=False)

    def noldw():
        il = nc.cur_bb.bb.instructions
        assert type(il[-1]).__name__ == "InstMatmult"
        il[-1] = dataclasses.replace(il[-1], ldweights=False)

    x_sh = nc.dram_tensor("x_sh", [T, D], F32, kind="ExternalInput")
    wqT = nc.dram_tensor("wqT", [D, D], F32, kind="ExternalInput")
    wkT = nc.dram_tensor("wkT", [D, D], F32, kind="ExternalInput")
    wvT = nc.dram_tensor("wvT", [D, D], F32, kind="ExternalInput")
    wrT = nc.dram_tensor("wrT", [D, D], F32, kind="ExternalInput")
    wfkT = nc.dram_tensor("wfkT", [D, D], F32, kind="ExternalInput")
    wfvT = nc.dram_tensor("wfvT", [D, D], F32, kind="ExternalInput")
    bq_d = nc.dram_tensor("bq", [128, DC], F32, kind="ExternalInput")
    bk_d = nc.dram_tensor("bk", [128, DC], F32, kind="ExternalInput")
    bv_d = nc.dram_tensor("bv", [1, D], F32, kind="ExternalInput")
    mur_d = nc.dram_tensor("mur", [128, DC], F32, kind="ExternalInput")
    muk_d = nc.dram_tensor("muk", [128, DC], F32, kind="ExternalInput")
    ident_d = nc.dram_tensor("ident", [128, 128], F32, kind="ExternalInput")
    bsel_d = nc.dram_tensor("bsel", [NCORES, 1], F32, kind="ExternalInput")
    y = nc.dram_tensor("y", [T, D], F32, kind="ExternalOutput")

    k_ag_in = nc.dram_tensor("k_ag_in", [4, DC, 128, 512], BF16, kind="Internal")
    k_ag_o = [nc.dram_tensor(f"k_ag_o{j}", [NCORES, DC, 128, 512], BF16, kind="Internal",
                             addr_space="Shared") for j in range(4)]
    v_ag_in = nc.dram_tensor("v_ag_in", [T, D], BF16, kind="Internal")
    v_ag_o = [nc.dram_tensor(f"v_ag_o{j}", [NCORES * 512, D], BF16, kind="Internal",
                             addr_space="Shared") for j in range(4)]
    n_ag_in = nc.dram_tensor("n_ag_in", [1, T], BF16, kind="Internal")
    n_ag_out = nc.dram_tensor("n_ag_out", [128, 128], BF16, kind="Internal")
    xb_ag_in = nc.dram_tensor("xb_ag_in", [1, D], F32, kind="Internal")
    xb_ag_out = nc.dram_tensor("xb_ag_out", [NCORES, D], F32, kind="Internal")
    if dbg:
        dbg_oacc = nc.dram_tensor("dbg_oacc", [128, J * D], F32, kind="ExternalOutput")
        dbg_rl = nc.dram_tensor("dbg_rl", [128, 16], F32, kind="ExternalOutput")
        dbg_negm = nc.dram_tensor("dbg_negm", [128, 1], F32, kind="ExternalOutput")
        dbg_h2 = nc.dram_tensor("dbg_h2", [128, DC * (T + 1)], U16, kind="ExternalOutput")
        dbg_xp = nc.dram_tensor("dbg_xp", [128, J * D], F32, kind="ExternalOutput")
        dbg_misc = nc.dram_tensor("dbg_misc", [1, 8], F32, kind="ExternalOutput")
        dbg_qn = nc.dram_tensor("dbg_qn", [1, T], U16, kind="ExternalOutput")
        dbg_kn = nc.dram_tensor("dbg_kn", [1, T], U16, kind="ExternalOutput")
        dbg_nrm = nc.dram_tensor("dbg_nrm", [128, 128], U16, kind="ExternalOutput")
        dbg_kag = nc.dram_tensor("dbg_kag", [128, 512], U16, kind="ExternalOutput")
        dbg_vag = nc.dram_tensor("dbg_vag", [128, 1024], U16, kind="ExternalOutput")
        dbg_h1 = nc.dram_tensor("dbg_h1", [128, DC * T], U16, kind="ExternalOutput")
        dbg_q = nc.dram_tensor("dbg_q", [128, DC * T], U16, kind="ExternalOutput")
        dbg_hbf = nc.dram_tensor("dbg_hbf", [128, 2 * D], U16, kind="ExternalOutput")

    RG = [list(range(NCORES))]

    root = ExitStack()
    E = root.enter_context
    ident_b = E(nc.sbuf_tensor("ident_b", [128, 128], BF16))
    ones_col = E(nc.sbuf_tensor("ones_col", [128, 1], BF16))
    ones_row = E(nc.sbuf_tensor("ones_row", [1, 128], BF16))
    ones32 = E(nc.sbuf_tensor("ones32", [128, 1], F32))
    bq_t = E(nc.sbuf_tensor("bq_t", [128, DC], F32))
    bk_t = E(nc.sbuf_tensor("bk_t", [128, DC], F32))
    mur_t = E(nc.sbuf_tensor("mur_t", [128, DC], F32))
    muk_t = E(nc.sbuf_tensor("muk_t", [128, DC], F32))
    bsel_t = E(nc.sbuf_tensor("bsel_t", [NCORES, 1], F32))
    neg_m = E(nc.sbuf_tensor("neg_m", [128, 1], F32))
    qmax2 = E(nc.sbuf_tensor("qmax2", [1, 1], F32))
    kmax2 = E(nc.sbuf_tensor("kmax2", [1, 1], F32))
    m2 = E(nc.sbuf_tensor("m2", [1, 1], F32))
    negm_s = E(nc.sbuf_tensor("negm_s", [1, 1], F32))
    negm_b = E(nc.sbuf_tensor("negm_b", [1, 1], BF16))
    nrm_all = E(nc.sbuf_tensor("nrm_all", [128, 128], BF16))
    knmax_c = E(nc.sbuf_tensor("knmax_c", [128, 1], BF16))
    recip_l = E(nc.sbuf_tensor("recip_l", [128, 16], F32))
    mneg = [E(nc.sbuf_tensor(f"mneg{i}", [128, 1], F32)) for i in range(4)]
    v_raw = [E(nc.sbuf_tensor(f"v_raw{i}", [128, 1], F32)) for i in range(4)]
    sq_in = [E(nc.sbuf_tensor(f"sq_in{i}", [128, 1], F32)) for i in range(4)]
    std_t = [E(nc.sbuf_tensor(f"std_t{i}", [128, 1], F32)) for i in range(4)]
    inv_t = [E(nc.sbuf_tensor(f"inv_t{i}", [128, 1], F32)) for i in range(4)]
    recip_g = E(nc.sbuf_tensor("recip_g", [128, 16], F32))
    negm_s2 = E(nc.sbuf_tensor("negm_s2", [1, 1], F32))
    bm2 = E(nc.sbuf_tensor("bm2", [1, 1], F32))
    btmpv = E(nc.sbuf_tensor("btmpv", [1, 1], F32))
    bnmi2 = E(nc.sbuf_tensor("bnmi2", [1, 1], F32))

    qctx = ExitStack()
    q_fm = qctx.enter_context(nc.sbuf_tensor("q_fm", [128, DC, T], BF16, side="right"))

    sems = {name: E(nc.semaphore(name)) for name in SEM_NAMES}
    S = type("S", (), sems)
    _nums = sorted(h.num for h in sems.values())
    assert _nums[-1] - _nums[0] == len(_nums) - 1, _nums
    sem_lo, sem_hi = _nums[0], _nums[-1] + 1

    ctxA = ExitStack()
    EA = ctxA.enter_context
    wq_sb = EA(nc.sbuf_tensor("wq_sb", [128, DC, D], BF16))
    wk_sb = EA(nc.sbuf_tensor("wk_sb", [128, DC, D], BF16))
    wv_sb = EA(nc.sbuf_tensor("wv_sb", [128, DC, D], BF16))
    h1_fm = EA(nc.sbuf_tensor("h1_fm", [128, DC, T], BF16))
    xstag = EA(nc.sbuf_tensor("xstag", [128, 2, D], F32))
    t_stag = EA(nc.sbuf_tensor("t_stag", [128, 2, D], F32))
    tsq_st = EA(nc.sbuf_tensor("tsq_st", [128, D], BF16))
    h_bf = EA(nc.sbuf_tensor("h_bf", [128, 2, D], BF16))
    kstag = EA(nc.sbuf_tensor("kstag", [128, 2, T], BF16))
    vstag = EA(nc.sbuf_tensor("vstag", [128, 2, D], BF16))
    sq_st = EA(nc.sbuf_tensor("sq_st", [128, 2, T], BF16))
    qn_sb = EA(nc.sbuf_tensor("qn_sb", [1, T], BF16))
    kn_sb = EA(nc.sbuf_tensor("kn_sb", [1, T], BF16))
    bv_row = EA(nc.sbuf_tensor("bv_row", [1, D], BF16))

    with ExitStack() as pctx, nc.Block() as blk:
        trp = [pctx.enter_context(nc.psum_tensor(f"trpA{i}", [128, 128], BF16)) for i in range(2)]

        @blk.sync
        def _(sync):
            sync.wait_ge(S.start, BIG)
            for j in range(J):
                if j >= 2:
                    sync.wait_ge(S.s_xfree, j - 1)
                sync.dma_start(out=xstag[:, j % 2, :], in_=x_sh[j * 128:(j + 1) * 128, :]).then_inc(
                    S.s_x if j % 2 == 0 else S.s_x1, 16)
                if j == 1:
                    sync.dma_start(out=bq_t[:, :], in_=bq_d[:, :]).then_inc(S.s_w, 16)
                    sync.dma_start(out=bk_t[:, :], in_=bk_d[:, :]).then_inc(S.s_w, 16)
                    sync.dma_start(out=mur_t[:, :], in_=mur_d[:, :]).then_inc(S.s_w, 16)
                    sync.dma_start(out=muk_t[:, :], in_=muk_d[:, :]).then_inc(S.s_w, 16)
                    sync.dma_start(out=bsel_t[:, :], in_=bsel_d[:, :]).then_inc(S.s_w, 16)

        @blk.gpsimd
        def _(gpsimd):
            gpsimd.sem_clear(range(sem_lo, sem_hi))
            gpsimd.sem_inc(S.start, BIG)
            gpsimd.dma_start(out=ident_b[:, :], in_=ident_d[:, :]).then_inc(S.init, 16)
            gpsimd.dma_start(out=bv_row[:, :], in_=bv_d[:, :]).then_inc(S.init, 16)
            gpsimd.memset(ones_col[:, :], 1.0)
            gpsimd.memset(ones_row[:, :], 1.0).then_inc(S.init, 1)
            gpsimd.dma_start(out=wq_sb.ap(),
                             in_=wqT.ap().rearrange("(c p) n -> p c n", p=128)).then_inc(S.init, 16)
            gpsimd.dma_start(out=wk_sb.ap(),
                             in_=wkT.ap().rearrange("(c p) n -> p c n", p=128)).then_inc(S.init, 16)
            gpsimd.dma_start(out=wv_sb.ap(),
                             in_=wvT.ap().rearrange("(c p) n -> p c n", p=128)).then_inc(S.init, 16)

        @blk.vector
        def _(vector):
            vector.wait_ge(S.start, BIG)

            def lnA(j):
                vector.wait_ge(S.s_x if j % 2 == 0 else S.s_x1, 16 * (j // 2 + 1))
                if j >= 4:
                    vector.wait_ge(S.s_xfree, j - 3)
                vector.reduce_sum(mneg[j % 4][:, :], xstag[:, j % 2, :], axis=AX,
                                  negate=True).then_inc(S.lnA, 1)

            def evacs(j):
                for dc in range(DC):
                    cnt = j * DC + dc
                    vector.wait_ge(S.tr_p, cnt + 1)
                    vector.tensor_copy(h1_fm[:, dc, j * 128:(j + 1) * 128],
                                       trp[cnt % 2][:, :]).then_inc(S.tr_ev, 1)

            lnA(0)
            lnA(1)
            for j in range(J):
                if j + 2 < J:
                    lnA(j + 2)
                vector.wait_ge(S.lnB, j + 1)
                if j >= 4:
                    vector.wait_ge(S.lnD, j - 3)
                vector.tensor_scalar(sq_in[j % 4][:, :], v_raw[j % 4][:, :], 1.0 / D,
                                     D * D * EPS, ALU.mult, ALU.add).then_inc(S.lnC, 1)
                vector.wait_ge(S.lnD, j + 1)
                if j >= 4:
                    vector.wait_ge(S.ln_h, j - 3)
                vector.reciprocal(inv_t[j % 4][:, :], std_t[j % 4][:, :]).then_inc(S.lnE, 1)
                if j >= 1:
                    evacs(j - 1)
            evacs(J - 1)

        @blk.scalar
        def _(scalar):
            scalar.wait_ge(S.start, BIG)

            def lnBC(j):
                scalar.wait_ge(S.lnA, j + 1)
                scalar.activation(t_stag[:, j % 2, :], xstag[:, j % 2, :], AF.Identity,
                                  bias=mneg[j % 4][:, :], scale=float(D)).then_inc(S.s_xfree, 1)
                if j >= 4:
                    scalar.wait_ge(S.lnC, j - 3)
                scalar.activation(tsq_st[:, :], t_stag[:, j % 2, :], AF.Square,
                                  accum_out=v_raw[j % 4][:, :]).then_inc(S.lnB, 1)

            lnBC(0)
            for j in range(J):
                if j + 1 < J:
                    lnBC(j + 1)
                scalar.wait_ge(S.lnC, j + 1)
                if j >= 4:
                    scalar.wait_ge(S.lnE, j - 3)
                scalar.activation(std_t[j % 4][:, :], sq_in[j % 4][:, :], AF.Sqrt).then_inc(S.lnD, 1)
                scalar.wait_ge(S.lnE, j + 1)
                if j >= 2:
                    scalar.wait_ge(S.tr_p, 8 * (j - 1))
                scalar.activation(h_bf[:, j % 2, :], t_stag[:, j % 2, :], AF.Copy,
                                  scale=inv_t[j % 4][:, :]).then_inc(S.ln_h, 1)

        @blk.tensor
        def _(tensor):
            tensor.wait_ge(S.start, BIG)
            tensor.wait_ge(S.init, 33)
            for j in range(J):
                tensor.wait_ge(S.ln_h, j + 1)
                for dc in range(DC):
                    cnt = j * DC + dc
                    if cnt >= 2:
                        tensor.wait_ge(S.tr_ev, cnt - 1)
                    tensor.transpose(trp[cnt % 2][:, :], h_bf[:, j % 2, dc * 128:(dc + 1) * 128],
                                     ident_b[:, :]).then_inc(S.tr_p, 1)

    with ExitStack() as pctx, nc.Block() as blk:
        kq_ps = [pctx.enter_context(nc.psum_tensor(f"kqA{i}", [128, 512], F32)) for i in range(3)]
        vv_ps = [pctx.enter_context(nc.psum_tensor(f"vvA{i}", [128, D], F32)) for i in range(2)]

        def _nrm_mms(tensor, c, off):
            tensor.wait_ge(S.a_sq, off + c + 1)
            if off == DC and c == 0:
                tensor.wait_ge(S.a_nst, 1)
                tensor.wait_ge(S.a_vev, 16)
            for b in range(NB):
                mm = tensor.matmul(vv_ps[b // 2][0:1, (b % 2) * 512:(b % 2 + 1) * 512],
                                   ones_col[:, :],
                                   sq_st[:, c % 2, b * 512:(b + 1) * 512],
                                   start=(c == 0), stop=(c == DC - 1))
                if b == NB - 1:
                    mm.then_inc(S.a_nmm, 1)

        @blk.tensor
        def _(tensor):
            tensor.wait_ge(S.tr_ev, J * DC)
            tensor.wait_ge(S.init, 81)
            for c in range(DC):
                for b in range(NB):
                    n = c * NB + b
                    ps = kq_ps[n % 3]
                    if n >= 3:
                        tensor.wait_ge(S.a_kev, n - 2)
                    for dc in range(DC):
                        mm = tensor.matmul(ps[:, :], wk_sb[:, dc, c * 128:(c + 1) * 128],
                                           h1_fm[:, dc, b * 512:(b + 1) * 512],
                                           start=(dc == 0), stop=(dc == DC - 1))
                        if dc == DC - 1:
                            mm.then_inc(S.a_mm, 1)
                if c >= 1:
                    _nrm_mms(tensor, c - 1, 0)
            _nrm_mms(tensor, DC - 1, 0)
            for t in range(J):
                ps = vv_ps[t % 2]
                if t == 0:
                    tensor.wait_ge(S.a_nst, 1)
                if t >= 2:
                    tensor.wait_ge(S.a_vev, t - 1)
                for dc in range(DC):
                    for half in range(2):
                        tensor.matmul(ps[:, half * 512:(half + 1) * 512],
                                      h1_fm[:, dc, t * 128:(t + 1) * 128],
                                      wv_sb[:, dc, half * 512:(half + 1) * 512],
                                      start=(dc == 0), stop=False)
                for half in range(2):
                    mm = tensor.matmul(ps[:, half * 512:(half + 1) * 512], ones_row[:, :],
                                       bv_row[:, half * 512:(half + 1) * 512], start=False, stop=True)
                    if half == 1:
                        mm.then_inc(S.a_mm, 1)
            for c in range(DC):
                for b in range(NB):
                    n = c * NB + b
                    ps = kq_ps[n % 3]
                    if n == 0:
                        tensor.wait_ge(S.a_kev, 31)
                    elif n == 1:
                        tensor.wait_ge(S.a_kev, 32)
                    elif n >= 3:
                        tensor.wait_ge(S.a_qev, n - 2)
                    for dc in range(DC):
                        mm = tensor.matmul(ps[:, :], wq_sb[:, dc, c * 128:(c + 1) * 128],
                                           h1_fm[:, dc, b * 512:(b + 1) * 512],
                                           start=(dc == 0), stop=(dc == DC - 1))
                        if dc == DC - 1:
                            mm.then_inc(S.a_mm, 1)
                if c >= 1:
                    _nrm_mms(tensor, c - 1, DC)
            _nrm_mms(tensor, DC - 1, DC)

        @blk.scalar
        def _(scalar):
            scalar.wait_ge(S.s_w, 80)
            for c in range(DC):
                if c >= 2:
                    scalar.wait_ge(S.a_kst if c % 2 == 0 else S.a_kst1, 64 * (c // 2))
                for b in range(NB):
                    n = c * NB + b
                    scalar.wait_ge(S.a_mm, n + 1)
                    scalar.activation(kstag[:, c % 2, b * 512:(b + 1) * 512], kq_ps[n % 3][:, :],
                                      AF.Identity, bias=bk_t[:, c:c + 1]).then_inc(S.a_kev, 1)
                if c >= 2:
                    scalar.wait_ge(S.a_nmm, c - 1)
                scalar.activation(sq_st[:, c % 2, :], kstag[:, c % 2, :], AF.Square).then_inc(S.a_sq, 1)
            scalar.wait_ge(S.a_nmm, DC)
            scalar.activation(kn_sb[0:1, 0:1024], vv_ps[0][0:1, 0:1024], AF.Copy)
            scalar.activation(kn_sb[0:1, 1024:2048], vv_ps[1][0:1, 0:1024], AF.Copy).then_inc(S.a_nst, 1)
            for c in range(DC):
                for b in range(NB):
                    n = 48 + c * NB + b
                    scalar.wait_ge(S.a_mm, n + 1)
                    scalar.activation(q_fm[:, c, b * 512:(b + 1) * 512], kq_ps[(c * NB + b) % 3][:, :],
                                      AF.Identity, bias=bq_t[:, c:c + 1]).then_inc(S.a_qev, 1)
                scalar.wait_ge(S.a_nmm, 7 + c)
                scalar.activation(sq_st[:, c % 2, :], q_fm[:, c, :], AF.Square).then_inc(S.a_sq, 1)
            scalar.wait_ge(S.a_nmm, 2 * DC)
            scalar.activation(qn_sb[0:1, 0:1024], vv_ps[0][0:1, 0:1024], AF.Copy)
            scalar.activation(qn_sb[0:1, 1024:2048], vv_ps[1][0:1, 0:1024], AF.Copy).then_inc(S.a_nev, 1)

        @blk.vector
        def _(vector):
            for t in range(J):
                n = 32 + t
                if t >= 2:
                    vector.wait_ge(S.a_vst if t % 2 == 0 else S.a_vst1, 16 * (t // 2))
                vector.wait_ge(S.a_mm, n + 1)
                vector.tensor_copy(vstag[:, t % 2, :], vv_ps[t % 2][:, :]).then_inc(S.a_vev, 1)
            vector.wait_ge(S.a_nev, 1)
            vector.reduce_max(qmax2[:, :], qn_sb[:, :], axis=AX).then_inc(S.a_qm, 1)

        @blk.sync
        def _(sync):
            for c in range(DC):
                sync.wait_ge(S.a_kev, NB * (c + 1))
                for gl in range(4):
                    sync.dma_start(out=k_ag_in[gl, c, :, :],
                                   in_=kstag[:, c % 2, gl * 512:(gl + 1) * 512]).then_inc(
                                       S.a_kst if c % 2 == 0 else S.a_kst1, 16)
            for t in range(J):
                sync.wait_ge(S.a_vev, t + 1)
                sync.dma_start(out=v_ag_in[t * 128:(t + 1) * 128, :], in_=vstag[:, t % 2, :]).then_inc(
                    S.a_vst if t % 2 == 0 else S.a_vst1, 16)
            sync.wait_ge(S.a_nst, 1)
            sync.dma_start(out=n_ag_in[:, :], in_=kn_sb[:, :]).then_inc(S.a_nst, 16)

        @blk.gpsimd
        def _(gpsimd):
            gpsimd.wait_ge(S.a_kst, 16 * 16)
            gpsimd.wait_ge(S.a_kst1, 16 * 16)
            gpsimd.collective_compute("AllGather", ALU.bypass, replica_groups=RG,
                                      ins=[k_ag_in.ap()[0:1, :, :, :].opt()],
                                      outs=[k_ag_o[0].ap().opt()]).then_inc(S.cc, 1)
            gpsimd.wait_ge(S.a_nst, 17)
            gpsimd.collective_compute("AllGather", ALU.bypass, replica_groups=RG,
                                      ins=[n_ag_in.ap().opt()], outs=[n_ag_out.ap().opt()]).then_inc(S.cc, 1)
            for j in range(4):
                gpsimd.wait_ge(S.a_vst, 16 * (2 * j + 2))
                gpsimd.wait_ge(S.a_vst1, 16 * (2 * j + 2))
                gpsimd.collective_compute("AllGather", ALU.bypass, replica_groups=RG,
                                          ins=[v_ag_in.ap()[j * 512:(j + 1) * 512, :].opt()],
                                          outs=[v_ag_o[j].ap().opt()]).then_inc(S.cc, 1)
                if j < 3:
                    gpsimd.collective_compute("AllGather", ALU.bypass, replica_groups=RG,
                                              ins=[k_ag_in.ap()[j + 1:j + 2, :, :, :].opt()],
                                              outs=[k_ag_o[j + 1].ap().opt()]).then_inc(S.cc, 1)
            gpsimd.wait_ge(S.cc, 2)
            gpsimd.dma_start(out=nrm_all[:, :], in_=n_ag_out[:, :]).then_inc(S.b_nld, 16)

    ctxA.close()
    if stop_after == "A":
        root.close()
        return nc
    o_acc = E(nc.sbuf_tensor("o_acc", [128, J, D], F32))

    ctxB = ExitStack()
    EB = ctxB.enter_context
    vbuf = EB(nc.sbuf_tensor("vbuf", [128, 2, KC, 1024], BF16))
    kbuf = EB(nc.sbuf_tensor("kbuf", [128, 2, DC, 512], BF16))
    pt = [EB(nc.sbuf_tensor(f"pt{i}", [128, KC, T], BF16)) for i in range(2)]
    gsum = EB(nc.sbuf_tensor("gsum", [128, T], F32))

    with ExitStack() as pctx, nc.Block() as blk:
        s_ps = [pctx.enter_context(nc.psum_tensor(f"sB{i}", [128, 512], F32)) for i in range(2)]
        o_ps = [pctx.enter_context(nc.psum_tensor(f"oB{i}", [128, D], F32)) for i in range(2)]
        l_ps = pctx.enter_context(nc.psum_tensor("l_ps", [128, 512], F32))
        nm_ps = pctx.enter_context(nc.psum_tensor("nmB", [128, 512], F32))

        @blk.gpsimd
        def _(gpsimd):
            gpsimd.memset(ones32[:, :], 1.0).then_inc(S.init, 1)
            gpsimd.wait_ge(S.nm, 3)
            gpsimd.tensor_mul(m2[:, :], kmax2[:, :], qmax2[:, :]).then_inc(S.nm, 1)
            gpsimd.wait_ge(S.nm, 5)
            gpsimd.tensor_scalar_mul(negm_s2[:, :], negm_s[:, :], -SCALE).then_inc(S.nm, 1)
            gpsimd.wait_ge(S.rcp, 1)
            gpsimd.tensor_copy(recip_g[:, :], recip_l[:, :]).then_inc(S.rcp, 1)

        @blk.vector
        def _(vector):
            vector.wait_ge(S.b_nld, 16)
            vector.reduce_max(knmax_c[:, :], nrm_all[:, :], axis=AX).then_inc(S.nm, 1)
            vector.wait_ge(S.nm, 2)
            vector.reduce_max(kmax2[:, :], nm_ps[0:1, 0:64].bitcast(BF16), axis=AX).then_inc(S.nm, 1)
            vector.wait_ge(S.nm, 8)
            vector.tensor_copy(neg_m[:, :], nm_ps[:, 128:129]).then_inc(S.nm, 1)
            n = 0
            for g in range(NG):
                vector.wait_ge(S.pe_p, 16 * (g + 1))
                ptg2 = pt[g % 2]
                if g == 0:
                    vector.tensor_tensor(gsum[:, :], ptg2[:, 0, :], ptg2[:, 1, :], ALU.add)
                else:
                    vector.tensor_tensor(gsum[:, :], gsum[:, :], ptg2[:, 0, :], ALU.add)
                    vector.tensor_tensor(gsum[:, :], gsum[:, :], ptg2[:, 1, :], ALU.add)
                vector.tensor_tensor(gsum[:, :], gsum[:, :], ptg2[:, 2, :], ALU.add)
                vector.tensor_tensor(gsum[:, :], gsum[:, :], ptg2[:, 3, :], ALU.add).then_inc(S.pe_pts, 1)
                for j in range(J):
                    vector.wait_ge(S.pe_o, n + 1)
                    if g == 0:
                        vector.tensor_copy(o_acc[:, j, :], o_ps[n % 2][:, :]).then_inc(S.pe_e, 1)
                    else:
                        vector.tensor_tensor(o_acc[:, j, :], o_acc[:, j, :], o_ps[n % 2][:, :],
                                             ALU.add).then_inc(S.pe_e, 1)
                    n += 1
            vector.wait_ge(S.pe_g, NG + 1)
            vector.reciprocal(recip_l[:, :], l_ps[:, 0:16]).then_inc(S.rcp, 1)

        @blk.scalar
        def _(scalar):
            scalar.wait_ge(S.nm, 4)
            scalar.activation(negm_s[:, :], m2[:, :], AF.Sqrt).then_inc(S.nm, 1)
            scalar.wait_ge(S.nm, 6)
            scalar.activation(negm_b[:, :], negm_s2[:, :], AF.Copy).then_inc(S.nm, 1)
            n = 0
            scalar.wait_ge(S.a_nst, 17)
            scalar.wait_ge(S.a_qm, 1)
            for g in range(NG):
                if g == 0:
                    scalar.wait_ge(S.nm, 9)
                for kc in range(KC):
                    for b in range(NB):
                        scalar.wait_ge(S.pe_s, n + 1)
                        scalar.activation(pt[g % 2][:, kc, b * 512:(b + 1) * 512], s_ps[n % 2][:, :],
                                          AF.Exp, bias=neg_m[:, :], scale=SCALE).then_inc(S.pe_p, 1)
                        n += 1

        @blk.tensor
        def _(tensor):
            tensor.wait_ge(S.nm, 1)
            tensor.transpose(nm_ps[0:1, 0:64].bitcast(BF16), knmax_c[:, :], ident_b[:, :]).then_inc(S.nm, 1)
            tensor.wait_ge(S.nm, 7)
            tensor.matmul(nm_ps[:, 128:129], ones_row[:, :], negm_b[:, :], start=True, stop=True).then_inc(S.nm, 1)
            tensor.wait_ge(S.a_qev, 32)
            tensor.wait_ge(S.init, 82)
            n_s = 0
            n_o = 0
            for g in range(NG + 1):
                if g < NG:
                    tensor.wait_ge(S.kA if g % 2 == 0 else S.kA1, 16 * (g // 2 + 1))
                    for kc in range(KC):
                        for b in range(NB):
                            ps = s_ps[n_s % 2]
                            if n_s >= 2:
                                tensor.wait_ge(S.pe_p, n_s - 1)
                            for dc in range(DC):
                                mm = tensor.matmul(ps[:, :], kbuf[:, g % 2, dc, kc * 128:(kc + 1) * 128],
                                                   q_fm[:, dc, b * 512:(b + 1) * 512],
                                                   start=(dc == 0), stop=(dc == DC - 1))
                                if dc == DC - 1:
                                    mm.then_inc(S.pe_s, 1)
                            n_s += 1
                    tensor.nop().then_inc(S.pe_sk, 1)
                if g >= 1:
                    gp = g - 1
                    tensor.wait_ge(S.pe_p, gp * 16 + 16)
                    tensor.wait_ge(S.kB if gp % 2 == 0 else S.kB1, 16 * (gp // 2 + 1))
                    ptg = pt[gp % 2]
                    for j in range(J):
                        op = o_ps[n_o % 2]
                        if n_o >= 2:
                            tensor.wait_ge(S.pe_e, n_o - 1)
                        for kc in range(KC):
                            stat = ptg[:, kc, j * 128:(j + 1) * 128]
                            tensor.matmul(op[:, 0:512], stat, vbuf[:, gp % 2, kc, 0:512],
                                          start=(kc == 0), stop=(kc == KC - 1))
                            mm = tensor.matmul(op[:, 512:1024], stat, vbuf[:, gp % 2, kc, 512:1024],
                                               start=(kc == 0), stop=(kc == KC - 1))
                            if kc == KC - 1:
                                mm.then_inc(S.pe_o, 1)
                            noldw()
                        n_o += 1
                    tensor.nop().then_inc(S.pe_g, 1)
            tensor.wait_ge(S.pe_pts, NG)
            for j in range(J):
                mm = tensor.matmul(l_ps[:, j:j + 1], gsum[:, j * 128:(j + 1) * 128],
                                   ones32[:, :], start=(j == 0), stop=(j == J - 1))
            mm.then_inc(S.pe_g, 1)

        @blk.sync
        def _(sync):
            for gi in range(NG):
                j, r = gi // 8, gi % 8
                ka = S.kA if gi % 2 == 0 else S.kA1
                kb = S.kB if gi % 2 == 0 else S.kB1
                if gi >= 2:
                    sync.wait_ge(S.pe_sk, gi - 1)
                if gi == 0:
                    sync.wait_ge(S.a_vst, 16 * 8)
                    sync.wait_ge(S.a_vst1, 16 * 8)
                sync.wait_ge(S.cc, 1 if j == 0 else 2 * j + 2)
                sync.dma_start(out=kbuf.ap()[:, gi % 2:gi % 2 + 1, :, :],
                               in_=k_ag_o[j].ap()[r:r + 1, :, :, :].rearrange(
                                   "o c p t -> p o c t")).then_inc(ka, 16)
                if gi >= 2:
                    sync.wait_ge(S.pe_g, gi - 1)
                if gi == 0:
                    sync.wait_ge(S.a_mm, 80)
                sync.wait_ge(S.cc, 2 * j + 3)
                sync.dma_start(out=vbuf.ap()[:, gi % 2, :, :],
                               in_=v_ag_o[j].ap()[r * 512:(r + 1) * 512, :].rearrange(
                                   "(kc p) d -> p kc d", p=128)).then_inc(kb, 16)

        if dbg:
            @blk.scalar
            def _(scalar):
                scalar.wait_ge(S.rcp, 1)
                scalar.dma_start(out=dbg_rl[:, :], in_=recip_l[:, :]).then_inc(S.dbgs, 16)
                scalar.dma_start(out=dbg_negm[:, :], in_=neg_m[:, :]).then_inc(S.dbgs, 16)
                scalar.dma_start(out=dbg_oacc[:, :], in_=o_acc.ap().rearrange("p j d -> p (j d)")).then_inc(S.dbgs, 16)
                scalar.dma_start(out=dbg_misc[:, 0:1], in_=qmax2[:, :]).then_inc(S.dbgs, 16)
                scalar.dma_start(out=dbg_misc[:, 1:2], in_=kmax2[:, :]).then_inc(S.dbgs, 16)
                scalar.dma_start(out=dbg_misc[:, 2:3], in_=m2[:, :]).then_inc(S.dbgs, 16)
                scalar.dma_start(out=dbg_nrm[:, :], in_=nrm_all.ap().bitcast(U16)).then_inc(S.dbgs, 16)
                scalar.dma_start(out=dbg_kag[:, :],
                                 in_=k_ag_o[0].ap()[0, 0, :, :].bitcast(U16)).then_inc(S.dbgs, 16)
                scalar.dma_start(out=dbg_vag[:, :],
                                 in_=v_ag_o[0].ap()[0:128, :].bitcast(U16)).then_inc(S.dbgs, 16)

    ctxB.close()
    qctx.close()
    if stop_after == "B":
        root.close()
        return nc
    wr_sb = E(nc.sbuf_tensor("wr_sb", [128, DC, D], BF16))
    wfk_sb = E(nc.sbuf_tensor("wfk_sb", [128, DC, D], BF16))
    wfv_sb = E(nc.sbuf_tensor("wfv_sb", [128, DC, D], BF16))
    h2_fm = E(nc.sbuf_tensor("h2_fm", [128, DC, T + 1], BF16))
    cctx = ExitStack()
    EC = cctx.enter_context
    xstag2 = EC(nc.sbuf_tensor("xstag2", [128, 2, D], F32))
    h2tok = EC(nc.sbuf_tensor("h2tok", [128, 2, D], BF16))
    tsq2 = EC(nc.sbuf_tensor("tsq2", [128, 2, D], F32))
    cdust = EC(nc.sbuf_tensor("cdust", [128, D], BF16))
    ag3_sb = EC(nc.sbuf_tensor("ag3_sb", [NCORES, D], F32))
    xbnd_f = EC(nc.sbuf_tensor("xbnd_f", [128, DC], F32))
    xbnd_b = EC(nc.sbuf_tensor("xbnd_b", [128, DC], BF16))
    sqb = EC(nc.sbuf_tensor("sqb", [128, DC], BF16))
    bm = EC(nc.sbuf_tensor("bm", [1, 1], F32))
    bs2 = EC(nc.sbuf_tensor("bs2", [1, 1], F32))
    bvar = EC(nc.sbuf_tensor("bvar", [1, 1], F32))
    bstd = EC(nc.sbuf_tensor("bstd", [1, 1], F32))
    binv = EC(nc.sbuf_tensor("binv", [1, 1], F32))
    bnegmi = EC(nc.sbuf_tensor("bnegmi", [1, 1], F32))
    binv_b = EC(nc.sbuf_tensor("binv_b", [1, 1], BF16))
    bnmi_b = EC(nc.sbuf_tensor("bnmi_b", [1, 1], BF16))
    binv_t = EC(nc.sbuf_tensor("binv_t", [128, 1], F32))
    bnmi_t = EC(nc.sbuf_tensor("bnmi_t", [128, 1], F32))

    JORD = [15] + list(range(15))

    with ExitStack() as pctx, nc.Block() as blk:
        trp2 = [pctx.enter_context(nc.psum_tensor(f"trpC{i}", [128, 128], BF16)) for i in range(2)]
        bnd_ps = pctx.enter_context(nc.psum_tensor("bndC", [128, 512], F32))
        st_ps = pctx.enter_context(nc.psum_tensor("stC", [1, 512], F32))

        @blk.sync
        def _(sync):
            for i, j in enumerate(JORD):
                if i >= 2:
                    sync.wait_ge(S.c_xf, i - 1)
                sync.dma_start(out=xstag2[:, i % 2, :],
                               in_=x_sh[j * 128:(j + 1) * 128, :]).then_inc(
                                   S.c_x if i % 2 == 0 else S.c_x1, 16)
            sync.wait_ge(S.cc, 10)
            sync.dma_start(out=ag3_sb[:, :], in_=xb_ag_out[:, :]).then_inc(S.ag3r, 16)

        @blk.gpsimd
        def _(gpsimd):
            gpsimd.wait_ge(S.c_xf, 1)
            gpsimd.dma_start(out=xb_ag_in[:, :], in_=o_acc.ap()[127:128, 15, :]).then_inc(S.ag3w, 16)
            gpsimd.wait_ge(S.ag3w, 16)
            gpsimd.collective_compute("AllGather", ALU.bypass, replica_groups=RG,
                                      ins=[xb_ag_in.ap().opt()], outs=[xb_ag_out.ap().opt()]).then_inc(S.cc, 1)
            gpsimd.wait_ge(S.pe_g, NG)
            gpsimd.dma_start(out=wfk_sb.ap(),
                             in_=wfkT.ap().rearrange("(c p) n -> p c n", p=128)).then_inc(S.init, 16)
            gpsimd.dma_start(out=wr_sb.ap(),
                             in_=wrT.ap().rearrange("(c p) n -> p c n", p=128)).then_inc(S.init, 16)
            gpsimd.dma_start(out=wfv_sb.ap(),
                             in_=wfvT.ap().rearrange("(c p) n -> p c n", p=128)).then_inc(S.init, 16)
            gpsimd.wait_ge(S.bnd, 4)
            gpsimd.tensor_scalar_mul(bm[:, :], bm[:, :], 1.0 / D).then_inc(S.bnd, 1)
            gpsimd.wait_ge(S.bnd, 6)
            gpsimd.tensor_scalar(btmpv[:, :], bs2[:, :], 1.0 / D, EPS, ALU.mult, ALU.add).then_inc(S.bnd, 1)
            gpsimd.wait_ge(S.bnd, 10)
            gpsimd.tensor_mul(bnegmi[:, :], bm[:, :], binv[:, :]).then_inc(S.bnd, 1)

        @blk.vector
        def _(vector):
            def cXA(i):
                j = JORD[i]
                vector.wait_ge(S.c_x if i % 2 == 0 else S.c_x1, 16 * (i // 2 + 1))
                if i == 0:
                    vector.wait_ge(S.rcp, 2)
                    if dbg:
                        vector.wait_ge(S.dbgs, 16 * 14)
                vector.scalar_tensor_tensor(o_acc[:, j, :], o_acc[:, j, :], recip_g[:, j:j + 1],
                                            xstag2[:, i % 2, :], ALU.mult, ALU.add).then_inc(S.c_xf, 1)
                if i >= 4:
                    vector.wait_ge(S.cB, i - 3)
                vector.reduce_sum(mneg[i % 4][:, :], o_acc[:, j, :], axis=AX,
                                  negate=True).then_inc(S.cA, 1)

            def cevacs(i):
                j = JORD[i]
                for dc in range(DC):
                    cnt = i * DC + dc
                    vector.wait_ge(S.c_trp, cnt + 1)
                    vector.tensor_copy(h2_fm[:, dc, 1 + j * 128:1 + (j + 1) * 128],
                                       trp2[cnt % 2][:, :]).then_inc(S.c_trev, 1)

            cXA(0)
            cXA(1)
            for i in range(J):
                if i + 2 < J:
                    cXA(i + 2)
                vector.wait_ge(S.cB, i + 1)
                if i >= 4:
                    vector.wait_ge(S.cD, i - 3)
                vector.tensor_scalar(sq_in[i % 4][:, :], v_raw[i % 4][:, :], 1.0 / D,
                                     D * D * EPS, ALU.mult, ALU.add).then_inc(S.cC, 1)
                vector.wait_ge(S.cD, i + 1)
                if i >= 4:
                    vector.wait_ge(S.c_h, i - 3)
                vector.reciprocal(inv_t[i % 4][:, :], std_t[i % 4][:, :]).then_inc(S.cE, 1)
                if i >= 1:
                    cevacs(i - 1)
            cevacs(J - 1)
            vector.wait_ge(S.bnd, 3)
            vector.reduce_sum(bm[:, :], st_ps[0:1, 0:DC], axis=AX)
            vector.reduce_sum(bs2[:, :], st_ps[0:1, 256:256 + DC], axis=AX).then_inc(S.bnd, 1)
            vector.wait_ge(S.bnd, 5)
            vector.tensor_mul(bm2[:, :], bm[:, :], bm[:, :]).then_inc(S.bnd, 1)
            vector.wait_ge(S.bnd, 7)
            vector.tensor_sub(bvar[:, :], btmpv[:, :], bm2[:, :]).then_inc(S.bnd, 1)
            vector.wait_ge(S.bnd, 9)
            vector.reciprocal(binv[:, :], bstd[:, :]).then_inc(S.bnd, 1)
            vector.wait_ge(S.bnd, 11)
            vector.tensor_scalar_mul(bnmi2[:, :], bnegmi[:, :], -1.0).then_inc(S.bnd, 1)
            vector.wait_ge(S.bnd, 14)
            vector.tensor_copy(binv_t[:, :], bnd_ps[:, 64:65])
            vector.tensor_copy(bnmi_t[:, :], bnd_ps[:, 65:66]).then_inc(S.bnd, 1)

        @blk.scalar
        def _(scalar):
            scalar.wait_ge(S.start, BIG)

            def cBC(i):
                j = JORD[i]
                scalar.wait_ge(S.cA, i + 1)
                scalar.activation(tsq2[:, i % 2, :], o_acc[:, j, :], AF.Identity,
                                  bias=mneg[i % 4][:, :], scale=float(D))
                if i >= 4:
                    scalar.wait_ge(S.cC, i - 3)
                scalar.activation(cdust[:, :], tsq2[:, i % 2, :], AF.Square,
                                  accum_out=v_raw[i % 4][:, :]).then_inc(S.cB, 1)

            cBC(0)
            for i in range(J):
                if i + 1 < J:
                    cBC(i + 1)
                scalar.wait_ge(S.cC, i + 1)
                if i >= 4:
                    scalar.wait_ge(S.cE, i - 3)
                scalar.activation(std_t[i % 4][:, :], sq_in[i % 4][:, :], AF.Sqrt).then_inc(S.cD, 1)
                scalar.wait_ge(S.cE, i + 1)
                if i >= 2:
                    scalar.wait_ge(S.c_trp, 8 * (i - 1))
                scalar.activation(h2tok[:, i % 2, :], tsq2[:, i % 2, :], AF.Copy,
                                  scale=inv_t[i % 4][:, :]).then_inc(S.c_h, 1)
            scalar.wait_ge(S.bnd, 1)
            scalar.activation(xbnd_f[:, :], bnd_ps[:, 0:DC], AF.Copy)
            scalar.activation(xbnd_b[:, :], bnd_ps[:, 0:DC], AF.Copy)
            scalar.activation(sqb[:, :], bnd_ps[:, 0:DC], AF.Square).then_inc(S.bnd, 1)
            scalar.wait_ge(S.bnd, 8)
            scalar.activation(bstd[:, :], bvar[:, :], AF.Sqrt).then_inc(S.bnd, 1)
            scalar.wait_ge(S.bnd, 12)
            scalar.activation(binv_b[:, :], binv[:, :], AF.Copy)
            scalar.activation(bnmi_b[:, :], bnmi2[:, :], AF.Copy).then_inc(S.bnd, 1)
            scalar.wait_ge(S.bnd, 15)
            for dc in range(DC):
                scalar.activation(h2_fm[:, dc, 0:1], xbnd_f[:, dc:dc + 1], AF.Identity,
                                  scale=binv_t[:, :], bias=bnmi_t[:, :]).then_inc(S.bnd5, 1)

        @blk.tensor
        def _(tensor):
            for i, j in enumerate(JORD):
                tensor.wait_ge(S.c_h, i + 1)
                for dc in range(DC):
                    cnt = i * DC + dc
                    if cnt >= 2:
                        tensor.wait_ge(S.c_trev, cnt - 1)
                    tensor.transpose(trp2[cnt % 2][:, :], h2tok[:, i % 2, dc * 128:(dc + 1) * 128],
                                     ident_b[:, :]).then_inc(S.c_trp, 1)
            tensor.wait_ge(S.ag3r, 16)
            tensor.wait_ge(S.s_w, 80)
            for dc in range(DC):
                mm = tensor.matmul(bnd_ps[:, dc:dc + 1], ag3_sb[:, dc * 128:(dc + 1) * 128], bsel_t[:, :],
                                   start=(dc == 0), stop=(dc == DC - 1))
                if dc == DC - 1:
                    mm.then_inc(S.bnd, 1)
            tensor.wait_ge(S.bnd, 2)
            tensor.matmul(st_ps[0:1, 0:DC], ones_col[:, :], xbnd_b[:, :], start=True, stop=False)
            tensor.matmul(st_ps[0:1, 256:256 + DC], ones_col[:, :], sqb[:, :],
                          start=False, stop=True).then_inc(S.bnd, 1)
            tensor.wait_ge(S.bnd, 13)
            tensor.matmul(bnd_ps[:, 64:65], ones_row[:, :], binv_b[:, :], start=True, stop=False)
            tensor.matmul(bnd_ps[:, 65:66], ones_row[:, :], bnmi_b[:, :],
                          start=False, stop=True).then_inc(S.bnd, 1)

        if dbg:
            @blk.scalar
            def _(scalar):
                scalar.wait_ge(S.bnd5, DC)
                scalar.wait_ge(S.c_trev, 128)
                scalar.dma_start(out=dbg_h2[:, :],
                                 in_=h2_fm.ap().rearrange("p c n -> p (c n)").bitcast(U16)).then_inc(S.dbgs, 16)
                scalar.wait_ge(S.c_xf, 16)
                scalar.dma_start(out=dbg_xp[:, :], in_=o_acc.ap().rearrange("p j d -> p (j d)")).then_inc(S.dbgs, 16)

    cctx.close()
    if stop_after == "C":
        root.close()
        return nc
    xr_b = E(nc.sbuf_tensor("xr_b", [128, 2, DC, 512], BF16))
    xk_b = E(nc.sbuf_tensor("xk_b", [128, 2, DC, 512], BF16))
    kact_b = E(nc.sbuf_tensor("kact_b", [128, DC, 512], BF16))
    d_stag = E(nc.sbuf_tensor("d_stag", [128, 512], BF16))
    krelu = E(nc.sbuf_tensor("krelu", [128, 512], F32))
    r_tok = E(nc.sbuf_tensor("r_tok", [128, 2, D], F32))

    with ExitStack() as pctx, nc.Block() as blk:
        kfm_ps = [pctx.enter_context(nc.psum_tensor(f"kfmD{i}", [128, 512], F32)) for i in range(2)]
        r_ps = pctx.enter_context(nc.psum_tensor("rD", [128, D], F32))
        v_ps = pctx.enter_context(nc.psum_tensor("vD", [128, D], F32))

        @blk.vector
        def _(vector):
            vector.wait_ge(S.c_trev, 128)
            vector.wait_ge(S.bnd5, DC)
            vector.wait_ge(S.s_w, 80)
            def mix(b):
                if b >= 2:
                    vector.wait_ge(S.d_rv, b - 1)
                for dc in range(DC):
                    lo = b * 512
                    vector.tensor_sub(d_stag[:, :], h2_fm[:, dc, lo + 1:lo + 513], h2_fm[:, dc, lo:lo + 512])
                    vector.scalar_tensor_tensor(xr_b[:, b % 2, dc, :], d_stag[:, :], mur_t[:, dc:dc + 1],
                                                h2_fm[:, dc, lo:lo + 512], ALU.mult, ALU.add)
                    mx = vector.scalar_tensor_tensor(xk_b[:, b % 2, dc, :], d_stag[:, :], muk_t[:, dc:dc + 1],
                                                     h2_fm[:, dc, lo:lo + 512], ALU.mult, ALU.add)
                    if dc == DC - 1:
                        mx.then_inc(S.d_mix, 1)
            mix(0)
            mix(1)
            for b in range(NB):
                for jj in range(NB):
                    j = b * NB + jj
                    vector.wait_ge(S.d_vmm, j + 1)
                    vector.wait_ge(S.d_rev, j + 1)
                    vector.tensor_mul(r_tok[:, j % 2, :], r_tok[:, j % 2, :], v_ps[:, :]).then_inc(S.d_vev, 1)
                    vector.tensor_add(r_tok[:, j % 2, :], r_tok[:, j % 2, :], o_acc[:, j, :]).then_inc(S.d_oev, 1)
                if b + 2 < NB:
                    mix(b + 2)
            vector.sem_inc(S.fin, 1)

        @blk.tensor
        def _(tensor):
            tensor.wait_ge(S.init, 82 + 48)
            for b in range(NB):
                tensor.wait_ge(S.d_mix, b + 1)
                for c in range(DC):
                    rot = b * DC + c
                    ps = kfm_ps[rot % 2]
                    if rot >= 2:
                        tensor.wait_ge(S.d_krel, rot - 1)
                    for dc in range(DC):
                        mm = tensor.matmul(ps[:, :], wfk_sb[:, dc, c * 128:(c + 1) * 128],
                                           xk_b[:, b % 2, dc, :],
                                           start=(dc == 0), stop=(dc == DC - 1))
                        if dc == DC - 1:
                            mm.then_inc(S.d_kmm, 1)
                for jj in range(NB):
                    j = b * NB + jj
                    if j >= 1:
                        tensor.wait_ge(S.d_rev, j)
                    for dc in range(DC):
                        for half in range(2):
                            mm = tensor.matmul(r_ps[:, half * 512:(half + 1) * 512],
                                               xr_b[:, b % 2, dc, jj * 128:(jj + 1) * 128],
                                               wr_sb[:, dc, half * 512:(half + 1) * 512],
                                               start=(dc == 0), stop=(dc == DC - 1))
                            if dc == DC - 1 and half == 1:
                                mm.then_inc(S.d_rmm, 1)
                    if j >= 1:
                        tensor.wait_ge(S.d_vev, j)
                    tensor.wait_ge(S.d_kact, DC * (b + 1))
                    for dc in range(DC):
                        for half in range(2):
                            mm = tensor.matmul(v_ps[:, half * 512:(half + 1) * 512],
                                               kact_b[:, dc, jj * 128:(jj + 1) * 128],
                                               wfv_sb[:, dc, half * 512:(half + 1) * 512],
                                               start=(dc == 0), stop=(dc == DC - 1))
                            if dc == DC - 1 and half == 1:
                                mm.then_inc(S.d_vmm, 1)
                                if jj == NB - 1:
                                    tensor.sem_inc(S.d_rv, 1)
            tensor.sem_inc(S.fin, 1)

        @blk.scalar
        def _(scalar):
            for b in range(NB):
                for c in range(DC):
                    rot = b * DC + c
                    scalar.wait_ge(S.d_kmm, rot + 1)
                    scalar.activation(krelu[:, :], kfm_ps[rot % 2][:, :], AF.Relu).then_inc(S.d_krel, 1)
                    scalar.activation(kact_b[:, c, :], krelu[:, :], AF.Square).then_inc(S.d_kact, 1)
                for jj in range(NB):
                    j = b * NB + jj
                    scalar.wait_ge(S.d_rmm, j + 1)
                    if j >= 2:
                        scalar.wait_ge(S.y_st if j % 2 == 0 else S.y_st1, 16 * (j // 2))
                    scalar.activation(r_tok[:, j % 2, :], r_ps[:, :], AF.Sigmoid).then_inc(S.d_rev, 1)
            scalar.sem_inc(S.fin, 1)

        @blk.sync
        def _(sync):
            for j in range(J):
                sync.wait_ge(S.d_oev, j + 1)
                sync.dma_start(out=y[j * 128:(j + 1) * 128, :],
                               in_=r_tok[:, j % 2, :]).then_inc(
                                   S.y_st if j % 2 == 0 else S.y_st1, 16)
            sync.wait_ge(S.y_st, 16 * (J // 2))
            sync.wait_ge(S.y_st1, 16 * (J // 2))
            if dbg:
                sync.wait_ge(S.dbgs, 16 * 16)

    root.close()
    return nc


_NC_CACHE = {}


def _get_nc(dbg=False):
    if dbg not in _NC_CACHE:
        _NC_CACHE[dbg] = build(dbg)
    return _NC_CACHE[dbg]


def make_in_maps(inputs):
    asn = lambda a: np.ascontiguousarray(np.asarray(a, dtype=np.float32))
    x = asn(inputs["L0_attn_x"]).reshape(NCORES * T, D)
    common = {
        "wqT": asn(np.asarray(inputs["L0_attn_wq"]).T),
        "wkT": asn(np.asarray(inputs["L0_attn_wk"]).T),
        "wvT": asn(np.asarray(inputs["L0_attn_wv"]).T),
        "wrT": asn(np.asarray(inputs["L0_ffn_W_r"]).T),
        "wfkT": asn(np.asarray(inputs["L0_ffn_W_k"]).T),
        "wfvT": asn(np.asarray(inputs["L0_ffn_W_v"]).T),
        "bq": np.ascontiguousarray(asn(inputs["L0_attn_bq"]).reshape(DC, 128).T),
        "bk": np.ascontiguousarray(asn(inputs["L0_attn_bk"]).reshape(DC, 128).T),
        "bv": asn(inputs["L0_attn_bv"]).reshape(1, D),
        "mur": np.ascontiguousarray(asn(inputs["L0_ffn_mu_r"]).reshape(DC, 128).T),
        "muk": np.ascontiguousarray(asn(inputs["L0_ffn_mu_k"]).reshape(DC, 128).T),
        "ident": np.eye(128, dtype=np.float32),
    }
    in_maps = []
    for i in range(NCORES):
        sel = np.zeros((NCORES, 1), np.float32)
        if i > 0:
            sel[i - 1, 0] = 1.0
        im = dict(common)
        im["x_sh"] = np.ascontiguousarray(x[i * T:(i + 1) * T, :])
        im["bsel"] = sel
        in_maps.append(im)
    return in_maps


def kernel(**inputs):
    nc = _get_nc(False)
    in_maps = make_in_maps(inputs)
    res = run_bass_kernel_spmd(nc, in_maps, core_ids=list(range(NCORES)))
    out = np.concatenate([r["y"] for r in res.results], axis=0)
    return out.reshape(1, NCORES * T, D)


# version 10
# speedup vs baseline: 1.3640x; 1.0048x over previous
import dataclasses
import numpy as np
from contextlib import ExitStack

import concourse.bass as bass
import concourse.mybir as mybir
from concourse.bass_utils import run_bass_kernel_spmd

F32 = mybir.dt.float32
BF16 = mybir.dt.bfloat16
U16 = mybir.dt.uint16
AF = mybir.ActivationFunctionType
ALU = mybir.AluOpType
AX = mybir.AxisListType.X

NCORES = 8
D = 1024
DC = 8
T = 2048
J = 16
NG = 32
KC = 4
NB = 4
SCALE = 1.0 / 32.0
EPS = 1e-5

SEM_NAMES = [
    "init", "s_w", "s_x", "s_xfree", "lnA", "lnB", "lnC", "lnD", "lnE", "ln_h", "tr_p", "tr_ev",
    "a_mm", "a_nmm", "a_qev", "a_kev", "a_vev", "a_nev", "a_sq", "a_kst", "a_vst", "a_nst",
    "cc", "b_nld", "nm", "pe_s", "pe_p", "pe_o", "pe_e", "pe_g", "pe_sk", "pe_pts", "rcp",
    "c_x", "c_xf", "ag3w", "ag3r", "cA", "cB", "cC", "cD", "cE", "c_h",
    "c_trp", "c_trev", "bnd", "bnd5",
    "d_mix", "d_rv", "d_kmm", "d_krel", "d_kact", "d_rmm", "d_rev", "d_vmm", "d_vev",
    "d_oev", "y_st", "dbgs", "s_x1", "a_kst1", "a_vst1", "kA", "kA1", "kB", "kB1", "y_st1", "a_qm", "c_x1",
    "start", "fin",
]
BIG = 1 << 30


def build(dbg=False, stop_after=None):
    nc = bass.Bass("TRN2", target_bir_lowering=False)

    def noldw():
        il = nc.cur_bb.bb.instructions
        assert type(il[-1]).__name__ == "InstMatmult"
        il[-1] = dataclasses.replace(il[-1], ldweights=False)

    x_sh = nc.dram_tensor("x_sh", [T, D], F32, kind="ExternalInput")
    wqT = nc.dram_tensor("wqT", [D, D], F32, kind="ExternalInput")
    wkT = nc.dram_tensor("wkT", [D, D], F32, kind="ExternalInput")
    wvT = nc.dram_tensor("wvT", [D, D], F32, kind="ExternalInput")
    wrT = nc.dram_tensor("wrT", [D, D], F32, kind="ExternalInput")
    wfkT = nc.dram_tensor("wfkT", [D, D], F32, kind="ExternalInput")
    wfvT = nc.dram_tensor("wfvT", [D, D], F32, kind="ExternalInput")
    bq_d = nc.dram_tensor("bq", [128, DC], F32, kind="ExternalInput")
    bk_d = nc.dram_tensor("bk", [128, DC], F32, kind="ExternalInput")
    bv_d = nc.dram_tensor("bv", [1, D], F32, kind="ExternalInput")
    mur_d = nc.dram_tensor("mur", [128, DC], F32, kind="ExternalInput")
    muk_d = nc.dram_tensor("muk", [128, DC], F32, kind="ExternalInput")
    ident_d = nc.dram_tensor("ident", [128, 128], F32, kind="ExternalInput")
    bsel_d = nc.dram_tensor("bsel", [NCORES, 1], F32, kind="ExternalInput")
    y = nc.dram_tensor("y", [T, D], F32, kind="ExternalOutput")

    k_ag_in = nc.dram_tensor("k_ag_in", [4, DC, 128, 512], BF16, kind="Internal")
    k_ag_o = [nc.dram_tensor(f"k_ag_o{j}", [NCORES, DC, 128, 512], BF16, kind="Internal",
                             addr_space="Shared") for j in range(4)]
    v_ag_in = nc.dram_tensor("v_ag_in", [T, D], BF16, kind="Internal")
    v_ag_o = [nc.dram_tensor(f"v_ag_o{j}", [NCORES * 512, D], BF16, kind="Internal",
                             addr_space="Shared") for j in range(4)]
    n_ag_in = nc.dram_tensor("n_ag_in", [1, T], BF16, kind="Internal")
    n_ag_out = nc.dram_tensor("n_ag_out", [128, 128], BF16, kind="Internal")
    xb_ag_in = nc.dram_tensor("xb_ag_in", [1, D], F32, kind="Internal")
    xb_ag_out = nc.dram_tensor("xb_ag_out", [NCORES, D], F32, kind="Internal")
    if dbg:
        dbg_oacc = nc.dram_tensor("dbg_oacc", [128, J * D], F32, kind="ExternalOutput")
        dbg_rl = nc.dram_tensor("dbg_rl", [128, 16], F32, kind="ExternalOutput")
        dbg_negm = nc.dram_tensor("dbg_negm", [128, 1], F32, kind="ExternalOutput")
        dbg_h2 = nc.dram_tensor("dbg_h2", [128, DC * (T + 1)], U16, kind="ExternalOutput")
        dbg_xp = nc.dram_tensor("dbg_xp", [128, J * D], F32, kind="ExternalOutput")
        dbg_misc = nc.dram_tensor("dbg_misc", [1, 8], F32, kind="ExternalOutput")
        dbg_qn = nc.dram_tensor("dbg_qn", [1, T], U16, kind="ExternalOutput")
        dbg_kn = nc.dram_tensor("dbg_kn", [1, T], U16, kind="ExternalOutput")
        dbg_nrm = nc.dram_tensor("dbg_nrm", [128, 128], U16, kind="ExternalOutput")
        dbg_kag = nc.dram_tensor("dbg_kag", [128, 512], U16, kind="ExternalOutput")
        dbg_vag = nc.dram_tensor("dbg_vag", [128, 1024], U16, kind="ExternalOutput")
        dbg_h1 = nc.dram_tensor("dbg_h1", [128, DC * T], U16, kind="ExternalOutput")
        dbg_q = nc.dram_tensor("dbg_q", [128, DC * T], U16, kind="ExternalOutput")
        dbg_hbf = nc.dram_tensor("dbg_hbf", [128, 2 * D], U16, kind="ExternalOutput")

    RG = [list(range(NCORES))]

    root = ExitStack()
    E = root.enter_context
    ident_b = E(nc.sbuf_tensor("ident_b", [128, 128], BF16))
    ones_col = E(nc.sbuf_tensor("ones_col", [128, 1], BF16))
    ones_row = E(nc.sbuf_tensor("ones_row", [1, 128], BF16))
    ones32 = E(nc.sbuf_tensor("ones32", [128, 1], F32))
    epsb = E(nc.sbuf_tensor("epsb", [128, 1], F32))
    bq_t = E(nc.sbuf_tensor("bq_t", [128, DC], F32))
    bk_t = E(nc.sbuf_tensor("bk_t", [128, DC], F32))
    mur_t = E(nc.sbuf_tensor("mur_t", [128, DC], F32))
    muk_t = E(nc.sbuf_tensor("muk_t", [128, DC], F32))
    bsel_t = E(nc.sbuf_tensor("bsel_t", [NCORES, 1], F32))
    neg_m = E(nc.sbuf_tensor("neg_m", [128, 1], F32))
    qmax2 = E(nc.sbuf_tensor("qmax2", [1, 1], F32))
    kmax2 = E(nc.sbuf_tensor("kmax2", [1, 1], F32))
    m2 = E(nc.sbuf_tensor("m2", [1, 1], F32))
    negm_s = E(nc.sbuf_tensor("negm_s", [1, 1], F32))
    negm_b = E(nc.sbuf_tensor("negm_b", [1, 1], BF16))
    nrm_all = E(nc.sbuf_tensor("nrm_all", [128, 128], BF16))
    knmax_c = E(nc.sbuf_tensor("knmax_c", [128, 1], BF16))
    recip_l = E(nc.sbuf_tensor("recip_l", [128, 16], F32))
    mneg = [E(nc.sbuf_tensor(f"mneg{i}", [128, 1], F32)) for i in range(4)]
    v_raw = [E(nc.sbuf_tensor(f"v_raw{i}", [128, 1], F32)) for i in range(4)]
    sq_in = [E(nc.sbuf_tensor(f"sq_in{i}", [128, 1], F32)) for i in range(4)]
    std_t = [E(nc.sbuf_tensor(f"std_t{i}", [128, 1], F32)) for i in range(4)]
    inv_t = [E(nc.sbuf_tensor(f"inv_t{i}", [128, 1], F32)) for i in range(4)]
    recip_g = E(nc.sbuf_tensor("recip_g", [128, 16], F32))
    negm_s2 = E(nc.sbuf_tensor("negm_s2", [1, 1], F32))
    bm2 = E(nc.sbuf_tensor("bm2", [1, 1], F32))
    btmpv = E(nc.sbuf_tensor("btmpv", [1, 1], F32))
    bnmi2 = E(nc.sbuf_tensor("bnmi2", [1, 1], F32))

    qctx = ExitStack()
    q_fm = qctx.enter_context(nc.sbuf_tensor("q_fm", [128, DC, T], BF16, side="right"))

    sems = {name: E(nc.semaphore(name)) for name in SEM_NAMES}
    S = type("S", (), sems)
    _nums = sorted(h.num for h in sems.values())
    assert _nums[-1] - _nums[0] == len(_nums) - 1, _nums
    sem_lo, sem_hi = _nums[0], _nums[-1] + 1

    ctxA = ExitStack()
    EA = ctxA.enter_context
    wq_sb = EA(nc.sbuf_tensor("wq_sb", [128, DC, D], BF16))
    wk_sb = EA(nc.sbuf_tensor("wk_sb", [128, DC, D], BF16))
    wv_sb = EA(nc.sbuf_tensor("wv_sb", [128, DC, D], BF16))
    h1_fm = EA(nc.sbuf_tensor("h1_fm", [128, DC, T], BF16))
    xstag = EA(nc.sbuf_tensor("xstag", [128, 2, D], F32))
    t_stag = EA(nc.sbuf_tensor("t_stag", [128, 2, D], F32))
    tsq_st = EA(nc.sbuf_tensor("tsq_st", [128, D], BF16))
    h_bf = EA(nc.sbuf_tensor("h_bf", [128, 2, D], BF16))
    kstag = EA(nc.sbuf_tensor("kstag", [128, 2, T], BF16))
    vstag = EA(nc.sbuf_tensor("vstag", [128, 2, D], BF16))
    sq_st = EA(nc.sbuf_tensor("sq_st", [128, 2, T], BF16))
    qn_sb = EA(nc.sbuf_tensor("qn_sb", [1, T], BF16))
    kn_sb = EA(nc.sbuf_tensor("kn_sb", [1, T], BF16))
    bv_row = EA(nc.sbuf_tensor("bv_row", [1, D], BF16))

    with ExitStack() as pctx, nc.Block() as blk:
        trp = [pctx.enter_context(nc.psum_tensor(f"trpA{i}", [128, 128], BF16)) for i in range(2)]

        @blk.sync
        def _(sync):
            sync.wait_ge(S.start, BIG)
            for j in range(J):
                if j >= 2:
                    sync.wait_ge(S.s_xfree, j - 1)
                sync.dma_start(out=xstag[:, j % 2, :], in_=x_sh[j * 128:(j + 1) * 128, :]).then_inc(
                    S.s_x if j % 2 == 0 else S.s_x1, 16)
                if j == 1:
                    sync.dma_start(out=bq_t[:, :], in_=bq_d[:, :]).then_inc(S.s_w, 16)
                    sync.dma_start(out=bk_t[:, :], in_=bk_d[:, :]).then_inc(S.s_w, 16)
                    sync.dma_start(out=mur_t[:, :], in_=mur_d[:, :]).then_inc(S.s_w, 16)
                    sync.dma_start(out=muk_t[:, :], in_=muk_d[:, :]).then_inc(S.s_w, 16)
                    sync.dma_start(out=bsel_t[:, :], in_=bsel_d[:, :]).then_inc(S.s_w, 16)

        @blk.gpsimd
        def _(gpsimd):
            gpsimd.sem_clear(range(sem_lo, sem_hi))
            gpsimd.sem_inc(S.start, BIG)
            gpsimd.dma_start(out=ident_b[:, :], in_=ident_d[:, :]).then_inc(S.init, 16)
            gpsimd.dma_start(out=bv_row[:, :], in_=bv_d[:, :]).then_inc(S.init, 16)
            gpsimd.memset(ones_col[:, :], 1.0)
            gpsimd.memset(epsb[:, :], float(D) * D * EPS)
            gpsimd.memset(ones_row[:, :], 1.0).then_inc(S.init, 1)
            gpsimd.dma_start(out=wq_sb.ap(),
                             in_=wqT.ap().rearrange("(c p) n -> p c n", p=128)).then_inc(S.init, 16)
            gpsimd.dma_start(out=wk_sb.ap(),
                             in_=wkT.ap().rearrange("(c p) n -> p c n", p=128)).then_inc(S.init, 16)
            gpsimd.dma_start(out=wv_sb.ap(),
                             in_=wvT.ap().rearrange("(c p) n -> p c n", p=128)).then_inc(S.init, 16)

        @blk.vector
        def _(vector):
            vector.wait_ge(S.start, BIG)

            def lnA(j):
                vector.wait_ge(S.s_x if j % 2 == 0 else S.s_x1, 16 * (j // 2 + 1))
                if j >= 4:
                    vector.wait_ge(S.s_xfree, j - 3)
                vector.reduce_sum(mneg[j % 4][:, :], xstag[:, j % 2, :], axis=AX,
                                  negate=True).then_inc(S.lnA, 1)

            def evacs(j):
                for dc in range(DC):
                    cnt = j * DC + dc
                    vector.wait_ge(S.tr_p, cnt + 1)
                    vector.tensor_copy(h1_fm[:, dc, j * 128:(j + 1) * 128],
                                       trp[cnt % 2][:, :]).then_inc(S.tr_ev, 1)

            lnA(0)
            lnA(1)
            for j in range(J):
                if j + 2 < J:
                    lnA(j + 2)
                vector.wait_ge(S.lnD, j + 1)
                if j >= 4:
                    vector.wait_ge(S.ln_h, j - 3)
                vector.reciprocal(inv_t[j % 4][:, :], std_t[j % 4][:, :]).then_inc(S.lnE, 1)
                if j >= 1:
                    evacs(j - 1)
            evacs(J - 1)

        @blk.scalar
        def _(scalar):
            scalar.wait_ge(S.start, BIG)
            scalar.wait_ge(S.init, 33)

            def lnBC(j):
                scalar.wait_ge(S.lnA, j + 1)
                scalar.activation(t_stag[:, j % 2, :], xstag[:, j % 2, :], AF.Identity,
                                  bias=mneg[j % 4][:, :], scale=float(D)).then_inc(S.s_xfree, 1)
                scalar.activation(tsq_st[:, :], t_stag[:, j % 2, :], AF.Square,
                                  accum_out=v_raw[j % 4][:, :]).then_inc(S.lnB, 1)

            lnBC(0)
            for j in range(J):
                if j + 1 < J:
                    lnBC(j + 1)
                if j >= 4:
                    scalar.wait_ge(S.lnE, j - 3)
                scalar.activation(std_t[j % 4][:, :], v_raw[j % 4][:, :], AF.Sqrt,
                                  scale=1.0 / D, bias=epsb[:, :]).then_inc(S.lnD, 1)
                scalar.wait_ge(S.lnE, j + 1)
                if j >= 2:
                    scalar.wait_ge(S.tr_p, 8 * (j - 1))
                scalar.activation(h_bf[:, j % 2, :], t_stag[:, j % 2, :], AF.Copy,
                                  scale=inv_t[j % 4][:, :]).then_inc(S.ln_h, 1)

        @blk.tensor
        def _(tensor):
            tensor.wait_ge(S.start, BIG)
            tensor.wait_ge(S.init, 33)
            for j in range(J):
                tensor.wait_ge(S.ln_h, j + 1)
                for dc in range(DC):
                    cnt = j * DC + dc
                    if cnt >= 2:
                        tensor.wait_ge(S.tr_ev, cnt - 1)
                    tensor.transpose(trp[cnt % 2][:, :], h_bf[:, j % 2, dc * 128:(dc + 1) * 128],
                                     ident_b[:, :]).then_inc(S.tr_p, 1)

    with ExitStack() as pctx, nc.Block() as blk:
        kq_ps = [pctx.enter_context(nc.psum_tensor(f"kqA{i}", [128, 512], F32)) for i in range(3)]
        vv_ps = [pctx.enter_context(nc.psum_tensor(f"vvA{i}", [128, D], F32)) for i in range(2)]

        def _nrm_mms(tensor, c, off):
            tensor.wait_ge(S.a_sq, off + c + 1)
            if off == DC and c == 0:
                tensor.wait_ge(S.a_nst, 1)
                tensor.wait_ge(S.a_vev, 16)
            for b in range(NB):
                mm = tensor.matmul(vv_ps[b // 2][0:1, (b % 2) * 512:(b % 2 + 1) * 512],
                                   ones_col[:, :],
                                   sq_st[:, c % 2, b * 512:(b + 1) * 512],
                                   start=(c == 0), stop=(c == DC - 1))
                if b == NB - 1:
                    mm.then_inc(S.a_nmm, 1)

        @blk.tensor
        def _(tensor):
            tensor.wait_ge(S.tr_ev, J * DC)
            tensor.wait_ge(S.init, 81)
            for c in range(DC):
                for b in range(NB):
                    n = c * NB + b
                    ps = kq_ps[n % 3]
                    if n >= 3:
                        tensor.wait_ge(S.a_kev, n - 2)
                    for dc in range(DC):
                        mm = tensor.matmul(ps[:, :], wk_sb[:, dc, c * 128:(c + 1) * 128],
                                           h1_fm[:, dc, b * 512:(b + 1) * 512],
                                           start=(dc == 0), stop=(dc == DC - 1))
                        if dc == DC - 1:
                            mm.then_inc(S.a_mm, 1)
                if c >= 1:
                    _nrm_mms(tensor, c - 1, 0)
            _nrm_mms(tensor, DC - 1, 0)
            for t in range(J):
                ps = vv_ps[t % 2]
                if t == 0:
                    tensor.wait_ge(S.a_nst, 1)
                if t >= 2:
                    tensor.wait_ge(S.a_vev, t - 1)
                for dc in range(DC):
                    for half in range(2):
                        tensor.matmul(ps[:, half * 512:(half + 1) * 512],
                                      h1_fm[:, dc, t * 128:(t + 1) * 128],
                                      wv_sb[:, dc, half * 512:(half + 1) * 512],
                                      start=(dc == 0), stop=False)
                for half in range(2):
                    mm = tensor.matmul(ps[:, half * 512:(half + 1) * 512], ones_row[:, :],
                                       bv_row[:, half * 512:(half + 1) * 512], start=False, stop=True)
                    if half == 1:
                        mm.then_inc(S.a_mm, 1)
            for c in range(DC):
                for b in range(NB):
                    n = c * NB + b
                    ps = kq_ps[n % 3]
                    if n == 0:
                        tensor.wait_ge(S.a_kev, 31)
                    elif n == 1:
                        tensor.wait_ge(S.a_kev, 32)
                    elif n >= 3:
                        tensor.wait_ge(S.a_qev, n - 2)
                    for dc in range(DC):
                        mm = tensor.matmul(ps[:, :], wq_sb[:, dc, c * 128:(c + 1) * 128],
                                           h1_fm[:, dc, b * 512:(b + 1) * 512],
                                           start=(dc == 0), stop=(dc == DC - 1))
                        if dc == DC - 1:
                            mm.then_inc(S.a_mm, 1)
                if c >= 1:
                    _nrm_mms(tensor, c - 1, DC)
            _nrm_mms(tensor, DC - 1, DC)

        @blk.scalar
        def _(scalar):
            scalar.wait_ge(S.s_w, 80)
            for c in range(DC):
                if c >= 2:
                    scalar.wait_ge(S.a_kst if c % 2 == 0 else S.a_kst1, 64 * (c // 2))
                for b in range(NB):
                    n = c * NB + b
                    scalar.wait_ge(S.a_mm, n + 1)
                    scalar.activation(kstag[:, c % 2, b * 512:(b + 1) * 512], kq_ps[n % 3][:, :],
                                      AF.Identity, bias=bk_t[:, c:c + 1]).then_inc(S.a_kev, 1)
                if c >= 2:
                    scalar.wait_ge(S.a_nmm, c - 1)
                scalar.activation(sq_st[:, c % 2, :], kstag[:, c % 2, :], AF.Square).then_inc(S.a_sq, 1)
            scalar.wait_ge(S.a_nmm, DC)
            scalar.activation(kn_sb[0:1, 0:1024], vv_ps[0][0:1, 0:1024], AF.Copy)
            scalar.activation(kn_sb[0:1, 1024:2048], vv_ps[1][0:1, 0:1024], AF.Copy).then_inc(S.a_nst, 1)
            for c in range(DC):
                for b in range(NB):
                    n = 48 + c * NB + b
                    scalar.wait_ge(S.a_mm, n + 1)
                    scalar.activation(q_fm[:, c, b * 512:(b + 1) * 512], kq_ps[(c * NB + b) % 3][:, :],
                                      AF.Identity, bias=bq_t[:, c:c + 1]).then_inc(S.a_qev, 1)
                scalar.wait_ge(S.a_nmm, 7 + c)
                scalar.activation(sq_st[:, c % 2, :], q_fm[:, c, :], AF.Square).then_inc(S.a_sq, 1)
            scalar.wait_ge(S.a_nmm, 2 * DC)
            scalar.activation(qn_sb[0:1, 0:1024], vv_ps[0][0:1, 0:1024], AF.Copy)
            scalar.activation(qn_sb[0:1, 1024:2048], vv_ps[1][0:1, 0:1024], AF.Copy).then_inc(S.a_nev, 1)

        @blk.vector
        def _(vector):
            for t in range(J):
                n = 32 + t
                if t >= 2:
                    vector.wait_ge(S.a_vst if t % 2 == 0 else S.a_vst1, 16 * (t // 2))
                vector.wait_ge(S.a_mm, n + 1)
                vector.tensor_copy(vstag[:, t % 2, :], vv_ps[t % 2][:, :]).then_inc(S.a_vev, 1)
            vector.wait_ge(S.a_nev, 1)
            vector.reduce_max(qmax2[:, :], qn_sb[:, :], axis=AX).then_inc(S.a_qm, 1)

        @blk.sync
        def _(sync):
            for c in range(DC):
                sync.wait_ge(S.a_kev, NB * (c + 1))
                for gl in range(4):
                    sync.dma_start(out=k_ag_in[gl, c, :, :],
                                   in_=kstag[:, c % 2, gl * 512:(gl + 1) * 512]).then_inc(
                                       S.a_kst if c % 2 == 0 else S.a_kst1, 16)
            for t in range(J):
                sync.wait_ge(S.a_vev, t + 1)
                sync.dma_start(out=v_ag_in[t * 128:(t + 1) * 128, :], in_=vstag[:, t % 2, :]).then_inc(
                    S.a_vst if t % 2 == 0 else S.a_vst1, 16)
            sync.wait_ge(S.a_nst, 1)
            sync.dma_start(out=n_ag_in[:, :], in_=kn_sb[:, :]).then_inc(S.a_nst, 16)

        @blk.gpsimd
        def _(gpsimd):
            gpsimd.wait_ge(S.a_kst, 16 * 16)
            gpsimd.wait_ge(S.a_kst1, 16 * 16)
            gpsimd.collective_compute("AllGather", ALU.bypass, replica_groups=RG,
                                      ins=[k_ag_in.ap()[0:1, :, :, :].opt()],
                                      outs=[k_ag_o[0].ap().opt()]).then_inc(S.cc, 1)
            gpsimd.wait_ge(S.a_nst, 17)
            gpsimd.collective_compute("AllGather", ALU.bypass, replica_groups=RG,
                                      ins=[n_ag_in.ap().opt()], outs=[n_ag_out.ap().opt()]).then_inc(S.cc, 1)
            for j in range(4):
                gpsimd.wait_ge(S.a_vst, 16 * (2 * j + 2))
                gpsimd.wait_ge(S.a_vst1, 16 * (2 * j + 2))
                gpsimd.collective_compute("AllGather", ALU.bypass, replica_groups=RG,
                                          ins=[v_ag_in.ap()[j * 512:(j + 1) * 512, :].opt()],
                                          outs=[v_ag_o[j].ap().opt()]).then_inc(S.cc, 1)
                if j < 3:
                    gpsimd.collective_compute("AllGather", ALU.bypass, replica_groups=RG,
                                              ins=[k_ag_in.ap()[j + 1:j + 2, :, :, :].opt()],
                                              outs=[k_ag_o[j + 1].ap().opt()]).then_inc(S.cc, 1)
            gpsimd.wait_ge(S.cc, 2)
            gpsimd.dma_start(out=nrm_all[:, :], in_=n_ag_out[:, :]).then_inc(S.b_nld, 16)

    ctxA.close()
    if stop_after == "A":
        root.close()
        return nc
    o_acc = E(nc.sbuf_tensor("o_acc", [128, J, D], F32))

    ctxB = ExitStack()
    EB = ctxB.enter_context
    vbuf = EB(nc.sbuf_tensor("vbuf", [128, 2, KC, 1024], BF16))
    kbuf = EB(nc.sbuf_tensor("kbuf", [128, 2, DC, 512], BF16))
    pt = [EB(nc.sbuf_tensor(f"pt{i}", [128, KC, T], BF16)) for i in range(2)]
    gsum = EB(nc.sbuf_tensor("gsum", [128, T], F32))

    with ExitStack() as pctx, nc.Block() as blk:
        s_ps = [pctx.enter_context(nc.psum_tensor(f"sB{i}", [128, 512], F32)) for i in range(2)]
        o_ps = [pctx.enter_context(nc.psum_tensor(f"oB{i}", [128, D], F32)) for i in range(2)]
        l_ps = pctx.enter_context(nc.psum_tensor("l_ps", [128, 512], F32))
        nm_ps = pctx.enter_context(nc.psum_tensor("nmB", [128, 512], F32))

        @blk.gpsimd
        def _(gpsimd):
            gpsimd.memset(ones32[:, :], 1.0).then_inc(S.init, 1)
            gpsimd.wait_ge(S.nm, 3)
            gpsimd.tensor_mul(m2[:, :], kmax2[:, :], qmax2[:, :]).then_inc(S.nm, 1)
            gpsimd.wait_ge(S.nm, 5)
            gpsimd.tensor_scalar_mul(negm_s2[:, :], negm_s[:, :], -SCALE).then_inc(S.nm, 1)
            gpsimd.wait_ge(S.rcp, 1)
            gpsimd.tensor_copy(recip_g[:, :], recip_l[:, :]).then_inc(S.rcp, 1)

        @blk.vector
        def _(vector):
            vector.wait_ge(S.b_nld, 16)
            vector.reduce_max(knmax_c[:, :], nrm_all[:, :], axis=AX).then_inc(S.nm, 1)
            vector.wait_ge(S.nm, 2)
            vector.reduce_max(kmax2[:, :], nm_ps[0:1, 0:64].bitcast(BF16), axis=AX).then_inc(S.nm, 1)
            vector.wait_ge(S.nm, 8)
            vector.tensor_copy(neg_m[:, :], nm_ps[:, 128:129]).then_inc(S.nm, 1)
            n = 0
            for g in range(NG):
                vector.wait_ge(S.pe_p, 16 * (g + 1))
                ptg2 = pt[g % 2]
                if g == 0:
                    vector.tensor_tensor(gsum[:, :], ptg2[:, 0, :], ptg2[:, 1, :], ALU.add)
                else:
                    vector.tensor_tensor(gsum[:, :], gsum[:, :], ptg2[:, 0, :], ALU.add)
                    vector.tensor_tensor(gsum[:, :], gsum[:, :], ptg2[:, 1, :], ALU.add)
                vector.tensor_tensor(gsum[:, :], gsum[:, :], ptg2[:, 2, :], ALU.add)
                vector.tensor_tensor(gsum[:, :], gsum[:, :], ptg2[:, 3, :], ALU.add).then_inc(S.pe_pts, 1)
                for j in range(J):
                    vector.wait_ge(S.pe_o, n + 1)
                    if g == 0:
                        vector.tensor_copy(o_acc[:, j, :], o_ps[n % 2][:, :]).then_inc(S.pe_e, 1)
                    else:
                        vector.tensor_tensor(o_acc[:, j, :], o_acc[:, j, :], o_ps[n % 2][:, :],
                                             ALU.add).then_inc(S.pe_e, 1)
                    n += 1
            vector.wait_ge(S.pe_g, NG + 1)
            vector.reciprocal(recip_l[:, :], l_ps[:, 0:16]).then_inc(S.rcp, 1)

        @blk.scalar
        def _(scalar):
            scalar.wait_ge(S.nm, 4)
            scalar.activation(negm_s[:, :], m2[:, :], AF.Sqrt).then_inc(S.nm, 1)
            scalar.wait_ge(S.nm, 6)
            scalar.activation(negm_b[:, :], negm_s2[:, :], AF.Copy).then_inc(S.nm, 1)
            n = 0
            scalar.wait_ge(S.a_nst, 17)
            scalar.wait_ge(S.a_qm, 1)
            for g in range(NG):
                if g == 0:
                    scalar.wait_ge(S.nm, 9)
                for kc in range(KC):
                    for b in range(NB):
                        scalar.wait_ge(S.pe_s, n + 1)
                        scalar.activation(pt[g % 2][:, kc, b * 512:(b + 1) * 512], s_ps[n % 2][:, :],
                                          AF.Exp, bias=neg_m[:, :], scale=SCALE).then_inc(S.pe_p, 1)
                        n += 1

        @blk.tensor
        def _(tensor):
            tensor.wait_ge(S.nm, 1)
            tensor.transpose(nm_ps[0:1, 0:64].bitcast(BF16), knmax_c[:, :], ident_b[:, :]).then_inc(S.nm, 1)
            tensor.wait_ge(S.nm, 7)
            tensor.matmul(nm_ps[:, 128:129], ones_row[:, :], negm_b[:, :], start=True, stop=True).then_inc(S.nm, 1)
            tensor.wait_ge(S.a_qev, 32)
            tensor.wait_ge(S.init, 82)
            n_s = 0
            n_o = 0
            for g in range(NG + 1):
                if g < NG:
                    tensor.wait_ge(S.kA if g % 2 == 0 else S.kA1, 16 * (g // 2 + 1))
                    for kc in range(KC):
                        for b in range(NB):
                            ps = s_ps[n_s % 2]
                            if n_s >= 2:
                                tensor.wait_ge(S.pe_p, n_s - 1)
                            for dc in range(DC):
                                mm = tensor.matmul(ps[:, :], kbuf[:, g % 2, dc, kc * 128:(kc + 1) * 128],
                                                   q_fm[:, dc, b * 512:(b + 1) * 512],
                                                   start=(dc == 0), stop=(dc == DC - 1))
                                if dc == DC - 1:
                                    mm.then_inc(S.pe_s, 1)
                            n_s += 1
                    tensor.nop().then_inc(S.pe_sk, 1)
                if g >= 1:
                    gp = g - 1
                    tensor.wait_ge(S.pe_p, gp * 16 + 16)
                    tensor.wait_ge(S.kB if gp % 2 == 0 else S.kB1, 16 * (gp // 2 + 1))
                    ptg = pt[gp % 2]
                    for j in range(J):
                        op = o_ps[n_o % 2]
                        if n_o >= 2:
                            tensor.wait_ge(S.pe_e, n_o - 1)
                        for kc in range(KC):
                            stat = ptg[:, kc, j * 128:(j + 1) * 128]
                            tensor.matmul(op[:, 0:512], stat, vbuf[:, gp % 2, kc, 0:512],
                                          start=(kc == 0), stop=(kc == KC - 1))
                            mm = tensor.matmul(op[:, 512:1024], stat, vbuf[:, gp % 2, kc, 512:1024],
                                               start=(kc == 0), stop=(kc == KC - 1))
                            if kc == KC - 1:
                                mm.then_inc(S.pe_o, 1)
                            noldw()
                        n_o += 1
                    tensor.nop().then_inc(S.pe_g, 1)
            tensor.wait_ge(S.pe_pts, NG)
            for j in range(J):
                mm = tensor.matmul(l_ps[:, j:j + 1], gsum[:, j * 128:(j + 1) * 128],
                                   ones32[:, :], start=(j == 0), stop=(j == J - 1))
            mm.then_inc(S.pe_g, 1)

        @blk.sync
        def _(sync):
            for gi in range(NG):
                j, r = gi // 8, gi % 8
                ka = S.kA if gi % 2 == 0 else S.kA1
                kb = S.kB if gi % 2 == 0 else S.kB1
                if gi >= 2:
                    sync.wait_ge(S.pe_sk, gi - 1)
                if gi == 0:
                    sync.wait_ge(S.a_vst, 16 * 8)
                    sync.wait_ge(S.a_vst1, 16 * 8)
                sync.wait_ge(S.cc, 1 if j == 0 else 2 * j + 2)
                sync.dma_start(out=kbuf.ap()[:, gi % 2:gi % 2 + 1, :, :],
                               in_=k_ag_o[j].ap()[r:r + 1, :, :, :].rearrange(
                                   "o c p t -> p o c t")).then_inc(ka, 16)
                if gi >= 2:
                    sync.wait_ge(S.pe_g, gi - 1)
                if gi == 0:
                    sync.wait_ge(S.a_mm, 80)
                sync.wait_ge(S.cc, 2 * j + 3)
                sync.dma_start(out=vbuf.ap()[:, gi % 2, :, :],
                               in_=v_ag_o[j].ap()[r * 512:(r + 1) * 512, :].rearrange(
                                   "(kc p) d -> p kc d", p=128)).then_inc(kb, 16)

        if dbg:
            @blk.scalar
            def _(scalar):
                scalar.wait_ge(S.rcp, 1)
                scalar.dma_start(out=dbg_rl[:, :], in_=recip_l[:, :]).then_inc(S.dbgs, 16)
                scalar.dma_start(out=dbg_negm[:, :], in_=neg_m[:, :]).then_inc(S.dbgs, 16)
                scalar.dma_start(out=dbg_oacc[:, :], in_=o_acc.ap().rearrange("p j d -> p (j d)")).then_inc(S.dbgs, 16)
                scalar.dma_start(out=dbg_misc[:, 0:1], in_=qmax2[:, :]).then_inc(S.dbgs, 16)
                scalar.dma_start(out=dbg_misc[:, 1:2], in_=kmax2[:, :]).then_inc(S.dbgs, 16)
                scalar.dma_start(out=dbg_misc[:, 2:3], in_=m2[:, :]).then_inc(S.dbgs, 16)
                scalar.dma_start(out=dbg_nrm[:, :], in_=nrm_all.ap().bitcast(U16)).then_inc(S.dbgs, 16)
                scalar.dma_start(out=dbg_kag[:, :],
                                 in_=k_ag_o[0].ap()[0, 0, :, :].bitcast(U16)).then_inc(S.dbgs, 16)
                scalar.dma_start(out=dbg_vag[:, :],
                                 in_=v_ag_o[0].ap()[0:128, :].bitcast(U16)).then_inc(S.dbgs, 16)

    ctxB.close()
    qctx.close()
    if stop_after == "B":
        root.close()
        return nc
    wr_sb = E(nc.sbuf_tensor("wr_sb", [128, DC, D], BF16))
    wfk_sb = E(nc.sbuf_tensor("wfk_sb", [128, DC, D], BF16))
    wfv_sb = E(nc.sbuf_tensor("wfv_sb", [128, DC, D], BF16))
    h2_fm = E(nc.sbuf_tensor("h2_fm", [128, DC, T + 1], BF16))
    cctx = ExitStack()
    EC = cctx.enter_context
    xstag2 = EC(nc.sbuf_tensor("xstag2", [128, 2, D], F32))
    h2tok = EC(nc.sbuf_tensor("h2tok", [128, 2, D], BF16))
    tsq2 = EC(nc.sbuf_tensor("tsq2", [128, 2, D], F32))
    cdust = EC(nc.sbuf_tensor("cdust", [128, D], BF16))
    ag3_sb = EC(nc.sbuf_tensor("ag3_sb", [NCORES, D], F32))
    xbnd_f = EC(nc.sbuf_tensor("xbnd_f", [128, DC], F32))
    xbnd_b = EC(nc.sbuf_tensor("xbnd_b", [128, DC], BF16))
    sqb = EC(nc.sbuf_tensor("sqb", [128, DC], BF16))
    bm = EC(nc.sbuf_tensor("bm", [1, 1], F32))
    bs2 = EC(nc.sbuf_tensor("bs2", [1, 1], F32))
    bvar = EC(nc.sbuf_tensor("bvar", [1, 1], F32))
    bstd = EC(nc.sbuf_tensor("bstd", [1, 1], F32))
    binv = EC(nc.sbuf_tensor("binv", [1, 1], F32))
    bnegmi = EC(nc.sbuf_tensor("bnegmi", [1, 1], F32))
    binv_b = EC(nc.sbuf_tensor("binv_b", [1, 1], BF16))
    bnmi_b = EC(nc.sbuf_tensor("bnmi_b", [1, 1], BF16))
    binv_t = EC(nc.sbuf_tensor("binv_t", [128, 1], F32))
    bnmi_t = EC(nc.sbuf_tensor("bnmi_t", [128, 1], F32))

    JORD = [15] + list(range(15))

    with ExitStack() as pctx, nc.Block() as blk:
        trp2 = [pctx.enter_context(nc.psum_tensor(f"trpC{i}", [128, 128], BF16)) for i in range(2)]
        bnd_ps = pctx.enter_context(nc.psum_tensor("bndC", [128, 512], F32))
        st_ps = pctx.enter_context(nc.psum_tensor("stC", [1, 512], F32))

        @blk.sync
        def _(sync):
            for i, j in enumerate(JORD):
                if i >= 2:
                    sync.wait_ge(S.c_xf, i - 1)
                sync.dma_start(out=xstag2[:, i % 2, :],
                               in_=x_sh[j * 128:(j + 1) * 128, :]).then_inc(
                                   S.c_x if i % 2 == 0 else S.c_x1, 16)
            sync.wait_ge(S.cc, 10)
            sync.dma_start(out=ag3_sb[:, :], in_=xb_ag_out[:, :]).then_inc(S.ag3r, 16)

        @blk.gpsimd
        def _(gpsimd):
            gpsimd.wait_ge(S.c_xf, 1)
            gpsimd.dma_start(out=xb_ag_in[:, :], in_=o_acc.ap()[127:128, 15, :]).then_inc(S.ag3w, 16)
            gpsimd.wait_ge(S.ag3w, 16)
            gpsimd.collective_compute("AllGather", ALU.bypass, replica_groups=RG,
                                      ins=[xb_ag_in.ap().opt()], outs=[xb_ag_out.ap().opt()]).then_inc(S.cc, 1)
            gpsimd.wait_ge(S.pe_g, NG)
            gpsimd.dma_start(out=wfk_sb.ap(),
                             in_=wfkT.ap().rearrange("(c p) n -> p c n", p=128)).then_inc(S.init, 16)
            gpsimd.dma_start(out=wr_sb.ap(),
                             in_=wrT.ap().rearrange("(c p) n -> p c n", p=128)).then_inc(S.init, 16)
            gpsimd.dma_start(out=wfv_sb.ap(),
                             in_=wfvT.ap().rearrange("(c p) n -> p c n", p=128)).then_inc(S.init, 16)
            gpsimd.wait_ge(S.bnd, 4)
            gpsimd.tensor_scalar_mul(bm[:, :], bm[:, :], 1.0 / D).then_inc(S.bnd, 1)
            gpsimd.wait_ge(S.bnd, 6)
            gpsimd.tensor_scalar(btmpv[:, :], bs2[:, :], 1.0 / D, EPS, ALU.mult, ALU.add).then_inc(S.bnd, 1)
            gpsimd.wait_ge(S.bnd, 10)
            gpsimd.tensor_mul(bnegmi[:, :], bm[:, :], binv[:, :]).then_inc(S.bnd, 1)

        @blk.vector
        def _(vector):
            def cXA(i):
                j = JORD[i]
                vector.wait_ge(S.c_x if i % 2 == 0 else S.c_x1, 16 * (i // 2 + 1))
                if i == 0:
                    vector.wait_ge(S.rcp, 2)
                    if dbg:
                        vector.wait_ge(S.dbgs, 16 * 14)
                vector.scalar_tensor_tensor(o_acc[:, j, :], o_acc[:, j, :], recip_g[:, j:j + 1],
                                            xstag2[:, i % 2, :], ALU.mult, ALU.add).then_inc(S.c_xf, 1)
                if i >= 4:
                    vector.wait_ge(S.cB, i - 3)
                vector.reduce_sum(mneg[i % 4][:, :], o_acc[:, j, :], axis=AX,
                                  negate=True).then_inc(S.cA, 1)

            def cevacs(i):
                j = JORD[i]
                for dc in range(DC):
                    cnt = i * DC + dc
                    vector.wait_ge(S.c_trp, cnt + 1)
                    vector.tensor_copy(h2_fm[:, dc, 1 + j * 128:1 + (j + 1) * 128],
                                       trp2[cnt % 2][:, :]).then_inc(S.c_trev, 1)

            cXA(0)
            cXA(1)
            for i in range(J):
                if i + 2 < J:
                    cXA(i + 2)
                vector.wait_ge(S.cD, i + 1)
                if i >= 4:
                    vector.wait_ge(S.c_h, i - 3)
                vector.reciprocal(inv_t[i % 4][:, :], std_t[i % 4][:, :]).then_inc(S.cE, 1)
                if i >= 1:
                    cevacs(i - 1)
            cevacs(J - 1)
            vector.wait_ge(S.bnd, 3)
            vector.reduce_sum(bm[:, :], st_ps[0:1, 0:DC], axis=AX)
            vector.reduce_sum(bs2[:, :], st_ps[0:1, 256:256 + DC], axis=AX).then_inc(S.bnd, 1)
            vector.wait_ge(S.bnd, 5)
            vector.tensor_mul(bm2[:, :], bm[:, :], bm[:, :]).then_inc(S.bnd, 1)
            vector.wait_ge(S.bnd, 7)
            vector.tensor_sub(bvar[:, :], btmpv[:, :], bm2[:, :]).then_inc(S.bnd, 1)
            vector.wait_ge(S.bnd, 9)
            vector.reciprocal(binv[:, :], bstd[:, :]).then_inc(S.bnd, 1)
            vector.wait_ge(S.bnd, 11)
            vector.tensor_scalar_mul(bnmi2[:, :], bnegmi[:, :], -1.0).then_inc(S.bnd, 1)
            vector.wait_ge(S.bnd, 14)
            vector.tensor_copy(binv_t[:, :], bnd_ps[:, 64:65])
            vector.tensor_copy(bnmi_t[:, :], bnd_ps[:, 65:66]).then_inc(S.bnd, 1)

        @blk.scalar
        def _(scalar):
            scalar.wait_ge(S.start, BIG)

            def cBC(i):
                j = JORD[i]
                scalar.wait_ge(S.cA, i + 1)
                scalar.activation(tsq2[:, i % 2, :], o_acc[:, j, :], AF.Identity,
                                  bias=mneg[i % 4][:, :], scale=float(D))
                scalar.activation(cdust[:, :], tsq2[:, i % 2, :], AF.Square,
                                  accum_out=v_raw[i % 4][:, :]).then_inc(S.cB, 1)

            cBC(0)
            for i in range(J):
                if i + 1 < J:
                    cBC(i + 1)
                if i >= 4:
                    scalar.wait_ge(S.cE, i - 3)
                scalar.activation(std_t[i % 4][:, :], v_raw[i % 4][:, :], AF.Sqrt,
                                  scale=1.0 / D, bias=epsb[:, :]).then_inc(S.cD, 1)
                scalar.wait_ge(S.cE, i + 1)
                if i >= 2:
                    scalar.wait_ge(S.c_trp, 8 * (i - 1))
                scalar.activation(h2tok[:, i % 2, :], tsq2[:, i % 2, :], AF.Copy,
                                  scale=inv_t[i % 4][:, :]).then_inc(S.c_h, 1)
            scalar.wait_ge(S.bnd, 1)
            scalar.activation(xbnd_f[:, :], bnd_ps[:, 0:DC], AF.Copy)
            scalar.activation(xbnd_b[:, :], bnd_ps[:, 0:DC], AF.Copy)
            scalar.activation(sqb[:, :], bnd_ps[:, 0:DC], AF.Square).then_inc(S.bnd, 1)
            scalar.wait_ge(S.bnd, 8)
            scalar.activation(bstd[:, :], bvar[:, :], AF.Sqrt).then_inc(S.bnd, 1)
            scalar.wait_ge(S.bnd, 12)
            scalar.activation(binv_b[:, :], binv[:, :], AF.Copy)
            scalar.activation(bnmi_b[:, :], bnmi2[:, :], AF.Copy).then_inc(S.bnd, 1)
            scalar.wait_ge(S.bnd, 15)
            for dc in range(DC):
                scalar.activation(h2_fm[:, dc, 0:1], xbnd_f[:, dc:dc + 1], AF.Identity,
                                  scale=binv_t[:, :], bias=bnmi_t[:, :]).then_inc(S.bnd5, 1)

        @blk.tensor
        def _(tensor):
            for i, j in enumerate(JORD):
                tensor.wait_ge(S.c_h, i + 1)
                for dc in range(DC):
                    cnt = i * DC + dc
                    if cnt >= 2:
                        tensor.wait_ge(S.c_trev, cnt - 1)
                    tensor.transpose(trp2[cnt % 2][:, :], h2tok[:, i % 2, dc * 128:(dc + 1) * 128],
                                     ident_b[:, :]).then_inc(S.c_trp, 1)
            tensor.wait_ge(S.ag3r, 16)
            tensor.wait_ge(S.s_w, 80)
            for dc in range(DC):
                mm = tensor.matmul(bnd_ps[:, dc:dc + 1], ag3_sb[:, dc * 128:(dc + 1) * 128], bsel_t[:, :],
                                   start=(dc == 0), stop=(dc == DC - 1))
                if dc == DC - 1:
                    mm.then_inc(S.bnd, 1)
            tensor.wait_ge(S.bnd, 2)
            tensor.matmul(st_ps[0:1, 0:DC], ones_col[:, :], xbnd_b[:, :], start=True, stop=False)
            tensor.matmul(st_ps[0:1, 256:256 + DC], ones_col[:, :], sqb[:, :],
                          start=False, stop=True).then_inc(S.bnd, 1)
            tensor.wait_ge(S.bnd, 13)
            tensor.matmul(bnd_ps[:, 64:65], ones_row[:, :], binv_b[:, :], start=True, stop=False)
            tensor.matmul(bnd_ps[:, 65:66], ones_row[:, :], bnmi_b[:, :],
                          start=False, stop=True).then_inc(S.bnd, 1)

        if dbg:
            @blk.scalar
            def _(scalar):
                scalar.wait_ge(S.bnd5, DC)
                scalar.wait_ge(S.c_trev, 128)
                scalar.dma_start(out=dbg_h2[:, :],
                                 in_=h2_fm.ap().rearrange("p c n -> p (c n)").bitcast(U16)).then_inc(S.dbgs, 16)
                scalar.wait_ge(S.c_xf, 16)
                scalar.dma_start(out=dbg_xp[:, :], in_=o_acc.ap().rearrange("p j d -> p (j d)")).then_inc(S.dbgs, 16)

    cctx.close()
    if stop_after == "C":
        root.close()
        return nc
    xr_b = E(nc.sbuf_tensor("xr_b", [128, 2, DC, 512], BF16))
    xk_b = E(nc.sbuf_tensor("xk_b", [128, 2, DC, 512], BF16))
    kact_b = E(nc.sbuf_tensor("kact_b", [128, DC, 512], BF16))
    d_stag = E(nc.sbuf_tensor("d_stag", [128, 512], BF16))
    krelu = E(nc.sbuf_tensor("krelu", [128, 512], F32))
    r_tok = E(nc.sbuf_tensor("r_tok", [128, 2, D], F32))

    with ExitStack() as pctx, nc.Block() as blk:
        kfm_ps = [pctx.enter_context(nc.psum_tensor(f"kfmD{i}", [128, 512], F32)) for i in range(2)]
        r_ps = pctx.enter_context(nc.psum_tensor("rD", [128, D], F32))
        v_ps = pctx.enter_context(nc.psum_tensor("vD", [128, D], F32))

        @blk.vector
        def _(vector):
            vector.wait_ge(S.c_trev, 128)
            vector.wait_ge(S.bnd5, DC)
            vector.wait_ge(S.s_w, 80)
            def mix(b):
                if b >= 2:
                    vector.wait_ge(S.d_rv, b - 1)
                for dc in range(DC):
                    lo = b * 512
                    vector.tensor_sub(d_stag[:, :], h2_fm[:, dc, lo + 1:lo + 513], h2_fm[:, dc, lo:lo + 512])
                    vector.scalar_tensor_tensor(xr_b[:, b % 2, dc, :], d_stag[:, :], mur_t[:, dc:dc + 1],
                                                h2_fm[:, dc, lo:lo + 512], ALU.mult, ALU.add)
                    mx = vector.scalar_tensor_tensor(xk_b[:, b % 2, dc, :], d_stag[:, :], muk_t[:, dc:dc + 1],
                                                     h2_fm[:, dc, lo:lo + 512], ALU.mult, ALU.add)
                    if dc == DC - 1:
                        mx.then_inc(S.d_mix, 1)
            mix(0)
            mix(1)
            for b in range(NB):
                for jj in range(NB):
                    j = b * NB + jj
                    vector.wait_ge(S.d_vmm, j + 1)
                    vector.wait_ge(S.d_rev, j + 1)
                    vector.tensor_mul(r_tok[:, j % 2, :], r_tok[:, j % 2, :], v_ps[:, :]).then_inc(S.d_vev, 1)
                    vector.tensor_add(r_tok[:, j % 2, :], r_tok[:, j % 2, :], o_acc[:, j, :]).then_inc(S.d_oev, 1)
                if b + 2 < NB:
                    mix(b + 2)
            vector.sem_inc(S.fin, 1)

        @blk.tensor
        def _(tensor):
            tensor.wait_ge(S.init, 82 + 48)
            for b in range(NB):
                tensor.wait_ge(S.d_mix, b + 1)
                for c in range(DC):
                    rot = b * DC + c
                    ps = kfm_ps[rot % 2]
                    if rot >= 2:
                        tensor.wait_ge(S.d_krel, rot - 1)
                    for dc in range(DC):
                        mm = tensor.matmul(ps[:, :], wfk_sb[:, dc, c * 128:(c + 1) * 128],
                                           xk_b[:, b % 2, dc, :],
                                           start=(dc == 0), stop=(dc == DC - 1))
                        if dc == DC - 1:
                            mm.then_inc(S.d_kmm, 1)
                for jj in range(NB):
                    j = b * NB + jj
                    if j >= 1:
                        tensor.wait_ge(S.d_rev, j)
                    for dc in range(DC):
                        for half in range(2):
                            mm = tensor.matmul(r_ps[:, half * 512:(half + 1) * 512],
                                               xr_b[:, b % 2, dc, jj * 128:(jj + 1) * 128],
                                               wr_sb[:, dc, half * 512:(half + 1) * 512],
                                               start=(dc == 0), stop=(dc == DC - 1))
                            if dc == DC - 1 and half == 1:
                                mm.then_inc(S.d_rmm, 1)
                    if j >= 1:
                        tensor.wait_ge(S.d_vev, j)
                    tensor.wait_ge(S.d_kact, DC * (b + 1))
                    for dc in range(DC):
                        for half in range(2):
                            mm = tensor.matmul(v_ps[:, half * 512:(half + 1) * 512],
                                               kact_b[:, dc, jj * 128:(jj + 1) * 128],
                                               wfv_sb[:, dc, half * 512:(half + 1) * 512],
                                               start=(dc == 0), stop=(dc == DC - 1))
                            if dc == DC - 1 and half == 1:
                                mm.then_inc(S.d_vmm, 1)
                                if jj == NB - 1:
                                    tensor.sem_inc(S.d_rv, 1)
            tensor.sem_inc(S.fin, 1)

        @blk.scalar
        def _(scalar):
            for b in range(NB):
                for c in range(DC):
                    rot = b * DC + c
                    scalar.wait_ge(S.d_kmm, rot + 1)
                    scalar.activation(krelu[:, :], kfm_ps[rot % 2][:, :], AF.Relu).then_inc(S.d_krel, 1)
                    scalar.activation(kact_b[:, c, :], krelu[:, :], AF.Square).then_inc(S.d_kact, 1)
                for jj in range(NB):
                    j = b * NB + jj
                    scalar.wait_ge(S.d_rmm, j + 1)
                    if j >= 2:
                        scalar.wait_ge(S.y_st if j % 2 == 0 else S.y_st1, 16 * (j // 2))
                    scalar.activation(r_tok[:, j % 2, :], r_ps[:, :], AF.Sigmoid).then_inc(S.d_rev, 1)
            scalar.sem_inc(S.fin, 1)

        @blk.sync
        def _(sync):
            for j in range(J):
                sync.wait_ge(S.d_oev, j + 1)
                sync.dma_start(out=y[j * 128:(j + 1) * 128, :],
                               in_=r_tok[:, j % 2, :]).then_inc(
                                   S.y_st if j % 2 == 0 else S.y_st1, 16)
            sync.wait_ge(S.y_st, 16 * (J // 2))
            sync.wait_ge(S.y_st1, 16 * (J // 2))
            if dbg:
                sync.wait_ge(S.dbgs, 16 * 16)

    root.close()
    return nc


_NC_CACHE = {}


def _get_nc(dbg=False):
    if dbg not in _NC_CACHE:
        _NC_CACHE[dbg] = build(dbg)
    return _NC_CACHE[dbg]


def make_in_maps(inputs):
    asn = lambda a: np.ascontiguousarray(np.asarray(a, dtype=np.float32))
    x = asn(inputs["L0_attn_x"]).reshape(NCORES * T, D)
    common = {
        "wqT": asn(np.asarray(inputs["L0_attn_wq"]).T),
        "wkT": asn(np.asarray(inputs["L0_attn_wk"]).T),
        "wvT": asn(np.asarray(inputs["L0_attn_wv"]).T),
        "wrT": asn(np.asarray(inputs["L0_ffn_W_r"]).T),
        "wfkT": asn(np.asarray(inputs["L0_ffn_W_k"]).T),
        "wfvT": asn(np.asarray(inputs["L0_ffn_W_v"]).T),
        "bq": np.ascontiguousarray(asn(inputs["L0_attn_bq"]).reshape(DC, 128).T),
        "bk": np.ascontiguousarray(asn(inputs["L0_attn_bk"]).reshape(DC, 128).T),
        "bv": asn(inputs["L0_attn_bv"]).reshape(1, D),
        "mur": np.ascontiguousarray(asn(inputs["L0_ffn_mu_r"]).reshape(DC, 128).T),
        "muk": np.ascontiguousarray(asn(inputs["L0_ffn_mu_k"]).reshape(DC, 128).T),
        "ident": np.eye(128, dtype=np.float32),
    }
    in_maps = []
    for i in range(NCORES):
        sel = np.zeros((NCORES, 1), np.float32)
        if i > 0:
            sel[i - 1, 0] = 1.0
        im = dict(common)
        im["x_sh"] = np.ascontiguousarray(x[i * T:(i + 1) * T, :])
        im["bsel"] = sel
        in_maps.append(im)
    return in_maps


def kernel(**inputs):
    nc = _get_nc(False)
    in_maps = make_in_maps(inputs)
    res = run_bass_kernel_spmd(nc, in_maps, core_ids=list(range(NCORES)))
    out = np.concatenate([r["y"] for r in res.results], axis=0)
    return out.reshape(1, NCORES * T, D)


# version 11
# speedup vs baseline: 1.3656x; 1.0012x over previous
import numpy as np
from contextlib import ExitStack

import concourse.bass as bass
import concourse.mybir as mybir
from concourse.bass_utils import run_bass_kernel_spmd

F32 = mybir.dt.float32
BF16 = mybir.dt.bfloat16
U16 = mybir.dt.uint16
AF = mybir.ActivationFunctionType
ALU = mybir.AluOpType
AX = mybir.AxisListType.X

NCORES = 8
D = 1024
DC = 8
T = 2048
J = 16
NG = 32
KC = 4
NB = 4
SCALE = 1.0 / 32.0
EPS = 1e-5

SEM_NAMES = [
    "init", "s_w", "s_x", "s_xfree", "lnA", "lnB", "lnC", "lnD", "lnE", "ln_h", "tr_p", "tr_ev",
    "a_mm", "a_nmm", "a_qev", "a_kev", "a_vev", "a_nev", "a_sq", "a_kst", "a_vst", "a_nst",
    "cc", "b_nld", "nm", "pe_s", "pe_p", "pe_o", "pe_e", "pe_g", "pe_sk", "pe_pts", "rcp",
    "c_x", "c_xf", "ag3w", "ag3r", "cA", "cB", "cC", "cD", "cE", "c_h",
    "c_trp", "c_trev", "bnd", "bnd5",
    "d_mix", "d_rv", "d_kmm", "d_krel", "d_kact", "d_rmm", "d_rev", "d_vmm", "d_vev",
    "d_oev", "y_st", "dbgs", "s_x1", "a_kst1", "a_vst1", "kA", "kA1", "kB", "kB1", "y_st1", "a_qm", "c_x1",
    "start", "fin",
]
BIG = 1 << 30


def build(dbg=False, stop_after=None):
    nc = bass.Bass("TRN2", target_bir_lowering=False)

    x_sh = nc.dram_tensor("x_sh", [T, D], F32, kind="ExternalInput")
    wqT = nc.dram_tensor("wqT", [D, D], F32, kind="ExternalInput")
    wkT = nc.dram_tensor("wkT", [D, D], F32, kind="ExternalInput")
    wvT = nc.dram_tensor("wvT", [D, D], F32, kind="ExternalInput")
    wrT = nc.dram_tensor("wrT", [D, D], F32, kind="ExternalInput")
    wfkT = nc.dram_tensor("wfkT", [D, D], F32, kind="ExternalInput")
    wfvT = nc.dram_tensor("wfvT", [D, D], F32, kind="ExternalInput")
    bq_d = nc.dram_tensor("bq", [128, DC], F32, kind="ExternalInput")
    bk_d = nc.dram_tensor("bk", [128, DC], F32, kind="ExternalInput")
    bv_d = nc.dram_tensor("bv", [1, D], F32, kind="ExternalInput")
    mur_d = nc.dram_tensor("mur", [128, DC], F32, kind="ExternalInput")
    muk_d = nc.dram_tensor("muk", [128, DC], F32, kind="ExternalInput")
    ident_d = nc.dram_tensor("ident", [128, 128], F32, kind="ExternalInput")
    bsel_d = nc.dram_tensor("bsel", [NCORES, 1], F32, kind="ExternalInput")
    y = nc.dram_tensor("y", [T, D], F32, kind="ExternalOutput")

    k_ag_in = nc.dram_tensor("k_ag_in", [4, DC, 128, 512], BF16, kind="Internal")
    k_ag_o = [nc.dram_tensor(f"k_ag_o{j}", [NCORES, DC, 128, 512], BF16, kind="Internal",
                             addr_space="Shared") for j in range(4)]
    v_ag_in = nc.dram_tensor("v_ag_in", [T, D], BF16, kind="Internal")
    v_ag_o = [nc.dram_tensor(f"v_ag_o{j}", [NCORES * 512, D], BF16, kind="Internal",
                             addr_space="Shared") for j in range(4)]
    n_ag_in = nc.dram_tensor("n_ag_in", [1, T], BF16, kind="Internal")
    n_ag_out = nc.dram_tensor("n_ag_out", [128, 128], BF16, kind="Internal")
    xb_ag_in = nc.dram_tensor("xb_ag_in", [1, D], F32, kind="Internal")
    xb_ag_out = nc.dram_tensor("xb_ag_out", [NCORES, D], F32, kind="Internal")
    if dbg:
        dbg_oacc = nc.dram_tensor("dbg_oacc", [128, J * D], F32, kind="ExternalOutput")
        dbg_rl = nc.dram_tensor("dbg_rl", [128, 16], F32, kind="ExternalOutput")
        dbg_negm = nc.dram_tensor("dbg_negm", [128, 1], F32, kind="ExternalOutput")
        dbg_h2 = nc.dram_tensor("dbg_h2", [128, DC * (T + 1)], U16, kind="ExternalOutput")
        dbg_xp = nc.dram_tensor("dbg_xp", [128, J * D], F32, kind="ExternalOutput")
        dbg_misc = nc.dram_tensor("dbg_misc", [1, 8], F32, kind="ExternalOutput")
        dbg_qn = nc.dram_tensor("dbg_qn", [1, T], U16, kind="ExternalOutput")
        dbg_kn = nc.dram_tensor("dbg_kn", [1, T], U16, kind="ExternalOutput")
        dbg_nrm = nc.dram_tensor("dbg_nrm", [128, 128], U16, kind="ExternalOutput")
        dbg_kag = nc.dram_tensor("dbg_kag", [128, 512], U16, kind="ExternalOutput")
        dbg_vag = nc.dram_tensor("dbg_vag", [128, 1024], U16, kind="ExternalOutput")
        dbg_h1 = nc.dram_tensor("dbg_h1", [128, DC * T], U16, kind="ExternalOutput")
        dbg_q = nc.dram_tensor("dbg_q", [128, DC * T], U16, kind="ExternalOutput")
        dbg_hbf = nc.dram_tensor("dbg_hbf", [128, 2 * D], U16, kind="ExternalOutput")

    RG = [list(range(NCORES))]

    root = ExitStack()
    E = root.enter_context
    ident_b = E(nc.sbuf_tensor("ident_b", [128, 128], BF16))
    ones_col = E(nc.sbuf_tensor("ones_col", [128, 1], BF16))
    ones_row = E(nc.sbuf_tensor("ones_row", [1, 128], BF16))
    ones32 = E(nc.sbuf_tensor("ones32", [128, 1], F32))
    epsb = E(nc.sbuf_tensor("epsb", [128, 1], F32))
    bq_t = E(nc.sbuf_tensor("bq_t", [128, DC], F32))
    bk_t = E(nc.sbuf_tensor("bk_t", [128, DC], F32))
    mur_t = E(nc.sbuf_tensor("mur_t", [128, DC], F32))
    muk_t = E(nc.sbuf_tensor("muk_t", [128, DC], F32))
    bsel_t = E(nc.sbuf_tensor("bsel_t", [NCORES, 1], F32))
    neg_m = E(nc.sbuf_tensor("neg_m", [128, 1], F32))
    qmax2 = E(nc.sbuf_tensor("qmax2", [1, 1], F32))
    kmax2 = E(nc.sbuf_tensor("kmax2", [1, 1], F32))
    m2 = E(nc.sbuf_tensor("m2", [1, 1], F32))
    negm_s = E(nc.sbuf_tensor("negm_s", [1, 1], F32))
    negm_b = E(nc.sbuf_tensor("negm_b", [1, 1], BF16))
    nrm_all = E(nc.sbuf_tensor("nrm_all", [128, 128], BF16))
    knmax_c = E(nc.sbuf_tensor("knmax_c", [128, 1], BF16))
    recip_l = E(nc.sbuf_tensor("recip_l", [128, 16], F32))
    mneg = [E(nc.sbuf_tensor(f"mneg{i}", [128, 1], F32)) for i in range(4)]
    v_raw = [E(nc.sbuf_tensor(f"v_raw{i}", [128, 1], F32)) for i in range(4)]
    sq_in = [E(nc.sbuf_tensor(f"sq_in{i}", [128, 1], F32)) for i in range(4)]
    std_t = [E(nc.sbuf_tensor(f"std_t{i}", [128, 1], F32)) for i in range(4)]
    inv_t = [E(nc.sbuf_tensor(f"inv_t{i}", [128, 1], F32)) for i in range(4)]
    recip_g = E(nc.sbuf_tensor("recip_g", [128, 16], F32))
    negm_s2 = E(nc.sbuf_tensor("negm_s2", [1, 1], F32))
    bm2 = E(nc.sbuf_tensor("bm2", [1, 1], F32))
    btmpv = E(nc.sbuf_tensor("btmpv", [1, 1], F32))
    bnmi2 = E(nc.sbuf_tensor("bnmi2", [1, 1], F32))

    qctx = ExitStack()
    q_fm = qctx.enter_context(nc.sbuf_tensor("q_fm", [128, DC, T], BF16, side="right"))

    sems = {name: E(nc.semaphore(name)) for name in SEM_NAMES}
    S = type("S", (), sems)
    _nums = sorted(h.num for h in sems.values())
    assert _nums[-1] - _nums[0] == len(_nums) - 1, _nums
    sem_lo, sem_hi = _nums[0], _nums[-1] + 1

    ctxA = ExitStack()
    EA = ctxA.enter_context
    wq_sb = EA(nc.sbuf_tensor("wq_sb", [128, DC, D], BF16))
    wk_sb = EA(nc.sbuf_tensor("wk_sb", [128, DC, D], BF16))
    wv_sb = EA(nc.sbuf_tensor("wv_sb", [128, DC, D], BF16))
    h1_fm = EA(nc.sbuf_tensor("h1_fm", [128, DC, T], BF16))
    xstag = EA(nc.sbuf_tensor("xstag", [128, 2, D], F32))
    t_stag = EA(nc.sbuf_tensor("t_stag", [128, 2, D], F32))
    tsq_st = EA(nc.sbuf_tensor("tsq_st", [128, D], BF16))
    h_bf = EA(nc.sbuf_tensor("h_bf", [128, 2, D], BF16))
    kstag = EA(nc.sbuf_tensor("kstag", [128, 2, T], BF16))
    vstag = EA(nc.sbuf_tensor("vstag", [128, 2, D], BF16))
    sq_st = EA(nc.sbuf_tensor("sq_st", [128, 2, T], BF16))
    qn_sb = EA(nc.sbuf_tensor("qn_sb", [1, T], BF16))
    kn_sb = EA(nc.sbuf_tensor("kn_sb", [1, T], BF16))
    bv_row = EA(nc.sbuf_tensor("bv_row", [1, D], BF16))

    with ExitStack() as pctx, nc.Block() as blk:
        trp = [pctx.enter_context(nc.psum_tensor(f"trpA{i}", [128, 128], BF16)) for i in range(2)]

        @blk.sync
        def _(sync):
            sync.wait_ge(S.start, BIG)
            for j in range(J):
                if j >= 2:
                    sync.wait_ge(S.s_xfree, j - 1)
                sync.dma_start(out=xstag[:, j % 2, :], in_=x_sh[j * 128:(j + 1) * 128, :]).then_inc(
                    S.s_x if j % 2 == 0 else S.s_x1, 16)
                if j == 1:
                    sync.dma_start(out=bq_t[:, :], in_=bq_d[:, :]).then_inc(S.s_w, 16)
                    sync.dma_start(out=bk_t[:, :], in_=bk_d[:, :]).then_inc(S.s_w, 16)
                    sync.dma_start(out=mur_t[:, :], in_=mur_d[:, :]).then_inc(S.s_w, 16)
                    sync.dma_start(out=muk_t[:, :], in_=muk_d[:, :]).then_inc(S.s_w, 16)
                    sync.dma_start(out=bsel_t[:, :], in_=bsel_d[:, :]).then_inc(S.s_w, 16)

        @blk.gpsimd
        def _(gpsimd):
            gpsimd.sem_clear(range(sem_lo, sem_hi))
            gpsimd.sem_inc(S.start, BIG)
            gpsimd.dma_start(out=ident_b[:, :], in_=ident_d[:, :]).then_inc(S.init, 16)
            gpsimd.dma_start(out=bv_row[:, :], in_=bv_d[:, :]).then_inc(S.init, 16)
            gpsimd.memset(ones_col[:, :], 1.0)
            gpsimd.memset(epsb[:, :], float(D) * D * EPS)
            gpsimd.memset(ones_row[:, :], 1.0).then_inc(S.init, 1)
            gpsimd.dma_start(out=wq_sb.ap(),
                             in_=wqT.ap().rearrange("(c p) n -> p c n", p=128)).then_inc(S.init, 16)
            gpsimd.dma_start(out=wk_sb.ap(),
                             in_=wkT.ap().rearrange("(c p) n -> p c n", p=128)).then_inc(S.init, 16)
            gpsimd.dma_start(out=wv_sb.ap(),
                             in_=wvT.ap().rearrange("(c p) n -> p c n", p=128)).then_inc(S.init, 16)

        @blk.vector
        def _(vector):
            vector.wait_ge(S.start, BIG)

            def lnA(j):
                vector.wait_ge(S.s_x if j % 2 == 0 else S.s_x1, 16 * (j // 2 + 1))
                if j >= 4:
                    vector.wait_ge(S.s_xfree, j - 3)
                vector.reduce_sum(mneg[j % 4][:, :], xstag[:, j % 2, :], axis=AX,
                                  negate=True).then_inc(S.lnA, 1)

            def evacs(j):
                for dc in range(DC):
                    cnt = j * DC + dc
                    vector.wait_ge(S.tr_p, cnt + 1)
                    vector.tensor_copy(h1_fm[:, dc, j * 128:(j + 1) * 128],
                                       trp[cnt % 2][:, :]).then_inc(S.tr_ev, 1)

            lnA(0)
            lnA(1)
            for j in range(J):
                if j + 2 < J:
                    lnA(j + 2)
                vector.wait_ge(S.lnD, j + 1)
                if j >= 4:
                    vector.wait_ge(S.ln_h, j - 3)
                vector.reciprocal(inv_t[j % 4][:, :], std_t[j % 4][:, :]).then_inc(S.lnE, 1)
                if j >= 1:
                    evacs(j - 1)
            evacs(J - 1)

        @blk.scalar
        def _(scalar):
            scalar.wait_ge(S.start, BIG)
            scalar.wait_ge(S.init, 33)

            def lnBC(j):
                scalar.wait_ge(S.lnA, j + 1)
                scalar.activation(t_stag[:, j % 2, :], xstag[:, j % 2, :], AF.Identity,
                                  bias=mneg[j % 4][:, :], scale=float(D)).then_inc(S.s_xfree, 1)
                scalar.activation(tsq_st[:, :], t_stag[:, j % 2, :], AF.Square,
                                  accum_out=v_raw[j % 4][:, :]).then_inc(S.lnB, 1)

            lnBC(0)
            for j in range(J):
                if j + 1 < J:
                    lnBC(j + 1)
                if j >= 4:
                    scalar.wait_ge(S.lnE, j - 3)
                scalar.activation(std_t[j % 4][:, :], v_raw[j % 4][:, :], AF.Sqrt,
                                  scale=1.0 / D, bias=epsb[:, :]).then_inc(S.lnD, 1)
                scalar.wait_ge(S.lnE, j + 1)
                if j >= 2:
                    scalar.wait_ge(S.tr_p, 8 * (j - 1))
                scalar.activation(h_bf[:, j % 2, :], t_stag[:, j % 2, :], AF.Copy,
                                  scale=inv_t[j % 4][:, :]).then_inc(S.ln_h, 1)

        @blk.tensor
        def _(tensor):
            tensor.wait_ge(S.start, BIG)
            tensor.wait_ge(S.init, 33)
            for j in range(J):
                tensor.wait_ge(S.ln_h, j + 1)
                for dc in range(DC):
                    cnt = j * DC + dc
                    if cnt >= 2:
                        tensor.wait_ge(S.tr_ev, cnt - 1)
                    tensor.transpose(trp[cnt % 2][:, :], h_bf[:, j % 2, dc * 128:(dc + 1) * 128],
                                     ident_b[:, :]).then_inc(S.tr_p, 1)

    with ExitStack() as pctx, nc.Block() as blk:
        kq_ps = [pctx.enter_context(nc.psum_tensor(f"kqA{i}", [128, 512], F32)) for i in range(3)]
        vv_ps = [pctx.enter_context(nc.psum_tensor(f"vvA{i}", [128, D], F32)) for i in range(2)]

        def _nrm_mms(tensor, c, off):
            tensor.wait_ge(S.a_sq, off + c + 1)
            if off == DC and c == 0:
                tensor.wait_ge(S.a_nst, 1)
                tensor.wait_ge(S.a_vev, 16)
            for b in range(NB):
                mm = tensor.matmul(vv_ps[b // 2][0:1, (b % 2) * 512:(b % 2 + 1) * 512],
                                   ones_col[:, :],
                                   sq_st[:, c % 2, b * 512:(b + 1) * 512],
                                   start=(c == 0), stop=(c == DC - 1))
                if b == NB - 1:
                    mm.then_inc(S.a_nmm, 1)

        @blk.tensor
        def _(tensor):
            tensor.wait_ge(S.tr_ev, J * DC)
            tensor.wait_ge(S.init, 81)
            for c in range(DC):
                for b in range(NB):
                    n = c * NB + b
                    ps = kq_ps[n % 3]
                    if n >= 3:
                        tensor.wait_ge(S.a_kev, n - 2)
                    for dc in range(DC):
                        mm = tensor.matmul(ps[:, :], wk_sb[:, dc, c * 128:(c + 1) * 128],
                                           h1_fm[:, dc, b * 512:(b + 1) * 512],
                                           start=(dc == 0), stop=(dc == DC - 1))
                        if dc == DC - 1:
                            mm.then_inc(S.a_mm, 1)
                if c >= 1:
                    _nrm_mms(tensor, c - 1, 0)
            _nrm_mms(tensor, DC - 1, 0)
            for t in range(J):
                ps = vv_ps[t % 2]
                if t == 0:
                    tensor.wait_ge(S.a_nst, 1)
                if t >= 2:
                    tensor.wait_ge(S.a_vev, t - 1)
                for dc in range(DC):
                    for half in range(2):
                        tensor.matmul(ps[:, half * 512:(half + 1) * 512],
                                      h1_fm[:, dc, t * 128:(t + 1) * 128],
                                      wv_sb[:, dc, half * 512:(half + 1) * 512],
                                      start=(dc == 0), stop=False)
                for half in range(2):
                    mm = tensor.matmul(ps[:, half * 512:(half + 1) * 512], ones_row[:, :],
                                       bv_row[:, half * 512:(half + 1) * 512], start=False, stop=True)
                    if half == 1:
                        mm.then_inc(S.a_mm, 1)
            for c in range(DC):
                for b in range(NB):
                    n = c * NB + b
                    ps = kq_ps[n % 3]
                    if n == 0:
                        tensor.wait_ge(S.a_kev, 31)
                    elif n == 1:
                        tensor.wait_ge(S.a_kev, 32)
                    elif n >= 3:
                        tensor.wait_ge(S.a_qev, n - 2)
                    for dc in range(DC):
                        mm = tensor.matmul(ps[:, :], wq_sb[:, dc, c * 128:(c + 1) * 128],
                                           h1_fm[:, dc, b * 512:(b + 1) * 512],
                                           start=(dc == 0), stop=(dc == DC - 1))
                        if dc == DC - 1:
                            mm.then_inc(S.a_mm, 1)
                if c >= 1:
                    _nrm_mms(tensor, c - 1, DC)
            _nrm_mms(tensor, DC - 1, DC)

        @blk.scalar
        def _(scalar):
            scalar.wait_ge(S.s_w, 80)
            for c in range(DC):
                if c >= 2:
                    scalar.wait_ge(S.a_kst if c % 2 == 0 else S.a_kst1, 64 * (c // 2))
                for b in range(NB):
                    n = c * NB + b
                    scalar.wait_ge(S.a_mm, n + 1)
                    scalar.activation(kstag[:, c % 2, b * 512:(b + 1) * 512], kq_ps[n % 3][:, :],
                                      AF.Identity, bias=bk_t[:, c:c + 1]).then_inc(S.a_kev, 1)
                if c >= 2:
                    scalar.wait_ge(S.a_nmm, c - 1)
                scalar.activation(sq_st[:, c % 2, :], kstag[:, c % 2, :], AF.Square).then_inc(S.a_sq, 1)
            scalar.wait_ge(S.a_nmm, DC)
            scalar.activation(kn_sb[0:1, 0:1024], vv_ps[0][0:1, 0:1024], AF.Copy)
            scalar.activation(kn_sb[0:1, 1024:2048], vv_ps[1][0:1, 0:1024], AF.Copy).then_inc(S.a_nst, 1)
            for c in range(DC):
                for b in range(NB):
                    n = 48 + c * NB + b
                    scalar.wait_ge(S.a_mm, n + 1)
                    scalar.activation(q_fm[:, c, b * 512:(b + 1) * 512], kq_ps[(c * NB + b) % 3][:, :],
                                      AF.Identity, bias=bq_t[:, c:c + 1]).then_inc(S.a_qev, 1)
                scalar.wait_ge(S.a_nmm, 7 + c)
                scalar.activation(sq_st[:, c % 2, :], q_fm[:, c, :], AF.Square).then_inc(S.a_sq, 1)
            scalar.wait_ge(S.a_nmm, 2 * DC)
            scalar.activation(qn_sb[0:1, 0:1024], vv_ps[0][0:1, 0:1024], AF.Copy)
            scalar.activation(qn_sb[0:1, 1024:2048], vv_ps[1][0:1, 0:1024], AF.Copy).then_inc(S.a_nev, 1)

        @blk.vector
        def _(vector):
            for t in range(J):
                n = 32 + t
                if t >= 2:
                    vector.wait_ge(S.a_vst if t % 2 == 0 else S.a_vst1, 16 * (t // 2))
                vector.wait_ge(S.a_mm, n + 1)
                vector.tensor_copy(vstag[:, t % 2, :], vv_ps[t % 2][:, :]).then_inc(S.a_vev, 1)
            vector.wait_ge(S.a_nev, 1)
            vector.reduce_max(qmax2[:, :], qn_sb[:, :], axis=AX).then_inc(S.a_qm, 1)

        @blk.sync
        def _(sync):
            for c in range(DC):
                sync.wait_ge(S.a_kev, NB * (c + 1))
                for gl in range(4):
                    sync.dma_start(out=k_ag_in[gl, c, :, :],
                                   in_=kstag[:, c % 2, gl * 512:(gl + 1) * 512]).then_inc(
                                       S.a_kst if c % 2 == 0 else S.a_kst1, 16)
            for t in range(J):
                sync.wait_ge(S.a_vev, t + 1)
                sync.dma_start(out=v_ag_in[t * 128:(t + 1) * 128, :], in_=vstag[:, t % 2, :]).then_inc(
                    S.a_vst if t % 2 == 0 else S.a_vst1, 16)
            sync.wait_ge(S.a_nst, 1)
            sync.dma_start(out=n_ag_in[:, :], in_=kn_sb[:, :]).then_inc(S.a_nst, 16)

        @blk.gpsimd
        def _(gpsimd):
            gpsimd.wait_ge(S.a_kst, 16 * 16)
            gpsimd.wait_ge(S.a_kst1, 16 * 16)
            gpsimd.collective_compute("AllGather", ALU.bypass, replica_groups=RG,
                                      ins=[k_ag_in.ap()[0:1, :, :, :].opt()],
                                      outs=[k_ag_o[0].ap().opt()]).then_inc(S.cc, 1)
            gpsimd.wait_ge(S.a_nst, 17)
            gpsimd.collective_compute("AllGather", ALU.bypass, replica_groups=RG,
                                      ins=[n_ag_in.ap().opt()], outs=[n_ag_out.ap().opt()]).then_inc(S.cc, 1)
            for j in range(4):
                gpsimd.wait_ge(S.a_vst, 16 * (2 * j + 2))
                gpsimd.wait_ge(S.a_vst1, 16 * (2 * j + 2))
                gpsimd.collective_compute("AllGather", ALU.bypass, replica_groups=RG,
                                          ins=[v_ag_in.ap()[j * 512:(j + 1) * 512, :].opt()],
                                          outs=[v_ag_o[j].ap().opt()]).then_inc(S.cc, 1)
                if j < 3:
                    gpsimd.collective_compute("AllGather", ALU.bypass, replica_groups=RG,
                                              ins=[k_ag_in.ap()[j + 1:j + 2, :, :, :].opt()],
                                              outs=[k_ag_o[j + 1].ap().opt()]).then_inc(S.cc, 1)
            gpsimd.wait_ge(S.cc, 2)
            gpsimd.dma_start(out=nrm_all[:, :], in_=n_ag_out[:, :]).then_inc(S.b_nld, 16)

    ctxA.close()
    if stop_after == "A":
        root.close()
        return nc
    o_acc = E(nc.sbuf_tensor("o_acc", [128, J, D], F32))

    ctxB = ExitStack()
    EB = ctxB.enter_context
    vbuf = EB(nc.sbuf_tensor("vbuf", [128, 2, KC, 1024], BF16))
    kbuf = EB(nc.sbuf_tensor("kbuf", [128, 2, DC, 512], BF16))
    pt = [EB(nc.sbuf_tensor(f"pt{i}", [128, KC, T], BF16)) for i in range(2)]
    gsum = EB(nc.sbuf_tensor("gsum", [128, T], F32))

    with ExitStack() as pctx, nc.Block() as blk:
        s_ps = [pctx.enter_context(nc.psum_tensor(f"sB{i}", [128, 512], F32)) for i in range(2)]
        o_ps = [pctx.enter_context(nc.psum_tensor(f"oB{i}", [128, D], F32)) for i in range(2)]
        l_ps = pctx.enter_context(nc.psum_tensor("l_ps", [128, 512], F32))
        nm_ps = pctx.enter_context(nc.psum_tensor("nmB", [128, 512], F32))

        @blk.gpsimd
        def _(gpsimd):
            gpsimd.memset(ones32[:, :], 1.0).then_inc(S.init, 1)
            gpsimd.wait_ge(S.nm, 3)
            gpsimd.tensor_mul(m2[:, :], kmax2[:, :], qmax2[:, :]).then_inc(S.nm, 1)
            gpsimd.wait_ge(S.nm, 5)
            gpsimd.tensor_scalar_mul(negm_s2[:, :], negm_s[:, :], -SCALE).then_inc(S.nm, 1)
            gpsimd.wait_ge(S.rcp, 1)
            gpsimd.tensor_copy(recip_g[:, :], recip_l[:, :]).then_inc(S.rcp, 1)

        @blk.vector
        def _(vector):
            vector.wait_ge(S.b_nld, 16)
            vector.reduce_max(knmax_c[:, :], nrm_all[:, :], axis=AX).then_inc(S.nm, 1)
            vector.wait_ge(S.nm, 2)
            vector.reduce_max(kmax2[:, :], nm_ps[0:1, 0:64].bitcast(BF16), axis=AX).then_inc(S.nm, 1)
            vector.wait_ge(S.nm, 8)
            vector.tensor_copy(neg_m[:, :], nm_ps[:, 128:129]).then_inc(S.nm, 1)
            n = 0
            for g in range(NG):
                vector.wait_ge(S.pe_p, 16 * (g + 1))
                ptg2 = pt[g % 2]
                if g == 0:
                    vector.tensor_tensor(gsum[:, :], ptg2[:, 0, :], ptg2[:, 1, :], ALU.add)
                else:
                    vector.tensor_tensor(gsum[:, :], gsum[:, :], ptg2[:, 0, :], ALU.add)
                    vector.tensor_tensor(gsum[:, :], gsum[:, :], ptg2[:, 1, :], ALU.add)
                vector.tensor_tensor(gsum[:, :], gsum[:, :], ptg2[:, 2, :], ALU.add)
                vector.tensor_tensor(gsum[:, :], gsum[:, :], ptg2[:, 3, :], ALU.add).then_inc(S.pe_pts, 1)
                for j in range(J):
                    vector.wait_ge(S.pe_o, n + 1)
                    if g == 0:
                        vector.tensor_copy(o_acc[:, j, :], o_ps[n % 2][:, :]).then_inc(S.pe_e, 1)
                    else:
                        vector.tensor_tensor(o_acc[:, j, :], o_acc[:, j, :], o_ps[n % 2][:, :],
                                             ALU.add).then_inc(S.pe_e, 1)
                    n += 1
            vector.wait_ge(S.pe_g, NG + 1)
            vector.reciprocal(recip_l[:, :], l_ps[:, 0:16]).then_inc(S.rcp, 1)

        @blk.scalar
        def _(scalar):
            scalar.wait_ge(S.nm, 4)
            scalar.activation(negm_s[:, :], m2[:, :], AF.Sqrt).then_inc(S.nm, 1)
            scalar.wait_ge(S.nm, 6)
            scalar.activation(negm_b[:, :], negm_s2[:, :], AF.Copy).then_inc(S.nm, 1)
            n = 0
            scalar.wait_ge(S.a_nst, 17)
            scalar.wait_ge(S.a_qm, 1)
            for g in range(NG):
                if g == 0:
                    scalar.wait_ge(S.nm, 9)
                for kc in range(KC):
                    for b in range(NB):
                        scalar.wait_ge(S.pe_s, n + 1)
                        scalar.activation(pt[g % 2][:, kc, b * 512:(b + 1) * 512], s_ps[n % 2][:, :],
                                          AF.Exp, bias=neg_m[:, :], scale=SCALE).then_inc(S.pe_p, 1)
                        n += 1

        @blk.tensor
        def _(tensor):
            tensor.wait_ge(S.nm, 1)
            tensor.transpose(nm_ps[0:1, 0:64].bitcast(BF16), knmax_c[:, :], ident_b[:, :]).then_inc(S.nm, 1)
            tensor.wait_ge(S.nm, 7)
            tensor.matmul(nm_ps[:, 128:129], ones_row[:, :], negm_b[:, :], start=True, stop=True).then_inc(S.nm, 1)
            tensor.wait_ge(S.a_qev, 32)
            tensor.wait_ge(S.init, 82)
            n_s = 0
            n_o = 0
            for g in range(NG + 1):
                if g < NG:
                    tensor.wait_ge(S.kA if g % 2 == 0 else S.kA1, 16 * (g // 2 + 1))
                    for kc in range(KC):
                        for b in range(NB):
                            ps = s_ps[n_s % 2]
                            if n_s >= 2:
                                tensor.wait_ge(S.pe_p, n_s - 1)
                            for dc in range(DC):
                                mm = tensor.matmul(ps[:, :], kbuf[:, g % 2, dc, kc * 128:(kc + 1) * 128],
                                                   q_fm[:, dc, b * 512:(b + 1) * 512],
                                                   start=(dc == 0), stop=(dc == DC - 1))
                                if dc == DC - 1:
                                    mm.then_inc(S.pe_s, 1)
                            n_s += 1
                    tensor.nop().then_inc(S.pe_sk, 1)
                if g >= 1:
                    gp = g - 1
                    tensor.wait_ge(S.pe_p, gp * 16 + 16)
                    tensor.wait_ge(S.kB if gp % 2 == 0 else S.kB1, 16 * (gp // 2 + 1))
                    ptg = pt[gp % 2]
                    for j in range(J):
                        op = o_ps[n_o % 2]
                        if n_o >= 2:
                            tensor.wait_ge(S.pe_e, n_o - 1)
                        for kc in range(KC):
                            stat = ptg[:, kc, j * 128:(j + 1) * 128]
                            tensor.matmul(op[:, 0:512], stat, vbuf[:, gp % 2, kc, 0:512],
                                          start=(kc == 0), stop=(kc == KC - 1))
                            mm = tensor.matmul(op[:, 512:1024], stat, vbuf[:, gp % 2, kc, 512:1024],
                                               start=(kc == 0), stop=(kc == KC - 1))
                            if kc == KC - 1:
                                mm.then_inc(S.pe_o, 1)
                        n_o += 1
                    tensor.nop().then_inc(S.pe_g, 1)
            tensor.wait_ge(S.pe_pts, NG)
            for j in range(J):
                mm = tensor.matmul(l_ps[:, j:j + 1], gsum[:, j * 128:(j + 1) * 128],
                                   ones32[:, :], start=(j == 0), stop=(j == J - 1))
            mm.then_inc(S.pe_g, 1)

        @blk.sync
        def _(sync):
            for gi in range(NG):
                j, r = gi // 8, gi % 8
                ka = S.kA if gi % 2 == 0 else S.kA1
                kb = S.kB if gi % 2 == 0 else S.kB1
                if gi >= 2:
                    sync.wait_ge(S.pe_sk, gi - 1)
                if gi == 0:
                    sync.wait_ge(S.a_vst, 16 * 8)
                    sync.wait_ge(S.a_vst1, 16 * 8)
                sync.wait_ge(S.cc, 1 if j == 0 else 2 * j + 2)
                sync.dma_start(out=kbuf.ap()[:, gi % 2:gi % 2 + 1, :, :],
                               in_=k_ag_o[j].ap()[r:r + 1, :, :, :].rearrange(
                                   "o c p t -> p o c t")).then_inc(ka, 16)
                if gi >= 2:
                    sync.wait_ge(S.pe_g, gi - 1)
                if gi == 0:
                    sync.wait_ge(S.a_mm, 80)
                sync.wait_ge(S.cc, 2 * j + 3)
                sync.dma_start(out=vbuf.ap()[:, gi % 2, :, :],
                               in_=v_ag_o[j].ap()[r * 512:(r + 1) * 512, :].rearrange(
                                   "(kc p) d -> p kc d", p=128)).then_inc(kb, 16)

        if dbg:
            @blk.scalar
            def _(scalar):
                scalar.wait_ge(S.rcp, 1)
                scalar.dma_start(out=dbg_rl[:, :], in_=recip_l[:, :]).then_inc(S.dbgs, 16)
                scalar.dma_start(out=dbg_negm[:, :], in_=neg_m[:, :]).then_inc(S.dbgs, 16)
                scalar.dma_start(out=dbg_oacc[:, :], in_=o_acc.ap().rearrange("p j d -> p (j d)")).then_inc(S.dbgs, 16)
                scalar.dma_start(out=dbg_misc[:, 0:1], in_=qmax2[:, :]).then_inc(S.dbgs, 16)
                scalar.dma_start(out=dbg_misc[:, 1:2], in_=kmax2[:, :]).then_inc(S.dbgs, 16)
                scalar.dma_start(out=dbg_misc[:, 2:3], in_=m2[:, :]).then_inc(S.dbgs, 16)
                scalar.dma_start(out=dbg_nrm[:, :], in_=nrm_all.ap().bitcast(U16)).then_inc(S.dbgs, 16)
                scalar.dma_start(out=dbg_kag[:, :],
                                 in_=k_ag_o[0].ap()[0, 0, :, :].bitcast(U16)).then_inc(S.dbgs, 16)
                scalar.dma_start(out=dbg_vag[:, :],
                                 in_=v_ag_o[0].ap()[0:128, :].bitcast(U16)).then_inc(S.dbgs, 16)

    ctxB.close()
    qctx.close()
    if stop_after == "B":
        root.close()
        return nc
    wr_sb = E(nc.sbuf_tensor("wr_sb", [128, DC, D], BF16))
    wfk_sb = E(nc.sbuf_tensor("wfk_sb", [128, DC, D], BF16))
    wfv_sb = E(nc.sbuf_tensor("wfv_sb", [128, DC, D], BF16))
    h2_fm = E(nc.sbuf_tensor("h2_fm", [128, DC, T + 1], BF16))
    cctx = ExitStack()
    EC = cctx.enter_context
    xstag2 = EC(nc.sbuf_tensor("xstag2", [128, 2, D], F32))
    h2tok = EC(nc.sbuf_tensor("h2tok", [128, 2, D], BF16))
    tsq2 = EC(nc.sbuf_tensor("tsq2", [128, 2, D], F32))
    cdust = EC(nc.sbuf_tensor("cdust", [128, D], BF16))
    ag3_sb = EC(nc.sbuf_tensor("ag3_sb", [NCORES, D], F32))
    xbnd_f = EC(nc.sbuf_tensor("xbnd_f", [128, DC], F32))
    xbnd_b = EC(nc.sbuf_tensor("xbnd_b", [128, DC], BF16))
    sqb = EC(nc.sbuf_tensor("sqb", [128, DC], BF16))
    bm = EC(nc.sbuf_tensor("bm", [1, 1], F32))
    bs2 = EC(nc.sbuf_tensor("bs2", [1, 1], F32))
    bvar = EC(nc.sbuf_tensor("bvar", [1, 1], F32))
    bstd = EC(nc.sbuf_tensor("bstd", [1, 1], F32))
    binv = EC(nc.sbuf_tensor("binv", [1, 1], F32))
    bnegmi = EC(nc.sbuf_tensor("bnegmi", [1, 1], F32))
    binv_b = EC(nc.sbuf_tensor("binv_b", [1, 1], BF16))
    bnmi_b = EC(nc.sbuf_tensor("bnmi_b", [1, 1], BF16))
    binv_t = EC(nc.sbuf_tensor("binv_t", [128, 1], F32))
    bnmi_t = EC(nc.sbuf_tensor("bnmi_t", [128, 1], F32))

    JORD = [15] + list(range(15))

    with ExitStack() as pctx, nc.Block() as blk:
        trp2 = [pctx.enter_context(nc.psum_tensor(f"trpC{i}", [128, 128], BF16)) for i in range(2)]
        bnd_ps = pctx.enter_context(nc.psum_tensor("bndC", [128, 512], F32))
        st_ps = pctx.enter_context(nc.psum_tensor("stC", [1, 512], F32))

        @blk.sync
        def _(sync):
            for i, j in enumerate(JORD):
                if i >= 2:
                    sync.wait_ge(S.c_xf, i - 1)
                sync.dma_start(out=xstag2[:, i % 2, :],
                               in_=x_sh[j * 128:(j + 1) * 128, :]).then_inc(
                                   S.c_x if i % 2 == 0 else S.c_x1, 16)
            sync.wait_ge(S.cc, 10)
            sync.dma_start(out=ag3_sb[:, :], in_=xb_ag_out[:, :]).then_inc(S.ag3r, 16)

        @blk.gpsimd
        def _(gpsimd):
            gpsimd.wait_ge(S.c_xf, 1)
            gpsimd.dma_start(out=xb_ag_in[:, :], in_=o_acc.ap()[127:128, 15, :]).then_inc(S.ag3w, 16)
            gpsimd.wait_ge(S.ag3w, 16)
            gpsimd.collective_compute("AllGather", ALU.bypass, replica_groups=RG,
                                      ins=[xb_ag_in.ap().opt()], outs=[xb_ag_out.ap().opt()]).then_inc(S.cc, 1)
            gpsimd.wait_ge(S.pe_g, NG)
            gpsimd.dma_start(out=wfk_sb.ap(),
                             in_=wfkT.ap().rearrange("(c p) n -> p c n", p=128)).then_inc(S.init, 16)
            gpsimd.dma_start(out=wr_sb.ap(),
                             in_=wrT.ap().rearrange("(c p) n -> p c n", p=128)).then_inc(S.init, 16)
            gpsimd.dma_start(out=wfv_sb.ap(),
                             in_=wfvT.ap().rearrange("(c p) n -> p c n", p=128)).then_inc(S.init, 16)
            gpsimd.wait_ge(S.bnd, 4)
            gpsimd.tensor_scalar_mul(bm[:, :], bm[:, :], 1.0 / D).then_inc(S.bnd, 1)
            gpsimd.wait_ge(S.bnd, 6)
            gpsimd.tensor_scalar(btmpv[:, :], bs2[:, :], 1.0 / D, EPS, ALU.mult, ALU.add).then_inc(S.bnd, 1)
            gpsimd.wait_ge(S.bnd, 10)
            gpsimd.tensor_mul(bnegmi[:, :], bm[:, :], binv[:, :]).then_inc(S.bnd, 1)

        @blk.vector
        def _(vector):
            def cXA(i):
                j = JORD[i]
                vector.wait_ge(S.c_x if i % 2 == 0 else S.c_x1, 16 * (i // 2 + 1))
                if i == 0:
                    vector.wait_ge(S.rcp, 2)
                    if dbg:
                        vector.wait_ge(S.dbgs, 16 * 14)
                vector.scalar_tensor_tensor(o_acc[:, j, :], o_acc[:, j, :], recip_g[:, j:j + 1],
                                            xstag2[:, i % 2, :], ALU.mult, ALU.add).then_inc(S.c_xf, 1)
                if i >= 4:
                    vector.wait_ge(S.cB, i - 3)
                vector.reduce_sum(mneg[i % 4][:, :], o_acc[:, j, :], axis=AX,
                                  negate=True).then_inc(S.cA, 1)

            def cevacs(i):
                j = JORD[i]
                for dc in range(DC):
                    cnt = i * DC + dc
                    vector.wait_ge(S.c_trp, cnt + 1)
                    vector.tensor_copy(h2_fm[:, dc, 1 + j * 128:1 + (j + 1) * 128],
                                       trp2[cnt % 2][:, :]).then_inc(S.c_trev, 1)

            cXA(0)
            cXA(1)
            for i in range(J):
                if i + 2 < J:
                    cXA(i + 2)
                vector.wait_ge(S.cD, i + 1)
                if i >= 4:
                    vector.wait_ge(S.c_h, i - 3)
                vector.reciprocal(inv_t[i % 4][:, :], std_t[i % 4][:, :]).then_inc(S.cE, 1)
                if i >= 1:
                    cevacs(i - 1)
            cevacs(J - 1)
            vector.wait_ge(S.bnd, 3)
            vector.reduce_sum(bm[:, :], st_ps[0:1, 0:DC], axis=AX)
            vector.reduce_sum(bs2[:, :], st_ps[0:1, 256:256 + DC], axis=AX).then_inc(S.bnd, 1)
            vector.wait_ge(S.bnd, 5)
            vector.tensor_mul(bm2[:, :], bm[:, :], bm[:, :]).then_inc(S.bnd, 1)
            vector.wait_ge(S.bnd, 7)
            vector.tensor_sub(bvar[:, :], btmpv[:, :], bm2[:, :]).then_inc(S.bnd, 1)
            vector.wait_ge(S.bnd, 9)
            vector.reciprocal(binv[:, :], bstd[:, :]).then_inc(S.bnd, 1)
            vector.wait_ge(S.bnd, 11)
            vector.tensor_scalar_mul(bnmi2[:, :], bnegmi[:, :], -1.0).then_inc(S.bnd, 1)
            vector.wait_ge(S.bnd, 14)
            vector.tensor_copy(binv_t[:, :], bnd_ps[:, 64:65])
            vector.tensor_copy(bnmi_t[:, :], bnd_ps[:, 65:66]).then_inc(S.bnd, 1)

        @blk.scalar
        def _(scalar):
            scalar.wait_ge(S.start, BIG)

            def cBC(i):
                j = JORD[i]
                scalar.wait_ge(S.cA, i + 1)
                scalar.activation(tsq2[:, i % 2, :], o_acc[:, j, :], AF.Identity,
                                  bias=mneg[i % 4][:, :], scale=float(D))
                scalar.activation(cdust[:, :], tsq2[:, i % 2, :], AF.Square,
                                  accum_out=v_raw[i % 4][:, :]).then_inc(S.cB, 1)

            cBC(0)
            for i in range(J):
                if i + 1 < J:
                    cBC(i + 1)
                if i >= 4:
                    scalar.wait_ge(S.cE, i - 3)
                scalar.activation(std_t[i % 4][:, :], v_raw[i % 4][:, :], AF.Sqrt,
                                  scale=1.0 / D, bias=epsb[:, :]).then_inc(S.cD, 1)
                scalar.wait_ge(S.cE, i + 1)
                if i >= 2:
                    scalar.wait_ge(S.c_trp, 8 * (i - 1))
                scalar.activation(h2tok[:, i % 2, :], tsq2[:, i % 2, :], AF.Copy,
                                  scale=inv_t[i % 4][:, :]).then_inc(S.c_h, 1)
            scalar.wait_ge(S.bnd, 1)
            scalar.activation(xbnd_f[:, :], bnd_ps[:, 0:DC], AF.Copy)
            scalar.activation(xbnd_b[:, :], bnd_ps[:, 0:DC], AF.Copy)
            scalar.activation(sqb[:, :], bnd_ps[:, 0:DC], AF.Square).then_inc(S.bnd, 1)
            scalar.wait_ge(S.bnd, 8)
            scalar.activation(bstd[:, :], bvar[:, :], AF.Sqrt).then_inc(S.bnd, 1)
            scalar.wait_ge(S.bnd, 12)
            scalar.activation(binv_b[:, :], binv[:, :], AF.Copy)
            scalar.activation(bnmi_b[:, :], bnmi2[:, :], AF.Copy).then_inc(S.bnd, 1)
            scalar.wait_ge(S.bnd, 15)
            for dc in range(DC):
                scalar.activation(h2_fm[:, dc, 0:1], xbnd_f[:, dc:dc + 1], AF.Identity,
                                  scale=binv_t[:, :], bias=bnmi_t[:, :]).then_inc(S.bnd5, 1)

        @blk.tensor
        def _(tensor):
            for i, j in enumerate(JORD):
                tensor.wait_ge(S.c_h, i + 1)
                for dc in range(DC):
                    cnt = i * DC + dc
                    if cnt >= 2:
                        tensor.wait_ge(S.c_trev, cnt - 1)
                    tensor.transpose(trp2[cnt % 2][:, :], h2tok[:, i % 2, dc * 128:(dc + 1) * 128],
                                     ident_b[:, :]).then_inc(S.c_trp, 1)
            tensor.wait_ge(S.ag3r, 16)
            tensor.wait_ge(S.s_w, 80)
            for dc in range(DC):
                mm = tensor.matmul(bnd_ps[:, dc:dc + 1], ag3_sb[:, dc * 128:(dc + 1) * 128], bsel_t[:, :],
                                   start=(dc == 0), stop=(dc == DC - 1))
                if dc == DC - 1:
                    mm.then_inc(S.bnd, 1)
            tensor.wait_ge(S.bnd, 2)
            tensor.matmul(st_ps[0:1, 0:DC], ones_col[:, :], xbnd_b[:, :], start=True, stop=False)
            tensor.matmul(st_ps[0:1, 256:256 + DC], ones_col[:, :], sqb[:, :],
                          start=False, stop=True).then_inc(S.bnd, 1)
            tensor.wait_ge(S.bnd, 13)
            tensor.matmul(bnd_ps[:, 64:65], ones_row[:, :], binv_b[:, :], start=True, stop=False)
            tensor.matmul(bnd_ps[:, 65:66], ones_row[:, :], bnmi_b[:, :],
                          start=False, stop=True).then_inc(S.bnd, 1)

        if dbg:
            @blk.scalar
            def _(scalar):
                scalar.wait_ge(S.bnd5, DC)
                scalar.wait_ge(S.c_trev, 128)
                scalar.dma_start(out=dbg_h2[:, :],
                                 in_=h2_fm.ap().rearrange("p c n -> p (c n)").bitcast(U16)).then_inc(S.dbgs, 16)
                scalar.wait_ge(S.c_xf, 16)
                scalar.dma_start(out=dbg_xp[:, :], in_=o_acc.ap().rearrange("p j d -> p (j d)")).then_inc(S.dbgs, 16)

    cctx.close()
    if stop_after == "C":
        root.close()
        return nc
    xr_b = E(nc.sbuf_tensor("xr_b", [128, 2, DC, 512], BF16))
    xk_b = E(nc.sbuf_tensor("xk_b", [128, 2, DC, 512], BF16))
    kact_b = E(nc.sbuf_tensor("kact_b", [128, DC, 512], BF16))
    d_stag = E(nc.sbuf_tensor("d_stag", [128, 512], BF16))
    krelu = E(nc.sbuf_tensor("krelu", [128, 512], F32))
    r_tok = E(nc.sbuf_tensor("r_tok", [128, 2, D], F32))

    with ExitStack() as pctx, nc.Block() as blk:
        kfm_ps = [pctx.enter_context(nc.psum_tensor(f"kfmD{i}", [128, 512], F32)) for i in range(2)]
        r_ps = pctx.enter_context(nc.psum_tensor("rD", [128, D], F32))
        v_ps = pctx.enter_context(nc.psum_tensor("vD", [128, D], F32))

        @blk.vector
        def _(vector):
            vector.wait_ge(S.c_trev, 128)
            vector.wait_ge(S.bnd5, DC)
            vector.wait_ge(S.s_w, 80)
            def mix(b):
                if b >= 2:
                    vector.wait_ge(S.d_rv, b - 1)
                for dc in range(DC):
                    lo = b * 512
                    vector.tensor_sub(d_stag[:, :], h2_fm[:, dc, lo + 1:lo + 513], h2_fm[:, dc, lo:lo + 512])
                    vector.scalar_tensor_tensor(xr_b[:, b % 2, dc, :], d_stag[:, :], mur_t[:, dc:dc + 1],
                                                h2_fm[:, dc, lo:lo + 512], ALU.mult, ALU.add)
                    mx = vector.scalar_tensor_tensor(xk_b[:, b % 2, dc, :], d_stag[:, :], muk_t[:, dc:dc + 1],
                                                     h2_fm[:, dc, lo:lo + 512], ALU.mult, ALU.add)
                    if dc == DC - 1:
                        mx.then_inc(S.d_mix, 1)
            mix(0)
            mix(1)
            for b in range(NB):
                for jj in range(NB):
                    j = b * NB + jj
                    vector.wait_ge(S.d_vmm, j + 1)
                    vector.wait_ge(S.d_rev, j + 1)
                    vector.tensor_mul(r_tok[:, j % 2, :], r_tok[:, j % 2, :], v_ps[:, :]).then_inc(S.d_vev, 1)
                    vector.tensor_add(r_tok[:, j % 2, :], r_tok[:, j % 2, :], o_acc[:, j, :]).then_inc(S.d_oev, 1)
                if b + 2 < NB:
                    mix(b + 2)
            vector.sem_inc(S.fin, 1)

        @blk.tensor
        def _(tensor):
            tensor.wait_ge(S.init, 82 + 48)
            for b in range(NB):
                tensor.wait_ge(S.d_mix, b + 1)
                for c in range(DC):
                    rot = b * DC + c
                    ps = kfm_ps[rot % 2]
                    if rot >= 2:
                        tensor.wait_ge(S.d_krel, rot - 1)
                    for dc in range(DC):
                        mm = tensor.matmul(ps[:, :], wfk_sb[:, dc, c * 128:(c + 1) * 128],
                                           xk_b[:, b % 2, dc, :],
                                           start=(dc == 0), stop=(dc == DC - 1))
                        if dc == DC - 1:
                            mm.then_inc(S.d_kmm, 1)
                for jj in range(NB):
                    j = b * NB + jj
                    if j >= 1:
                        tensor.wait_ge(S.d_rev, j)
                    for dc in range(DC):
                        for half in range(2):
                            mm = tensor.matmul(r_ps[:, half * 512:(half + 1) * 512],
                                               xr_b[:, b % 2, dc, jj * 128:(jj + 1) * 128],
                                               wr_sb[:, dc, half * 512:(half + 1) * 512],
                                               start=(dc == 0), stop=(dc == DC - 1))
                            if dc == DC - 1 and half == 1:
                                mm.then_inc(S.d_rmm, 1)
                    if j >= 1:
                        tensor.wait_ge(S.d_vev, j)
                    tensor.wait_ge(S.d_kact, DC * (b + 1))
                    for dc in range(DC):
                        for half in range(2):
                            mm = tensor.matmul(v_ps[:, half * 512:(half + 1) * 512],
                                               kact_b[:, dc, jj * 128:(jj + 1) * 128],
                                               wfv_sb[:, dc, half * 512:(half + 1) * 512],
                                               start=(dc == 0), stop=(dc == DC - 1))
                            if dc == DC - 1 and half == 1:
                                mm.then_inc(S.d_vmm, 1)
                                if jj == NB - 1:
                                    tensor.sem_inc(S.d_rv, 1)
            tensor.sem_inc(S.fin, 1)

        @blk.scalar
        def _(scalar):
            for b in range(NB):
                for c in range(DC):
                    rot = b * DC + c
                    scalar.wait_ge(S.d_kmm, rot + 1)
                    scalar.activation(krelu[:, :], kfm_ps[rot % 2][:, :], AF.Relu).then_inc(S.d_krel, 1)
                    scalar.activation(kact_b[:, c, :], krelu[:, :], AF.Square).then_inc(S.d_kact, 1)
                for jj in range(NB):
                    j = b * NB + jj
                    scalar.wait_ge(S.d_rmm, j + 1)
                    if j >= 2:
                        scalar.wait_ge(S.y_st if j % 2 == 0 else S.y_st1, 16 * (j // 2))
                    scalar.activation(r_tok[:, j % 2, :], r_ps[:, :], AF.Sigmoid).then_inc(S.d_rev, 1)
            scalar.sem_inc(S.fin, 1)

        @blk.sync
        def _(sync):
            for j in range(J):
                sync.wait_ge(S.d_oev, j + 1)
                sync.dma_start(out=y[j * 128:(j + 1) * 128, :],
                               in_=r_tok[:, j % 2, :]).then_inc(
                                   S.y_st if j % 2 == 0 else S.y_st1, 16)
            sync.wait_ge(S.y_st, 16 * (J // 2))
            sync.wait_ge(S.y_st1, 16 * (J // 2))
            if dbg:
                sync.wait_ge(S.dbgs, 16 * 16)

    root.close()
    return nc


_NC_CACHE = {}


def _get_nc(dbg=False):
    if dbg not in _NC_CACHE:
        _NC_CACHE[dbg] = build(dbg)
    return _NC_CACHE[dbg]


def make_in_maps(inputs):
    asn = lambda a: np.ascontiguousarray(np.asarray(a, dtype=np.float32))
    x = asn(inputs["L0_attn_x"]).reshape(NCORES * T, D)
    common = {
        "wqT": asn(np.asarray(inputs["L0_attn_wq"]).T),
        "wkT": asn(np.asarray(inputs["L0_attn_wk"]).T),
        "wvT": asn(np.asarray(inputs["L0_attn_wv"]).T),
        "wrT": asn(np.asarray(inputs["L0_ffn_W_r"]).T),
        "wfkT": asn(np.asarray(inputs["L0_ffn_W_k"]).T),
        "wfvT": asn(np.asarray(inputs["L0_ffn_W_v"]).T),
        "bq": np.ascontiguousarray(asn(inputs["L0_attn_bq"]).reshape(DC, 128).T),
        "bk": np.ascontiguousarray(asn(inputs["L0_attn_bk"]).reshape(DC, 128).T),
        "bv": asn(inputs["L0_attn_bv"]).reshape(1, D),
        "mur": np.ascontiguousarray(asn(inputs["L0_ffn_mu_r"]).reshape(DC, 128).T),
        "muk": np.ascontiguousarray(asn(inputs["L0_ffn_mu_k"]).reshape(DC, 128).T),
        "ident": np.eye(128, dtype=np.float32),
    }
    in_maps = []
    for i in range(NCORES):
        sel = np.zeros((NCORES, 1), np.float32)
        if i > 0:
            sel[i - 1, 0] = 1.0
        im = dict(common)
        im["x_sh"] = np.ascontiguousarray(x[i * T:(i + 1) * T, :])
        im["bsel"] = sel
        in_maps.append(im)
    return in_maps


def kernel(**inputs):
    nc = _get_nc(False)
    in_maps = make_in_maps(inputs)
    res = run_bass_kernel_spmd(nc, in_maps, core_ids=list(range(NCORES)))
    out = np.concatenate([r["y"] for r in res.results], axis=0)
    return out.reshape(1, NCORES * T, D)
